# version 05
# baseline (speedup 1.0000x reference)
import numpy as np
import ml_dtypes
import concourse.bass as bass
import concourse.mybir as mybir
from concourse.bass_utils import run_bass_kernel_spmd

F32 = mybir.dt.float32
BF16 = mybir.dt.bfloat16
AF = mybir.ActivationFunctionType
ALU = mybir.AluOpType
AXX = mybir.AxisListType.X

D = 1024
T = 2048
KD = 8
DI = 2048
KI = 16
NH = 8
DH = 64
WIN = 512
HID = 2730
DTR = 64
DST = 16
NCH = 4
TC = 512
TH = 1024
EPS = 1e-5
QK_SCALE = 8.0
HT = [(j * 128, min(128, HID - j * 128)) for j in range(22)]
XSW = T + 8


def _f(x):
    return np.ascontiguousarray(np.asarray(x), dtype=np.float32)


def _bf(x):
    return np.ascontiguousarray(np.asarray(x, dtype=np.float32).astype(ml_dtypes.bfloat16))


def host_consts():
    c = {}
    c["cI"] = _bf(np.eye(128))
    R = np.zeros((128, 128), np.float32)
    for blk in range(2):
        for m in range(64):
            if m < 32:
                R[blk * 64 + m + 32, blk * 64 + m] = -1.0
            else:
                R[blk * 64 + m - 32, blk * 64 + m] = 1.0
    c["cR"] = _bf(R)
    half = np.arange(0, DH, 2, dtype=np.float64)
    inv_freq = 1.0 / 10000.0 ** (half / DH)
    t = np.arange(T, dtype=np.float64)
    freqs = np.concatenate([t[:, None] * inv_freq[None, :]] * 2, -1)
    base = (half + 0.4 * DH) / (1.4 * DH)
    power = (t - T // 2) / float(WIN // 2)
    scale = np.concatenate([base[None, :] ** power[:, None]] * 2, -1)
    cos, sin = np.cos(freqs), np.sin(freqs)
    rep = lambda a: np.concatenate([a.T, a.T], 0)
    c["cT1q"] = _bf(rep(cos * scale))
    c["cT2q"] = _bf(rep(sin * scale))
    c["cT1k"] = _bf(rep(cos / scale))
    c["cT2k"] = _bf(rep(sin / scale))
    idx = np.arange(128)
    MgA = np.zeros((128, 640), np.float32)
    for r in range(128):
        kr = np.arange(640)
        MgA[r, :] = np.where((kr >= r) & (kr <= r + 512), 0.0, -1.0e30)
    c["cMgA"] = _bf(MgA)
    c["cM0A"] = _bf(np.where(idx[None, :] <= idx[:, None], 0.0, -1.0e30))
    c["cOnesB"] = _bf(np.ones((128, 1)))
    c["cOnesRowF"] = _f(np.ones((1, 128)))
    c["cOnesRowB"] = _bf(np.ones((1, 128)))
    B2 = np.zeros((128, 2), np.float32)
    B2[:64, 0] = 1.0
    B2[64:, 1] = 1.0
    c["cB2b"] = _bf(B2)
    B2T = np.zeros((2, 128), np.float32)
    B2T[0, :64] = 1.0
    B2T[1, 64:] = 1.0
    c["cB2T"] = _bf(B2T)
    c["cMisc"] = _f(np.tile(np.array([[EPS, 0.0]], np.float32), (128, 1)))
    return c


CONST_SPECS = {
    "cI": ([128, 128], BF16), "cR": ([128, 128], BF16),
    "cMgA": ([128, 640], BF16), "cM0A": ([128, 128], BF16),
    "cOnesB": ([128, 1], BF16), "cOnesRowF": ([1, 128], F32),
    "cOnesRowB": ([1, 128], BF16),
    "cB2b": ([128, 2], BF16), "cB2T": ([2, 128], BF16),
    "cMisc": ([128, 2], F32),
}
ROT_SPECS = {
    "cT1q": ([128, T], BF16), "cT2q": ([128, T], BF16),
    "cT1k": ([128, T], BF16), "cT2k": ([128, T], BF16),
}

WEIGHT_SPECS = {
    "L0_attn_gamma": [D], "L0_attn_beta": [D],
    "L0_attn_in_w": [D, 2 * DI],
    "L0_attn_conv_w": [DI, 4], "L0_attn_conv_b": [DI],
    "L0_attn_xproj_w": [DI, 96], "L0_attn_dt_w": [DTR, DI], "L0_attn_dt_b": [DI],
    "L0_attn_A_log": [DI, DST], "L0_attn_D_skip": [DI],
    "L0_attn_conv_w_b": [DI, 4], "L0_attn_conv_b_b": [DI],
    "L0_attn_xproj_w_b": [DI, 96], "L0_attn_dt_w_b": [DTR, DI], "L0_attn_dt_b_b": [DI],
    "L0_attn_A_log_b": [DI, DST], "L0_attn_D_skip_b": [DI],
    "L0_attn_merge_w": [2 * DI, DI], "L0_attn_out_w": [DI, D],
    "L0_ffn_wqkv": [2, D, 1536], "L0_ffn_q_scale": [2, DH], "L0_ffn_k_scale": [2, DH],
    "L0_ffn_wo": [2, 512, D], "L0_ffn_ln_g": [2, D], "L0_ffn_ln_b": [2, D],
    "L0_ffn_w1": [2, D, 2 * HID], "L0_ffn_b1": [2, 2 * HID],
    "L0_ffn_w2": [2, HID, D], "L0_ffn_b2": [2, D],
}


class EngProxy:

    _PASS = {"wait_ge", "drain", "nop", "dma_start", "sem_inc"}

    def __init__(self, prog, key, raw):
        self.prog = prog
        self.key = key
        self.raw = raw

    def __getattr__(self, name):
        fn = getattr(self.raw, name)
        if name in self._PASS or not callable(fn):
            return fn
        key = self.key
        prog = self.prog

        def wrapped(*args, **kwargs):
            aps = [a for a in args if isinstance(a, bass.AP)]
            aps += [v for v in kwargs.values() if isinstance(v, bass.AP)]
            dirty = prog.dirty[key]
            if aps:
                prog.sync_marks(self.raw, aps[0])
                ins_names = {a.tensor.name for a in aps[1:]}
                if ins_names & dirty:
                    self.raw.drain()
                    dirty.clear()
            instr = fn(*args, **kwargs)
            if aps:
                dirty.add(aps[0].tensor.name)
            return instr

        return wrapped


class Prog:
    def __init__(self, nc, want_dumps=()):
        self.nc = nc
        self._cms = {}
        self.sem = {}
        self.cnt = {}
        for name in ["t", "v", "s", "g", "y", "dy", "dg", "ds"]:
            cm = nc.semaphore("sem_" + name)
            self.sem[name] = cm.__enter__()
            self._cms["sem_" + name] = cm
            self.cnt[name] = 0
        self.PE = nc.tensor
        self.V = EngProxy(self, "v", nc.vector)
        self.S = EngProxy(self, "s", nc.scalar)
        self.G = EngProxy(self, "g", nc.gpsimd)
        self.SY = nc.sync
        self.dirty = {"v": set(), "s": set(), "g": set()}
        self._eng = {"t": self.PE, "v": self.V, "s": self.S, "g": self.G, "y": self.SY}
        self.pending_sync = {}
        self.dumps = []
        self.dump_off = 0
        self.want_dumps = set(want_dumps)
        self.dbg = None
        self.consts = {}
        self.dram = {}
        self.par = {}
        self.uid = 0
        self.stop_flag = False
        self.stop_after = None

    ARENA_BYTES = 206 * 1024

    def _ensure_arena(self):
        if not hasattr(self, "arena_base"):
            cm = self.nc.sbuf_tensor("big_arena", [128, self.ARENA_BYTES], mybir.dt.uint8)
            h = cm.__enter__()
            self._cms["big_arena"] = cm
            self.arena_base = self.nc.lookup_mloc(h).addr
            self.freelist = [(0, self.ARENA_BYTES)]
            self.alloc_info = {}

    def sbuf(self, name, shape, dtype, hi=False):
        self._ensure_arena()
        per = int(np.prod(shape[1:])) * mybir.dt.size(dtype)
        per = (per + 31) // 32 * 32
        ranges = list(enumerate(self.freelist))
        if hi:
            ranges = ranges[::-1]
        for idx, (off, sz) in ranges:
            if sz >= per:
                break
        else:
            amap = sorted((v[0], v[1], k) for k, v in self.alloc_info.items())
            raise MemoryError(f"SBUF arena full for {name} ({per}B): {self.freelist}\n" +
                              "\n".join(f"  {o:7d} +{s:6d} {k}" for o, s, k in amap))
        if sz == per:
            del self.freelist[idx]
        elif hi:
            off = off + sz - per
            self.freelist[idx] = (self.freelist[idx][0], sz - per)
        else:
            self.freelist[idx] = (off + per, sz - per)
        self.uid += 1
        name = f"{name}_{self.uid}"
        h = self.nc.virtual_sbuf_tensor(name, list(shape), dtype)
        self.nc._tensor(h.name, list(shape), dtype, type="SB")
        mloc = self.nc.lookup_mloc(h)
        mloc.allocated = True
        mloc.addr = self.arena_base + off
        self.alloc_info[h.name] = (off, per)
        self.pending_sync[h.name] = dict(self.cnt)
        return h

    def psum(self, name, shape, dtype=F32):
        self.uid += 1
        name = f"{name}_{self.uid}"
        esz = mybir.dt.size(dtype)
        free = int(np.prod(shape[1:]))
        pad = (-(free * esz) % 2048) // esz
        cm = self.nc.psum_tensor(name, [128, free + pad], dtype)
        t = cm.__enter__()
        self._cms[name] = (self.uid, cm)
        return t

    def free(self, *handles):
        psums = []
        for h in handles:
            name = h.name if hasattr(h, "name") else h.tensor.name
            if name in self.alloc_info:
                off, per = self.alloc_info.pop(name)
                self.freelist.append((off, per))
            else:
                psums.append(name)
        self.freelist.sort()
        merged = []
        for off, sz in self.freelist:
            if merged and merged[-1][0] + merged[-1][1] == off:
                merged[-1] = (merged[-1][0], merged[-1][1] + sz)
            else:
                merged.append((off, sz))
        self.freelist = [tuple(x) for x in merged]
        for name in sorted(psums, key=lambda n: self._cms[n][0], reverse=True):
            _, cm = self._cms.pop(name)
            cm.__exit__(None, None, None)

    def tick(self, name, instr, k=1):
        if name in ("v", "s", "g"):
            instr = self._eng[name].raw.drain()
            self.dirty[name].clear()
        elif name == "t":
            instr = self.PE.drain()
        instr.then_inc(self.sem[name], k)
        self.cnt[name] += k

    def wait(self, who, dep, val=None):
        self._eng[who].wait_ge(self.sem[dep], self.cnt[dep] if val is None else val)

    def waitv(self, who, dep, val):
        if val:
            self._eng[who].wait_ge(self.sem[dep], val)

    def sync_marks(self, eng, out_ap):
        name = out_ap.tensor.name
        marks = self.pending_sync.pop(name, None)
        if marks:
            for s, v in marks.items():
                if v > 0:
                    eng.wait_ge(self.sem[s], v)

    def sdma(self, qname, sname, out, in_, slow=False):
        if sname not in self.sem:
            cm = self.nc.semaphore("sem_" + sname)
            self.sem[sname] = cm.__enter__()
            self._cms["sem_" + sname] = cm
            self.cnt[sname] = 0
        if hasattr(out, "tensor") and out.tensor.name in self.pending_sync:
            self.sync_marks(self._eng[qname] if qname == "y" else self._eng[qname].raw, out)
        i = self._eng[qname].dma_start(out, in_, allow_slow_non_contiguous=slow)
        i.then_inc(self.sem[sname], 16)
        self.cnt[sname] += 16
        return i

    def dma(self, qname, out, in_, slow=False):
        if hasattr(out, "tensor") and out.tensor.name in self.pending_sync:
            self.sync_marks(self._eng[qname] if qname == "y" else self._eng[qname].raw, out)
        i = self._eng[qname].dma_start(out, in_, allow_slow_non_contiguous=slow)
        self.tick("d" + qname, i, 16)
        return i

    def barrier(self, engines="tvsgy"):
        marks = {}
        for e in engines:
            i = self._eng[e].nop()
            self.tick(e, i, 1)
            marks[e] = self.cnt[e]
        for e in engines:
            for o in engines:
                if o != e:
                    self._eng[e].wait_ge(self.sem[o], marks[o])
            for q in ["dy", "dg", "ds"]:
                self._eng[e].wait_ge(self.sem[q], self.cnt[q])

    def dump(self, tag, ap, producer="v", parts=128):
        if tag not in self.want_dumps:
            return
        w = int(np.prod(ap.shape)) // int(ap.shape[0])
        q = "y" if "float32" in str(ap.dtype) else "g"
        for e in "tvsg":
            self.tick(e, self._eng[e].nop())
            self.wait(q, e)
        self.dma(q, self.dbg[0:parts, self.dump_off:self.dump_off + w], ap)
        self.pending_sync[ap.tensor.name] = dict(self.cnt)
        self.dumps.append((tag, self.dump_off, parts, w, str(ap.dtype)))
        self.dump_off += w

    def stop(self, stage):
        if self.stop_after is not None and stage == self.stop_after:
            self.stop_flag = True
        return self.stop_flag



def emit_ln(P, src_tiles, out_writer, gamma_ap=None, beta_ap=None, fuse_double=False,
            keep_ab=False, direct_dst=None):
    need_cast = "float32" in str(src_tiles[0].dtype)
    sq = P.sbuf("ln_sq", [128, 2 * TC], BF16)
    xb = P.sbuf("ln_xb", [128, 2 * TC], BF16) if need_cast else None
    rows = P.sbuf("ln_rows", [1, 3 * T], F32)
    ab = P.sbuf("ln_ab", [128, 2 * T], F32)
    ab2 = P.sbuf("ln_ab2", [128, 2 * T], F32) if (fuse_double and keep_ab) else None
    ps_s = P.psum("ln_ps_s", [1, TC], F32)
    ps_q = P.psum("ln_ps_q", [1, TC], F32)
    ps_b = P.psum("ln_ps_b", [128, TC], F32)
    ones = P.consts["cOnesB"]

    for c in range(NCH):
        csl = slice(c * TC, (c + 1) * TC)
        for kt in range(KD):
            par = kt % 2
            psl = slice(par * TC, (par + 1) * TC)
            src = src_tiles[kt][:, csl]
            P.wait("s", "t")
            if need_cast:
                i = P.S.activation(xb[:, psl], src, AF.Identity)
                P.tick("s", i)
                xsrc = xb[:, psl]
            else:
                xsrc = src
            i = P.S.activation(sq[:, psl], src, AF.Square)
            P.tick("s", i)
            P.wait("t", "s")
            i = P.PE.matmul(ps_s[0:1, :], ones[:, 0:1], xsrc, start=(kt == 0), stop=(kt == KD - 1))
            i = P.PE.matmul(ps_q[0:1, :], ones[:, 0:1], sq[:, psl], start=(kt == 0), stop=(kt == KD - 1))
            P.tick("t", i)
        P.wait("v", "t")
        i = P.V.tensor_copy(rows[0:1, csl], ps_s[0:1, :])
        i = P.V.tensor_copy(rows[0:1, T + c * TC:T + (c + 1) * TC], ps_q[0:1, :])
        P.tick("v", i)
        P.wait("t", "v")

    muA = rows[0:1, 0:T]
    rB = rows[0:1, T:2 * T]
    bZ = rows[0:1, 2 * T:3 * T]
    vB = rB
    i = P.V.tensor_scalar_mul(muA, rows[0:1, 0:T], 1.0 / D)
    P.tick("v", i)
    P.wait("s", "v")
    i = P.S.activation(bZ, muA, AF.Square)
    P.tick("s", i)
    P.wait("v", "s")
    i = P.V.scalar_tensor_tensor(vB, vB, 1.0 / D, bZ, ALU.mult, ALU.subtract)
    P.tick("v", i)

    brow = P.sbuf("ln_brow", [1, T], F32)

    def bcast_pair(dst_ab):
        P.wait("v", "s")
        i = P.V.tensor_mul(brow[0:1, :], muA, rB_cur[0])
        P.tick("v", i)
        P.wait("y", "v")
        P.dma("y", P.dram["scr_rows"][0:1, :], rB_cur[0])
        P.dma("y", P.dram["scr_rows"][1:2, :], brow[0:1, :])
        P.wait("g", "dy")
        P.dma("g", dst_ab[:, 0:T], P.dram["scr_rows"][0:1, :].broadcast_to([128, T]))
        P.dma("g", dst_ab[:, T:2 * T], P.dram["scr_rows"][1:2, :].broadcast_to([128, T]))
        P.wait("v", "dg")
        P.wait("s", "dg")

    P.wait("s", "v")
    i = P.S.activation(bZ, vB, AF.Ln, bias=EPS)
    P.S.drain()
    i = P.S.activation(bZ, bZ, AF.Exp, scale=-0.5)
    P.tick("s", i)
    rB_cur = [bZ]
    if fuse_double:
        if ab2 is not None:
            bcast_pair(ab2)
        w1 = P.sbuf("ln_w1", [1, 2 * T], F32)
        wa = w1[0:1, 0:T]
        wb = w1[0:1, T:2 * T]
        i = P.S.activation(wa, vB, AF.Ln)
        i = P.S.activation(wb, vB, AF.Ln, bias=EPS)
        P.tick("s", i)
        P.wait("v", "s")
        i = P.V.tensor_sub(wa, wa, wb)
        P.tick("v", i)
        P.wait("s", "v")
        i = P.S.activation(wa, wa, AF.Exp)
        P.tick("s", i)
        P.wait("v", "s")
        i = P.V.tensor_scalar_add(wa, wa, EPS)
        P.tick("v", i)
        P.wait("s", "v")
        i = P.S.activation(wa, wa, AF.Ln)
        P.S.drain()
        i = P.S.activation(wa, wa, AF.Exp, scale=-0.5)
        P.tick("s", i)
        P.wait("v", "s")
        i = P.V.tensor_mul(bZ, bZ, wa)
        P.tick("v", i)
        P.free(w1)
    bcast_pair(ab)

    P.dump("lnab", ab[:, :], "v")
    a_b = ab[:, 0:T]
    b_b = ab[:, T:2 * T]
    tmp = P.sbuf("ln_tmp", [128, T], F32) if direct_dst is None else None
    for kt in range(KD):
        P.wait("v", "s")
        dst = tmp[:, :] if direct_dst is None else direct_dst(kt)
        i = P.V.tensor_mul(dst, src_tiles[kt], a_b)
        i = P.V.tensor_sub(dst, dst, b_b)
        P.tick("v", i)
        if out_writer is not None:
            out_writer(kt, dst,
                       None if gamma_ap is None else gamma_ap[:, kt:kt + 1],
                       None if beta_ap is None else beta_ap[:, kt:kt + 1])
    P.free(sq, rows, brow, ps_s, ps_q, ps_b)
    if tmp is not None:
        P.free(tmp)
    if xb is not None:
        P.free(xb)
    if keep_ab:
        return ab, ab2
    P.free(ab)
    return None


def emit_shift(P, xp, dst):
    for kt in range(KD):
        if kt < 4:
            i = P.S.activation(dst[:, kt * T:(kt + 1) * T], xp[:, kt * T:(kt + 1) * T], AF.Identity)
        else:
            i = P.S.activation(dst[:, kt * T + 1:(kt + 1) * T], xp[:, kt * T:kt * T + T - 1],
                               AF.Identity)
        P.tick("s", i)
    P.wait("v", "s")
    for kt in range(4, KD):
        i = P.V.memset(dst[:, kt * T:kt * T + 1], 0.0)
    P.tick("v", i)



def emit_ffn_layer(P, l, xp, wsb, dump_pref=""):
    xpt = lambda kt: xp[:, kt * T:(kt + 1) * T]

    sh = P.sbuf("ffn_sh", [128, KD * T], BF16, hi=True)
    emit_shift(P, xp, sh)
    P.dump(dump_pref + "shift1", sh[:, :], "v")
    wqkv = P.sbuf("ffc_wqkv", [128, KD * 1536], BF16)
    P.wait("g", "t")
    P.sdma("g", "wqS", wqkv[:, :].rearrange("p (kt m) -> p kt m", kt=KD),
           P.dram["L0_ffn_wqkv"][l].rearrange("(kt p) m -> p kt m", p=128))

    q4 = P.sbuf("ffn_q4", [128, 4 * T], BF16)
    k4 = P.sbuf("ffn_k4", [128, 4 * T], BF16)
    sqb = P.sbuf("ffn_sqb", [128, TC], BF16)
    tt1 = P.sbuf("ffn_tt1", [128, TC], F32)
    s2 = P.sbuf("ffn_s2", [2, T], F32)
    rbf2 = P.sbuf("ffn_rbf2", [2, T], BF16)
    rb = P.sbuf("ffn_rb", [128, T], BF16)
    qn = P.sbuf("ffn_qn", [128, T], BF16)
    t2t = P.sbuf("ffn_t2t", [128, TC], F32)
    ps_m = P.psum("ffn_psm", [128, TC], F32)
    ps_m2 = P.psum("ffn_psm2", [128, TC], F32)
    ps_n = P.psum("ffn_psn", [2, TC], F32)
    ps_r = P.psum("ffn_psr", [128, TC], F32)
    P.wait("t", "wqS")
    rtab = P.sbuf("ffn_rtab", [128, 2 * T], BF16)

    snap = {}
    for side, (out4, scl, tn1, tn2) in enumerate(
            [(q4, wsb["qs"], "cT1q", "cT2q"),
             (k4, wsb["ks"], "cT1k", "cT2k")]):
        t1c = rtab[:, 0:T]
        t2c = rtab[:, T:2 * T]
        P.wait("y", "v")
        P.sdma("y", "rtS", t1c, P.dram[tn1])
        P.sdma("y", "rtS", t2c, P.dram[tn2])
        P.wait("v", "rtS")
        for mt in range(4):
            col0 = side * 512 + mt * 128
            for c in range(NCH):
                csl = slice(c * TC, (c + 1) * TC)
                psx = ps_m if c % 2 == 0 else ps_m2
                P.waitv("t", "v", snap.get(("qv", c % 2)))
                P.waitv("t", "s", snap.get(("qs", c % 2)))
                for kt in range(KD):
                    i = P.PE.matmul(psx[:, :], wqkv[:, kt * 1536 + col0:kt * 1536 + col0 + 128],
                                    sh[:, kt * T + c * TC:kt * T + (c + 1) * TC],
                                    start=(kt == 0), stop=(kt == KD - 1))
                P.tick("t", i)
                P.wait("s", "t")
                i = P.S.activation(sqb[:, :], psx[:, :], AF.Square)
                P.tick("s", i)
                snap[("qs", c % 2)] = P.cnt["s"]
                P.wait("t", "s")
                P.waitv("t", "v", snap.get("psn_v"))
                i = P.PE.matmul(ps_n[0:2, :], P.consts["cB2b"][:, :], sqb[:, :], start=True, stop=True)
                P.tick("t", i)
                P.wait("v", "t")
                i = P.V.tensor_copy(s2[:, csl], ps_n[0:2, :])
                i = P.V.tensor_scalar_mul(qn[:, csl], psx[:, :], scl)
                P.tick("v", i)
                snap[("qv", c % 2)] = P.cnt["v"]
                snap["psn_v"] = P.cnt["v"]
            i = P.V.tensor_scalar_max(s2[:, :], s2[:, :], 1e-24)
            P.tick("v", i)
            P.wait("s", "v")
            i = P.S.activation(s2[:, :], s2[:, :], AF.Ln)
            P.S.drain()
            i = P.S.activation(rbf2[:, :], s2[:, :], AF.Exp, scale=-0.5)
            P.tick("s", i)
            for c in range(NCH):
                csl = slice(c * TC, (c + 1) * TC)
                P.wait("t", "s")
                P.wait("t", "v")
                i = P.PE.matmul(ps_r[:, :], P.consts["cB2T"][:, :], rbf2[:, csl],
                                start=True, stop=True)
                P.tick("t", i)
                P.wait("v", "t")
                i = P.V.tensor_copy(rb[:, csl], ps_r[:, :])
                P.tick("v", i)
                P.wait("t", "v")
            for c in range(NCH):
                csl = slice(c * TC, (c + 1) * TC)
                P.wait("t", "v")
                i = P.PE.matmul(ps_r[:, :], P.consts["cR"][:, :], qn[:, csl], start=True, stop=True)
                P.tick("t", i)
                P.wait("v", "t")
                i = P.V.tensor_tensor(t2t[:, :], ps_r[:, :], t2c[:, csl], ALU.mult)
                P.tick("v", i)
                P.wait("t", "v")
                i = P.V.tensor_tensor(tt1[:, :], qn[:, csl], t1c[:, csl], ALU.mult)
                i = P.V.tensor_add(tt1[:, :], tt1[:, :], t2t[:, :])
                i = P.V.tensor_tensor(out4[:, mt * T + c * TC:mt * T + (c + 1) * TC],
                                      tt1[:, :], rb[:, csl], ALU.mult)
                P.tick("v", i)
    P.dump(dump_pref + "q4", q4[:, :], "v")
    P.dump(dump_pref + "k4", k4[:, :], "v")
    if P.stop(dump_pref + "qk"):
        return xp

    P.free(sqb, tt1, s2, rbf2, rb, qn, t2t, rtab, ps_n, ps_r)
    vtm = P.sbuf("ffn_vtm", [128, 16 * 512], BF16)
    if P.stop(dump_pref + "v0"):
        return xp
    snap = {}
    for tt in range(16):
        psx = ps_m if tt % 2 == 0 else ps_m2
        P.waitv("t", "v", snap.get(tt % 2))
        for kt in range(KD):
            i = P.PE.matmul(psx[:, :], sh[:, kt * T + tt * 128:kt * T + (tt + 1) * 128],
                            wqkv[:, kt * 1536 + 1024:kt * 1536 + 1536],
                            start=(kt == 0), stop=(kt == KD - 1))
        P.tick("t", i)
        P.wait("v", "t")
        i = P.V.tensor_copy(vtm[:, tt * 512:tt * 512 + 512], psx[:, :])
        P.tick("v", i)
        snap[tt % 2] = P.cnt["v"]
    P.free(sh, wqkv, ps_m, ps_m2)
    P.dump(dump_pref + "vtm", vtm[:, :], "s")
    if P.stop(dump_pref + "v"):
        return xp

    smax = P.sbuf("ffn_smax", [128, 1], F32)
    nmax = P.sbuf("ffn_nmax", [128, 1], F32)
    smf = P.sbuf("ffn_smf", [128, 640], F32)
    Pb = P.sbuf("ffn_P", [128, 640], BF16)
    PT = P.sbuf("ffn_PT", [128, 8 * 512], BF16)
    rsr = P.sbuf("ffn_rsr", [1, TC], F32)
    rbq = P.sbuf("ffn_rbq", [128, TC], F32)
    osb = P.sbuf("ffn_o", [128, 4 * T], BF16)
    ps_sim = P.psum("ffn_pssim", [128, 640], F32)
    ps_pt = P.psum("ffn_pspt", [128, 640], BF16)
    ps_o = P.psum("ffn_pso", [128, TC], F32)
    ps_rb = P.psum("ffn_psrb", [128, TC], F32)
    ps_row = P.psum("ffn_psrow", [1, TC], F32)

    for h in range(NH):
        rt, po = h // 2, (h % 2) * 64
        for qw in range(4):
            for qt in range(4):
                i_abs = qw * 4 + qt
                if qw == 0:
                    nblk, ktok0, slot0 = qt + 1, 0, 4
                else:
                    nblk, ktok0, slot0 = 5, (qw - 1) * 4 + qt, 0
                width = nblk * 128
                lhs = q4[po:po + 64, rt * T + i_abs * 128:rt * T + (i_abs + 1) * 128]
                P.wait("t", "v")
                P.wait("t", "s")
                off = 0
                while off < width:
                    n = min(TC, width - off)
                    i = P.PE.matmul(ps_sim[:, off:off + n], lhs,
                                    k4[po:po + 64,
                                       rt * T + ktok0 * 128 + off:rt * T + ktok0 * 128 + off + n],
                                    start=True, stop=True)
                    off += n
                P.tick("t", i)
                P.wait("v", "t")
                if qw == 0:
                    if nblk > 1:
                        i = P.V.tensor_copy(smf[:, 0:width - 128], ps_sim[:, 0:width - 128])
                    i = P.V.tensor_tensor(smf[:, width - 128:width],
                                          ps_sim[:, width - 128:width],
                                          P.consts["cM0A"][:, :], ALU.add)
                else:
                    i = P.V.tensor_tensor(smf[:, 0:width], ps_sim[:, 0:width],
                                          P.consts["cMgA"][:, 0:width], ALU.add)
                i = P.V.reduce_max(smax[:, :], smf[:, 0:width], axis=AXX)
                i = P.V.tensor_scalar_mul(nmax[:, :], smax[:, :], -QK_SCALE)
                P.tick("v", i)
                P.wait("s", "v")
                i = P.S.activation(Pb[:, 0:width], smf[:, 0:width], AF.Exp,
                                   bias=nmax[:, 0:1], scale=QK_SCALE)
                P.tick("s", i)
                if h == 0 and qw == 1 and qt == 0:
                    P.dump(dump_pref + "att_sim", smf[:, 0:width], "v")
                    P.dump(dump_pref + "att_P", Pb[:, 0:width], "s")
                P.wait("t", "s")
                P.wait("t", "v")
                for j in range(nblk):
                    i = P.PE.transpose(ps_pt[:, j * 128:(j + 1) * 128], Pb[:, j * 128:(j + 1) * 128],
                                       P.consts["cI"][:, :])
                P.tick("t", i)
                P.wait("v", "t")
                qcol = qt * 128
                written = set()
                for j in range(nblk):
                    slot = slot0 + (qt + j if qw > 0 else j)
                    written.add(slot)
                    dst = PT[:, slot * 512 + qcol:slot * 512 + qcol + 128]
                    i = P.V.tensor_copy(dst, ps_pt[:, j * 128:(j + 1) * 128])
                allslots = range(0, 8) if qw > 0 else range(4, 8)
                for slot in allslots:
                    if slot not in written:
                        i = P.V.memset(PT[:, slot * 512 + qcol:slot * 512 + qcol + 128], 0.0)
                P.tick("v", i)
            if h == 1 and qw == 1:
                P.dump(dump_pref + "att_PT", PT[:, :], "v")
            slots = list(range(0, 8)) if qw > 0 else list(range(4, 8))
            first = True
            P.wait("t", "v")
            for s_ in slots:
                vt = (s_ - 4) if qw == 0 else (qw - 1) * 4 + s_
                lhsv = vtm[:, vt * 512 + h * 64:vt * 512 + h * 64 + 64]
                i = P.PE.matmul(ps_o[po:po + 64, :], lhsv, PT[:, s_ * 512:(s_ + 1) * 512],
                                start=first, stop=(s_ == slots[-1]),
                                tile_position=(0, po))
                i = P.PE.matmul(ps_row[0:1, :], P.consts["cOnesB"][:, 0:1],
                                PT[:, s_ * 512:(s_ + 1) * 512],
                                start=first, stop=(s_ == slots[-1]))
                first = False
            P.tick("t", i)
            if h == 1 and qw == 1:
                P.wait("v", "t")
                tdbg = P.sbuf("att_tdbg", [128, TC], F32)
                i = P.V.tensor_copy(tdbg[64:128, :], ps_o[64:128, :])
                P.tick("v", i)
                P.dump(dump_pref + "att_pso", tdbg[64:128, :], "v", parts=64)
                P.free(tdbg)
            P.wait("s", "t")
            i = P.S.activation(rsr[:, :], ps_row[0:1, :], AF.Ln)
            P.S.drain()
            i = P.S.activation(rsr[:, :], rsr[:, :], AF.Exp, scale=-1.0)
            P.tick("s", i)
            P.wait("t", "s")
            i = P.PE.matmul(ps_rb[po:po + 64, :], P.consts["cOnesRowF"][0:1, 0:64], rsr[:, :],
                            start=True, stop=True, tile_position=(0, po))
            P.tick("t", i)
            P.wait("v", "t")
            i = P.V.tensor_tensor(osb[po:po + 64, rt * T + qw * TC:rt * T + (qw + 1) * TC],
                                  ps_o[po:po + 64, :], ps_rb[po:po + 64, :], ALU.mult) \
                if False else None
            i = P.V.tensor_copy(rbq[po:po + 64, :], ps_rb[po:po + 64, :])
            P.tick("v", i)
            if h == 1 and qw == 1:
                P.dump(dump_pref + "att_rs", rbq[64:128, :], "v", parts=64)
            i = P.V.tensor_tensor(osb[po:po + 64, rt * T + qw * TC:rt * T + (qw + 1) * TC],
                                  ps_o[po:po + 64, :], rbq[po:po + 64, :], ALU.mult)
            P.tick("v", i)
            P.wait("t", "v")
    P.free(q4, k4, vtm, smax, nmax, smf, Pb, PT, rsr, rbq)
    P.free(ps_sim, ps_pt, ps_o, ps_rb, ps_row)
    P.dump(dump_pref + "o", osb[:, :], "v")
    if P.stop(dump_pref + "attn"):
        return xp

    wo = P.sbuf("ffc_wo", [128, 4 * D], BF16)
    P.sdma("g", "woS", wo[:, :].rearrange("p (ik m) -> p ik m", ik=4),
           P.dram["L0_ffn_wo"][l].rearrange("(ik p) m -> p ik m", p=128))
    ps_a = P.psum("ffn_psa", [128, TC], F32)
    ps_b2 = P.psum("ffn_psb", [128, TC], F32)
    P.wait("t", "woS")
    snap = {}
    for kt in range(KD):
        for c in range(NCH):
            par = (kt * NCH + c) % 2
            pss = ps_a if par == 0 else ps_b2
            P.waitv("t", "v", snap.get(par))
            for ik in range(4):
                i = P.PE.matmul(pss[:, :], wo[:, ik * D + kt * 128:ik * D + (kt + 1) * 128],
                                osb[:, ik * T + c * TC:ik * T + (c + 1) * TC],
                                start=(ik == 0), stop=(ik == 3))
            P.tick("t", i)
            P.wait("v", "t")
            i = P.V.tensor_add(xpt(kt)[:, c * TC:(c + 1) * TC],
                               xpt(kt)[:, c * TC:(c + 1) * TC], pss[:, :])
            P.tick("v", i)
            snap[par] = P.cnt["v"]
    P.free(osb, wo, ps_a, ps_b2)
    P.dump(dump_pref + "x_attn", xp[:, :], "v")
    if P.stop(dump_pref + "wo"):
        return xp

    lnx = P.sbuf("ffn_lnx", [128, KD * T], BF16, hi=True)
    sh2 = P.sbuf("ffn_sh2", [128, KD * T], BF16, hi=True)
    emit_shift(P, xp, sh2)

    def wr(kt, tmp, g, b):
        P.wait("s", "v")
        i = P.S.activation(lnx[:, kt * T:(kt + 1) * T], tmp[:, :], AF.Identity, bias=b, scale=g)
        P.tick("s", i)

    emit_ln(P, [sh2[:, kt * T:(kt + 1) * T] for kt in range(KD)], wr,
            gamma_ap=wsb["lng"], beta_ap=wsb["lnb"])
    P.free(sh2)
    P.dump(dump_pref + "lnx2", lnx[:, :], "s")
    P.tick("v", P.V.nop())
    P.wait("y", "v")
    P.dma("y", P.dram["scr_xp"].rearrange("(kt p) w -> p kt w", p=128),
          xp[:, :].rearrange("p (kt w) -> p kt w", kt=KD))
    P.wait("y", "dy")
    P.free(xp)
    if P.stop(dump_pref + "ln2i"):
        return None

    gated = P.sbuf("ffn_gated", [128, 22 * T], BF16)
    w1a = P.sbuf("ffn_w1a", [128, 2 * KD * 128], BF16)
    w1g = P.sbuf("ffn_w1g", [128, 2 * KD * 128], BF16)
    ps_w = [P.psum("ffn_psw%d" % k, [128, TC], F32) for k in range(4)]
    w1d = P.dram["L0_ffn_w1"]
    snap = {}
    for j, (off, sz) in enumerate(HT):
        par = j % 2
        base = par * KD * 128
        sname = "w1p%d" % par
        P.waitv("g", "t", snap.get(("t", par)))
        P.sdma("g", sname, w1a[:, base:base + KD * sz].rearrange("p (kt m) -> p kt m", kt=KD),
               w1d[l, :, off:off + sz].rearrange("(kt p) m -> p kt m", p=128))
        P.sdma("g", sname, w1g[:, base:base + KD * sz].rearrange("p (kt m) -> p kt m", kt=KD),
               w1d[l, :, HID + off:HID + off + sz].rearrange("(kt p) m -> p kt m", p=128))
        dgs = P.cnt[sname]
        for c in range(NCH):
            cp = c % 2
            ps_wa, ps_wg = ps_w[cp * 2], ps_w[cp * 2 + 1]
            P.waitv("t", sname, dgs)
            P.waitv("t", "v", snap.get(("v", cp)))
            for kt in range(KD):
                i = P.PE.matmul(ps_wa[0:sz, :], w1a[:, base + kt * sz:base + (kt + 1) * sz],
                                lnx[:, kt * T + c * TC:kt * T + (c + 1) * TC],
                                start=(kt == 0), stop=(kt == KD - 1))
            for kt in range(KD):
                i = P.PE.matmul(ps_wg[0:sz, :], w1g[:, base + kt * sz:base + (kt + 1) * sz],
                                lnx[:, kt * T + c * TC:kt * T + (c + 1) * TC],
                                start=(kt == 0), stop=(kt == KD - 1))
            P.tick("t", i)
            gsl = gated[0:sz, j * T + c * TC:j * T + (c + 1) * TC]
            P.wait("s", "t")
            i = P.S.activation(gsl, ps_wg[0:sz, :], AF.Gelu, bias=wsb["b1g"][0:sz, j:j + 1])
            P.tick("s", i)
            P.wait("v", "s")
            i = P.V.scalar_tensor_tensor(gsl, ps_wa[0:sz, :], wsb["b1a"][0:sz, j:j + 1], gsl,
                                         ALU.add, ALU.mult)
            P.tick("v", i)
            snap[("v", cp)] = P.cnt["v"]
        snap[("t", par)] = P.cnt["t"]
    P.free(w1a, w1g, *ps_w)
    P.free(lnx)
    P.dump(dump_pref + "gated", gated[:, :], "v")
    if P.stop(dump_pref + "gated"):
        return None

    xp = P.sbuf("xp_arena", [128, KD * T], F32, hi=True)
    P.dma("y", xp[:, :].rearrange("p (kt w) -> p kt w", kt=KD),
          P.dram["scr_xp"].rearrange("(kt p) w -> p kt w", p=128))
    w2s = P.sbuf("ffn_w2s", [128, 2 * 22 * 128], BF16)
    ps_a2 = P.psum("ffn_ps2a", [128, TC], F32)
    ps_b3 = P.psum("ffn_ps2b", [128, TC], F32)
    w2d = P.dram["L0_ffn_w2"]
    P.wait("v", "dy")
    snap = {}
    for m in range(KD):
        par = m % 2
        base = par * 22 * 128
        P.waitv("g", "t", snap.get(("t", par)))
        sname = "w2p%d" % par
        for j, (off, sz) in enumerate(HT):
            P.sdma("g", sname, w2s[0:sz, base + j * 128:base + (j + 1) * 128],
                   w2d[l, off:off + sz, m * 128:(m + 1) * 128])
        snap[("dg", par)] = (sname, P.cnt[sname])
        for c in range(NCH):
            cp = c % 2
            pss = ps_a2 if cp == 0 else ps_b3
            P.waitv("t", *snap[("dg", par)])
            P.waitv("t", "v", snap.get(("v", cp)))
            for j, (off, sz) in enumerate(HT):
                i = P.PE.matmul(pss[:, :], w2s[0:sz, base + j * 128:base + j * 128 + 128],
                                gated[0:sz, j * T + c * TC:j * T + (c + 1) * TC],
                                start=(j == 0), stop=(j == 21))
            P.tick("t", i)
            P.wait("v", "t")
            i = P.V.scalar_tensor_tensor(xpt(m)[:, c * TC:(c + 1) * TC], pss[:, :],
                                         wsb["b2"][:, m:m + 1], xpt(m)[:, c * TC:(c + 1) * TC],
                                         ALU.add, ALU.add)
            P.tick("v", i)
            snap[("v", cp)] = P.cnt["v"]
        snap[("t", par)] = P.cnt["t"]
    P.free(w2s, ps_a2, ps_b3, gated)
    P.dump(dump_pref + f"x_l{l}", xp[:, :], "v")
    return xp


def emit_ffn_call(P, xp, dump_pref=""):
    for l in range(2):
        wsb = {
            "qs": P.par["qs"][:, l:l + 1], "ks": P.par["ks"][:, l:l + 1],
            "lng": P.par["lng"][:, l * KD:(l + 1) * KD], "lnb": P.par["lnb"][:, l * KD:(l + 1) * KD],
            "b1a": P.par["b1a"][:, l * 22:(l + 1) * 22], "b1g": P.par["b1g"][:, l * 22:(l + 1) * 22],
            "b2": P.par["b2"][:, l * KD:(l + 1) * KD],
        }
        P.barrier()
        xp = emit_ffn_layer(P, l, xp, wsb, dump_pref=dump_pref + f"l{l}_")
        if P.stop_flag:
            break
        P.barrier()
        if P.stop(dump_pref + f"l{l}"):
            break
    return xp



def emit_mamba(P, lnx, dump_pref=""):
    xsp = P.sbuf("mb_xsp", [128, 2 * TC], BF16)
    ztile = P.sbuf("mb_zt", [128, KI * 4], BF16)
    i = P.V.memset(ztile[:, :], 0.0)
    P.tick("v", i)
    P.wait("y", "v")
    P.sdma("y", "xpadS", P.dram["scr_xs"][:, 0:4].rearrange("(kt p) w -> p kt w", p=128),
           ztile[:, :].rearrange("p (kt w) -> p kt w", kt=KI))
    P.sdma("y", "xpadS", P.dram["scr_xs"][:, XSW - 4:XSW].rearrange("(kt p) w -> p kt w", p=128),
           ztile[:, :].rearrange("p (kt w) -> p kt w", kt=KI))
    inw = P.sbuf("mb_inw", [128, 2 * KD * 128], BF16)
    zbuf = P.sbuf("mb_zb", [128, 2 * TC], BF16)
    ps_i = P.psum("mb_psi", [128, TC], F32)
    ps_i2 = P.psum("mb_psi2", [128, TC], F32)
    inwd = P.dram["L0_attn_in_w"]
    snap = {}
    for m in range(2 * KI):
        par = m % 2
        base = par * KD * 128
        P.waitv("g", "t", snap.get(("t", par)))
        sname = "inwp%d" % par
        P.sdma("g", sname, inw[:, base:base + KD * 128].rearrange("p (kt mm) -> p kt mm", kt=KD),
               inwd[:, m * 128:(m + 1) * 128].rearrange("(kt p) mm -> p kt mm", p=128))
        snap[("dg", par)] = (sname, P.cnt[sname])
        for c in range(NCH):
            cp = c % 2
            pss = ps_i if cp == 0 else ps_i2
            P.waitv("t", *snap[("dg", par)])
            P.waitv("t", "s", snap.get(("s", cp)))
            for kt in range(KD):
                i = P.PE.matmul(pss[:, :], inw[:, base + kt * 128:base + (kt + 1) * 128],
                                lnx[:, kt * T + c * TC:kt * T + (c + 1) * TC],
                                start=(kt == 0), stop=(kt == KD - 1))
            P.tick("t", i)
            P.wait("s", "t")
            if m < KI:
                xp2 = c % 2
                xn = "xswp%d" % xp2
                P.waitv("s", xn, snap.get(("xdy", xp2)))
                i = P.S.activation(xsp[:, xp2 * TC:(xp2 + 1) * TC], pss[:, :], AF.Identity)
                P.tick("s", i)
                P.wait("y", "s")
                P.sdma("y", xn, P.dram["scr_xs"][m * 128:(m + 1) * 128, 4 + c * TC:4 + (c + 1) * TC],
                       xsp[:, xp2 * TC:(xp2 + 1) * TC])
                snap[("xdy", xp2)] = P.cnt[xn]
            else:
                zp = c % 2
                zn = "zspp%d" % zp
                P.waitv("s", zn, snap.get(("zdy", zp)))
                i = P.S.activation(zbuf[:, zp * TC:(zp + 1) * TC], pss[:, :], AF.Silu)
                P.tick("s", i)
                P.wait("y", "s")
                P.sdma("y", zn, P.dram["scr_z"][(m - KI) * 128:(m - KI + 1) * 128, c * TC:(c + 1) * TC],
                       zbuf[:, zp * TC:(zp + 1) * TC])
                snap[("zdy", zp)] = P.cnt[zn]
            snap[("s", cp)] = P.cnt["s"]
        snap[("t", par)] = P.cnt["t"]
    P.free(inw, zbuf, ps_i, ps_i2, lnx, xsp, ztile)
    xsw_fin = (P.cnt.get("xswp0", 0), P.cnt.get("xswp1", 0), P.cnt.get("xpadS", 0))
    P.barrier()
    if P.stop("mamba_xs"):
        return

    for dirn in range(2):
        sfx = "" if dirn == 0 else "_b"
        wn = sfx
        pA = P.par["A" + sfx]
        pcw = P.par["cw" + sfx]
        pcb = P.par["cb" + sfx]
        pdtb = P.par["dtb" + sfx]
        pD = P.par["Dsk" + sfx]
        xst = P.sbuf("mb_xst", [128, 2 * XSW], BF16)
        xld = {}
        vconv = {}
        P.waitv("y", "xswp0", xsw_fin[0])
        P.waitv("y", "xswp1", xsw_fin[1])
        P.waitv("y", "xpadS", xsw_fin[2])

        def load_xs(kt):
            par = kt % 2
            xn = "xsL%d" % par
            P.waitv("y", "v", vconv.get(par))
            P.sdma("y", xn, xst[:, par * XSW:(par + 1) * XSW],
                   P.dram["scr_xs"][kt * 128:(kt + 1) * 128, :])
            xld[par] = P.cnt[xn]

        xprojT = P.sbuf("mb_xproj", [128, KI * 96], BF16)
        dtwT = P.sbuf("mb_dtw", [64, KI * 128], BF16)
        P.dma("g", xprojT[:, :].rearrange("p (kt m) -> p kt m", kt=KI),
              P.dram["L0_attn_xproj_w" + wn].rearrange("(kt p) m -> p kt m", p=128))
        P.dma("g", dtwT[:, :], P.dram["L0_attn_dt_w" + wn])
        xproj = xprojT[:, :]
        dtw = dtwT[:, :]
        ydram = P.dram["scr_yf" if dirn == 0 else "scr_yb"]

        def conv_tile(kt, dst_f32, t0, tw):
            tb = (kt % 2) * XSW
            P.waitv("v", "xsL%d" % (kt % 2), xld[kt % 2])
            for k in range(4):
                if dirn == 0:
                    src = xst[:, tb + 1 + k + t0:tb + 1 + k + t0 + tw]
                else:
                    st = tb + T + 6 - k - t0
                    src = xst[:, st:st - tw:-1]
                w_ap = pcw[:, kt * 4 + k:kt * 4 + k + 1]
                if k == 0:
                    i = P.V.tensor_scalar_mul(dst_f32, src, w_ap)
                else:
                    i = P.V.scalar_tensor_tensor(dst_f32, src, w_ap, dst_f32, ALU.mult, ALU.add)
            P.tick("v", i)

        xcb = P.sbuf("mbA_xc", [128, 2 * T], BF16)
        accA = P.sbuf("mbA_acc", [128, T], F32)
        dtpre = P.sbuf("mbA_dtpre", [64, T], BF16)
        bcrow = P.sbuf("mbA_bcrow", [48, T], BF16)
        ps_dt = P.psum("mbA_psdt", [64, T], F32)
        ps_bc = P.psum("mbA_psbc", [48, T], F32)
        load_xs(0)
        for kt in range(KI):
            par = kt % 2
            P.wait("v", "t")
            P.wait("v", "s")
            conv_tile(kt, accA[:, :], 0, T)
            vconv[kt % 2] = P.cnt["v"]
            if kt + 1 < KI:
                load_xs(kt + 1)
            P.wait("s", "v")
            P.wait("s", "t")
            i = P.S.activation(xcb[:, par * T:(par + 1) * T], accA[:, :], AF.Silu,
                               bias=pcb[:, kt:kt + 1])
            P.tick("s", i)
            P.wait("t", "s")
            for c in range(NCH):
                csl = slice(c * TC, (c + 1) * TC)
                mv = xcb[:, par * T + c * TC:par * T + (c + 1) * TC]
                i = P.PE.matmul(ps_dt[0:64, csl], xproj[:, kt * 96:kt * 96 + 64], mv,
                                start=(kt == 0), stop=(kt == KI - 1))
                i = P.PE.matmul(ps_bc[0:16, csl], xproj[:, kt * 96 + 64:kt * 96 + 80], mv,
                                start=(kt == 0), stop=(kt == KI - 1))
                i = P.PE.matmul(ps_bc[32:48, csl], xproj[:, kt * 96 + 80:kt * 96 + 96], mv,
                                start=(kt == 0), stop=(kt == KI - 1), tile_position=(0, 32))
            P.tick("t", i)
        P.wait("s", "t")
        i = P.S.activation(dtpre[:, :], ps_dt[0:64, :], AF.Identity)
        i = P.S.activation(bcrow[0:16, :], ps_bc[0:16, :], AF.Identity)
        i = P.S.activation(bcrow[32:48, :], ps_bc[32:48, :], AF.Identity)
        P.tick("s", i)
        P.wait("y", "s")
        P.dma("y", P.dram["scr_b"], bcrow[0:16, :])
        P.dma("y", P.dram["scr_c"], bcrow[32:48, :])
        P.wait("y", "dy")
        P.free(xcb, accA, ps_bc, bcrow)
        P.dump(dump_pref + "dtpre" + sfx, dtpre[:, :], "s", parts=64)

        NGS = 4
        NGRP = DST // NGS
        SEGW = TH + 1
        GWS = NGS * SEGW
        state = P.sbuf("mb_state", [128, KI * DST], F32)
        bcb = P.sbuf("mb_bcb", [128, 2 * DST * TH], BF16)
        dts = P.sbuf("mb_dts", [128, TH], F32)
        xcs = P.sbuf("mb_xcs", [128, TH], BF16)
        us = P.sbuf("mb_us", [128, TH], BF16)
        NB = 3
        dAg = P.sbuf("mb_dAg", [128, 2 * GWS], F32)
        dbug = P.sbuf("mb_dbug", [128, NB * GWS], BF16)
        cyg = P.sbuf("mb_cyg", [128, NGS * TH], BF16)
        tmp2 = P.sbuf("mb_tmp2", [128, 2 * TH], BF16)
        hbg = P.sbuf("mb_hbg", [128, GWS], BF16)
        yg = P.sbuf("mb_yg", [128, NGRP * 2 * TH], BF16)
        ysum = P.sbuf("mb_ysum", [128, TH], BF16)
        yout = P.sbuf("mb_yout", [128, TH], BF16)
        yrev = P.sbuf("mb_yrev", [128, TH], BF16) if dirn == 1 else None
        ps_d2 = P.psum("mb_psd2", [128, TC], F32)
        acc2 = P.sbuf("mb_acc2", [128, 2 * TH], F32)
        accS = acc2[:, 0:TH]
        sp2 = acc2[:, TH:2 * TH]
        i = P.V.memset(dAg[:, 0:2 * GWS:SEGW], 0.0)
        P.tick("v", i)
        for nb in range(NB):
            i = P.G.memset(dbug[:, nb * GWS:(nb + 1) * GWS:SEGW], 0.0)
        P.tick("g", i)

        gctr = 0
        snap_scan = None
        snap_scan_p = {}
        state_v = None
        for half in range(2):
            t0 = half * TH
            P.wait("y", "g")
            P.wait("y", "v")
            for n in range(DST):
                P.sdma("y", "bcbS", bcb[:, n * TH:(n + 1) * TH],
                       P.dram["scr_b"][n:n + 1, t0:t0 + TH].broadcast_to([128, TH]))
                P.sdma("y", "bcbS", bcb[:, (DST + n) * TH:(DST + n + 1) * TH],
                       P.dram["scr_c"][n:n + 1, t0:t0 + TH].broadcast_to([128, TH]))
            bc_dy = P.cnt["bcbS"]
            load_xs(0)
            for kt in range(KI):
                P.wait("v", "s")
                P.wait("v", "g")
                conv_tile(kt, accS, t0, TH)
                vconv[kt % 2] = P.cnt["v"]
                if kt + 1 < KI:
                    load_xs(kt + 1)
                P.wait("s", "v")
                i = P.S.activation(xcs[:, :], accS, AF.Silu, bias=pcb[:, kt:kt + 1])
                P.tick("s", i)
                for c in range(2):
                    P.wait("t", "v")
                    P.wait("t", "s")
                    i = P.PE.matmul(ps_d2[:, :], dtw[:, kt * 128:(kt + 1) * 128],
                                    dtpre[:, t0 + c * TC:t0 + (c + 1) * TC], start=True, stop=True)
                    P.tick("t", i)
                    P.wait("v", "t")
                    i = P.V.tensor_scalar(accS[:, c * TC:(c + 1) * TC], ps_d2[:, :],
                                          pdtb[:, kt:kt + 1], 0.0, ALU.add, ALU.max)
                    i = P.V.tensor_scalar(sp2[:, c * TC:(c + 1) * TC], ps_d2[:, :],
                                          pdtb[:, kt:kt + 1], None, ALU.add)
                    P.tick("v", i)
                P.wait("s", "v")
                i = P.S.activation(sp2, sp2, AF.Abs)
                P.S.drain()
                i = P.S.activation(sp2, sp2, AF.Exp, scale=-1.0)
                P.S.drain()
                i = P.S.activation(sp2, sp2, AF.Ln, bias=1.0)
                P.tick("s", i)
                P.wait("v", "s")
                i = P.V.tensor_add(dts[:, :], accS, sp2)
                i = P.V.tensor_tensor(us[:, :], dts[:, :], xcs[:, :], ALU.mult)
                P.tick("v", i)
                us_v = P.cnt["v"]
                for g in range(NGRP):
                    gp = gctr % NB
                    ap = gctr % 2
                    gctr += 1
                    n0 = g * NGS
                    s0 = kt * DST + n0
                    P.waitv("s", "v", max(us_v, snap_scan_p.get(("a", ap)) or 0))
                    for j in range(NGS):
                        i = P.S.activation(dAg[:, ap * GWS + j * SEGW + 1:ap * GWS + (j + 1) * SEGW],
                                           dts[:, :], AF.Exp, scale=pA[:, s0 + j:s0 + j + 1])
                    P.tick("s", i)
                    sexp_g = P.cnt["s"]
                    P.waitv("g", "v", max(us_v, snap_scan_p.get(("b", gp)) or 0, state_v or 0))
                    P.waitv("g", "bcbS", bc_dy)
                    for j in range(NGS):
                        i = P.G.tensor_tensor(dbug[:, gp * GWS + j * SEGW + 1:gp * GWS + (j + 1) * SEGW],
                                              us[:, :], bcb[:, (n0 + j) * TH:(n0 + j + 1) * TH],
                                              ALU.mult)
                    if half == 1:
                        i = P.G.tensor_copy(dbug[:, gp * GWS:gp * GWS + GWS:SEGW],
                                            state[:, s0:s0 + NGS])
                    P.tick("g", i)
                    gfill_g = P.cnt["g"]
                    P.waitv("v", "s", sexp_g)
                    P.waitv("v", "g", gfill_g)
                    i = P.V.tensor_tensor_scan(hbg[:, :], dAg[:, ap * GWS:(ap + 1) * GWS],
                                               dbug[:, gp * GWS:(gp + 1) * GWS],
                                               0.0, ALU.mult, ALU.add)
                    P.tick("v", i)
                    snap_scan = P.cnt["v"]
                    snap_scan_p[("a", ap)] = snap_scan
                    snap_scan_p[("b", gp)] = snap_scan
                    if half == 0:
                        i = P.V.tensor_copy(state[:, s0:s0 + NGS], hbg[:, TH:GWS:SEGW])
                    i = P.V.tensor_tensor(
                        cyg[:, :].rearrange("p (n t) -> p n t", n=NGS),
                        hbg[:, :].rearrange("p (n t) -> p n t", n=NGS)[:, :, 1:SEGW],
                        bcb[:, DST * TH + n0 * TH:DST * TH + (n0 + NGS) * TH]
                           .rearrange("p (n t) -> p n t", n=NGS),
                        ALU.mult)
                    i = P.V.tensor_add(yg[:, g * 2 * TH:(g + 1) * 2 * TH],
                                       cyg[:, 0:2 * TH], cyg[:, 2 * TH:4 * TH])
                    P.tick("v", i)
                if "yspS" in P.cnt:
                    P.waitv("v", "yspS", P.cnt["yspS"])
                i = P.V.tensor_add(tmp2[:, :], yg[:, 0:2 * TH], yg[:, 2 * TH:4 * TH])
                i = P.V.tensor_add(cyg[:, 0:2 * TH], yg[:, 4 * TH:6 * TH], yg[:, 6 * TH:8 * TH])
                i = P.V.tensor_add(cyg[:, 2 * TH:4 * TH], tmp2[:, :], cyg[:, 0:2 * TH])
                i = P.V.tensor_add(ysum[:, :], cyg[:, 2 * TH:3 * TH], cyg[:, 3 * TH:4 * TH])
                i = P.V.scalar_tensor_tensor(yout[:, :], xcs[:, :], pD[:, kt:kt + 1], ysum[:, :],
                                             ALU.mult, ALU.add)
                P.tick("v", i)
                if dirn == 0:
                    P.wait("y", "v")
                    P.sdma("y", "yspS", ydram[kt * 128:(kt + 1) * 128, t0:t0 + TH], yout[:, :])
                else:
                    i = P.V.tensor_copy(yrev[:, :], yout[:, TH - 1::-1])
                    P.tick("v", i)
                    P.wait("y", "v")
                    P.sdma("y", "yspS", ydram[kt * 128:(kt + 1) * 128, T - t0 - TH:T - t0], yrev[:, :])
            if half == 0:
                state_v = P.cnt["v"]
        P.free(dtpre, ps_dt, state, bcb, dts, xcs, us, dAg, dbug, cyg, tmp2, acc2,
               hbg, yg, ysum, yout, ps_d2, xprojT, dtwT, xst)
        if yrev is not None:
            P.free(yrev)
        P.barrier()
        if P.stop("mamba_dir%d" % dirn):
            return

    ych = P.sbuf("mg_ych", [128, 32 * TC], BF16)
    mw = P.sbuf("mg_mw", [128, 2 * 32 * 128], BF16)
    zs = P.sbuf("mg_zs", [128, 2 * TC], BF16)
    ymb = P.sbuf("mg_ym", [128, 2 * TC], BF16)
    ps_g = P.psum("mg_psg", [128, TC], F32)
    ps_g2 = P.psum("mg_psg2", [128, TC], F32)
    mwd = P.dram["L0_attn_merge_w"]
    snap = {}
    for c in range(NCH):
        P.wait("y", "t")
        P.sdma("y", "ychS", ych[:, 0:16 * TC].rearrange("p (kt w) -> p kt w", kt=KI),
               P.dram["scr_yf"][:, c * TC:(c + 1) * TC].rearrange("(kt p) w -> p kt w", p=128))
        P.sdma("y", "ychS", ych[:, 16 * TC:32 * TC].rearrange("p (kt w) -> p kt w", kt=KI),
               P.dram["scr_yb"][:, c * TC:(c + 1) * TC].rearrange("(kt p) w -> p kt w", p=128))
        ych_dy = P.cnt["ychS"]
        for m in range(KI):
            par = m % 2
            base = par * 32 * 128
            P.waitv("g", "t", snap.get(("t", par)))
            mn = "mwp%d" % par
            P.sdma("g", mn, mw[:, base:base + 32 * 128].rearrange("p (kt mm) -> p kt mm", kt=32),
                   mwd[:, m * 128:(m + 1) * 128].rearrange("(kt p) mm -> p kt mm", p=128))
            zn2 = "mzsp%d" % par
            P.waitv("y", "v", snap.get(("v", par)))
            P.sdma("y", zn2, zs[:, par * TC:(par + 1) * TC],
                   P.dram["scr_z"][m * 128:(m + 1) * 128, c * TC:(c + 1) * TC])
            pss = ps_g if par == 0 else ps_g2
            P.waitv("t", mn, P.cnt[mn])
            P.waitv("t", "ychS", ych_dy)
            P.waitv("t", "v", snap.get(("v", par)))
            for kt in range(32):
                i = P.PE.matmul(pss[:, :], mw[:, base + kt * 128:base + (kt + 1) * 128],
                                ych[:, kt * TC:(kt + 1) * TC], start=(kt == 0), stop=(kt == 31))
            P.tick("t", i)
            P.wait("v", "t")
            P.waitv("v", zn2, P.cnt[zn2])
            ymn = "ymbp%d" % par
            if ymn in P.cnt:
                P.waitv("v", ymn, P.cnt[ymn])
            i = P.V.tensor_tensor(ymb[:, par * TC:(par + 1) * TC], pss[:, :],
                                  zs[:, par * TC:(par + 1) * TC], ALU.mult)
            P.tick("v", i)
            snap[("v", par)] = P.cnt["v"]
            snap[("t", par)] = P.cnt["t"]
            P.wait("y", "v")
            P.sdma("y", ymn, P.dram["scr_ym"][m * 128:(m + 1) * 128, c * TC:(c + 1) * TC],
                   ymb[:, par * TC:(par + 1) * TC])
    P.free(ych, mw, zs, ymb, ps_g, ps_g2)
    P.barrier()
    if P.stop("mamba_merge"):
        return


def emit_outw_and_residual(P, x, ab2):
    ymch = P.sbuf("ow_ymch", [128, KI * TC], BF16)
    ow = P.sbuf("ow_w", [128, 2 * KI * 128], BF16)
    ps_o = P.psum("ow_ps", [128, TC], F32)
    ps_o2 = P.psum("ow_ps2", [128, TC], F32)
    tmp = P.sbuf("ow_tmp", [128, TC], F32)
    owd = P.dram["L0_attn_out_w"]
    a_b = ab2[:, 0:T]
    b_b = ab2[:, T:2 * T]
    snap = {}
    for c in range(NCH):
        P.wait("y", "t")
        P.sdma("y", "ymchS", ymch[:, :].rearrange("p (kt w) -> p kt w", kt=KI),
               P.dram["scr_ym"][:, c * TC:(c + 1) * TC].rearrange("(kt p) w -> p kt w", p=128))
        ych_dy = P.cnt["ymchS"]
        for mo in range(KD):
            par = mo % 2
            base = par * KI * 128
            P.waitv("g", "t", snap.get(("t", par)))
            on = "owp%d" % par
            P.sdma("g", on, ow[:, base:base + KI * 128].rearrange("p (kt mm) -> p kt mm", kt=KI),
                   owd[:, mo * 128:(mo + 1) * 128].rearrange("(kt p) mm -> p kt mm", p=128))
            pss = ps_o if par == 0 else ps_o2
            P.waitv("t", on, P.cnt[on])
            P.waitv("t", "ymchS", ych_dy)
            P.waitv("t", "v", snap.get(("v", par)))
            for kt in range(KI):
                i = P.PE.matmul(pss[:, :], ow[:, base + kt * 128:base + (kt + 1) * 128],
                                ymch[:, kt * TC:(kt + 1) * TC], start=(kt == 0), stop=(kt == KI - 1))
            P.tick("t", i)
            xsl = x[:, mo * T + c * TC:mo * T + (c + 1) * TC]
            P.wait("v", "t")
            i = P.V.tensor_tensor(tmp[:, :], xsl, a_b[:, c * TC:(c + 1) * TC], ALU.mult)
            i = P.V.tensor_sub(tmp[:, :], tmp[:, :], b_b[:, c * TC:(c + 1) * TC])
            i = P.V.tensor_add(tmp[:, :], tmp[:, :], pss[:, :])
            i = P.V.tensor_add(xsl, xsl, tmp[:, :])
            P.tick("v", i)
            snap[("v", par)] = P.cnt["v"]
            snap[("t", par)] = P.cnt["t"]
    P.free(ymch, ow, ps_o, ps_o2, tmp)


def self_val(P, e):
    return P.cnt[e]



def build_program(want_dumps=(), stop_after=None):
    nc = bass.Bass("TRN2", target_bir_lowering=False)
    P = Prog(nc, want_dumps)
    P.stop_after = stop_after

    P.dram["xt"] = nc.dram_tensor("xt", [D, T], F32, kind="ExternalInput").ap()
    P.dram["xo"] = nc.dram_tensor("xo", [D, T], F32, kind="ExternalOutput").ap()
    for nm, (shape, dt) in {**CONST_SPECS, **ROT_SPECS}.items():
        P.dram[nm] = nc.dram_tensor(nm, shape, dt, kind="ExternalInput").ap()
    for nm, shape in WEIGHT_SPECS.items():
        P.dram[nm] = nc.dram_tensor(nm, shape, F32, kind="ExternalInput").ap()
    for nm, shape, dt in [
        ("scr_x", [D, T], F32), ("scr_z", [DI, T], BF16),
        ("scr_yf", [DI, T], BF16), ("scr_yb", [DI, T], BF16),
        ("scr_ym", [DI, T], BF16), ("scr_b", [DST, T], BF16), ("scr_c", [DST, T], BF16),
        ("scr_ab", [128, 2 * T], F32), ("scr_rows", [2, T], F32),
        ("scr_xp", [D, T], F32), ("scr_xs", [DI, XSW], BF16),
    ]:
        P.dram[nm] = nc.dram_tensor(nm, shape, dt, kind="Internal").ap()
    if want_dumps:
        P.dbg = nc.dram_tensor("dbg", [128, 400000], F32, kind="ExternalOutput").ap()

    for nm, (shape, dt) in CONST_SPECS.items():
        P.consts[nm] = P.sbuf(nm, shape, dt)
        P.dma("y", P.consts[nm][:, :], P.dram[nm])
    nc.const_aps.register(F32, EPS, P.consts["cMisc"][:, 0:1])

    def par_alloc(nm, shape, dtype=F32):
        P.par[nm] = P.sbuf("par_" + nm, shape, dtype)
        return P.par[nm]

    qsd, ksd = P.dram["L0_ffn_q_scale"], P.dram["L0_ffn_k_scale"]
    par_alloc("qs", [128, 2]); par_alloc("ks", [128, 2])
    for l in range(2):
        for half in range(2):
            P.dma("y", P.par["qs"][half * 64:(half + 1) * 64, l:l + 1],
                  qsd[l, :].rearrange("(d one) -> d one", one=1))
            P.dma("y", P.par["ks"][half * 64:(half + 1) * 64, l:l + 1],
                  ksd[l, :].rearrange("(d one) -> d one", one=1))
    par_alloc("lng", [128, 2 * KD]); par_alloc("lnb", [128, 2 * KD])
    par_alloc("b2", [128, 2 * KD])
    for l in range(2):
        P.dma("y", P.par["lng"][:, l * KD:(l + 1) * KD],
              P.dram["L0_ffn_ln_g"][l].rearrange("(kt p) -> p kt", p=128), slow=True)
        P.dma("y", P.par["lnb"][:, l * KD:(l + 1) * KD],
              P.dram["L0_ffn_ln_b"][l].rearrange("(kt p) -> p kt", p=128), slow=True)
        P.dma("y", P.par["b2"][:, l * KD:(l + 1) * KD],
              P.dram["L0_ffn_b2"][l].rearrange("(kt p) -> p kt", p=128), slow=True)
    par_alloc("b1a", [128, 2 * 22]); par_alloc("b1g", [128, 2 * 22])
    for l in range(2):
        for which, off0 in (("b1a", 0), ("b1g", HID)):
            P.dma("y", P.par[which][:, l * 22:l * 22 + 21],
                  P.dram["L0_ffn_b1"][l, off0:off0 + 2688].rearrange("(j p) -> p j", p=128), slow=True)
            P.dma("y", P.par[which][0:42, l * 22 + 21:l * 22 + 22],
                  P.dram["L0_ffn_b1"][l, off0 + 2688:off0 + 2730].rearrange("(d one) -> d one", one=1))
    par_alloc("gamma", [128, KD]); par_alloc("beta", [128, KD])
    P.dma("y", P.par["gamma"][:, :], P.dram["L0_attn_gamma"].rearrange("(kt p) -> p kt", p=128), slow=True)
    P.dma("y", P.par["beta"][:, :], P.dram["L0_attn_beta"].rearrange("(kt p) -> p kt", p=128), slow=True)
    for sfx, wn in (("", ""), ("_b", "_b")):
        par_alloc("A" + sfx, [128, KI * DST])
        P.dma("y", P.par["A" + sfx][:, :].rearrange("p (kt n) -> p kt n", kt=KI),
              P.dram["L0_attn_A_log" + wn].rearrange("(kt p) n -> p kt n", p=128))
        par_alloc("cw" + sfx, [128, KI * 4])
        P.dma("y", P.par["cw" + sfx][:, :].rearrange("p (kt k) -> p kt k", kt=KI),
              P.dram["L0_attn_conv_w" + wn].rearrange("(kt p) k -> p kt k", p=128))
        for pn, dn in (("cb", "conv_b"), ("dtb", "dt_b"), ("Dsk", "D_skip")):
            par_alloc(pn + sfx, [128, KI])
            P.dma("y", P.par[pn + sfx][:, :],
                  P.dram["L0_attn_" + dn + wn].rearrange("(kt p) -> p kt", p=128), slow=True)
    P.barrier()
    for sfx in ("", "_b"):
        i = P.S.activation(P.par["A" + sfx][:, :], P.par["A" + sfx][:, :], AF.Exp)
        P.tick("s", i)
        P.wait("v", "s")
        i = P.V.tensor_scalar_mul(P.par["A" + sfx][:, :], P.par["A" + sfx][:, :], -1.0)
        P.tick("v", i)
    P.barrier()

    x = P.sbuf("x_arena", [128, KD * T], F32, hi=True)
    P.dma("y", x[:, :].rearrange("p (kt w) -> p kt w", kt=KD),
          P.dram["xt"].rearrange("(kt p) w -> p kt w", p=128))
    P.barrier()
    xtiles = lambda xx: [xx[:, kt * T:(kt + 1) * T] for kt in range(KD)]

    def run_ffn_macaron(tagp):
        xp = P.sbuf("xp_arena", [128, KD * T], F32, hi=True)
        emit_ln(P, xtiles(x), None, direct_dst=lambda kt: xp[:, kt * T:(kt + 1) * T])
        P.dump(tagp + "ln", xp[:, :], "v")
        P.wait("y", "v")
        P.dma("y", P.dram["scr_x"].rearrange("(kt p) w -> p kt w", p=128),
              x[:, :].rearrange("p (kt w) -> p kt w", kt=KD))
        P.barrier()
        P.free(x)
        xp = emit_ffn_call(P, xp, dump_pref=tagp)
        if P.stop_flag:
            return xp, None
        x2 = P.sbuf("x_arena", [128, KD * T], F32, hi=True)
        P.dma("y", x2[:, :].rearrange("p (kt w) -> p kt w", kt=KD),
              P.dram["scr_x"].rearrange("(kt p) w -> p kt w", p=128))
        P.barrier()
        for kt in range(KD):
            i = P.V.scalar_tensor_tensor(x2[:, kt * T:(kt + 1) * T], xp[:, kt * T:(kt + 1) * T],
                                         0.5, x2[:, kt * T:(kt + 1) * T], ALU.mult, ALU.add)
            P.tick("v", i)
        P.free(xp)
        P.dump(tagp + "x_out", x2[:, :], "v")
        return None, x2

    if stop_after == "ln1_only":
        P.dump("x0", x[:, :], "v")
        xp = P.sbuf("xp_arena", [128, KD * T], F32, hi=True)
        emit_ln(P, xtiles(x), None, direct_dst=lambda kt: xp[:, kt * T:(kt + 1) * T])
        P.dump("ln1", xp[:, :], "v")
        P.barrier()
        P.dump("ln1b", xp[:, :], "v")
        P.stop_flag = True
        finish(P)
        return nc, P
    xp, x = run_ffn_macaron("f1_")
    if P.stop_flag:
        finish(P)
        return nc, P

    lnm = P.sbuf("mb_lnm", [128, KD * T], BF16, hi=True)

    def wrm(kt, tmp, g, b):
        P.wait("s", "v")
        i = P.S.activation(lnm[:, kt * T:(kt + 1) * T], tmp[:, :], AF.Identity, bias=b, scale=g)
        P.tick("s", i)

    ab, ab2 = emit_ln(P, xtiles(x), wrm, gamma_ap=P.par["gamma"], beta_ap=P.par["beta"],
                      fuse_double=True, keep_ab=True)
    P.dump("mb_lnm", lnm[:, :], "s")
    P.wait("y", "v")
    P.dma("y", P.dram["scr_x"].rearrange("(kt p) w -> p kt w", p=128),
          x[:, :].rearrange("p (kt w) -> p kt w", kt=KD))
    P.dma("y", P.dram["scr_ab"], ab2[:, :])
    P.barrier()
    P.free(x, ab, ab2)
    if P.stop("ln2"):
        finish(P)
        return nc, P
    emit_mamba(P, lnm, dump_pref="mb_")
    if P.stop_flag:
        finish(P)
        return nc, P
    x = P.sbuf("x_arena", [128, KD * T], F32, hi=True)
    P.dma("y", x[:, :].rearrange("p (kt w) -> p kt w", kt=KD),
          P.dram["scr_x"].rearrange("(kt p) w -> p kt w", p=128))
    ab2 = P.sbuf("ow_ab2", [128, 2 * T], F32)
    P.dma("y", ab2[:, :], P.dram["scr_ab"])
    P.barrier()
    emit_outw_and_residual(P, x, ab2)
    P.free(ab2)
    P.dump("mb_x_out", x[:, :], "v")
    P.barrier()
    if P.stop("mamba"):
        finish(P)
        return nc, P

    xp, x = run_ffn_macaron("f2_")
    if P.stop_flag:
        finish(P)
        return nc, P

    P.wait("y", "v")
    P.dma("y", P.dram["xo"].rearrange("(kt p) w -> p kt w", p=128),
          x[:, :].rearrange("p (kt w) -> p kt w", kt=KD))
    finish(P)
    return nc, P


def finish(P):
    P.barrier()
    for q in ["dy", "dg", "ds"]:
        P.SY.wait_ge(P.sem[q], P.cnt[q])



_BUILD_CACHE = {}


def kernel(**inputs):
    key = ("plain",)
    if key not in _BUILD_CACHE:
        _BUILD_CACHE[key] = build_program()
    nc, P = _BUILD_CACHE[key]
    consts = host_consts()
    x = np.asarray(inputs["L0_attn_x"], np.float32)
    base = {}
    for nm in WEIGHT_SPECS:
        base[nm] = _f(inputs[nm])
    base.update(consts)
    in_maps = []
    for b in range(8):
        m = dict(base)
        m["xt"] = np.ascontiguousarray(x[b].T)
        in_maps.append(m)
    res = run_bass_kernel_spmd(nc, in_maps, core_ids=list(range(8)))
    out = np.stack([np.asarray(r["xo"]).T for r in res.results])
    return out.astype(np.float32)


# version 06
# speedup vs baseline: 1.0178x; 1.0178x over previous
import numpy as np
import ml_dtypes
import concourse.bass as bass
import concourse.mybir as mybir
from concourse.bass_utils import run_bass_kernel_spmd

F32 = mybir.dt.float32
BF16 = mybir.dt.bfloat16
AF = mybir.ActivationFunctionType
ALU = mybir.AluOpType
AXX = mybir.AxisListType.X

D = 1024
T = 2048
KD = 8
DI = 2048
KI = 16
NH = 8
DH = 64
WIN = 512
HID = 2730
DTR = 64
DST = 16
NCH = 4
TC = 512
TH = 1024
EPS = 1e-5
QK_SCALE = 8.0
HT = [(j * 128, min(128, HID - j * 128)) for j in range(22)]
XSW = T + 8


def _f(x):
    return np.ascontiguousarray(np.asarray(x), dtype=np.float32)


def _bf(x):
    return np.ascontiguousarray(np.asarray(x, dtype=np.float32).astype(ml_dtypes.bfloat16))


def host_consts():
    c = {}
    c["cI"] = _bf(np.eye(128))
    R = np.zeros((128, 128), np.float32)
    for blk in range(2):
        for m in range(64):
            if m < 32:
                R[blk * 64 + m + 32, blk * 64 + m] = -1.0
            else:
                R[blk * 64 + m - 32, blk * 64 + m] = 1.0
    c["cR"] = _bf(R)
    half = np.arange(0, DH, 2, dtype=np.float64)
    inv_freq = 1.0 / 10000.0 ** (half / DH)
    t = np.arange(T, dtype=np.float64)
    freqs = np.concatenate([t[:, None] * inv_freq[None, :]] * 2, -1)
    base = (half + 0.4 * DH) / (1.4 * DH)
    power = (t - T // 2) / float(WIN // 2)
    scale = np.concatenate([base[None, :] ** power[:, None]] * 2, -1)
    cos, sin = np.cos(freqs), np.sin(freqs)
    rep = lambda a: np.concatenate([a.T, a.T], 0)
    c["cT1q"] = _bf(rep(cos * scale))
    c["cT2q"] = _bf(rep(sin * scale))
    c["cT1k"] = _bf(rep(cos / scale))
    c["cT2k"] = _bf(rep(sin / scale))
    idx = np.arange(128)
    MgA = np.zeros((128, 640), np.float32)
    for r in range(128):
        kr = np.arange(640)
        MgA[r, :] = np.where((kr >= r) & (kr <= r + 512), 0.0, -1.0e30)
    c["cMgA"] = _bf(MgA)
    c["cM0A"] = _bf(np.where(idx[None, :] <= idx[:, None], 0.0, -1.0e30))
    c["cOnesB"] = _bf(np.ones((128, 1)))
    c["cOnesRowF"] = _f(np.ones((128, 128)))
    c["cOnesRowB"] = _bf(np.ones((1, 128)))
    B2 = np.zeros((128, 2), np.float32)
    B2[:64, 0] = 1.0
    B2[64:, 1] = 1.0
    c["cB2b"] = _bf(B2)
    B2T = np.zeros((2, 128), np.float32)
    B2T[0, :64] = 1.0
    B2T[1, 64:] = 1.0
    c["cB2T"] = _bf(B2T)
    c["cMisc"] = _f(np.tile(np.array([[EPS, 0.0]], np.float32), (128, 1)))
    return c


CONST_SPECS = {
    "cI": ([128, 128], BF16), "cR": ([128, 128], BF16),
    "cMgA": ([128, 640], BF16), "cM0A": ([128, 128], BF16),
    "cOnesB": ([128, 1], BF16), "cOnesRowF": ([128, 128], F32),
    "cOnesRowB": ([1, 128], BF16),
    "cB2b": ([128, 2], BF16), "cB2T": ([2, 128], BF16),
    "cMisc": ([128, 2], F32),
}
ROT_SPECS = {
    "cT1q": ([128, T], BF16), "cT2q": ([128, T], BF16),
    "cT1k": ([128, T], BF16), "cT2k": ([128, T], BF16),
}

WEIGHT_SPECS = {
    "L0_attn_gamma": [D], "L0_attn_beta": [D],
    "L0_attn_in_w": [D, 2 * DI],
    "L0_attn_conv_w": [DI, 4], "L0_attn_conv_b": [DI],
    "L0_attn_xproj_w": [DI, 96], "L0_attn_dt_w": [DTR, DI], "L0_attn_dt_b": [DI],
    "L0_attn_A_log": [DI, DST], "L0_attn_D_skip": [DI],
    "L0_attn_conv_w_b": [DI, 4], "L0_attn_conv_b_b": [DI],
    "L0_attn_xproj_w_b": [DI, 96], "L0_attn_dt_w_b": [DTR, DI], "L0_attn_dt_b_b": [DI],
    "L0_attn_A_log_b": [DI, DST], "L0_attn_D_skip_b": [DI],
    "L0_attn_merge_w": [2 * DI, DI], "L0_attn_out_w": [DI, D],
    "L0_ffn_wqkv": [2, D, 1536], "L0_ffn_q_scale": [2, DH], "L0_ffn_k_scale": [2, DH],
    "L0_ffn_wo": [2, 512, D], "L0_ffn_ln_g": [2, D], "L0_ffn_ln_b": [2, D],
    "L0_ffn_w1": [2, D, 2 * HID], "L0_ffn_b1": [2, 2 * HID],
    "L0_ffn_w2": [2, HID, D], "L0_ffn_b2": [2, D],
}


class EngProxy:

    _PASS = {"wait_ge", "drain", "nop", "dma_start", "sem_inc"}

    def __init__(self, prog, key, raw):
        self.prog = prog
        self.key = key
        self.raw = raw

    def __getattr__(self, name):
        fn = getattr(self.raw, name)
        if name in self._PASS or not callable(fn):
            return fn
        key = self.key
        prog = self.prog

        def wrapped(*args, **kwargs):
            aps = [a for a in args if isinstance(a, bass.AP)]
            aps += [v for v in kwargs.values() if isinstance(v, bass.AP)]
            dirty = prog.dirty[key]
            if aps:
                prog.sync_marks(self.raw, aps[0])
                ins_names = {a.tensor.name for a in aps[1:]}
                if ins_names & dirty:
                    self.raw.drain()
                    dirty.clear()
            instr = fn(*args, **kwargs)
            if aps:
                dirty.add(aps[0].tensor.name)
            return instr

        return wrapped


class Prog:
    def __init__(self, nc, want_dumps=()):
        self.nc = nc
        self._cms = {}
        self.sem = {}
        self.cnt = {}
        for name in ["t", "v", "s", "g", "y", "dy", "dg", "ds"]:
            cm = nc.semaphore("sem_" + name)
            self.sem[name] = cm.__enter__()
            self._cms["sem_" + name] = cm
            self.cnt[name] = 0
        self.PE = nc.tensor
        self.V = EngProxy(self, "v", nc.vector)
        self.S = EngProxy(self, "s", nc.scalar)
        self.G = EngProxy(self, "g", nc.gpsimd)
        self.SY = nc.sync
        self.dirty = {"v": set(), "s": set(), "g": set()}
        self._eng = {"t": self.PE, "v": self.V, "s": self.S, "g": self.G, "y": self.SY}
        self.pending_sync = {}
        self.dumps = []
        self.dump_off = 0
        self.want_dumps = set(want_dumps)
        self.dbg = None
        self.consts = {}
        self.dram = {}
        self.par = {}
        self.uid = 0
        self.stop_flag = False
        self.stop_after = None

    ARENA_BYTES = 206 * 1024

    def _ensure_arena(self):
        if not hasattr(self, "arena_base"):
            cm = self.nc.sbuf_tensor("big_arena", [128, self.ARENA_BYTES], mybir.dt.uint8)
            h = cm.__enter__()
            self._cms["big_arena"] = cm
            self.arena_base = self.nc.lookup_mloc(h).addr
            self.freelist = [(0, self.ARENA_BYTES)]
            self.alloc_info = {}

    def sbuf(self, name, shape, dtype, hi=False):
        self._ensure_arena()
        per = int(np.prod(shape[1:])) * mybir.dt.size(dtype)
        per = (per + 31) // 32 * 32
        ranges = list(enumerate(self.freelist))
        if hi:
            ranges = ranges[::-1]
        for idx, (off, sz) in ranges:
            if sz >= per:
                break
        else:
            amap = sorted((v[0], v[1], k) for k, v in self.alloc_info.items())
            raise MemoryError(f"SBUF arena full for {name} ({per}B): {self.freelist}\n" +
                              "\n".join(f"  {o:7d} +{s:6d} {k}" for o, s, k in amap))
        if sz == per:
            del self.freelist[idx]
        elif hi:
            off = off + sz - per
            self.freelist[idx] = (self.freelist[idx][0], sz - per)
        else:
            self.freelist[idx] = (off + per, sz - per)
        self.uid += 1
        name = f"{name}_{self.uid}"
        h = self.nc.virtual_sbuf_tensor(name, list(shape), dtype)
        self.nc._tensor(h.name, list(shape), dtype, type="SB")
        mloc = self.nc.lookup_mloc(h)
        mloc.allocated = True
        mloc.addr = self.arena_base + off
        self.alloc_info[h.name] = (off, per)
        self.pending_sync[h.name] = dict(self.cnt)
        return h

    def psum(self, name, shape, dtype=F32):
        self.uid += 1
        name = f"{name}_{self.uid}"
        esz = mybir.dt.size(dtype)
        free = int(np.prod(shape[1:]))
        pad = (-(free * esz) % 2048) // esz
        cm = self.nc.psum_tensor(name, [128, free + pad], dtype)
        t = cm.__enter__()
        self._cms[name] = (self.uid, cm)
        return t

    def free(self, *handles):
        psums = []
        for h in handles:
            name = h.name if hasattr(h, "name") else h.tensor.name
            if name in self.alloc_info:
                off, per = self.alloc_info.pop(name)
                self.freelist.append((off, per))
            else:
                psums.append(name)
        self.freelist.sort()
        merged = []
        for off, sz in self.freelist:
            if merged and merged[-1][0] + merged[-1][1] == off:
                merged[-1] = (merged[-1][0], merged[-1][1] + sz)
            else:
                merged.append((off, sz))
        self.freelist = [tuple(x) for x in merged]
        for name in sorted(psums, key=lambda n: self._cms[n][0], reverse=True):
            _, cm = self._cms.pop(name)
            cm.__exit__(None, None, None)

    def tick(self, name, instr, k=1):
        if name in ("v", "s", "g"):
            instr = self._eng[name].raw.drain()
            self.dirty[name].clear()
        elif name == "t":
            instr = self.PE.drain()
        instr.then_inc(self.sem[name], k)
        self.cnt[name] += k

    def wait(self, who, dep, val=None):
        self._eng[who].wait_ge(self.sem[dep], self.cnt[dep] if val is None else val)

    def waitv(self, who, dep, val):
        if val:
            self._eng[who].wait_ge(self.sem[dep], val)

    def sync_marks(self, eng, out_ap):
        name = out_ap.tensor.name
        marks = self.pending_sync.pop(name, None)
        if marks:
            for s, v in marks.items():
                if v > 0:
                    eng.wait_ge(self.sem[s], v)

    def sdma(self, qname, sname, out, in_, slow=False):
        if sname not in self.sem:
            cm = self.nc.semaphore("sem_" + sname)
            self.sem[sname] = cm.__enter__()
            self._cms["sem_" + sname] = cm
            self.cnt[sname] = 0
        if hasattr(out, "tensor") and out.tensor.name in self.pending_sync:
            self.sync_marks(self._eng[qname] if qname == "y" else self._eng[qname].raw, out)
        i = self._eng[qname].dma_start(out, in_, allow_slow_non_contiguous=slow)
        i.then_inc(self.sem[sname], 16)
        self.cnt[sname] += 16
        return i

    def dma(self, qname, out, in_, slow=False):
        if hasattr(out, "tensor") and out.tensor.name in self.pending_sync:
            self.sync_marks(self._eng[qname] if qname == "y" else self._eng[qname].raw, out)
        i = self._eng[qname].dma_start(out, in_, allow_slow_non_contiguous=slow)
        self.tick("d" + qname, i, 16)
        return i

    def barrier(self, engines="tvsgy"):
        marks = {}
        for e in engines:
            i = self._eng[e].nop()
            self.tick(e, i, 1)
            marks[e] = self.cnt[e]
        for e in engines:
            for o in engines:
                if o != e:
                    self._eng[e].wait_ge(self.sem[o], marks[o])
            for q in ["dy", "dg", "ds"]:
                self._eng[e].wait_ge(self.sem[q], self.cnt[q])

    def dump(self, tag, ap, producer="v", parts=128):
        if tag not in self.want_dumps:
            return
        w = int(np.prod(ap.shape)) // int(ap.shape[0])
        q = "y" if "float32" in str(ap.dtype) else "g"
        for e in "tvsg":
            self.tick(e, self._eng[e].nop())
            self.wait(q, e)
        self.dma(q, self.dbg[0:parts, self.dump_off:self.dump_off + w], ap)
        self.pending_sync[ap.tensor.name] = dict(self.cnt)
        self.dumps.append((tag, self.dump_off, parts, w, str(ap.dtype)))
        self.dump_off += w

    def stop(self, stage):
        if self.stop_after is not None and stage == self.stop_after:
            self.stop_flag = True
        return self.stop_flag



def emit_ln(P, src_tiles, out_writer, gamma_ap=None, beta_ap=None, fuse_double=False,
            keep_ab=False, direct_dst=None):
    need_cast = "float32" in str(src_tiles[0].dtype)
    sq = P.sbuf("ln_sq", [128, 2 * TC], BF16)
    xb = P.sbuf("ln_xb", [128, 2 * TC], BF16) if need_cast else None
    rows = P.sbuf("ln_rows", [1, 3 * T], F32)
    ab = P.sbuf("ln_ab", [128, 2 * T], F32)
    ab2 = P.sbuf("ln_ab2", [128, 2 * T], F32) if (fuse_double and keep_ab) else None
    ps_s = P.psum("ln_ps_s", [1, TC], F32)
    ps_q = P.psum("ln_ps_q", [1, TC], F32)
    ps_b = P.psum("ln_ps_b", [128, TC], F32)
    ones = P.consts["cOnesB"]

    for c in range(NCH):
        csl = slice(c * TC, (c + 1) * TC)
        for kt in range(KD):
            par = kt % 2
            psl = slice(par * TC, (par + 1) * TC)
            src = src_tiles[kt][:, csl]
            P.wait("s", "t")
            if need_cast:
                i = P.S.activation(xb[:, psl], src, AF.Identity)
                P.tick("s", i)
                xsrc = xb[:, psl]
            else:
                xsrc = src
            i = P.S.activation(sq[:, psl], src, AF.Square)
            P.tick("s", i)
            P.wait("t", "s")
            i = P.PE.matmul(ps_s[0:1, :], ones[:, 0:1], xsrc, start=(kt == 0), stop=(kt == KD - 1))
            i = P.PE.matmul(ps_q[0:1, :], ones[:, 0:1], sq[:, psl], start=(kt == 0), stop=(kt == KD - 1))
            P.tick("t", i)
        P.wait("v", "t")
        i = P.V.tensor_copy(rows[0:1, csl], ps_s[0:1, :])
        i = P.V.tensor_copy(rows[0:1, T + c * TC:T + (c + 1) * TC], ps_q[0:1, :])
        P.tick("v", i)
        P.wait("t", "v")

    muA = rows[0:1, 0:T]
    rB = rows[0:1, T:2 * T]
    bZ = rows[0:1, 2 * T:3 * T]
    vB = rB
    i = P.V.tensor_scalar_mul(muA, rows[0:1, 0:T], 1.0 / D)
    P.tick("v", i)
    P.wait("s", "v")
    i = P.S.activation(bZ, muA, AF.Square)
    P.tick("s", i)
    P.wait("v", "s")
    i = P.V.scalar_tensor_tensor(vB, vB, 1.0 / D, bZ, ALU.mult, ALU.subtract)
    P.tick("v", i)

    brow = P.sbuf("ln_brow", [1, T], F32)

    def bcast_pair(dst_ab):
        P.wait("v", "s")
        i = P.V.tensor_mul(brow[0:1, :], muA, rB_cur[0])
        P.tick("v", i)
        P.wait("y", "v")
        P.dma("y", P.dram["scr_rows"][0:1, :], rB_cur[0])
        P.dma("y", P.dram["scr_rows"][1:2, :], brow[0:1, :])
        P.wait("g", "dy")
        P.dma("g", dst_ab[:, 0:T], P.dram["scr_rows"][0:1, :].broadcast_to([128, T]))
        P.dma("g", dst_ab[:, T:2 * T], P.dram["scr_rows"][1:2, :].broadcast_to([128, T]))
        P.wait("v", "dg")
        P.wait("s", "dg")

    P.wait("s", "v")
    i = P.S.activation(bZ, vB, AF.Ln, bias=EPS)
    P.S.drain()
    i = P.S.activation(bZ, bZ, AF.Exp, scale=-0.5)
    P.tick("s", i)
    rB_cur = [bZ]
    if fuse_double:
        if ab2 is not None:
            bcast_pair(ab2)
        w1 = P.sbuf("ln_w1", [1, 2 * T], F32)
        wa = w1[0:1, 0:T]
        wb = w1[0:1, T:2 * T]
        i = P.S.activation(wa, vB, AF.Ln)
        i = P.S.activation(wb, vB, AF.Ln, bias=EPS)
        P.tick("s", i)
        P.wait("v", "s")
        i = P.V.tensor_sub(wa, wa, wb)
        P.tick("v", i)
        P.wait("s", "v")
        i = P.S.activation(wa, wa, AF.Exp)
        P.tick("s", i)
        P.wait("v", "s")
        i = P.V.tensor_scalar_add(wa, wa, EPS)
        P.tick("v", i)
        P.wait("s", "v")
        i = P.S.activation(wa, wa, AF.Ln)
        P.S.drain()
        i = P.S.activation(wa, wa, AF.Exp, scale=-0.5)
        P.tick("s", i)
        P.wait("v", "s")
        i = P.V.tensor_mul(bZ, bZ, wa)
        P.tick("v", i)
        P.free(w1)
    bcast_pair(ab)

    P.dump("lnab", ab[:, :], "v")
    a_b = ab[:, 0:T]
    b_b = ab[:, T:2 * T]
    tmp = P.sbuf("ln_tmp", [128, T], F32) if direct_dst is None else None
    for kt in range(KD):
        P.wait("v", "s")
        dst = tmp[:, :] if direct_dst is None else direct_dst(kt)
        i = P.V.tensor_mul(dst, src_tiles[kt], a_b)
        i = P.V.tensor_sub(dst, dst, b_b)
        P.tick("v", i)
        if out_writer is not None:
            out_writer(kt, dst,
                       None if gamma_ap is None else gamma_ap[:, kt:kt + 1],
                       None if beta_ap is None else beta_ap[:, kt:kt + 1])
    P.free(sq, rows, brow, ps_s, ps_q, ps_b)
    if tmp is not None:
        P.free(tmp)
    if xb is not None:
        P.free(xb)
    if keep_ab:
        return ab, ab2
    P.free(ab)
    return None


def emit_shift(P, xp, dst):
    for kt in range(KD):
        if kt < 4:
            i = P.S.activation(dst[:, kt * T:(kt + 1) * T], xp[:, kt * T:(kt + 1) * T], AF.Identity)
        else:
            i = P.S.activation(dst[:, kt * T + 1:(kt + 1) * T], xp[:, kt * T:kt * T + T - 1],
                               AF.Identity)
        P.tick("s", i)
    P.wait("v", "s")
    for kt in range(4, KD):
        i = P.V.memset(dst[:, kt * T:kt * T + 1], 0.0)
    P.tick("v", i)



def emit_ffn_layer(P, l, xp, wsb, dump_pref=""):
    xpt = lambda kt: xp[:, kt * T:(kt + 1) * T]

    sh = P.sbuf("ffn_sh", [128, KD * T], BF16, hi=True)
    emit_shift(P, xp, sh)
    P.dump(dump_pref + "shift1", sh[:, :], "v")
    wqkv = P.sbuf("ffc_wqkv", [128, KD * 1536], BF16)
    P.wait("g", "t")
    P.sdma("g", "wqS", wqkv[:, :].rearrange("p (kt m) -> p kt m", kt=KD),
           P.dram["L0_ffn_wqkv"][l].rearrange("(kt p) m -> p kt m", p=128))

    q4 = P.sbuf("ffn_q4", [128, 4 * T], BF16)
    k4 = P.sbuf("ffn_k4", [128, 4 * T], BF16)
    sqb = P.sbuf("ffn_sqb", [128, TC], BF16)
    tt1 = P.sbuf("ffn_tt1", [128, TC], F32)
    s2 = P.sbuf("ffn_s2", [2, T], F32)
    rbf2 = P.sbuf("ffn_rbf2", [2, T], BF16)
    rb = P.sbuf("ffn_rb", [128, T], BF16)
    qn = P.sbuf("ffn_qn", [128, T], BF16)
    t2t = P.sbuf("ffn_t2t", [128, TC], F32)
    ps_m = P.psum("ffn_psm", [128, TC], F32)
    ps_m2 = P.psum("ffn_psm2", [128, TC], F32)
    ps_n = P.psum("ffn_psn", [2, TC], F32)
    ps_r = P.psum("ffn_psr", [128, TC], F32)
    P.wait("t", "wqS")
    rtab = P.sbuf("ffn_rtab", [128, 2 * T], BF16)

    snap = {}
    for side, (out4, scl, tn1, tn2) in enumerate(
            [(q4, wsb["qs"], "cT1q", "cT2q"),
             (k4, wsb["ks"], "cT1k", "cT2k")]):
        t1c = rtab[:, 0:T]
        t2c = rtab[:, T:2 * T]
        P.wait("y", "v")
        P.sdma("y", "rtS", t1c, P.dram[tn1])
        P.sdma("y", "rtS", t2c, P.dram[tn2])
        P.wait("v", "rtS")
        for mt in range(4):
            col0 = side * 512 + mt * 128
            for c in range(NCH):
                csl = slice(c * TC, (c + 1) * TC)
                psx = ps_m if c % 2 == 0 else ps_m2
                P.waitv("t", "v", snap.get(("qv", c % 2)))
                P.waitv("t", "s", snap.get(("qs", c % 2)))
                for kt in range(KD):
                    i = P.PE.matmul(psx[:, :], wqkv[:, kt * 1536 + col0:kt * 1536 + col0 + 128],
                                    sh[:, kt * T + c * TC:kt * T + (c + 1) * TC],
                                    start=(kt == 0), stop=(kt == KD - 1))
                P.tick("t", i)
                P.wait("s", "t")
                i = P.S.activation(sqb[:, :], psx[:, :], AF.Square)
                P.tick("s", i)
                snap[("qs", c % 2)] = P.cnt["s"]
                P.wait("t", "s")
                P.waitv("t", "v", snap.get("psn_v"))
                i = P.PE.matmul(ps_n[0:2, :], P.consts["cB2b"][:, :], sqb[:, :], start=True, stop=True)
                P.tick("t", i)
                P.wait("v", "t")
                i = P.V.tensor_copy(s2[:, csl], ps_n[0:2, :])
                i = P.V.tensor_scalar_mul(qn[:, csl], psx[:, :], scl)
                P.tick("v", i)
                snap[("qv", c % 2)] = P.cnt["v"]
                snap["psn_v"] = P.cnt["v"]
            i = P.V.tensor_scalar_max(s2[:, :], s2[:, :], 1e-24)
            P.tick("v", i)
            P.wait("s", "v")
            i = P.S.activation(s2[:, :], s2[:, :], AF.Ln)
            P.S.drain()
            i = P.S.activation(rbf2[:, :], s2[:, :], AF.Exp, scale=-0.5)
            P.tick("s", i)
            for c in range(NCH):
                csl = slice(c * TC, (c + 1) * TC)
                P.wait("t", "s")
                P.wait("t", "v")
                i = P.PE.matmul(ps_r[:, :], P.consts["cB2T"][:, :], rbf2[:, csl],
                                start=True, stop=True)
                P.tick("t", i)
                P.wait("v", "t")
                i = P.V.tensor_copy(rb[:, csl], ps_r[:, :])
                P.tick("v", i)
                P.wait("t", "v")
            for c in range(NCH):
                csl = slice(c * TC, (c + 1) * TC)
                P.wait("t", "v")
                i = P.PE.matmul(ps_r[:, :], P.consts["cR"][:, :], qn[:, csl], start=True, stop=True)
                P.tick("t", i)
                P.wait("v", "t")
                i = P.V.tensor_tensor(t2t[:, :], ps_r[:, :], t2c[:, csl], ALU.mult)
                P.tick("v", i)
                P.wait("t", "v")
                i = P.V.tensor_tensor(tt1[:, :], qn[:, csl], t1c[:, csl], ALU.mult)
                i = P.V.tensor_add(tt1[:, :], tt1[:, :], t2t[:, :])
                i = P.V.tensor_tensor(out4[:, mt * T + c * TC:mt * T + (c + 1) * TC],
                                      tt1[:, :], rb[:, csl], ALU.mult)
                P.tick("v", i)
    P.dump(dump_pref + "q4", q4[:, :], "v")
    P.dump(dump_pref + "k4", k4[:, :], "v")
    if P.stop(dump_pref + "qk"):
        return xp

    P.free(sqb, tt1, s2, rbf2, rb, qn, t2t, rtab, ps_n, ps_r)
    vtm = P.sbuf("ffn_vtm", [128, 16 * 512], BF16)
    if P.stop(dump_pref + "v0"):
        return xp
    snap = {}
    for tt in range(16):
        psx = ps_m if tt % 2 == 0 else ps_m2
        P.waitv("t", "v", snap.get(tt % 2))
        for kt in range(KD):
            i = P.PE.matmul(psx[:, :], sh[:, kt * T + tt * 128:kt * T + (tt + 1) * 128],
                            wqkv[:, kt * 1536 + 1024:kt * 1536 + 1536],
                            start=(kt == 0), stop=(kt == KD - 1))
        P.tick("t", i)
        P.wait("v", "t")
        i = P.V.tensor_copy(vtm[:, tt * 512:tt * 512 + 512], psx[:, :])
        P.tick("v", i)
        snap[tt % 2] = P.cnt["v"]
    P.free(sh, wqkv, ps_m, ps_m2)
    P.dump(dump_pref + "vtm", vtm[:, :], "s")
    if P.stop(dump_pref + "v"):
        return xp

    smax = P.sbuf("ffn_smax", [128, 2], F32)
    nmax = P.sbuf("ffn_nmax", [128, 2], F32)
    smf = P.sbuf("ffn_smf", [128, 2 * 640], F32)
    Pb = P.sbuf("ffn_P", [128, 2 * 640], BF16)
    PT = P.sbuf("ffn_PT", [128, 2 * 8 * 512], BF16)
    rsr = P.sbuf("ffn_rsr", [128, TC], F32)
    rbq = P.sbuf("ffn_rbq", [128, TC], F32)
    osb = P.sbuf("ffn_o", [128, 4 * T], BF16)
    ps_sim = [P.psum("ffn_pssimA", [128, 640], F32), P.psum("ffn_pssimB", [128, 640], F32)]
    ps_pt = [P.psum("ffn_psptA", [128, 640], BF16), P.psum("ffn_psptB", [128, 640], BF16)]
    ps_o = P.psum("ffn_pso", [128, TC], F32)
    ps_rr = P.psum("ffn_psrr", [128, TC], F32)

    wins = [(h, qw) for h in range(NH) for qw in range(4)]
    steps = [(w, qt) for w in range(len(wins)) for qt in range(4)]
    sn = {}
    pend = None
    P.wait("t", "v")

    def meta(w, qt):
        h, qw = wins[w]
        rt, po = h // 2, (h % 2) * 64
        if qw == 0:
            nblk, ktok0 = qt + 1, 0
        else:
            nblk, ktok0 = 5, (qw - 1) * 4 + qt
        return h, qw, rt, po, nblk, ktok0, nblk * 128

    for it in range(len(steps) + 1):
        if it < len(steps):
            w, qt = steps[it]
            h, qw, rt, po, nblk, ktok0, width = meta(w, qt)
            par = it % 2
            i_abs = qw * 4 + qt
            P.waitv("t", "v", sn.get(("simfree", par)))
            lhs = q4[po:po + 64, rt * T + i_abs * 128:rt * T + (i_abs + 1) * 128]
            off = 0
            while off < width:
                n = min(TC, width - off)
                i = P.PE.matmul(ps_sim[par][:, off:off + n], lhs,
                                k4[po:po + 64,
                                   rt * T + ktok0 * 128 + off:rt * T + ktok0 * 128 + off + n],
                                start=True, stop=True)
                off += n
            P.tick("t", i)
            sn[("simdone", par)] = P.cnt["t"]
        if pend is not None:
            w1, rt1, po1, qw1, b, rsrs = pend
            pend = None
            P.waitv("t", "s", rsrs)
            i = P.PE.matmul(ps_rr[po1:po1 + 64, :], P.consts["cOnesRowF"][b:b + 1, 0:64],
                            rsr[b:b + 1, :], start=True, stop=True, tile_position=(b, po1))
            P.tick("t", i)
            sn[("rbdone", b)] = P.cnt["t"]
            P.waitv("v", "t", sn[("rbdone", b)])
            i = P.V.tensor_copy(rbq[po1:po1 + 64, :], ps_rr[po1:po1 + 64, :])
            i = P.V.tensor_tensor(osb[po1:po1 + 64, rt1 * T + qw1 * TC:rt1 * T + (qw1 + 1) * TC],
                                  ps_o[po1:po1 + 64, :], rbq[po1:po1 + 64, :], ALU.mult)
            P.tick("v", i)
            sn["osbdone"] = P.cnt["v"]
        if it >= 1:
            w1, qt1 = steps[it - 1]
            h1, qw1, rt1, po1, nblk1, ktok01, width1 = meta(w1, qt1)
            par1 = (it - 1) % 2
            sm = smf[:, par1 * 640:par1 * 640 + width1]
            pssim = ps_sim[par1]
            P.waitv("v", "t", sn[("simdone", par1)])
            P.waitv("v", "s", sn.get(("sexp", par1)))
            if qw1 == 0:
                if nblk1 > 1:
                    i = P.V.tensor_copy(sm[:, 0:width1 - 128], pssim[:, 0:width1 - 128])
                i = P.V.tensor_tensor(sm[:, width1 - 128:width1],
                                      pssim[:, width1 - 128:width1],
                                      P.consts["cM0A"][:, :], ALU.add)
            else:
                i = P.V.tensor_tensor(sm, pssim[:, 0:width1],
                                      P.consts["cMgA"][:, 0:width1], ALU.add)
            i = P.V.reduce_max(smax[:, par1:par1 + 1], sm, axis=AXX)
            i = P.V.tensor_scalar_mul(nmax[:, par1:par1 + 1], smax[:, par1:par1 + 1], -QK_SCALE)
            P.tick("v", i)
            sn[("simfree", par1)] = P.cnt["v"]
            P.wait("s", "v")
            i = P.S.activation(Pb[:, par1 * 640:par1 * 640 + width1], sm, AF.Exp,
                               bias=nmax[:, par1:par1 + 1], scale=QK_SCALE)
            P.tick("s", i)
            sn[("sexp", par1)] = P.cnt["s"]
            P.waitv("t", "s", sn[("sexp", par1)])
            P.waitv("t", "v", sn.get(("ptfree", par1)))
            for j in range(nblk1):
                i = P.PE.transpose(ps_pt[par1][:, j * 128:(j + 1) * 128],
                                   Pb[:, par1 * 640 + j * 128:par1 * 640 + (j + 1) * 128],
                                   P.consts["cI"][:, :])
            P.tick("t", i)
            sn[("tdone", par1)] = P.cnt["t"]
            PTb = PT[:, (w1 % 2) * 4096:(w1 % 2 + 1) * 4096]
            P.waitv("v", "t", sn[("tdone", par1)])
            P.waitv("v", "t", sn.get(("avdone", w1 % 2)))
            qcol = qt1 * 128
            written = set()
            for j in range(nblk1):
                slot = (qt1 + j) if qw1 > 0 else 4 + j
                written.add(slot)
                i = P.V.tensor_copy(PTb[:, slot * 512 + qcol:slot * 512 + qcol + 128],
                                    ps_pt[par1][:, j * 128:(j + 1) * 128])
            allslots = range(0, 8) if qw1 > 0 else range(4, 8)
            for slot in allslots:
                if slot not in written:
                    i = P.V.memset(PTb[:, slot * 512 + qcol:slot * 512 + qcol + 128], 0.0)
            P.tick("v", i)
            sn[("ptfree", par1)] = P.cnt["v"]
            if qt1 == 3:
                b = (po1 + 64) % 128
                P.waitv("t", "v", P.cnt["v"])
                P.waitv("t", "v", sn.get("osbdone"))
                slots = list(range(4, 8)) if qw1 == 0 else list(range(0, 8))
                first = True
                for s_ in slots:
                    vt = (s_ - 4) if qw1 == 0 else (qw1 - 1) * 4 + s_
                    lhsv = vtm[:, vt * 512 + h1 * 64:vt * 512 + h1 * 64 + 64]
                    i = P.PE.matmul(ps_o[po1:po1 + 64, :], lhsv, PTb[:, s_ * 512:(s_ + 1) * 512],
                                    start=first, stop=(s_ == slots[-1]), tile_position=(0, po1))
                    i = P.PE.matmul(ps_rr[b:b + 1, :], P.consts["cOnesB"][:, 0:1],
                                    PTb[:, s_ * 512:(s_ + 1) * 512],
                                    start=first, stop=(s_ == slots[-1]), tile_position=(0, b))
                    first = False
                P.tick("t", i)
                sn[("avdone", w1 % 2)] = P.cnt["t"]
                P.waitv("s", "t", sn[("avdone", w1 % 2)])
                P.waitv("s", "t", sn.get(("rbdone", b)))
                i = P.S.activation(rsr[b:b + 1, :], ps_rr[b:b + 1, :], AF.Ln)
                P.S.drain()
                i = P.S.activation(rsr[b:b + 1, :], rsr[b:b + 1, :], AF.Exp, scale=-1.0)
                P.tick("s", i)
                pend = (w1, rt1, po1, qw1, b, P.cnt["s"])
    if pend is not None:
        w1, rt1, po1, qw1, b, rsrs = pend
        pend = None
        P.waitv("t", "s", rsrs)
        i = P.PE.matmul(ps_rr[po1:po1 + 64, :], P.consts["cOnesRowF"][b:b + 1, 0:64],
                        rsr[b:b + 1, :], start=True, stop=True, tile_position=(b, po1))
        P.tick("t", i)
        P.wait("v", "t")
        i = P.V.tensor_copy(rbq[po1:po1 + 64, :], ps_rr[po1:po1 + 64, :])
        i = P.V.tensor_tensor(osb[po1:po1 + 64, rt1 * T + qw1 * TC:rt1 * T + (qw1 + 1) * TC],
                              ps_o[po1:po1 + 64, :], rbq[po1:po1 + 64, :], ALU.mult)
        P.tick("v", i)
    P.wait("t", "v")
    P.free(q4, k4, vtm, smax, nmax, smf, Pb, PT, rsr, rbq)
    P.free(ps_sim[0], ps_sim[1], ps_pt[0], ps_pt[1], ps_o, ps_rr)
    P.dump(dump_pref + "o", osb[:, :], "v")
    if P.stop(dump_pref + "attn"):
        return xp

    wo = P.sbuf("ffc_wo", [128, 4 * D], BF16)
    P.sdma("g", "woS", wo[:, :].rearrange("p (ik m) -> p ik m", ik=4),
           P.dram["L0_ffn_wo"][l].rearrange("(ik p) m -> p ik m", p=128))
    ps_a = P.psum("ffn_psa", [128, TC], F32)
    ps_b2 = P.psum("ffn_psb", [128, TC], F32)
    P.wait("t", "woS")
    snap = {}
    for kt in range(KD):
        for c in range(NCH):
            par = (kt * NCH + c) % 2
            pss = ps_a if par == 0 else ps_b2
            P.waitv("t", "v", snap.get(par))
            for ik in range(4):
                i = P.PE.matmul(pss[:, :], wo[:, ik * D + kt * 128:ik * D + (kt + 1) * 128],
                                osb[:, ik * T + c * TC:ik * T + (c + 1) * TC],
                                start=(ik == 0), stop=(ik == 3))
            P.tick("t", i)
            P.wait("v", "t")
            i = P.V.tensor_add(xpt(kt)[:, c * TC:(c + 1) * TC],
                               xpt(kt)[:, c * TC:(c + 1) * TC], pss[:, :])
            P.tick("v", i)
            snap[par] = P.cnt["v"]
    P.free(osb, wo, ps_a, ps_b2)
    P.dump(dump_pref + "x_attn", xp[:, :], "v")
    if P.stop(dump_pref + "wo"):
        return xp

    lnx = P.sbuf("ffn_lnx", [128, KD * T], BF16, hi=True)
    sh2 = P.sbuf("ffn_sh2", [128, KD * T], BF16, hi=True)
    emit_shift(P, xp, sh2)

    def wr(kt, tmp, g, b):
        P.wait("s", "v")
        i = P.S.activation(lnx[:, kt * T:(kt + 1) * T], tmp[:, :], AF.Identity, bias=b, scale=g)
        P.tick("s", i)

    emit_ln(P, [sh2[:, kt * T:(kt + 1) * T] for kt in range(KD)], wr,
            gamma_ap=wsb["lng"], beta_ap=wsb["lnb"])
    P.free(sh2)
    P.dump(dump_pref + "lnx2", lnx[:, :], "s")
    P.tick("v", P.V.nop())
    P.wait("y", "v")
    P.dma("y", P.dram["scr_xp"].rearrange("(kt p) w -> p kt w", p=128),
          xp[:, :].rearrange("p (kt w) -> p kt w", kt=KD))
    P.wait("y", "dy")
    P.free(xp)
    if P.stop(dump_pref + "ln2i"):
        return None

    gated = P.sbuf("ffn_gated", [128, 22 * T], BF16)
    w1a = P.sbuf("ffn_w1a", [128, 2 * KD * 128], BF16)
    w1g = P.sbuf("ffn_w1g", [128, 2 * KD * 128], BF16)
    ps_w = [P.psum("ffn_psw%d" % k, [128, TC], F32) for k in range(4)]
    w1d = P.dram["L0_ffn_w1"]
    snap = {}
    for j, (off, sz) in enumerate(HT):
        par = j % 2
        base = par * KD * 128
        sname = "w1p%d" % par
        P.waitv("g", "t", snap.get(("t", par)))
        P.sdma("g", sname, w1a[:, base:base + KD * sz].rearrange("p (kt m) -> p kt m", kt=KD),
               w1d[l, :, off:off + sz].rearrange("(kt p) m -> p kt m", p=128))
        P.sdma("g", sname, w1g[:, base:base + KD * sz].rearrange("p (kt m) -> p kt m", kt=KD),
               w1d[l, :, HID + off:HID + off + sz].rearrange("(kt p) m -> p kt m", p=128))
        dgs = P.cnt[sname]
        for c in range(NCH):
            cp = c % 2
            ps_wa, ps_wg = ps_w[cp * 2], ps_w[cp * 2 + 1]
            P.waitv("t", sname, dgs)
            P.waitv("t", "v", snap.get(("v", cp)))
            for kt in range(KD):
                i = P.PE.matmul(ps_wa[0:sz, :], w1a[:, base + kt * sz:base + (kt + 1) * sz],
                                lnx[:, kt * T + c * TC:kt * T + (c + 1) * TC],
                                start=(kt == 0), stop=(kt == KD - 1))
            for kt in range(KD):
                i = P.PE.matmul(ps_wg[0:sz, :], w1g[:, base + kt * sz:base + (kt + 1) * sz],
                                lnx[:, kt * T + c * TC:kt * T + (c + 1) * TC],
                                start=(kt == 0), stop=(kt == KD - 1))
            P.tick("t", i)
            gsl = gated[0:sz, j * T + c * TC:j * T + (c + 1) * TC]
            P.wait("s", "t")
            i = P.S.activation(gsl, ps_wg[0:sz, :], AF.Gelu, bias=wsb["b1g"][0:sz, j:j + 1])
            P.tick("s", i)
            P.wait("v", "s")
            i = P.V.scalar_tensor_tensor(gsl, ps_wa[0:sz, :], wsb["b1a"][0:sz, j:j + 1], gsl,
                                         ALU.add, ALU.mult)
            P.tick("v", i)
            snap[("v", cp)] = P.cnt["v"]
        snap[("t", par)] = P.cnt["t"]
    P.free(w1a, w1g, *ps_w)
    P.free(lnx)
    P.dump(dump_pref + "gated", gated[:, :], "v")
    if P.stop(dump_pref + "gated"):
        return None

    xp = P.sbuf("xp_arena", [128, KD * T], F32, hi=True)
    P.dma("y", xp[:, :].rearrange("p (kt w) -> p kt w", kt=KD),
          P.dram["scr_xp"].rearrange("(kt p) w -> p kt w", p=128))
    w2s = P.sbuf("ffn_w2s", [128, 2 * 22 * 128], BF16)
    ps_a2 = P.psum("ffn_ps2a", [128, TC], F32)
    ps_b3 = P.psum("ffn_ps2b", [128, TC], F32)
    w2d = P.dram["L0_ffn_w2"]
    P.wait("v", "dy")
    snap = {}
    for m in range(KD):
        par = m % 2
        base = par * 22 * 128
        P.waitv("g", "t", snap.get(("t", par)))
        sname = "w2p%d" % par
        for j, (off, sz) in enumerate(HT):
            P.sdma("g", sname, w2s[0:sz, base + j * 128:base + (j + 1) * 128],
                   w2d[l, off:off + sz, m * 128:(m + 1) * 128])
        snap[("dg", par)] = (sname, P.cnt[sname])
        for c in range(NCH):
            cp = c % 2
            pss = ps_a2 if cp == 0 else ps_b3
            P.waitv("t", *snap[("dg", par)])
            P.waitv("t", "v", snap.get(("v", cp)))
            for j, (off, sz) in enumerate(HT):
                i = P.PE.matmul(pss[:, :], w2s[0:sz, base + j * 128:base + j * 128 + 128],
                                gated[0:sz, j * T + c * TC:j * T + (c + 1) * TC],
                                start=(j == 0), stop=(j == 21))
            P.tick("t", i)
            P.wait("v", "t")
            i = P.V.scalar_tensor_tensor(xpt(m)[:, c * TC:(c + 1) * TC], pss[:, :],
                                         wsb["b2"][:, m:m + 1], xpt(m)[:, c * TC:(c + 1) * TC],
                                         ALU.add, ALU.add)
            P.tick("v", i)
            snap[("v", cp)] = P.cnt["v"]
        snap[("t", par)] = P.cnt["t"]
    P.free(w2s, ps_a2, ps_b3, gated)
    P.dump(dump_pref + f"x_l{l}", xp[:, :], "v")
    return xp


def emit_ffn_call(P, xp, dump_pref=""):
    for l in range(2):
        wsb = {
            "qs": P.par["qs"][:, l:l + 1], "ks": P.par["ks"][:, l:l + 1],
            "lng": P.par["lng"][:, l * KD:(l + 1) * KD], "lnb": P.par["lnb"][:, l * KD:(l + 1) * KD],
            "b1a": P.par["b1a"][:, l * 22:(l + 1) * 22], "b1g": P.par["b1g"][:, l * 22:(l + 1) * 22],
            "b2": P.par["b2"][:, l * KD:(l + 1) * KD],
        }
        P.barrier()
        xp = emit_ffn_layer(P, l, xp, wsb, dump_pref=dump_pref + f"l{l}_")
        if P.stop_flag:
            break
        P.barrier()
        if P.stop(dump_pref + f"l{l}"):
            break
    return xp



def emit_mamba(P, lnx, dump_pref=""):
    xsp = P.sbuf("mb_xsp", [128, 2 * TC], BF16)
    ztile = P.sbuf("mb_zt", [128, KI * 4], BF16)
    i = P.V.memset(ztile[:, :], 0.0)
    P.tick("v", i)
    P.wait("y", "v")
    P.sdma("y", "xpadS", P.dram["scr_xs"][:, 0:4].rearrange("(kt p) w -> p kt w", p=128),
           ztile[:, :].rearrange("p (kt w) -> p kt w", kt=KI))
    P.sdma("y", "xpadS", P.dram["scr_xs"][:, XSW - 4:XSW].rearrange("(kt p) w -> p kt w", p=128),
           ztile[:, :].rearrange("p (kt w) -> p kt w", kt=KI))
    inw = P.sbuf("mb_inw", [128, 2 * KD * 128], BF16)
    zbuf = P.sbuf("mb_zb", [128, 2 * TC], BF16)
    ps_i = P.psum("mb_psi", [128, TC], F32)
    ps_i2 = P.psum("mb_psi2", [128, TC], F32)
    inwd = P.dram["L0_attn_in_w"]
    snap = {}
    for m in range(2 * KI):
        par = m % 2
        base = par * KD * 128
        P.waitv("g", "t", snap.get(("t", par)))
        sname = "inwp%d" % par
        P.sdma("g", sname, inw[:, base:base + KD * 128].rearrange("p (kt mm) -> p kt mm", kt=KD),
               inwd[:, m * 128:(m + 1) * 128].rearrange("(kt p) mm -> p kt mm", p=128))
        snap[("dg", par)] = (sname, P.cnt[sname])
        for c in range(NCH):
            cp = c % 2
            pss = ps_i if cp == 0 else ps_i2
            P.waitv("t", *snap[("dg", par)])
            P.waitv("t", "s", snap.get(("s", cp)))
            for kt in range(KD):
                i = P.PE.matmul(pss[:, :], inw[:, base + kt * 128:base + (kt + 1) * 128],
                                lnx[:, kt * T + c * TC:kt * T + (c + 1) * TC],
                                start=(kt == 0), stop=(kt == KD - 1))
            P.tick("t", i)
            P.wait("s", "t")
            if m < KI:
                xp2 = c % 2
                xn = "xswp%d" % xp2
                P.waitv("s", xn, snap.get(("xdy", xp2)))
                i = P.S.activation(xsp[:, xp2 * TC:(xp2 + 1) * TC], pss[:, :], AF.Identity)
                P.tick("s", i)
                P.wait("y", "s")
                P.sdma("y", xn, P.dram["scr_xs"][m * 128:(m + 1) * 128, 4 + c * TC:4 + (c + 1) * TC],
                       xsp[:, xp2 * TC:(xp2 + 1) * TC])
                snap[("xdy", xp2)] = P.cnt[xn]
            else:
                zp = c % 2
                zn = "zspp%d" % zp
                P.waitv("s", zn, snap.get(("zdy", zp)))
                i = P.S.activation(zbuf[:, zp * TC:(zp + 1) * TC], pss[:, :], AF.Silu)
                P.tick("s", i)
                P.wait("y", "s")
                P.sdma("y", zn, P.dram["scr_z"][(m - KI) * 128:(m - KI + 1) * 128, c * TC:(c + 1) * TC],
                       zbuf[:, zp * TC:(zp + 1) * TC])
                snap[("zdy", zp)] = P.cnt[zn]
            snap[("s", cp)] = P.cnt["s"]
        snap[("t", par)] = P.cnt["t"]
    P.free(inw, zbuf, ps_i, ps_i2, lnx, xsp, ztile)
    xsw_fin = (P.cnt.get("xswp0", 0), P.cnt.get("xswp1", 0), P.cnt.get("xpadS", 0))
    P.barrier()
    if P.stop("mamba_xs"):
        return

    for dirn in range(2):
        sfx = "" if dirn == 0 else "_b"
        wn = sfx
        pA = P.par["A" + sfx]
        pcw = P.par["cw" + sfx]
        pcb = P.par["cb" + sfx]
        pdtb = P.par["dtb" + sfx]
        pD = P.par["Dsk" + sfx]
        xst = P.sbuf("mb_xst", [128, 2 * XSW], BF16)
        xld = {}
        vconv = {}
        P.waitv("y", "xswp0", xsw_fin[0])
        P.waitv("y", "xswp1", xsw_fin[1])
        P.waitv("y", "xpadS", xsw_fin[2])

        def load_xs(kt):
            par = kt % 2
            xn = "xsL%d" % par
            P.waitv("y", "v", vconv.get(par))
            P.sdma("y", xn, xst[:, par * XSW:(par + 1) * XSW],
                   P.dram["scr_xs"][kt * 128:(kt + 1) * 128, :])
            xld[par] = P.cnt[xn]

        xprojT = P.sbuf("mb_xproj", [128, KI * 96], BF16)
        dtwT = P.sbuf("mb_dtw", [64, KI * 128], BF16)
        P.dma("g", xprojT[:, :].rearrange("p (kt m) -> p kt m", kt=KI),
              P.dram["L0_attn_xproj_w" + wn].rearrange("(kt p) m -> p kt m", p=128))
        P.dma("g", dtwT[:, :], P.dram["L0_attn_dt_w" + wn])
        xproj = xprojT[:, :]
        dtw = dtwT[:, :]
        ydram = P.dram["scr_yf" if dirn == 0 else "scr_yb"]

        def conv_tile(kt, dst_f32, t0, tw):
            tb = (kt % 2) * XSW
            P.waitv("v", "xsL%d" % (kt % 2), xld[kt % 2])
            for k in range(4):
                if dirn == 0:
                    src = xst[:, tb + 1 + k + t0:tb + 1 + k + t0 + tw]
                else:
                    st = tb + T + 6 - k - t0
                    src = xst[:, st:st - tw:-1]
                w_ap = pcw[:, kt * 4 + k:kt * 4 + k + 1]
                if k == 0:
                    i = P.V.tensor_scalar_mul(dst_f32, src, w_ap)
                else:
                    i = P.V.scalar_tensor_tensor(dst_f32, src, w_ap, dst_f32, ALU.mult, ALU.add)
            P.tick("v", i)

        xcb = P.sbuf("mbA_xc", [128, 2 * T], BF16)
        accA = P.sbuf("mbA_acc", [128, T], F32)
        dtpre = P.sbuf("mbA_dtpre", [64, T], BF16)
        bcrow = P.sbuf("mbA_bcrow", [48, T], BF16)
        ps_dt = P.psum("mbA_psdt", [64, T], F32)
        ps_bc = P.psum("mbA_psbc", [48, T], F32)
        load_xs(0)
        for kt in range(KI):
            par = kt % 2
            P.wait("v", "t")
            P.wait("v", "s")
            conv_tile(kt, accA[:, :], 0, T)
            vconv[kt % 2] = P.cnt["v"]
            if kt + 1 < KI:
                load_xs(kt + 1)
            P.wait("s", "v")
            P.wait("s", "t")
            i = P.S.activation(xcb[:, par * T:(par + 1) * T], accA[:, :], AF.Silu,
                               bias=pcb[:, kt:kt + 1])
            P.tick("s", i)
            P.wait("t", "s")
            for c in range(NCH):
                csl = slice(c * TC, (c + 1) * TC)
                mv = xcb[:, par * T + c * TC:par * T + (c + 1) * TC]
                i = P.PE.matmul(ps_dt[0:64, csl], xproj[:, kt * 96:kt * 96 + 64], mv,
                                start=(kt == 0), stop=(kt == KI - 1))
                i = P.PE.matmul(ps_bc[0:16, csl], xproj[:, kt * 96 + 64:kt * 96 + 80], mv,
                                start=(kt == 0), stop=(kt == KI - 1))
                i = P.PE.matmul(ps_bc[32:48, csl], xproj[:, kt * 96 + 80:kt * 96 + 96], mv,
                                start=(kt == 0), stop=(kt == KI - 1), tile_position=(0, 32))
            P.tick("t", i)
        P.wait("s", "t")
        i = P.S.activation(dtpre[:, :], ps_dt[0:64, :], AF.Identity)
        i = P.S.activation(bcrow[0:16, :], ps_bc[0:16, :], AF.Identity)
        i = P.S.activation(bcrow[32:48, :], ps_bc[32:48, :], AF.Identity)
        P.tick("s", i)
        P.wait("y", "s")
        P.dma("y", P.dram["scr_b"], bcrow[0:16, :])
        P.dma("y", P.dram["scr_c"], bcrow[32:48, :])
        P.wait("y", "dy")
        P.free(xcb, accA, ps_bc, bcrow)
        P.dump(dump_pref + "dtpre" + sfx, dtpre[:, :], "s", parts=64)

        NGS = 4
        NGRP = DST // NGS
        SEGW = TH + 1
        GWS = NGS * SEGW
        state = P.sbuf("mb_state", [128, KI * DST], F32)
        bcb = P.sbuf("mb_bcb", [128, 2 * DST * TH], BF16)
        dts = P.sbuf("mb_dts", [128, TH], F32)
        xcs = P.sbuf("mb_xcs", [128, TH], BF16)
        us = P.sbuf("mb_us", [128, TH], BF16)
        NB = 3
        dAg = P.sbuf("mb_dAg", [128, 2 * GWS], F32)
        dbug = P.sbuf("mb_dbug", [128, NB * GWS], BF16)
        cyg = P.sbuf("mb_cyg", [128, NGS * TH], BF16)
        tmp2 = P.sbuf("mb_tmp2", [128, 2 * TH], BF16)
        hbg = P.sbuf("mb_hbg", [128, GWS], BF16)
        yg = P.sbuf("mb_yg", [128, NGRP * 2 * TH], BF16)
        ysum = P.sbuf("mb_ysum", [128, TH], BF16)
        yout = P.sbuf("mb_yout", [128, TH], BF16)
        yrev = P.sbuf("mb_yrev", [128, TH], BF16) if dirn == 1 else None
        ps_d2 = P.psum("mb_psd2", [128, TC], F32)
        acc2 = P.sbuf("mb_acc2", [128, 2 * TH], F32)
        accS = acc2[:, 0:TH]
        sp2 = acc2[:, TH:2 * TH]
        i = P.V.memset(dAg[:, 0:2 * GWS:SEGW], 0.0)
        P.tick("v", i)
        for nb in range(NB):
            i = P.G.memset(dbug[:, nb * GWS:(nb + 1) * GWS:SEGW], 0.0)
        P.tick("g", i)

        gctr = 0
        snap_scan = None
        snap_scan_p = {}
        state_v = None
        for half in range(2):
            t0 = half * TH
            P.wait("y", "g")
            P.wait("y", "v")
            for n in range(DST):
                P.sdma("y", "bcbS", bcb[:, n * TH:(n + 1) * TH],
                       P.dram["scr_b"][n:n + 1, t0:t0 + TH].broadcast_to([128, TH]))
                P.sdma("y", "bcbS", bcb[:, (DST + n) * TH:(DST + n + 1) * TH],
                       P.dram["scr_c"][n:n + 1, t0:t0 + TH].broadcast_to([128, TH]))
            bc_dy = P.cnt["bcbS"]
            load_xs(0)
            for kt in range(KI):
                P.wait("v", "s")
                P.wait("v", "g")
                conv_tile(kt, accS, t0, TH)
                vconv[kt % 2] = P.cnt["v"]
                if kt + 1 < KI:
                    load_xs(kt + 1)
                P.wait("s", "v")
                i = P.S.activation(xcs[:, :], accS, AF.Silu, bias=pcb[:, kt:kt + 1])
                P.tick("s", i)
                for c in range(2):
                    P.wait("t", "v")
                    P.wait("t", "s")
                    i = P.PE.matmul(ps_d2[:, :], dtw[:, kt * 128:(kt + 1) * 128],
                                    dtpre[:, t0 + c * TC:t0 + (c + 1) * TC], start=True, stop=True)
                    P.tick("t", i)
                    P.wait("v", "t")
                    i = P.V.tensor_scalar(accS[:, c * TC:(c + 1) * TC], ps_d2[:, :],
                                          pdtb[:, kt:kt + 1], 0.0, ALU.add, ALU.max)
                    i = P.V.tensor_scalar(sp2[:, c * TC:(c + 1) * TC], ps_d2[:, :],
                                          pdtb[:, kt:kt + 1], None, ALU.add)
                    P.tick("v", i)
                P.wait("s", "v")
                i = P.S.activation(sp2, sp2, AF.Abs)
                P.S.drain()
                i = P.S.activation(sp2, sp2, AF.Exp, scale=-1.0)
                P.S.drain()
                i = P.S.activation(sp2, sp2, AF.Ln, bias=1.0)
                P.tick("s", i)
                P.wait("v", "s")
                i = P.V.tensor_add(dts[:, :], accS, sp2)
                i = P.V.tensor_tensor(us[:, :], dts[:, :], xcs[:, :], ALU.mult)
                P.tick("v", i)
                us_v = P.cnt["v"]
                for g in range(NGRP):
                    gp = gctr % NB
                    ap = gctr % 2
                    gctr += 1
                    n0 = g * NGS
                    s0 = kt * DST + n0
                    P.waitv("s", "v", max(us_v, snap_scan_p.get(("a", ap)) or 0))
                    for j in range(NGS):
                        i = P.S.activation(dAg[:, ap * GWS + j * SEGW + 1:ap * GWS + (j + 1) * SEGW],
                                           dts[:, :], AF.Exp, scale=pA[:, s0 + j:s0 + j + 1])
                    P.tick("s", i)
                    sexp_g = P.cnt["s"]
                    P.waitv("g", "v", max(us_v, snap_scan_p.get(("b", gp)) or 0, state_v or 0))
                    P.waitv("g", "bcbS", bc_dy)
                    for j in range(NGS):
                        i = P.G.tensor_tensor(dbug[:, gp * GWS + j * SEGW + 1:gp * GWS + (j + 1) * SEGW],
                                              us[:, :], bcb[:, (n0 + j) * TH:(n0 + j + 1) * TH],
                                              ALU.mult)
                    if half == 1:
                        i = P.G.tensor_copy(dbug[:, gp * GWS:gp * GWS + GWS:SEGW],
                                            state[:, s0:s0 + NGS])
                    P.tick("g", i)
                    gfill_g = P.cnt["g"]
                    P.waitv("v", "s", sexp_g)
                    P.waitv("v", "g", gfill_g)
                    i = P.V.tensor_tensor_scan(hbg[:, :], dAg[:, ap * GWS:(ap + 1) * GWS],
                                               dbug[:, gp * GWS:(gp + 1) * GWS],
                                               0.0, ALU.mult, ALU.add)
                    P.tick("v", i)
                    snap_scan = P.cnt["v"]
                    snap_scan_p[("a", ap)] = snap_scan
                    snap_scan_p[("b", gp)] = snap_scan
                    if half == 0:
                        i = P.V.tensor_copy(state[:, s0:s0 + NGS], hbg[:, TH:GWS:SEGW])
                    i = P.V.tensor_tensor(
                        cyg[:, :].rearrange("p (n t) -> p n t", n=NGS),
                        hbg[:, :].rearrange("p (n t) -> p n t", n=NGS)[:, :, 1:SEGW],
                        bcb[:, DST * TH + n0 * TH:DST * TH + (n0 + NGS) * TH]
                           .rearrange("p (n t) -> p n t", n=NGS),
                        ALU.mult)
                    i = P.V.tensor_add(yg[:, g * 2 * TH:(g + 1) * 2 * TH],
                                       cyg[:, 0:2 * TH], cyg[:, 2 * TH:4 * TH])
                    P.tick("v", i)
                if "yspS" in P.cnt:
                    P.waitv("v", "yspS", P.cnt["yspS"])
                i = P.V.tensor_add(tmp2[:, :], yg[:, 0:2 * TH], yg[:, 2 * TH:4 * TH])
                i = P.V.tensor_add(cyg[:, 0:2 * TH], yg[:, 4 * TH:6 * TH], yg[:, 6 * TH:8 * TH])
                i = P.V.tensor_add(cyg[:, 2 * TH:4 * TH], tmp2[:, :], cyg[:, 0:2 * TH])
                i = P.V.tensor_add(ysum[:, :], cyg[:, 2 * TH:3 * TH], cyg[:, 3 * TH:4 * TH])
                i = P.V.scalar_tensor_tensor(yout[:, :], xcs[:, :], pD[:, kt:kt + 1], ysum[:, :],
                                             ALU.mult, ALU.add)
                P.tick("v", i)
                if dirn == 0:
                    P.wait("y", "v")
                    P.sdma("y", "yspS", ydram[kt * 128:(kt + 1) * 128, t0:t0 + TH], yout[:, :])
                else:
                    i = P.V.tensor_copy(yrev[:, :], yout[:, TH - 1::-1])
                    P.tick("v", i)
                    P.wait("y", "v")
                    P.sdma("y", "yspS", ydram[kt * 128:(kt + 1) * 128, T - t0 - TH:T - t0], yrev[:, :])
            if half == 0:
                state_v = P.cnt["v"]
        P.free(dtpre, ps_dt, state, bcb, dts, xcs, us, dAg, dbug, cyg, tmp2, acc2,
               hbg, yg, ysum, yout, ps_d2, xprojT, dtwT, xst)
        if yrev is not None:
            P.free(yrev)
        P.barrier()
        if P.stop("mamba_dir%d" % dirn):
            return

    ych = P.sbuf("mg_ych", [128, 32 * TC], BF16)
    mw = P.sbuf("mg_mw", [128, 2 * 32 * 128], BF16)
    zs = P.sbuf("mg_zs", [128, 2 * TC], BF16)
    ymb = P.sbuf("mg_ym", [128, 2 * TC], BF16)
    ps_g = P.psum("mg_psg", [128, TC], F32)
    ps_g2 = P.psum("mg_psg2", [128, TC], F32)
    mwd = P.dram["L0_attn_merge_w"]
    snap = {}
    for c in range(NCH):
        P.wait("y", "t")
        P.sdma("y", "ychS", ych[:, 0:16 * TC].rearrange("p (kt w) -> p kt w", kt=KI),
               P.dram["scr_yf"][:, c * TC:(c + 1) * TC].rearrange("(kt p) w -> p kt w", p=128))
        P.sdma("y", "ychS", ych[:, 16 * TC:32 * TC].rearrange("p (kt w) -> p kt w", kt=KI),
               P.dram["scr_yb"][:, c * TC:(c + 1) * TC].rearrange("(kt p) w -> p kt w", p=128))
        ych_dy = P.cnt["ychS"]
        for m in range(KI):
            par = m % 2
            base = par * 32 * 128
            P.waitv("g", "t", snap.get(("t", par)))
            mn = "mwp%d" % par
            P.sdma("g", mn, mw[:, base:base + 32 * 128].rearrange("p (kt mm) -> p kt mm", kt=32),
                   mwd[:, m * 128:(m + 1) * 128].rearrange("(kt p) mm -> p kt mm", p=128))
            zn2 = "mzsp%d" % par
            P.waitv("y", "v", snap.get(("v", par)))
            P.sdma("y", zn2, zs[:, par * TC:(par + 1) * TC],
                   P.dram["scr_z"][m * 128:(m + 1) * 128, c * TC:(c + 1) * TC])
            pss = ps_g if par == 0 else ps_g2
            P.waitv("t", mn, P.cnt[mn])
            P.waitv("t", "ychS", ych_dy)
            P.waitv("t", "v", snap.get(("v", par)))
            for kt in range(32):
                i = P.PE.matmul(pss[:, :], mw[:, base + kt * 128:base + (kt + 1) * 128],
                                ych[:, kt * TC:(kt + 1) * TC], start=(kt == 0), stop=(kt == 31))
            P.tick("t", i)
            P.wait("v", "t")
            P.waitv("v", zn2, P.cnt[zn2])
            ymn = "ymbp%d" % par
            if ymn in P.cnt:
                P.waitv("v", ymn, P.cnt[ymn])
            i = P.V.tensor_tensor(ymb[:, par * TC:(par + 1) * TC], pss[:, :],
                                  zs[:, par * TC:(par + 1) * TC], ALU.mult)
            P.tick("v", i)
            snap[("v", par)] = P.cnt["v"]
            snap[("t", par)] = P.cnt["t"]
            P.wait("y", "v")
            P.sdma("y", ymn, P.dram["scr_ym"][m * 128:(m + 1) * 128, c * TC:(c + 1) * TC],
                   ymb[:, par * TC:(par + 1) * TC])
    P.free(ych, mw, zs, ymb, ps_g, ps_g2)
    P.barrier()
    if P.stop("mamba_merge"):
        return


def emit_outw_and_residual(P, x, ab2):
    ymch = P.sbuf("ow_ymch", [128, KI * TC], BF16)
    ow = P.sbuf("ow_w", [128, 2 * KI * 128], BF16)
    ps_o = P.psum("ow_ps", [128, TC], F32)
    ps_o2 = P.psum("ow_ps2", [128, TC], F32)
    tmp = P.sbuf("ow_tmp", [128, TC], F32)
    owd = P.dram["L0_attn_out_w"]
    a_b = ab2[:, 0:T]
    b_b = ab2[:, T:2 * T]
    snap = {}
    for c in range(NCH):
        P.wait("y", "t")
        P.sdma("y", "ymchS", ymch[:, :].rearrange("p (kt w) -> p kt w", kt=KI),
               P.dram["scr_ym"][:, c * TC:(c + 1) * TC].rearrange("(kt p) w -> p kt w", p=128))
        ych_dy = P.cnt["ymchS"]
        for mo in range(KD):
            par = mo % 2
            base = par * KI * 128
            P.waitv("g", "t", snap.get(("t", par)))
            on = "owp%d" % par
            P.sdma("g", on, ow[:, base:base + KI * 128].rearrange("p (kt mm) -> p kt mm", kt=KI),
                   owd[:, mo * 128:(mo + 1) * 128].rearrange("(kt p) mm -> p kt mm", p=128))
            pss = ps_o if par == 0 else ps_o2
            P.waitv("t", on, P.cnt[on])
            P.waitv("t", "ymchS", ych_dy)
            P.waitv("t", "v", snap.get(("v", par)))
            for kt in range(KI):
                i = P.PE.matmul(pss[:, :], ow[:, base + kt * 128:base + (kt + 1) * 128],
                                ymch[:, kt * TC:(kt + 1) * TC], start=(kt == 0), stop=(kt == KI - 1))
            P.tick("t", i)
            xsl = x[:, mo * T + c * TC:mo * T + (c + 1) * TC]
            P.wait("v", "t")
            i = P.V.tensor_tensor(tmp[:, :], xsl, a_b[:, c * TC:(c + 1) * TC], ALU.mult)
            i = P.V.tensor_sub(tmp[:, :], tmp[:, :], b_b[:, c * TC:(c + 1) * TC])
            i = P.V.tensor_add(tmp[:, :], tmp[:, :], pss[:, :])
            i = P.V.tensor_add(xsl, xsl, tmp[:, :])
            P.tick("v", i)
            snap[("v", par)] = P.cnt["v"]
            snap[("t", par)] = P.cnt["t"]
    P.free(ymch, ow, ps_o, ps_o2, tmp)


def self_val(P, e):
    return P.cnt[e]



def build_program(want_dumps=(), stop_after=None):
    nc = bass.Bass("TRN2", target_bir_lowering=False)
    P = Prog(nc, want_dumps)
    P.stop_after = stop_after

    P.dram["xt"] = nc.dram_tensor("xt", [D, T], F32, kind="ExternalInput").ap()
    P.dram["xo"] = nc.dram_tensor("xo", [D, T], F32, kind="ExternalOutput").ap()
    for nm, (shape, dt) in {**CONST_SPECS, **ROT_SPECS}.items():
        P.dram[nm] = nc.dram_tensor(nm, shape, dt, kind="ExternalInput").ap()
    for nm, shape in WEIGHT_SPECS.items():
        P.dram[nm] = nc.dram_tensor(nm, shape, F32, kind="ExternalInput").ap()
    for nm, shape, dt in [
        ("scr_x", [D, T], F32), ("scr_z", [DI, T], BF16),
        ("scr_yf", [DI, T], BF16), ("scr_yb", [DI, T], BF16),
        ("scr_ym", [DI, T], BF16), ("scr_b", [DST, T], BF16), ("scr_c", [DST, T], BF16),
        ("scr_ab", [128, 2 * T], F32), ("scr_rows", [2, T], F32),
        ("scr_xp", [D, T], F32), ("scr_xs", [DI, XSW], BF16),
    ]:
        P.dram[nm] = nc.dram_tensor(nm, shape, dt, kind="Internal").ap()
    if want_dumps:
        P.dbg = nc.dram_tensor("dbg", [128, 400000], F32, kind="ExternalOutput").ap()

    for nm, (shape, dt) in CONST_SPECS.items():
        P.consts[nm] = P.sbuf(nm, shape, dt)
        P.dma("y", P.consts[nm][:, :], P.dram[nm])
    nc.const_aps.register(F32, EPS, P.consts["cMisc"][:, 0:1])

    def par_alloc(nm, shape, dtype=F32):
        P.par[nm] = P.sbuf("par_" + nm, shape, dtype)
        return P.par[nm]

    qsd, ksd = P.dram["L0_ffn_q_scale"], P.dram["L0_ffn_k_scale"]
    par_alloc("qs", [128, 2]); par_alloc("ks", [128, 2])
    for l in range(2):
        for half in range(2):
            P.dma("y", P.par["qs"][half * 64:(half + 1) * 64, l:l + 1],
                  qsd[l, :].rearrange("(d one) -> d one", one=1))
            P.dma("y", P.par["ks"][half * 64:(half + 1) * 64, l:l + 1],
                  ksd[l, :].rearrange("(d one) -> d one", one=1))
    par_alloc("lng", [128, 2 * KD]); par_alloc("lnb", [128, 2 * KD])
    par_alloc("b2", [128, 2 * KD])
    for l in range(2):
        P.dma("y", P.par["lng"][:, l * KD:(l + 1) * KD],
              P.dram["L0_ffn_ln_g"][l].rearrange("(kt p) -> p kt", p=128), slow=True)
        P.dma("y", P.par["lnb"][:, l * KD:(l + 1) * KD],
              P.dram["L0_ffn_ln_b"][l].rearrange("(kt p) -> p kt", p=128), slow=True)
        P.dma("y", P.par["b2"][:, l * KD:(l + 1) * KD],
              P.dram["L0_ffn_b2"][l].rearrange("(kt p) -> p kt", p=128), slow=True)
    par_alloc("b1a", [128, 2 * 22]); par_alloc("b1g", [128, 2 * 22])
    for l in range(2):
        for which, off0 in (("b1a", 0), ("b1g", HID)):
            P.dma("y", P.par[which][:, l * 22:l * 22 + 21],
                  P.dram["L0_ffn_b1"][l, off0:off0 + 2688].rearrange("(j p) -> p j", p=128), slow=True)
            P.dma("y", P.par[which][0:42, l * 22 + 21:l * 22 + 22],
                  P.dram["L0_ffn_b1"][l, off0 + 2688:off0 + 2730].rearrange("(d one) -> d one", one=1))
    par_alloc("gamma", [128, KD]); par_alloc("beta", [128, KD])
    P.dma("y", P.par["gamma"][:, :], P.dram["L0_attn_gamma"].rearrange("(kt p) -> p kt", p=128), slow=True)
    P.dma("y", P.par["beta"][:, :], P.dram["L0_attn_beta"].rearrange("(kt p) -> p kt", p=128), slow=True)
    for sfx, wn in (("", ""), ("_b", "_b")):
        par_alloc("A" + sfx, [128, KI * DST])
        P.dma("y", P.par["A" + sfx][:, :].rearrange("p (kt n) -> p kt n", kt=KI),
              P.dram["L0_attn_A_log" + wn].rearrange("(kt p) n -> p kt n", p=128))
        par_alloc("cw" + sfx, [128, KI * 4])
        P.dma("y", P.par["cw" + sfx][:, :].rearrange("p (kt k) -> p kt k", kt=KI),
              P.dram["L0_attn_conv_w" + wn].rearrange("(kt p) k -> p kt k", p=128))
        for pn, dn in (("cb", "conv_b"), ("dtb", "dt_b"), ("Dsk", "D_skip")):
            par_alloc(pn + sfx, [128, KI])
            P.dma("y", P.par[pn + sfx][:, :],
                  P.dram["L0_attn_" + dn + wn].rearrange("(kt p) -> p kt", p=128), slow=True)
    P.barrier()
    for sfx in ("", "_b"):
        i = P.S.activation(P.par["A" + sfx][:, :], P.par["A" + sfx][:, :], AF.Exp)
        P.tick("s", i)
        P.wait("v", "s")
        i = P.V.tensor_scalar_mul(P.par["A" + sfx][:, :], P.par["A" + sfx][:, :], -1.0)
        P.tick("v", i)
    P.barrier()

    x = P.sbuf("x_arena", [128, KD * T], F32, hi=True)
    P.dma("y", x[:, :].rearrange("p (kt w) -> p kt w", kt=KD),
          P.dram["xt"].rearrange("(kt p) w -> p kt w", p=128))
    P.barrier()
    xtiles = lambda xx: [xx[:, kt * T:(kt + 1) * T] for kt in range(KD)]

    def run_ffn_macaron(tagp):
        xp = P.sbuf("xp_arena", [128, KD * T], F32, hi=True)
        emit_ln(P, xtiles(x), None, direct_dst=lambda kt: xp[:, kt * T:(kt + 1) * T])
        P.dump(tagp + "ln", xp[:, :], "v")
        P.wait("y", "v")
        P.dma("y", P.dram["scr_x"].rearrange("(kt p) w -> p kt w", p=128),
              x[:, :].rearrange("p (kt w) -> p kt w", kt=KD))
        P.barrier()
        P.free(x)
        xp = emit_ffn_call(P, xp, dump_pref=tagp)
        if P.stop_flag:
            return xp, None
        x2 = P.sbuf("x_arena", [128, KD * T], F32, hi=True)
        P.dma("y", x2[:, :].rearrange("p (kt w) -> p kt w", kt=KD),
              P.dram["scr_x"].rearrange("(kt p) w -> p kt w", p=128))
        P.barrier()
        for kt in range(KD):
            i = P.V.scalar_tensor_tensor(x2[:, kt * T:(kt + 1) * T], xp[:, kt * T:(kt + 1) * T],
                                         0.5, x2[:, kt * T:(kt + 1) * T], ALU.mult, ALU.add)
            P.tick("v", i)
        P.free(xp)
        P.dump(tagp + "x_out", x2[:, :], "v")
        return None, x2

    if stop_after == "ln1_only":
        P.dump("x0", x[:, :], "v")
        xp = P.sbuf("xp_arena", [128, KD * T], F32, hi=True)
        emit_ln(P, xtiles(x), None, direct_dst=lambda kt: xp[:, kt * T:(kt + 1) * T])
        P.dump("ln1", xp[:, :], "v")
        P.barrier()
        P.dump("ln1b", xp[:, :], "v")
        P.stop_flag = True
        finish(P)
        return nc, P
    xp, x = run_ffn_macaron("f1_")
    if P.stop_flag:
        finish(P)
        return nc, P

    lnm = P.sbuf("mb_lnm", [128, KD * T], BF16, hi=True)

    def wrm(kt, tmp, g, b):
        P.wait("s", "v")
        i = P.S.activation(lnm[:, kt * T:(kt + 1) * T], tmp[:, :], AF.Identity, bias=b, scale=g)
        P.tick("s", i)

    ab, ab2 = emit_ln(P, xtiles(x), wrm, gamma_ap=P.par["gamma"], beta_ap=P.par["beta"],
                      fuse_double=True, keep_ab=True)
    P.dump("mb_lnm", lnm[:, :], "s")
    P.wait("y", "v")
    P.dma("y", P.dram["scr_x"].rearrange("(kt p) w -> p kt w", p=128),
          x[:, :].rearrange("p (kt w) -> p kt w", kt=KD))
    P.dma("y", P.dram["scr_ab"], ab2[:, :])
    P.barrier()
    P.free(x, ab, ab2)
    if P.stop("ln2"):
        finish(P)
        return nc, P
    emit_mamba(P, lnm, dump_pref="mb_")
    if P.stop_flag:
        finish(P)
        return nc, P
    x = P.sbuf("x_arena", [128, KD * T], F32, hi=True)
    P.dma("y", x[:, :].rearrange("p (kt w) -> p kt w", kt=KD),
          P.dram["scr_x"].rearrange("(kt p) w -> p kt w", p=128))
    ab2 = P.sbuf("ow_ab2", [128, 2 * T], F32)
    P.dma("y", ab2[:, :], P.dram["scr_ab"])
    P.barrier()
    emit_outw_and_residual(P, x, ab2)
    P.free(ab2)
    P.dump("mb_x_out", x[:, :], "v")
    P.barrier()
    if P.stop("mamba"):
        finish(P)
        return nc, P

    xp, x = run_ffn_macaron("f2_")
    if P.stop_flag:
        finish(P)
        return nc, P

    P.wait("y", "v")
    P.dma("y", P.dram["xo"].rearrange("(kt p) w -> p kt w", p=128),
          x[:, :].rearrange("p (kt w) -> p kt w", kt=KD))
    finish(P)
    return nc, P


def finish(P):
    P.barrier()
    for q in ["dy", "dg", "ds"]:
        P.SY.wait_ge(P.sem[q], P.cnt[q])



_BUILD_CACHE = {}


def kernel(**inputs):
    key = ("plain",)
    if key not in _BUILD_CACHE:
        _BUILD_CACHE[key] = build_program()
    nc, P = _BUILD_CACHE[key]
    consts = host_consts()
    x = np.asarray(inputs["L0_attn_x"], np.float32)
    base = {}
    for nm in WEIGHT_SPECS:
        base[nm] = _f(inputs[nm])
    base.update(consts)
    in_maps = []
    for b in range(8):
        m = dict(base)
        m["xt"] = np.ascontiguousarray(x[b].T)
        in_maps.append(m)
    res = run_bass_kernel_spmd(nc, in_maps, core_ids=list(range(8)))
    out = np.stack([np.asarray(r["xo"]).T for r in res.results])
    return out.astype(np.float32)


# version 07
# speedup vs baseline: 1.0750x; 1.0562x over previous
import numpy as np
import ml_dtypes
import concourse.bass as bass
import concourse.mybir as mybir
from concourse.bass_utils import run_bass_kernel_spmd

F32 = mybir.dt.float32
BF16 = mybir.dt.bfloat16
AF = mybir.ActivationFunctionType
ALU = mybir.AluOpType
AXX = mybir.AxisListType.X

D = 1024
T = 2048
KD = 8
DI = 2048
KI = 16
NH = 8
DH = 64
WIN = 512
HID = 2730
DTR = 64
DST = 16
NCH = 4
TC = 512
TH = 1024
EPS = 1e-5
QK_SCALE = 8.0
HT = [(j * 128, min(128, HID - j * 128)) for j in range(22)]
XSW = T + 8


def _f(x):
    return np.ascontiguousarray(np.asarray(x), dtype=np.float32)


def _bf(x):
    return np.ascontiguousarray(np.asarray(x, dtype=np.float32).astype(ml_dtypes.bfloat16))


def host_consts():
    c = {}
    c["cI"] = _bf(np.eye(128))
    R = np.zeros((128, 128), np.float32)
    for blk in range(2):
        for m in range(64):
            if m < 32:
                R[blk * 64 + m + 32, blk * 64 + m] = -1.0
            else:
                R[blk * 64 + m - 32, blk * 64 + m] = 1.0
    c["cR"] = _bf(R)
    half = np.arange(0, DH, 2, dtype=np.float64)
    inv_freq = 1.0 / 10000.0 ** (half / DH)
    t = np.arange(T, dtype=np.float64)
    freqs = np.concatenate([t[:, None] * inv_freq[None, :]] * 2, -1)
    base = (half + 0.4 * DH) / (1.4 * DH)
    power = (t - T // 2) / float(WIN // 2)
    scale = np.concatenate([base[None, :] ** power[:, None]] * 2, -1)
    cos, sin = np.cos(freqs), np.sin(freqs)
    rep = lambda a: np.concatenate([a.T, a.T], 0)
    c["cT1q"] = _bf(rep(cos * scale))
    c["cT2q"] = _bf(rep(sin * scale))
    c["cT1k"] = _bf(rep(cos / scale))
    c["cT2k"] = _bf(rep(sin / scale))
    idx = np.arange(128)
    MgA = np.zeros((128, 640), np.float32)
    for r in range(128):
        kr = np.arange(640)
        MgA[r, :] = np.where((kr >= r) & (kr <= r + 512), 0.0, -1.0e30)
    c["cMgA"] = _bf(MgA)
    c["cM0A"] = _bf(np.where(idx[None, :] <= idx[:, None], 0.0, -1.0e30))
    c["cOnesB"] = _bf(np.ones((128, 1)))
    c["cOnesRowF"] = _f(np.ones((128, 128)))
    c["cOnesRowB"] = _bf(np.ones((1, 128)))
    B2 = np.zeros((128, 2), np.float32)
    B2[:64, 0] = 1.0
    B2[64:, 1] = 1.0
    c["cB2b"] = _bf(B2)
    B2T = np.zeros((2, 128), np.float32)
    B2T[0, :64] = 1.0
    B2T[1, 64:] = 1.0
    c["cB2T"] = _bf(B2T)
    c["cMisc"] = _f(np.tile(np.array([[EPS, 0.0]], np.float32), (128, 1)))
    return c


CONST_SPECS = {
    "cI": ([128, 128], BF16), "cR": ([128, 128], BF16),
    "cMgA": ([128, 640], BF16), "cM0A": ([128, 128], BF16),
    "cOnesB": ([128, 1], BF16), "cOnesRowF": ([128, 128], F32),
    "cOnesRowB": ([1, 128], BF16),
    "cB2b": ([128, 2], BF16), "cB2T": ([2, 128], BF16),
    "cMisc": ([128, 2], F32),
}
ROT_SPECS = {
    "cT1q": ([128, T], BF16), "cT2q": ([128, T], BF16),
    "cT1k": ([128, T], BF16), "cT2k": ([128, T], BF16),
}

WEIGHT_SPECS = {
    "L0_attn_gamma": [D], "L0_attn_beta": [D],
    "L0_attn_in_w": [D, 2 * DI],
    "L0_attn_conv_w": [DI, 4], "L0_attn_conv_b": [DI],
    "L0_attn_xproj_w": [DI, 96], "L0_attn_dt_w": [DTR, DI], "L0_attn_dt_b": [DI],
    "L0_attn_A_log": [DI, DST], "L0_attn_D_skip": [DI],
    "L0_attn_conv_w_b": [DI, 4], "L0_attn_conv_b_b": [DI],
    "L0_attn_xproj_w_b": [DI, 96], "L0_attn_dt_w_b": [DTR, DI], "L0_attn_dt_b_b": [DI],
    "L0_attn_A_log_b": [DI, DST], "L0_attn_D_skip_b": [DI],
    "L0_attn_merge_w": [2 * DI, DI], "L0_attn_out_w": [DI, D],
    "L0_ffn_wqkv": [2, D, 1536], "L0_ffn_q_scale": [2, DH], "L0_ffn_k_scale": [2, DH],
    "L0_ffn_wo": [2, 512, D], "L0_ffn_ln_g": [2, D], "L0_ffn_ln_b": [2, D],
    "L0_ffn_w1": [2, D, 2 * HID], "L0_ffn_b1": [2, 2 * HID],
    "L0_ffn_w2": [2, HID, D], "L0_ffn_b2": [2, D],
}


class EngProxy:

    _PASS = {"wait_ge", "drain", "nop", "dma_start", "sem_inc"}

    def __init__(self, prog, key, raw):
        self.prog = prog
        self.key = key
        self.raw = raw

    def __getattr__(self, name):
        fn = getattr(self.raw, name)
        if name in self._PASS or not callable(fn):
            return fn
        key = self.key
        prog = self.prog

        def wrapped(*args, **kwargs):
            aps = [a for a in args if isinstance(a, bass.AP)]
            aps += [v for v in kwargs.values() if isinstance(v, bass.AP)]
            dirty = prog.dirty[key]
            if aps:
                prog.sync_marks(self.raw, aps[0])
                ins_names = {a.tensor.name for a in aps[1:]}
                if ins_names & dirty:
                    self.raw.drain()
                    dirty.clear()
            instr = fn(*args, **kwargs)
            if aps:
                dirty.add(aps[0].tensor.name)
            return instr

        return wrapped


class Prog:
    def __init__(self, nc, want_dumps=()):
        self.nc = nc
        self._cms = {}
        self.sem = {}
        self.cnt = {}
        for name in ["t", "v", "s", "g", "y", "dy", "dg", "ds"]:
            cm = nc.semaphore("sem_" + name)
            self.sem[name] = cm.__enter__()
            self._cms["sem_" + name] = cm
            self.cnt[name] = 0
        self.PE = nc.tensor
        self.V = EngProxy(self, "v", nc.vector)
        self.S = EngProxy(self, "s", nc.scalar)
        self.G = EngProxy(self, "g", nc.gpsimd)
        self.SY = nc.sync
        self.dirty = {"v": set(), "s": set(), "g": set()}
        self._eng = {"t": self.PE, "v": self.V, "s": self.S, "g": self.G, "y": self.SY}
        self.pending_sync = {}
        self.dumps = []
        self.dump_off = 0
        self.want_dumps = set(want_dumps)
        self.dbg = None
        self.consts = {}
        self.dram = {}
        self.par = {}
        self.uid = 0
        self.stop_flag = False
        self.stop_after = None

    ARENA_BYTES = 206 * 1024

    def _ensure_arena(self):
        if not hasattr(self, "arena_base"):
            cm = self.nc.sbuf_tensor("big_arena", [128, self.ARENA_BYTES], mybir.dt.uint8)
            h = cm.__enter__()
            self._cms["big_arena"] = cm
            self.arena_base = self.nc.lookup_mloc(h).addr
            self.freelist = [(0, self.ARENA_BYTES)]
            self.alloc_info = {}

    def sbuf(self, name, shape, dtype, hi=False):
        self._ensure_arena()
        per = int(np.prod(shape[1:])) * mybir.dt.size(dtype)
        per = (per + 31) // 32 * 32
        ranges = list(enumerate(self.freelist))
        if hi:
            ranges = ranges[::-1]
        for idx, (off, sz) in ranges:
            if sz >= per:
                break
        else:
            amap = sorted((v[0], v[1], k) for k, v in self.alloc_info.items())
            raise MemoryError(f"SBUF arena full for {name} ({per}B): {self.freelist}\n" +
                              "\n".join(f"  {o:7d} +{s:6d} {k}" for o, s, k in amap))
        if sz == per:
            del self.freelist[idx]
        elif hi:
            off = off + sz - per
            self.freelist[idx] = (self.freelist[idx][0], sz - per)
        else:
            self.freelist[idx] = (off + per, sz - per)
        self.uid += 1
        name = f"{name}_{self.uid}"
        h = self.nc.virtual_sbuf_tensor(name, list(shape), dtype)
        self.nc._tensor(h.name, list(shape), dtype, type="SB")
        mloc = self.nc.lookup_mloc(h)
        mloc.allocated = True
        mloc.addr = self.arena_base + off
        self.alloc_info[h.name] = (off, per)
        self.pending_sync[h.name] = dict(self.cnt)
        return h

    def psum(self, name, shape, dtype=F32):
        self.uid += 1
        name = f"{name}_{self.uid}"
        esz = mybir.dt.size(dtype)
        free = int(np.prod(shape[1:]))
        pad = (-(free * esz) % 2048) // esz
        cm = self.nc.psum_tensor(name, [128, free + pad], dtype)
        t = cm.__enter__()
        self._cms[name] = (self.uid, cm)
        return t

    def free(self, *handles):
        psums = []
        for h in handles:
            name = h.name if hasattr(h, "name") else h.tensor.name
            if name in self.alloc_info:
                off, per = self.alloc_info.pop(name)
                self.freelist.append((off, per))
            else:
                psums.append(name)
        self.freelist.sort()
        merged = []
        for off, sz in self.freelist:
            if merged and merged[-1][0] + merged[-1][1] == off:
                merged[-1] = (merged[-1][0], merged[-1][1] + sz)
            else:
                merged.append((off, sz))
        self.freelist = [tuple(x) for x in merged]
        for name in sorted(psums, key=lambda n: self._cms[n][0], reverse=True):
            _, cm = self._cms.pop(name)
            cm.__exit__(None, None, None)

    def tick(self, name, instr, k=1):
        if name in ("v", "s", "g"):
            instr = self._eng[name].raw.drain()
            self.dirty[name].clear()
        elif name == "t":
            instr = self.PE.drain()
        instr.then_inc(self.sem[name], k)
        self.cnt[name] += k

    def wait(self, who, dep, val=None):
        self._eng[who].wait_ge(self.sem[dep], self.cnt[dep] if val is None else val)

    def waitv(self, who, dep, val):
        if val:
            self._eng[who].wait_ge(self.sem[dep], val)

    def sync_marks(self, eng, out_ap):
        name = out_ap.tensor.name
        marks = self.pending_sync.pop(name, None)
        if marks:
            for s, v in marks.items():
                if v > 0:
                    eng.wait_ge(self.sem[s], v)

    def sdma(self, qname, sname, out, in_, slow=False):
        if sname not in self.sem:
            cm = self.nc.semaphore("sem_" + sname)
            self.sem[sname] = cm.__enter__()
            self._cms["sem_" + sname] = cm
            self.cnt[sname] = 0
        if hasattr(out, "tensor") and out.tensor.name in self.pending_sync:
            self.sync_marks(self._eng[qname] if qname == "y" else self._eng[qname].raw, out)
        i = self._eng[qname].dma_start(out, in_, allow_slow_non_contiguous=slow)
        i.then_inc(self.sem[sname], 16)
        self.cnt[sname] += 16
        return i

    def dma(self, qname, out, in_, slow=False):
        if hasattr(out, "tensor") and out.tensor.name in self.pending_sync:
            self.sync_marks(self._eng[qname] if qname == "y" else self._eng[qname].raw, out)
        i = self._eng[qname].dma_start(out, in_, allow_slow_non_contiguous=slow)
        self.tick("d" + qname, i, 16)
        return i

    def barrier(self, engines="tvsgy"):
        marks = {}
        for e in engines:
            i = self._eng[e].nop()
            self.tick(e, i, 1)
            marks[e] = self.cnt[e]
        for e in engines:
            for o in engines:
                if o != e:
                    self._eng[e].wait_ge(self.sem[o], marks[o])
            for q in ["dy", "dg", "ds"]:
                self._eng[e].wait_ge(self.sem[q], self.cnt[q])

    def dump(self, tag, ap, producer="v", parts=128):
        if tag not in self.want_dumps:
            return
        w = int(np.prod(ap.shape)) // int(ap.shape[0])
        q = "y" if "float32" in str(ap.dtype) else "g"
        for e in "tvsg":
            self.tick(e, self._eng[e].nop())
            self.wait(q, e)
        self.dma(q, self.dbg[0:parts, self.dump_off:self.dump_off + w], ap)
        self.pending_sync[ap.tensor.name] = dict(self.cnt)
        self.dumps.append((tag, self.dump_off, parts, w, str(ap.dtype)))
        self.dump_off += w

    def stop(self, stage):
        if self.stop_after is not None and stage == self.stop_after:
            self.stop_flag = True
        return self.stop_flag



def emit_ln(P, src_tiles, out_writer, gamma_ap=None, beta_ap=None, fuse_double=False,
            keep_ab=False, direct_dst=None):
    need_cast = "float32" in str(src_tiles[0].dtype)
    sq = P.sbuf("ln_sq", [128, 2 * TC], BF16)
    xb = P.sbuf("ln_xb", [128, 2 * TC], BF16) if need_cast else None
    rows = P.sbuf("ln_rows", [1, 3 * T], F32)
    ab = P.sbuf("ln_ab", [128, 2 * T], F32)
    ab2 = P.sbuf("ln_ab2", [128, 2 * T], F32) if (fuse_double and keep_ab) else None
    ps_s = P.psum("ln_ps_s", [1, TC], F32)
    ps_q = P.psum("ln_ps_q", [1, TC], F32)
    ps_b = P.psum("ln_ps_b", [128, TC], F32)
    ones = P.consts["cOnesB"]

    for c in range(NCH):
        csl = slice(c * TC, (c + 1) * TC)
        for kt in range(KD):
            par = kt % 2
            psl = slice(par * TC, (par + 1) * TC)
            src = src_tiles[kt][:, csl]
            P.wait("s", "t")
            if need_cast:
                i = P.S.activation(xb[:, psl], src, AF.Identity)
                P.tick("s", i)
                xsrc = xb[:, psl]
            else:
                xsrc = src
            i = P.S.activation(sq[:, psl], src, AF.Square)
            P.tick("s", i)
            P.wait("t", "s")
            i = P.PE.matmul(ps_s[0:1, :], ones[:, 0:1], xsrc, start=(kt == 0), stop=(kt == KD - 1))
            i = P.PE.matmul(ps_q[0:1, :], ones[:, 0:1], sq[:, psl], start=(kt == 0), stop=(kt == KD - 1))
            P.tick("t", i)
        P.wait("v", "t")
        i = P.V.tensor_copy(rows[0:1, csl], ps_s[0:1, :])
        i = P.V.tensor_copy(rows[0:1, T + c * TC:T + (c + 1) * TC], ps_q[0:1, :])
        P.tick("v", i)
        P.wait("t", "v")

    muA = rows[0:1, 0:T]
    rB = rows[0:1, T:2 * T]
    bZ = rows[0:1, 2 * T:3 * T]
    vB = rB
    i = P.V.tensor_scalar_mul(muA, rows[0:1, 0:T], 1.0 / D)
    P.tick("v", i)
    P.wait("s", "v")
    i = P.S.activation(bZ, muA, AF.Square)
    P.tick("s", i)
    P.wait("v", "s")
    i = P.V.scalar_tensor_tensor(vB, vB, 1.0 / D, bZ, ALU.mult, ALU.subtract)
    P.tick("v", i)

    brow = P.sbuf("ln_brow", [1, T], F32)

    def bcast_pair(dst_ab):
        P.wait("v", "s")
        i = P.V.tensor_mul(brow[0:1, :], muA, rB_cur[0])
        P.tick("v", i)
        P.wait("y", "v")
        P.dma("y", P.dram["scr_rows"][0:1, :], rB_cur[0])
        P.dma("y", P.dram["scr_rows"][1:2, :], brow[0:1, :])
        P.wait("g", "dy")
        P.dma("g", dst_ab[:, 0:T], P.dram["scr_rows"][0:1, :].broadcast_to([128, T]))
        P.dma("g", dst_ab[:, T:2 * T], P.dram["scr_rows"][1:2, :].broadcast_to([128, T]))
        P.wait("v", "dg")
        P.wait("s", "dg")

    P.wait("s", "v")
    i = P.S.activation(bZ, vB, AF.Ln, bias=EPS)
    P.S.drain()
    i = P.S.activation(bZ, bZ, AF.Exp, scale=-0.5)
    P.tick("s", i)
    rB_cur = [bZ]
    if fuse_double:
        if ab2 is not None:
            bcast_pair(ab2)
        w1 = P.sbuf("ln_w1", [1, 2 * T], F32)
        wa = w1[0:1, 0:T]
        wb = w1[0:1, T:2 * T]
        i = P.S.activation(wa, vB, AF.Ln)
        i = P.S.activation(wb, vB, AF.Ln, bias=EPS)
        P.tick("s", i)
        P.wait("v", "s")
        i = P.V.tensor_sub(wa, wa, wb)
        P.tick("v", i)
        P.wait("s", "v")
        i = P.S.activation(wa, wa, AF.Exp)
        P.tick("s", i)
        P.wait("v", "s")
        i = P.V.tensor_scalar_add(wa, wa, EPS)
        P.tick("v", i)
        P.wait("s", "v")
        i = P.S.activation(wa, wa, AF.Ln)
        P.S.drain()
        i = P.S.activation(wa, wa, AF.Exp, scale=-0.5)
        P.tick("s", i)
        P.wait("v", "s")
        i = P.V.tensor_mul(bZ, bZ, wa)
        P.tick("v", i)
        P.free(w1)
    bcast_pair(ab)

    P.dump("lnab", ab[:, :], "v")
    a_b = ab[:, 0:T]
    b_b = ab[:, T:2 * T]
    tmp = P.sbuf("ln_tmp", [128, T], F32) if direct_dst is None else None
    for kt in range(KD):
        P.wait("v", "s")
        dst = tmp[:, :] if direct_dst is None else direct_dst(kt)
        i = P.V.tensor_mul(dst, src_tiles[kt], a_b)
        i = P.V.tensor_sub(dst, dst, b_b)
        P.tick("v", i)
        if out_writer is not None:
            out_writer(kt, dst,
                       None if gamma_ap is None else gamma_ap[:, kt:kt + 1],
                       None if beta_ap is None else beta_ap[:, kt:kt + 1])
    P.free(sq, rows, brow, ps_s, ps_q, ps_b)
    if tmp is not None:
        P.free(tmp)
    if xb is not None:
        P.free(xb)
    if keep_ab:
        return ab, ab2
    P.free(ab)
    return None


def emit_shift(P, xp, dst):
    for kt in range(KD):
        if kt < 4:
            i = P.S.activation(dst[:, kt * T:(kt + 1) * T], xp[:, kt * T:(kt + 1) * T], AF.Identity)
        else:
            i = P.S.activation(dst[:, kt * T + 1:(kt + 1) * T], xp[:, kt * T:kt * T + T - 1],
                               AF.Identity)
        P.tick("s", i)
    P.wait("v", "s")
    for kt in range(4, KD):
        i = P.V.memset(dst[:, kt * T:kt * T + 1], 0.0)
    P.tick("v", i)



def emit_ffn_layer(P, l, xp, wsb, dump_pref=""):
    xpt = lambda kt: xp[:, kt * T:(kt + 1) * T]

    sh = P.sbuf("ffn_sh", [128, KD * T], BF16, hi=True)
    emit_shift(P, xp, sh)
    P.dump(dump_pref + "shift1", sh[:, :], "v")
    wqkv = P.sbuf("ffc_wqkv", [128, KD * 1536], BF16)
    P.wait("g", "t")
    P.sdma("g", "wqS", wqkv[:, :].rearrange("p (kt m) -> p kt m", kt=KD),
           P.dram["L0_ffn_wqkv"][l].rearrange("(kt p) m -> p kt m", p=128))

    q4 = P.sbuf("ffn_q4", [128, 4 * T], BF16)
    k4 = P.sbuf("ffn_k4", [128, 4 * T], BF16)
    sqb = P.sbuf("ffn_sqb", [128, TC], BF16)
    tt1 = P.sbuf("ffn_tt1", [128, TC], F32)
    s2 = P.sbuf("ffn_s2", [2, T], F32)
    rbf2 = P.sbuf("ffn_rbf2", [2, T], BF16)
    rb = P.sbuf("ffn_rb", [128, T], BF16)
    qn = P.sbuf("ffn_qn", [128, T], BF16)
    t2t = P.sbuf("ffn_t2t", [128, TC], F32)
    ps_m = P.psum("ffn_psm", [128, TC], F32)
    ps_m2 = P.psum("ffn_psm2", [128, TC], F32)
    ps_n = P.psum("ffn_psn", [2, TC], F32)
    ps_r = P.psum("ffn_psr", [128, TC], F32)
    P.wait("t", "wqS")
    rtab = P.sbuf("ffn_rtab", [128, 2 * T], BF16)

    snap = {}
    for side, (out4, scl, tn1, tn2) in enumerate(
            [(q4, wsb["qs"], "cT1q", "cT2q"),
             (k4, wsb["ks"], "cT1k", "cT2k")]):
        t1c = rtab[:, 0:T]
        t2c = rtab[:, T:2 * T]
        P.wait("y", "v")
        P.sdma("y", "rtS", t1c, P.dram[tn1])
        P.sdma("y", "rtS", t2c, P.dram[tn2])
        P.wait("v", "rtS")
        for mt in range(4):
            col0 = side * 512 + mt * 128
            for c in range(NCH):
                csl = slice(c * TC, (c + 1) * TC)
                psx = ps_m if c % 2 == 0 else ps_m2
                P.waitv("t", "v", snap.get(("qv", c % 2)))
                P.waitv("t", "s", snap.get(("qs", c % 2)))
                for kt in range(KD):
                    i = P.PE.matmul(psx[:, :], wqkv[:, kt * 1536 + col0:kt * 1536 + col0 + 128],
                                    sh[:, kt * T + c * TC:kt * T + (c + 1) * TC],
                                    start=(kt == 0), stop=(kt == KD - 1))
                P.tick("t", i)
                P.wait("s", "t")
                i = P.S.activation(sqb[:, :], psx[:, :], AF.Square)
                P.tick("s", i)
                snap[("qs", c % 2)] = P.cnt["s"]
                P.wait("t", "s")
                P.waitv("t", "v", snap.get("psn_v"))
                i = P.PE.matmul(ps_n[0:2, :], P.consts["cB2b"][:, :], sqb[:, :], start=True, stop=True)
                P.tick("t", i)
                P.wait("v", "t")
                i = P.V.tensor_copy(s2[:, csl], ps_n[0:2, :])
                i = P.V.tensor_scalar_mul(qn[:, csl], psx[:, :], scl)
                P.tick("v", i)
                snap[("qv", c % 2)] = P.cnt["v"]
                snap["psn_v"] = P.cnt["v"]
            i = P.V.tensor_scalar_max(s2[:, :], s2[:, :], 1e-24)
            P.tick("v", i)
            P.wait("s", "v")
            i = P.S.activation(s2[:, :], s2[:, :], AF.Ln)
            P.S.drain()
            i = P.S.activation(rbf2[:, :], s2[:, :], AF.Exp, scale=-0.5)
            P.tick("s", i)
            for c in range(NCH):
                csl = slice(c * TC, (c + 1) * TC)
                P.wait("t", "s")
                P.wait("t", "v")
                i = P.PE.matmul(ps_r[:, :], P.consts["cB2T"][:, :], rbf2[:, csl],
                                start=True, stop=True)
                P.tick("t", i)
                P.wait("v", "t")
                i = P.V.tensor_copy(rb[:, csl], ps_r[:, :])
                P.tick("v", i)
                P.wait("t", "v")
            for c in range(NCH):
                csl = slice(c * TC, (c + 1) * TC)
                P.wait("t", "v")
                i = P.PE.matmul(ps_r[:, :], P.consts["cR"][:, :], qn[:, csl], start=True, stop=True)
                P.tick("t", i)
                P.wait("v", "t")
                i = P.V.tensor_tensor(t2t[:, :], ps_r[:, :], t2c[:, csl], ALU.mult)
                P.tick("v", i)
                P.wait("t", "v")
                i = P.V.tensor_tensor(tt1[:, :], qn[:, csl], t1c[:, csl], ALU.mult)
                i = P.V.tensor_add(tt1[:, :], tt1[:, :], t2t[:, :])
                i = P.V.tensor_tensor(out4[:, mt * T + c * TC:mt * T + (c + 1) * TC],
                                      tt1[:, :], rb[:, csl], ALU.mult)
                P.tick("v", i)
    P.dump(dump_pref + "q4", q4[:, :], "v")
    P.dump(dump_pref + "k4", k4[:, :], "v")
    if P.stop(dump_pref + "qk"):
        return xp

    P.free(sqb, tt1, s2, rbf2, rb, qn, t2t, rtab, ps_n, ps_r)
    vtm = P.sbuf("ffn_vtm", [128, 16 * 512], BF16)
    if P.stop(dump_pref + "v0"):
        return xp
    snap = {}
    for tt in range(16):
        psx = ps_m if tt % 2 == 0 else ps_m2
        P.waitv("t", "v", snap.get(tt % 2))
        for kt in range(KD):
            i = P.PE.matmul(psx[:, :], sh[:, kt * T + tt * 128:kt * T + (tt + 1) * 128],
                            wqkv[:, kt * 1536 + 1024:kt * 1536 + 1536],
                            start=(kt == 0), stop=(kt == KD - 1))
        P.tick("t", i)
        P.wait("v", "t")
        i = P.V.tensor_copy(vtm[:, tt * 512:tt * 512 + 512], psx[:, :])
        P.tick("v", i)
        snap[tt % 2] = P.cnt["v"]
    P.free(sh, wqkv, ps_m, ps_m2)
    P.dump(dump_pref + "vtm", vtm[:, :], "s")
    if P.stop(dump_pref + "v"):
        return xp

    smax = P.sbuf("ffn_smax", [128, 2], F32)
    nmax = P.sbuf("ffn_nmax", [128, 2], F32)
    smf = P.sbuf("ffn_smf", [128, 2 * 640], F32)
    Pb = P.sbuf("ffn_P", [128, 2 * 640], BF16)
    PT = P.sbuf("ffn_PT", [128, 2 * 8 * 512], BF16)
    rsr = P.sbuf("ffn_rsr", [128, TC], F32)
    rbq = P.sbuf("ffn_rbq", [128, TC], F32)
    osb = P.sbuf("ffn_o", [128, 4 * T], BF16)
    ps_sim = [P.psum("ffn_pssimA", [128, 640], F32), P.psum("ffn_pssimB", [128, 640], F32)]
    ps_pt = [P.psum("ffn_psptA", [128, 640], BF16), P.psum("ffn_psptB", [128, 640], BF16)]
    ps_o = P.psum("ffn_pso", [128, TC], F32)
    ps_rr = P.psum("ffn_psrr", [128, TC], F32)

    wins = [(h, qw) for h in range(NH) for qw in range(4)]
    steps = [(w, qt) for w in range(len(wins)) for qt in range(4)]
    sn = {}
    pend = [None]
    P.wait("t", "v")

    def meta(s):
        w, qt = steps[s]
        h, qw = wins[w]
        rt, po = h // 2, (h % 2) * 64
        if qw == 0:
            nblk, ktok0 = qt + 1, 0
        else:
            nblk, ktok0 = 5, (qw - 1) * 4 + qt
        return w, qt, h, qw, rt, po, nblk, ktok0, nblk * 128

    def stA(s):
        w, qt, h, qw, rt, po, nblk, ktok0, width = meta(s)
        par = s % 2
        i_abs = qw * 4 + qt
        P.waitv("t", "v", sn.get(("simfree", par)))
        lhs = q4[po:po + 64, rt * T + i_abs * 128:rt * T + (i_abs + 1) * 128]
        off = 0
        while off < width:
            n = min(TC, width - off)
            i = P.PE.matmul(ps_sim[par][:, off:off + n], lhs,
                            k4[po:po + 64,
                               rt * T + ktok0 * 128 + off:rt * T + ktok0 * 128 + off + n],
                            start=True, stop=True)
            off += n
        P.tick("t", i)
        sn[("simdone", par)] = P.cnt["t"]

    def stB(s):
        w, qt, h, qw, rt, po, nblk, ktok0, width = meta(s)
        par = s % 2
        sm = smf[:, par * 640:par * 640 + width]
        pssim = ps_sim[par]
        P.waitv("v", "t", sn[("simdone", par)])
        P.waitv("v", "s", sn.get(("sexp", par)))
        if qw == 0:
            if nblk > 1:
                i = P.V.tensor_copy(sm[:, 0:width - 128], pssim[:, 0:width - 128])
            i = P.V.tensor_tensor(sm[:, width - 128:width], pssim[:, width - 128:width],
                                  P.consts["cM0A"][:, :], ALU.add)
        else:
            i = P.V.tensor_tensor(sm, pssim[:, 0:width], P.consts["cMgA"][:, 0:width], ALU.add)
        i = P.V.reduce_max(smax[:, par:par + 1], sm, axis=AXX)
        i = P.V.tensor_scalar_mul(nmax[:, par:par + 1], smax[:, par:par + 1], -QK_SCALE)
        P.tick("v", i)
        sn[("simfree", par)] = P.cnt["v"]
        sn[("smf", par)] = P.cnt["v"]

    def stX(s):
        w, qt, h, qw, rt, po, nblk, ktok0, width = meta(s)
        par = s % 2
        P.waitv("s", "v", sn[("smf", par)])
        P.waitv("s", "t", sn.get(("pbfree", par)))
        i = P.S.activation(Pb[:, par * 640:par * 640 + width],
                           smf[:, par * 640:par * 640 + width], AF.Exp,
                           bias=nmax[:, par:par + 1], scale=QK_SCALE)
        P.tick("s", i)
        sn[("sexp", par)] = P.cnt["s"]

    def stC(s):
        w, qt, h, qw, rt, po, nblk, ktok0, width = meta(s)
        par = s % 2
        P.waitv("t", "s", sn[("sexp", par)])
        P.waitv("t", "v", sn.get(("ptpsfree", par)))
        for j in range(nblk):
            i = P.PE.transpose(ps_pt[par][:, j * 128:(j + 1) * 128],
                               Pb[:, par * 640 + j * 128:par * 640 + (j + 1) * 128],
                               P.consts["cI"][:, :])
        P.tick("t", i)
        sn[("tdone", par)] = P.cnt["t"]
        sn[("pbfree", par)] = P.cnt["t"]

    def stE(s):
        w, qt, h, qw, rt, po, nblk, ktok0, width = meta(s)
        b = (po + 64) % 128
        PTb = PT[:, (w % 2) * 4096:(w % 2 + 1) * 4096]
        P.waitv("t", "v", P.cnt["v"])
        P.waitv("t", "v", sn.get("osbdone"))
        slots = list(range(4, 8)) if qw == 0 else list(range(0, 8))
        first = True
        for s_ in slots:
            vt = (s_ - 4) if qw == 0 else (qw - 1) * 4 + s_
            lhsv = vtm[:, vt * 512 + h * 64:vt * 512 + h * 64 + 64]
            i = P.PE.matmul(ps_o[po:po + 64, :], lhsv, PTb[:, s_ * 512:(s_ + 1) * 512],
                            start=first, stop=(s_ == slots[-1]), tile_position=(0, po))
            i = P.PE.matmul(ps_rr[b:b + 1, :], P.consts["cOnesB"][:, 0:1],
                            PTb[:, s_ * 512:(s_ + 1) * 512],
                            start=first, stop=(s_ == slots[-1]), tile_position=(0, b))
            first = False
        P.tick("t", i)
        sn[("avdone", w % 2)] = P.cnt["t"]
        P.waitv("s", "t", sn[("avdone", w % 2)])
        P.waitv("s", "t", sn.get(("rbdone", b)))
        i = P.S.activation(rsr[b:b + 1, :], ps_rr[b:b + 1, :], AF.Ln)
        P.S.drain()
        i = P.S.activation(rsr[b:b + 1, :], rsr[b:b + 1, :], AF.Exp, scale=-1.0)
        P.tick("s", i)
        pend[0] = (w, rt, po, qw, b, P.cnt["s"])

    def stD(s):
        w, qt, h, qw, rt, po, nblk, ktok0, width = meta(s)
        par = s % 2
        PTb = PT[:, (w % 2) * 4096:(w % 2 + 1) * 4096]
        P.waitv("v", "t", sn[("tdone", par)])
        P.waitv("v", "t", sn.get(("avdone", w % 2)))
        qcol = qt * 128
        written = set()
        for j in range(nblk):
            slot = (qt + j) if qw > 0 else 4 + j
            written.add(slot)
            i = P.V.tensor_copy(PTb[:, slot * 512 + qcol:slot * 512 + qcol + 128],
                                ps_pt[par][:, j * 128:(j + 1) * 128])
        allslots = range(0, 8) if qw > 0 else range(4, 8)
        for slot in allslots:
            if slot not in written:
                i = P.V.memset(PTb[:, slot * 512 + qcol:slot * 512 + qcol + 128], 0.0)
        P.tick("v", i)
        sn[("ptpsfree", par)] = P.cnt["v"]
        if qt == 3:
            stE(s)

    def emit_rb_osb():
        w, rt, po, qw, b, rsrs = pend[0]
        pend[0] = None
        P.waitv("t", "s", rsrs)
        i = P.PE.matmul(ps_rr[po:po + 64, :], P.consts["cOnesRowF"][b:b + 1, 0:64],
                        rsr[b:b + 1, :], start=True, stop=True, tile_position=(b, po))
        P.tick("t", i)
        sn[("rbdone", b)] = P.cnt["t"]
        P.waitv("v", "t", sn[("rbdone", b)])
        i = P.V.tensor_copy(rbq[po:po + 64, :], ps_rr[po:po + 64, :])
        i = P.V.tensor_tensor(osb[po:po + 64, rt * T + qw * TC:rt * T + (qw + 1) * TC],
                              ps_o[po:po + 64, :], rbq[po:po + 64, :], ALU.mult)
        P.tick("v", i)
        sn["osbdone"] = P.cnt["v"]

    NS = len(steps)
    for it in range(NS + 5):
        if it < NS:
            stA(it)
        if pend[0] is not None:
            emit_rb_osb()
        if 1 <= it and it - 1 < NS:
            stB(it - 1)
        if 2 <= it and it - 2 < NS:
            stX(it - 2)
        if 3 <= it and it - 3 < NS:
            stC(it - 3)
        if 4 <= it and it - 4 < NS:
            stD(it - 4)
    if pend[0] is not None:
        emit_rb_osb()
    P.wait("t", "v")
    P.free(q4, k4, vtm, smax, nmax, smf, Pb, PT, rsr, rbq)
    P.free(ps_sim[0], ps_sim[1], ps_pt[0], ps_pt[1], ps_o, ps_rr)
    P.dump(dump_pref + "o", osb[:, :], "v")
    if P.stop(dump_pref + "attn"):
        return xp

    wo = P.sbuf("ffc_wo", [128, 4 * D], BF16)
    P.sdma("g", "woS", wo[:, :].rearrange("p (ik m) -> p ik m", ik=4),
           P.dram["L0_ffn_wo"][l].rearrange("(ik p) m -> p ik m", p=128))
    ps_a = P.psum("ffn_psa", [128, TC], F32)
    ps_b2 = P.psum("ffn_psb", [128, TC], F32)
    P.wait("t", "woS")
    snap = {}
    for kt in range(KD):
        for c in range(NCH):
            par = (kt * NCH + c) % 2
            pss = ps_a if par == 0 else ps_b2
            P.waitv("t", "v", snap.get(par))
            for ik in range(4):
                i = P.PE.matmul(pss[:, :], wo[:, ik * D + kt * 128:ik * D + (kt + 1) * 128],
                                osb[:, ik * T + c * TC:ik * T + (c + 1) * TC],
                                start=(ik == 0), stop=(ik == 3))
            P.tick("t", i)
            P.wait("v", "t")
            i = P.V.tensor_add(xpt(kt)[:, c * TC:(c + 1) * TC],
                               xpt(kt)[:, c * TC:(c + 1) * TC], pss[:, :])
            P.tick("v", i)
            snap[par] = P.cnt["v"]
    P.free(osb, wo, ps_a, ps_b2)
    P.dump(dump_pref + "x_attn", xp[:, :], "v")
    if P.stop(dump_pref + "wo"):
        return xp

    lnx = P.sbuf("ffn_lnx", [128, KD * T], BF16, hi=True)
    sh2 = P.sbuf("ffn_sh2", [128, KD * T], BF16, hi=True)
    emit_shift(P, xp, sh2)

    def wr(kt, tmp, g, b):
        P.wait("s", "v")
        i = P.S.activation(lnx[:, kt * T:(kt + 1) * T], tmp[:, :], AF.Identity, bias=b, scale=g)
        P.tick("s", i)

    emit_ln(P, [sh2[:, kt * T:(kt + 1) * T] for kt in range(KD)], wr,
            gamma_ap=wsb["lng"], beta_ap=wsb["lnb"])
    P.free(sh2)
    P.dump(dump_pref + "lnx2", lnx[:, :], "s")
    P.tick("v", P.V.nop())
    P.wait("y", "v")
    P.dma("y", P.dram["scr_xp"].rearrange("(kt p) w -> p kt w", p=128),
          xp[:, :].rearrange("p (kt w) -> p kt w", kt=KD))
    P.wait("y", "dy")
    P.free(xp)
    if P.stop(dump_pref + "ln2i"):
        return None

    gated = P.sbuf("ffn_gated", [128, 22 * T], BF16)
    w1a = P.sbuf("ffn_w1a", [128, 2 * KD * 128], BF16)
    w1g = P.sbuf("ffn_w1g", [128, 2 * KD * 128], BF16)
    ps_w = [P.psum("ffn_psw%d" % k, [128, TC], F32) for k in range(4)]
    w1d = P.dram["L0_ffn_w1"]
    snap = {}
    for j, (off, sz) in enumerate(HT):
        par = j % 2
        base = par * KD * 128
        sname = "w1p%d" % par
        P.waitv("g", "t", snap.get(("t", par)))
        P.sdma("g", sname, w1a[:, base:base + KD * sz].rearrange("p (kt m) -> p kt m", kt=KD),
               w1d[l, :, off:off + sz].rearrange("(kt p) m -> p kt m", p=128))
        P.sdma("g", sname, w1g[:, base:base + KD * sz].rearrange("p (kt m) -> p kt m", kt=KD),
               w1d[l, :, HID + off:HID + off + sz].rearrange("(kt p) m -> p kt m", p=128))
        dgs = P.cnt[sname]
        for c in range(NCH):
            cp = c % 2
            ps_wa, ps_wg = ps_w[cp * 2], ps_w[cp * 2 + 1]
            P.waitv("t", sname, dgs)
            P.waitv("t", "v", snap.get(("v", cp)))
            for kt in range(KD):
                i = P.PE.matmul(ps_wa[0:sz, :], w1a[:, base + kt * sz:base + (kt + 1) * sz],
                                lnx[:, kt * T + c * TC:kt * T + (c + 1) * TC],
                                start=(kt == 0), stop=(kt == KD - 1))
            for kt in range(KD):
                i = P.PE.matmul(ps_wg[0:sz, :], w1g[:, base + kt * sz:base + (kt + 1) * sz],
                                lnx[:, kt * T + c * TC:kt * T + (c + 1) * TC],
                                start=(kt == 0), stop=(kt == KD - 1))
            P.tick("t", i)
            gsl = gated[0:sz, j * T + c * TC:j * T + (c + 1) * TC]
            P.wait("s", "t")
            i = P.S.activation(gsl, ps_wg[0:sz, :], AF.Gelu, bias=wsb["b1g"][0:sz, j:j + 1])
            P.tick("s", i)
            P.wait("v", "s")
            i = P.V.scalar_tensor_tensor(gsl, ps_wa[0:sz, :], wsb["b1a"][0:sz, j:j + 1], gsl,
                                         ALU.add, ALU.mult)
            P.tick("v", i)
            snap[("v", cp)] = P.cnt["v"]
        snap[("t", par)] = P.cnt["t"]
    P.free(w1a, w1g, *ps_w)
    P.free(lnx)
    P.dump(dump_pref + "gated", gated[:, :], "v")
    if P.stop(dump_pref + "gated"):
        return None

    xp = P.sbuf("xp_arena", [128, KD * T], F32, hi=True)
    P.dma("y", xp[:, :].rearrange("p (kt w) -> p kt w", kt=KD),
          P.dram["scr_xp"].rearrange("(kt p) w -> p kt w", p=128))
    w2s = P.sbuf("ffn_w2s", [128, 2 * 22 * 128], BF16)
    ps_a2 = P.psum("ffn_ps2a", [128, TC], F32)
    ps_b3 = P.psum("ffn_ps2b", [128, TC], F32)
    w2d = P.dram["L0_ffn_w2"]
    P.wait("v", "dy")
    snap = {}
    for m in range(KD):
        par = m % 2
        base = par * 22 * 128
        P.waitv("g", "t", snap.get(("t", par)))
        sname = "w2p%d" % par
        for j, (off, sz) in enumerate(HT):
            P.sdma("g", sname, w2s[0:sz, base + j * 128:base + (j + 1) * 128],
                   w2d[l, off:off + sz, m * 128:(m + 1) * 128])
        snap[("dg", par)] = (sname, P.cnt[sname])
        for c in range(NCH):
            cp = c % 2
            pss = ps_a2 if cp == 0 else ps_b3
            P.waitv("t", *snap[("dg", par)])
            P.waitv("t", "v", snap.get(("v", cp)))
            for j, (off, sz) in enumerate(HT):
                i = P.PE.matmul(pss[:, :], w2s[0:sz, base + j * 128:base + j * 128 + 128],
                                gated[0:sz, j * T + c * TC:j * T + (c + 1) * TC],
                                start=(j == 0), stop=(j == 21))
            P.tick("t", i)
            P.wait("v", "t")
            i = P.V.scalar_tensor_tensor(xpt(m)[:, c * TC:(c + 1) * TC], pss[:, :],
                                         wsb["b2"][:, m:m + 1], xpt(m)[:, c * TC:(c + 1) * TC],
                                         ALU.add, ALU.add)
            P.tick("v", i)
            snap[("v", cp)] = P.cnt["v"]
        snap[("t", par)] = P.cnt["t"]
    P.free(w2s, ps_a2, ps_b3, gated)
    P.dump(dump_pref + f"x_l{l}", xp[:, :], "v")
    return xp


def emit_ffn_call(P, xp, dump_pref=""):
    for l in range(2):
        wsb = {
            "qs": P.par["qs"][:, l:l + 1], "ks": P.par["ks"][:, l:l + 1],
            "lng": P.par["lng"][:, l * KD:(l + 1) * KD], "lnb": P.par["lnb"][:, l * KD:(l + 1) * KD],
            "b1a": P.par["b1a"][:, l * 22:(l + 1) * 22], "b1g": P.par["b1g"][:, l * 22:(l + 1) * 22],
            "b2": P.par["b2"][:, l * KD:(l + 1) * KD],
        }
        P.barrier()
        xp = emit_ffn_layer(P, l, xp, wsb, dump_pref=dump_pref + f"l{l}_")
        if P.stop_flag:
            break
        P.barrier()
        if P.stop(dump_pref + f"l{l}"):
            break
    return xp



def emit_mamba(P, lnx, dump_pref=""):
    xsp = P.sbuf("mb_xsp", [128, 2 * TC], BF16)
    ztile = P.sbuf("mb_zt", [128, KI * 4], BF16)
    i = P.V.memset(ztile[:, :], 0.0)
    P.tick("v", i)
    P.wait("y", "v")
    P.sdma("y", "xpadS", P.dram["scr_xs"][:, 0:4].rearrange("(kt p) w -> p kt w", p=128),
           ztile[:, :].rearrange("p (kt w) -> p kt w", kt=KI))
    P.sdma("y", "xpadS", P.dram["scr_xs"][:, XSW - 4:XSW].rearrange("(kt p) w -> p kt w", p=128),
           ztile[:, :].rearrange("p (kt w) -> p kt w", kt=KI))
    inw = P.sbuf("mb_inw", [128, 2 * KD * 128], BF16)
    zbuf = P.sbuf("mb_zb", [128, 2 * TC], BF16)
    ps_i = P.psum("mb_psi", [128, TC], F32)
    ps_i2 = P.psum("mb_psi2", [128, TC], F32)
    inwd = P.dram["L0_attn_in_w"]
    snap = {}
    for m in range(2 * KI):
        par = m % 2
        base = par * KD * 128
        P.waitv("g", "t", snap.get(("t", par)))
        sname = "inwp%d" % par
        P.sdma("g", sname, inw[:, base:base + KD * 128].rearrange("p (kt mm) -> p kt mm", kt=KD),
               inwd[:, m * 128:(m + 1) * 128].rearrange("(kt p) mm -> p kt mm", p=128))
        snap[("dg", par)] = (sname, P.cnt[sname])
        for c in range(NCH):
            cp = c % 2
            pss = ps_i if cp == 0 else ps_i2
            P.waitv("t", *snap[("dg", par)])
            P.waitv("t", "s", snap.get(("s", cp)))
            for kt in range(KD):
                i = P.PE.matmul(pss[:, :], inw[:, base + kt * 128:base + (kt + 1) * 128],
                                lnx[:, kt * T + c * TC:kt * T + (c + 1) * TC],
                                start=(kt == 0), stop=(kt == KD - 1))
            P.tick("t", i)
            P.wait("s", "t")
            if m < KI:
                xp2 = c % 2
                xn = "xswp%d" % xp2
                P.waitv("s", xn, snap.get(("xdy", xp2)))
                i = P.S.activation(xsp[:, xp2 * TC:(xp2 + 1) * TC], pss[:, :], AF.Identity)
                P.tick("s", i)
                P.wait("y", "s")
                P.sdma("y", xn, P.dram["scr_xs"][m * 128:(m + 1) * 128, 4 + c * TC:4 + (c + 1) * TC],
                       xsp[:, xp2 * TC:(xp2 + 1) * TC])
                snap[("xdy", xp2)] = P.cnt[xn]
            else:
                zp = c % 2
                zn = "zspp%d" % zp
                P.waitv("s", zn, snap.get(("zdy", zp)))
                i = P.S.activation(zbuf[:, zp * TC:(zp + 1) * TC], pss[:, :], AF.Silu)
                P.tick("s", i)
                P.wait("y", "s")
                P.sdma("y", zn, P.dram["scr_z"][(m - KI) * 128:(m - KI + 1) * 128, c * TC:(c + 1) * TC],
                       zbuf[:, zp * TC:(zp + 1) * TC])
                snap[("zdy", zp)] = P.cnt[zn]
            snap[("s", cp)] = P.cnt["s"]
        snap[("t", par)] = P.cnt["t"]
    P.free(inw, zbuf, ps_i, ps_i2, lnx, xsp, ztile)
    xsw_fin = (P.cnt.get("xswp0", 0), P.cnt.get("xswp1", 0), P.cnt.get("xpadS", 0))
    P.barrier()
    if P.stop("mamba_xs"):
        return

    for dirn in range(2):
        sfx = "" if dirn == 0 else "_b"
        wn = sfx
        pA = P.par["A" + sfx]
        pcw = P.par["cw" + sfx]
        pcb = P.par["cb" + sfx]
        pdtb = P.par["dtb" + sfx]
        pD = P.par["Dsk" + sfx]
        xst = P.sbuf("mb_xst", [128, 2 * XSW], BF16)
        xld = {}
        vconv = {}
        P.waitv("y", "xswp0", xsw_fin[0])
        P.waitv("y", "xswp1", xsw_fin[1])
        P.waitv("y", "xpadS", xsw_fin[2])

        def load_xs(kt):
            par = kt % 2
            xn = "xsL%d" % par
            P.waitv("y", "v", vconv.get(par))
            P.sdma("y", xn, xst[:, par * XSW:(par + 1) * XSW],
                   P.dram["scr_xs"][kt * 128:(kt + 1) * 128, :])
            xld[par] = P.cnt[xn]

        xprojT = P.sbuf("mb_xproj", [128, KI * 96], BF16)
        dtwT = P.sbuf("mb_dtw", [64, KI * 128], BF16)
        P.dma("g", xprojT[:, :].rearrange("p (kt m) -> p kt m", kt=KI),
              P.dram["L0_attn_xproj_w" + wn].rearrange("(kt p) m -> p kt m", p=128))
        P.dma("g", dtwT[:, :], P.dram["L0_attn_dt_w" + wn])
        xproj = xprojT[:, :]
        dtw = dtwT[:, :]
        ydram = P.dram["scr_yf" if dirn == 0 else "scr_yb"]

        def conv_tile(kt, dst_f32, t0, tw):
            tb = (kt % 2) * XSW
            P.waitv("v", "xsL%d" % (kt % 2), xld[kt % 2])
            for k in range(4):
                if dirn == 0:
                    src = xst[:, tb + 1 + k + t0:tb + 1 + k + t0 + tw]
                else:
                    st = tb + T + 6 - k - t0
                    src = xst[:, st:st - tw:-1]
                w_ap = pcw[:, kt * 4 + k:kt * 4 + k + 1]
                if k == 0:
                    i = P.V.tensor_scalar_mul(dst_f32, src, w_ap)
                else:
                    i = P.V.scalar_tensor_tensor(dst_f32, src, w_ap, dst_f32, ALU.mult, ALU.add)
            P.tick("v", i)

        xcb = P.sbuf("mbA_xc", [128, 2 * T], BF16)
        accA = P.sbuf("mbA_acc", [128, T], F32)
        dtpre = P.sbuf("mbA_dtpre", [64, T], BF16)
        bcrow = P.sbuf("mbA_bcrow", [48, T], BF16)
        ps_dt = P.psum("mbA_psdt", [64, T], F32)
        ps_bc = P.psum("mbA_psbc", [48, T], F32)
        load_xs(0)
        for kt in range(KI):
            par = kt % 2
            P.wait("v", "t")
            P.wait("v", "s")
            conv_tile(kt, accA[:, :], 0, T)
            vconv[kt % 2] = P.cnt["v"]
            if kt + 1 < KI:
                load_xs(kt + 1)
            P.wait("s", "v")
            P.wait("s", "t")
            i = P.S.activation(xcb[:, par * T:(par + 1) * T], accA[:, :], AF.Silu,
                               bias=pcb[:, kt:kt + 1])
            P.tick("s", i)
            P.wait("t", "s")
            for c in range(NCH):
                csl = slice(c * TC, (c + 1) * TC)
                mv = xcb[:, par * T + c * TC:par * T + (c + 1) * TC]
                i = P.PE.matmul(ps_dt[0:64, csl], xproj[:, kt * 96:kt * 96 + 64], mv,
                                start=(kt == 0), stop=(kt == KI - 1))
                i = P.PE.matmul(ps_bc[0:16, csl], xproj[:, kt * 96 + 64:kt * 96 + 80], mv,
                                start=(kt == 0), stop=(kt == KI - 1))
                i = P.PE.matmul(ps_bc[32:48, csl], xproj[:, kt * 96 + 80:kt * 96 + 96], mv,
                                start=(kt == 0), stop=(kt == KI - 1), tile_position=(0, 32))
            P.tick("t", i)
        P.wait("s", "t")
        i = P.S.activation(dtpre[:, :], ps_dt[0:64, :], AF.Identity)
        i = P.S.activation(bcrow[0:16, :], ps_bc[0:16, :], AF.Identity)
        i = P.S.activation(bcrow[32:48, :], ps_bc[32:48, :], AF.Identity)
        P.tick("s", i)
        P.wait("y", "s")
        P.dma("y", P.dram["scr_b"], bcrow[0:16, :])
        P.dma("y", P.dram["scr_c"], bcrow[32:48, :])
        P.wait("y", "dy")
        P.free(xcb, accA, ps_bc, bcrow)
        P.dump(dump_pref + "dtpre" + sfx, dtpre[:, :], "s", parts=64)

        NGS = 4
        NGRP = DST // NGS
        SEGW = TH + 1
        GWS = NGS * SEGW
        state = P.sbuf("mb_state", [128, KI * DST], F32)
        bcb = P.sbuf("mb_bcb", [128, 2 * DST * TH], BF16)
        dts = P.sbuf("mb_dts", [128, TH], F32)
        xcs = P.sbuf("mb_xcs", [128, TH], BF16)
        us = P.sbuf("mb_us", [128, TH], BF16)
        NB = 3
        dAg = P.sbuf("mb_dAg", [128, 2 * GWS], F32)
        dbug = P.sbuf("mb_dbug", [128, NB * GWS], BF16)
        cyg = P.sbuf("mb_cyg", [128, NGS * TH], BF16)
        tmp2 = P.sbuf("mb_tmp2", [128, 2 * TH], BF16)
        hbg = P.sbuf("mb_hbg", [128, GWS], BF16)
        yg = P.sbuf("mb_yg", [128, NGRP * 2 * TH], BF16)
        ysum = P.sbuf("mb_ysum", [128, TH], BF16)
        yout = P.sbuf("mb_yout", [128, TH], BF16)
        yrev = P.sbuf("mb_yrev", [128, TH], BF16) if dirn == 1 else None
        ps_d2 = P.psum("mb_psd2", [128, TC], F32)
        acc2 = P.sbuf("mb_acc2", [128, 2 * TH], F32)
        accS = acc2[:, 0:TH]
        sp2 = acc2[:, TH:2 * TH]
        i = P.V.memset(dAg[:, 0:2 * GWS:SEGW], 0.0)
        P.tick("v", i)
        for nb in range(NB):
            i = P.G.memset(dbug[:, nb * GWS:(nb + 1) * GWS:SEGW], 0.0)
        P.tick("g", i)

        gctr = 0
        snap_scan = None
        snap_scan_p = {}
        state_v = None
        for half in range(2):
            t0 = half * TH
            P.wait("y", "g")
            P.wait("y", "v")
            for n in range(DST):
                P.sdma("y", "bcbS", bcb[:, n * TH:(n + 1) * TH],
                       P.dram["scr_b"][n:n + 1, t0:t0 + TH].broadcast_to([128, TH]))
                P.sdma("y", "bcbS", bcb[:, (DST + n) * TH:(DST + n + 1) * TH],
                       P.dram["scr_c"][n:n + 1, t0:t0 + TH].broadcast_to([128, TH]))
            bc_dy = P.cnt["bcbS"]
            load_xs(0)
            for kt in range(KI):
                P.wait("v", "s")
                P.wait("v", "g")
                conv_tile(kt, accS, t0, TH)
                vconv[kt % 2] = P.cnt["v"]
                if kt + 1 < KI:
                    load_xs(kt + 1)
                P.wait("s", "v")
                i = P.S.activation(xcs[:, :], accS, AF.Silu, bias=pcb[:, kt:kt + 1])
                P.tick("s", i)
                for c in range(2):
                    P.wait("t", "v")
                    P.wait("t", "s")
                    i = P.PE.matmul(ps_d2[:, :], dtw[:, kt * 128:(kt + 1) * 128],
                                    dtpre[:, t0 + c * TC:t0 + (c + 1) * TC], start=True, stop=True)
                    P.tick("t", i)
                    P.wait("v", "t")
                    i = P.V.tensor_scalar(accS[:, c * TC:(c + 1) * TC], ps_d2[:, :],
                                          pdtb[:, kt:kt + 1], 0.0, ALU.add, ALU.max)
                    i = P.V.tensor_scalar(sp2[:, c * TC:(c + 1) * TC], ps_d2[:, :],
                                          pdtb[:, kt:kt + 1], None, ALU.add)
                    P.tick("v", i)
                P.wait("s", "v")
                i = P.S.activation(sp2, sp2, AF.Abs)
                P.S.drain()
                i = P.S.activation(sp2, sp2, AF.Exp, scale=-1.0)
                P.S.drain()
                i = P.S.activation(sp2, sp2, AF.Ln, bias=1.0)
                P.tick("s", i)
                P.wait("v", "s")
                i = P.V.tensor_add(dts[:, :], accS, sp2)
                i = P.V.tensor_tensor(us[:, :], dts[:, :], xcs[:, :], ALU.mult)
                P.tick("v", i)
                us_v = P.cnt["v"]
                for g in range(NGRP):
                    gp = gctr % NB
                    ap = gctr % 2
                    gctr += 1
                    n0 = g * NGS
                    s0 = kt * DST + n0
                    P.waitv("s", "v", max(us_v, snap_scan_p.get(("a", ap)) or 0))
                    for j in range(NGS):
                        i = P.S.activation(dAg[:, ap * GWS + j * SEGW + 1:ap * GWS + (j + 1) * SEGW],
                                           dts[:, :], AF.Exp, scale=pA[:, s0 + j:s0 + j + 1])
                    P.tick("s", i)
                    sexp_g = P.cnt["s"]
                    P.waitv("g", "v", max(us_v, snap_scan_p.get(("b", gp)) or 0, state_v or 0))
                    P.waitv("g", "bcbS", bc_dy)
                    for j in range(NGS):
                        i = P.G.tensor_tensor(dbug[:, gp * GWS + j * SEGW + 1:gp * GWS + (j + 1) * SEGW],
                                              us[:, :], bcb[:, (n0 + j) * TH:(n0 + j + 1) * TH],
                                              ALU.mult)
                    if half == 1:
                        i = P.G.tensor_copy(dbug[:, gp * GWS:gp * GWS + GWS:SEGW],
                                            state[:, s0:s0 + NGS])
                    P.tick("g", i)
                    gfill_g = P.cnt["g"]
                    P.waitv("v", "s", sexp_g)
                    P.waitv("v", "g", gfill_g)
                    i = P.V.tensor_tensor_scan(hbg[:, :], dAg[:, ap * GWS:(ap + 1) * GWS],
                                               dbug[:, gp * GWS:(gp + 1) * GWS],
                                               0.0, ALU.mult, ALU.add)
                    P.tick("v", i)
                    snap_scan = P.cnt["v"]
                    snap_scan_p[("a", ap)] = snap_scan
                    snap_scan_p[("b", gp)] = snap_scan
                    if half == 0:
                        i = P.V.tensor_copy(state[:, s0:s0 + NGS], hbg[:, TH:GWS:SEGW])
                    i = P.V.tensor_tensor(
                        cyg[:, :].rearrange("p (n t) -> p n t", n=NGS),
                        hbg[:, :].rearrange("p (n t) -> p n t", n=NGS)[:, :, 1:SEGW],
                        bcb[:, DST * TH + n0 * TH:DST * TH + (n0 + NGS) * TH]
                           .rearrange("p (n t) -> p n t", n=NGS),
                        ALU.mult)
                    i = P.V.tensor_add(yg[:, g * 2 * TH:(g + 1) * 2 * TH],
                                       cyg[:, 0:2 * TH], cyg[:, 2 * TH:4 * TH])
                    P.tick("v", i)
                if "yspS" in P.cnt:
                    P.waitv("v", "yspS", P.cnt["yspS"])
                i = P.V.tensor_add(tmp2[:, :], yg[:, 0:2 * TH], yg[:, 2 * TH:4 * TH])
                i = P.V.tensor_add(cyg[:, 0:2 * TH], yg[:, 4 * TH:6 * TH], yg[:, 6 * TH:8 * TH])
                i = P.V.tensor_add(cyg[:, 2 * TH:4 * TH], tmp2[:, :], cyg[:, 0:2 * TH])
                i = P.V.tensor_add(ysum[:, :], cyg[:, 2 * TH:3 * TH], cyg[:, 3 * TH:4 * TH])
                i = P.V.scalar_tensor_tensor(yout[:, :], xcs[:, :], pD[:, kt:kt + 1], ysum[:, :],
                                             ALU.mult, ALU.add)
                P.tick("v", i)
                if dirn == 0:
                    P.wait("y", "v")
                    P.sdma("y", "yspS", ydram[kt * 128:(kt + 1) * 128, t0:t0 + TH], yout[:, :])
                else:
                    i = P.V.tensor_copy(yrev[:, :], yout[:, TH - 1::-1])
                    P.tick("v", i)
                    P.wait("y", "v")
                    P.sdma("y", "yspS", ydram[kt * 128:(kt + 1) * 128, T - t0 - TH:T - t0], yrev[:, :])
            if half == 0:
                state_v = P.cnt["v"]
        P.free(dtpre, ps_dt, state, bcb, dts, xcs, us, dAg, dbug, cyg, tmp2, acc2,
               hbg, yg, ysum, yout, ps_d2, xprojT, dtwT, xst)
        if yrev is not None:
            P.free(yrev)
        P.barrier()
        if P.stop("mamba_dir%d" % dirn):
            return

    ych = P.sbuf("mg_ych", [128, 32 * TC], BF16)
    mw = P.sbuf("mg_mw", [128, 2 * 32 * 128], BF16)
    zs = P.sbuf("mg_zs", [128, 2 * TC], BF16)
    ymb = P.sbuf("mg_ym", [128, 2 * TC], BF16)
    ps_g = P.psum("mg_psg", [128, TC], F32)
    ps_g2 = P.psum("mg_psg2", [128, TC], F32)
    mwd = P.dram["L0_attn_merge_w"]
    snap = {}
    for c in range(NCH):
        P.wait("y", "t")
        P.sdma("y", "ychS", ych[:, 0:16 * TC].rearrange("p (kt w) -> p kt w", kt=KI),
               P.dram["scr_yf"][:, c * TC:(c + 1) * TC].rearrange("(kt p) w -> p kt w", p=128))
        P.sdma("y", "ychS", ych[:, 16 * TC:32 * TC].rearrange("p (kt w) -> p kt w", kt=KI),
               P.dram["scr_yb"][:, c * TC:(c + 1) * TC].rearrange("(kt p) w -> p kt w", p=128))
        ych_dy = P.cnt["ychS"]
        for m in range(KI):
            par = m % 2
            base = par * 32 * 128
            P.waitv("g", "t", snap.get(("t", par)))
            mn = "mwp%d" % par
            P.sdma("g", mn, mw[:, base:base + 32 * 128].rearrange("p (kt mm) -> p kt mm", kt=32),
                   mwd[:, m * 128:(m + 1) * 128].rearrange("(kt p) mm -> p kt mm", p=128))
            zn2 = "mzsp%d" % par
            P.waitv("y", "v", snap.get(("v", par)))
            P.sdma("y", zn2, zs[:, par * TC:(par + 1) * TC],
                   P.dram["scr_z"][m * 128:(m + 1) * 128, c * TC:(c + 1) * TC])
            pss = ps_g if par == 0 else ps_g2
            P.waitv("t", mn, P.cnt[mn])
            P.waitv("t", "ychS", ych_dy)
            P.waitv("t", "v", snap.get(("v", par)))
            for kt in range(32):
                i = P.PE.matmul(pss[:, :], mw[:, base + kt * 128:base + (kt + 1) * 128],
                                ych[:, kt * TC:(kt + 1) * TC], start=(kt == 0), stop=(kt == 31))
            P.tick("t", i)
            P.wait("v", "t")
            P.waitv("v", zn2, P.cnt[zn2])
            ymn = "ymbp%d" % par
            if ymn in P.cnt:
                P.waitv("v", ymn, P.cnt[ymn])
            i = P.V.tensor_tensor(ymb[:, par * TC:(par + 1) * TC], pss[:, :],
                                  zs[:, par * TC:(par + 1) * TC], ALU.mult)
            P.tick("v", i)
            snap[("v", par)] = P.cnt["v"]
            snap[("t", par)] = P.cnt["t"]
            P.wait("y", "v")
            P.sdma("y", ymn, P.dram["scr_ym"][m * 128:(m + 1) * 128, c * TC:(c + 1) * TC],
                   ymb[:, par * TC:(par + 1) * TC])
    P.free(ych, mw, zs, ymb, ps_g, ps_g2)
    P.barrier()
    if P.stop("mamba_merge"):
        return


def emit_outw_and_residual(P, x, ab2):
    ymch = P.sbuf("ow_ymch", [128, KI * TC], BF16)
    ow = P.sbuf("ow_w", [128, 2 * KI * 128], BF16)
    ps_o = P.psum("ow_ps", [128, TC], F32)
    ps_o2 = P.psum("ow_ps2", [128, TC], F32)
    tmp = P.sbuf("ow_tmp", [128, TC], F32)
    owd = P.dram["L0_attn_out_w"]
    a_b = ab2[:, 0:T]
    b_b = ab2[:, T:2 * T]
    snap = {}
    for c in range(NCH):
        P.wait("y", "t")
        P.sdma("y", "ymchS", ymch[:, :].rearrange("p (kt w) -> p kt w", kt=KI),
               P.dram["scr_ym"][:, c * TC:(c + 1) * TC].rearrange("(kt p) w -> p kt w", p=128))
        ych_dy = P.cnt["ymchS"]
        for mo in range(KD):
            par = mo % 2
            base = par * KI * 128
            P.waitv("g", "t", snap.get(("t", par)))
            on = "owp%d" % par
            P.sdma("g", on, ow[:, base:base + KI * 128].rearrange("p (kt mm) -> p kt mm", kt=KI),
                   owd[:, mo * 128:(mo + 1) * 128].rearrange("(kt p) mm -> p kt mm", p=128))
            pss = ps_o if par == 0 else ps_o2
            P.waitv("t", on, P.cnt[on])
            P.waitv("t", "ymchS", ych_dy)
            P.waitv("t", "v", snap.get(("v", par)))
            for kt in range(KI):
                i = P.PE.matmul(pss[:, :], ow[:, base + kt * 128:base + (kt + 1) * 128],
                                ymch[:, kt * TC:(kt + 1) * TC], start=(kt == 0), stop=(kt == KI - 1))
            P.tick("t", i)
            xsl = x[:, mo * T + c * TC:mo * T + (c + 1) * TC]
            P.wait("v", "t")
            i = P.V.tensor_tensor(tmp[:, :], xsl, a_b[:, c * TC:(c + 1) * TC], ALU.mult)
            i = P.V.tensor_sub(tmp[:, :], tmp[:, :], b_b[:, c * TC:(c + 1) * TC])
            i = P.V.tensor_add(tmp[:, :], tmp[:, :], pss[:, :])
            i = P.V.tensor_add(xsl, xsl, tmp[:, :])
            P.tick("v", i)
            snap[("v", par)] = P.cnt["v"]
            snap[("t", par)] = P.cnt["t"]
    P.free(ymch, ow, ps_o, ps_o2, tmp)


def self_val(P, e):
    return P.cnt[e]



def build_program(want_dumps=(), stop_after=None):
    nc = bass.Bass("TRN2", target_bir_lowering=False)
    P = Prog(nc, want_dumps)
    P.stop_after = stop_after

    P.dram["xt"] = nc.dram_tensor("xt", [D, T], F32, kind="ExternalInput").ap()
    P.dram["xo"] = nc.dram_tensor("xo", [D, T], F32, kind="ExternalOutput").ap()
    for nm, (shape, dt) in {**CONST_SPECS, **ROT_SPECS}.items():
        P.dram[nm] = nc.dram_tensor(nm, shape, dt, kind="ExternalInput").ap()
    for nm, shape in WEIGHT_SPECS.items():
        P.dram[nm] = nc.dram_tensor(nm, shape, F32, kind="ExternalInput").ap()
    for nm, shape, dt in [
        ("scr_x", [D, T], F32), ("scr_z", [DI, T], BF16),
        ("scr_yf", [DI, T], BF16), ("scr_yb", [DI, T], BF16),
        ("scr_ym", [DI, T], BF16), ("scr_b", [DST, T], BF16), ("scr_c", [DST, T], BF16),
        ("scr_ab", [128, 2 * T], F32), ("scr_rows", [2, T], F32),
        ("scr_xp", [D, T], F32), ("scr_xs", [DI, XSW], BF16),
    ]:
        P.dram[nm] = nc.dram_tensor(nm, shape, dt, kind="Internal").ap()
    if want_dumps:
        P.dbg = nc.dram_tensor("dbg", [128, 400000], F32, kind="ExternalOutput").ap()

    for nm, (shape, dt) in CONST_SPECS.items():
        P.consts[nm] = P.sbuf(nm, shape, dt)
        P.dma("y", P.consts[nm][:, :], P.dram[nm])
    nc.const_aps.register(F32, EPS, P.consts["cMisc"][:, 0:1])

    def par_alloc(nm, shape, dtype=F32):
        P.par[nm] = P.sbuf("par_" + nm, shape, dtype)
        return P.par[nm]

    qsd, ksd = P.dram["L0_ffn_q_scale"], P.dram["L0_ffn_k_scale"]
    par_alloc("qs", [128, 2]); par_alloc("ks", [128, 2])
    for l in range(2):
        for half in range(2):
            P.dma("y", P.par["qs"][half * 64:(half + 1) * 64, l:l + 1],
                  qsd[l, :].rearrange("(d one) -> d one", one=1))
            P.dma("y", P.par["ks"][half * 64:(half + 1) * 64, l:l + 1],
                  ksd[l, :].rearrange("(d one) -> d one", one=1))
    par_alloc("lng", [128, 2 * KD]); par_alloc("lnb", [128, 2 * KD])
    par_alloc("b2", [128, 2 * KD])
    for l in range(2):
        P.dma("y", P.par["lng"][:, l * KD:(l + 1) * KD],
              P.dram["L0_ffn_ln_g"][l].rearrange("(kt p) -> p kt", p=128), slow=True)
        P.dma("y", P.par["lnb"][:, l * KD:(l + 1) * KD],
              P.dram["L0_ffn_ln_b"][l].rearrange("(kt p) -> p kt", p=128), slow=True)
        P.dma("y", P.par["b2"][:, l * KD:(l + 1) * KD],
              P.dram["L0_ffn_b2"][l].rearrange("(kt p) -> p kt", p=128), slow=True)
    par_alloc("b1a", [128, 2 * 22]); par_alloc("b1g", [128, 2 * 22])
    for l in range(2):
        for which, off0 in (("b1a", 0), ("b1g", HID)):
            P.dma("y", P.par[which][:, l * 22:l * 22 + 21],
                  P.dram["L0_ffn_b1"][l, off0:off0 + 2688].rearrange("(j p) -> p j", p=128), slow=True)
            P.dma("y", P.par[which][0:42, l * 22 + 21:l * 22 + 22],
                  P.dram["L0_ffn_b1"][l, off0 + 2688:off0 + 2730].rearrange("(d one) -> d one", one=1))
    par_alloc("gamma", [128, KD]); par_alloc("beta", [128, KD])
    P.dma("y", P.par["gamma"][:, :], P.dram["L0_attn_gamma"].rearrange("(kt p) -> p kt", p=128), slow=True)
    P.dma("y", P.par["beta"][:, :], P.dram["L0_attn_beta"].rearrange("(kt p) -> p kt", p=128), slow=True)
    for sfx, wn in (("", ""), ("_b", "_b")):
        par_alloc("A" + sfx, [128, KI * DST])
        P.dma("y", P.par["A" + sfx][:, :].rearrange("p (kt n) -> p kt n", kt=KI),
              P.dram["L0_attn_A_log" + wn].rearrange("(kt p) n -> p kt n", p=128))
        par_alloc("cw" + sfx, [128, KI * 4])
        P.dma("y", P.par["cw" + sfx][:, :].rearrange("p (kt k) -> p kt k", kt=KI),
              P.dram["L0_attn_conv_w" + wn].rearrange("(kt p) k -> p kt k", p=128))
        for pn, dn in (("cb", "conv_b"), ("dtb", "dt_b"), ("Dsk", "D_skip")):
            par_alloc(pn + sfx, [128, KI])
            P.dma("y", P.par[pn + sfx][:, :],
                  P.dram["L0_attn_" + dn + wn].rearrange("(kt p) -> p kt", p=128), slow=True)
    P.barrier()
    for sfx in ("", "_b"):
        i = P.S.activation(P.par["A" + sfx][:, :], P.par["A" + sfx][:, :], AF.Exp)
        P.tick("s", i)
        P.wait("v", "s")
        i = P.V.tensor_scalar_mul(P.par["A" + sfx][:, :], P.par["A" + sfx][:, :], -1.0)
        P.tick("v", i)
    P.barrier()

    x = P.sbuf("x_arena", [128, KD * T], F32, hi=True)
    P.dma("y", x[:, :].rearrange("p (kt w) -> p kt w", kt=KD),
          P.dram["xt"].rearrange("(kt p) w -> p kt w", p=128))
    P.barrier()
    xtiles = lambda xx: [xx[:, kt * T:(kt + 1) * T] for kt in range(KD)]

    def run_ffn_macaron(tagp):
        xp = P.sbuf("xp_arena", [128, KD * T], F32, hi=True)
        emit_ln(P, xtiles(x), None, direct_dst=lambda kt: xp[:, kt * T:(kt + 1) * T])
        P.dump(tagp + "ln", xp[:, :], "v")
        P.wait("y", "v")
        P.dma("y", P.dram["scr_x"].rearrange("(kt p) w -> p kt w", p=128),
              x[:, :].rearrange("p (kt w) -> p kt w", kt=KD))
        P.barrier()
        P.free(x)
        xp = emit_ffn_call(P, xp, dump_pref=tagp)
        if P.stop_flag:
            return xp, None
        x2 = P.sbuf("x_arena", [128, KD * T], F32, hi=True)
        P.dma("y", x2[:, :].rearrange("p (kt w) -> p kt w", kt=KD),
              P.dram["scr_x"].rearrange("(kt p) w -> p kt w", p=128))
        P.barrier()
        for kt in range(KD):
            i = P.V.scalar_tensor_tensor(x2[:, kt * T:(kt + 1) * T], xp[:, kt * T:(kt + 1) * T],
                                         0.5, x2[:, kt * T:(kt + 1) * T], ALU.mult, ALU.add)
            P.tick("v", i)
        P.free(xp)
        P.dump(tagp + "x_out", x2[:, :], "v")
        return None, x2

    if stop_after == "ln1_only":
        P.dump("x0", x[:, :], "v")
        xp = P.sbuf("xp_arena", [128, KD * T], F32, hi=True)
        emit_ln(P, xtiles(x), None, direct_dst=lambda kt: xp[:, kt * T:(kt + 1) * T])
        P.dump("ln1", xp[:, :], "v")
        P.barrier()
        P.dump("ln1b", xp[:, :], "v")
        P.stop_flag = True
        finish(P)
        return nc, P
    xp, x = run_ffn_macaron("f1_")
    if P.stop_flag:
        finish(P)
        return nc, P

    lnm = P.sbuf("mb_lnm", [128, KD * T], BF16, hi=True)

    def wrm(kt, tmp, g, b):
        P.wait("s", "v")
        i = P.S.activation(lnm[:, kt * T:(kt + 1) * T], tmp[:, :], AF.Identity, bias=b, scale=g)
        P.tick("s", i)

    ab, ab2 = emit_ln(P, xtiles(x), wrm, gamma_ap=P.par["gamma"], beta_ap=P.par["beta"],
                      fuse_double=True, keep_ab=True)
    P.dump("mb_lnm", lnm[:, :], "s")
    P.wait("y", "v")
    P.dma("y", P.dram["scr_x"].rearrange("(kt p) w -> p kt w", p=128),
          x[:, :].rearrange("p (kt w) -> p kt w", kt=KD))
    P.dma("y", P.dram["scr_ab"], ab2[:, :])
    P.barrier()
    P.free(x, ab, ab2)
    if P.stop("ln2"):
        finish(P)
        return nc, P
    emit_mamba(P, lnm, dump_pref="mb_")
    if P.stop_flag:
        finish(P)
        return nc, P
    x = P.sbuf("x_arena", [128, KD * T], F32, hi=True)
    P.dma("y", x[:, :].rearrange("p (kt w) -> p kt w", kt=KD),
          P.dram["scr_x"].rearrange("(kt p) w -> p kt w", p=128))
    ab2 = P.sbuf("ow_ab2", [128, 2 * T], F32)
    P.dma("y", ab2[:, :], P.dram["scr_ab"])
    P.barrier()
    emit_outw_and_residual(P, x, ab2)
    P.free(ab2)
    P.dump("mb_x_out", x[:, :], "v")
    P.barrier()
    if P.stop("mamba"):
        finish(P)
        return nc, P

    xp, x = run_ffn_macaron("f2_")
    if P.stop_flag:
        finish(P)
        return nc, P

    P.wait("y", "v")
    P.dma("y", P.dram["xo"].rearrange("(kt p) w -> p kt w", p=128),
          x[:, :].rearrange("p (kt w) -> p kt w", kt=KD))
    finish(P)
    return nc, P


def finish(P):
    P.barrier()
    for q in ["dy", "dg", "ds"]:
        P.SY.wait_ge(P.sem[q], P.cnt[q])



_BUILD_CACHE = {}


def kernel(**inputs):
    key = ("plain",)
    if key not in _BUILD_CACHE:
        _BUILD_CACHE[key] = build_program()
    nc, P = _BUILD_CACHE[key]
    consts = host_consts()
    x = np.asarray(inputs["L0_attn_x"], np.float32)
    base = {}
    for nm in WEIGHT_SPECS:
        base[nm] = _f(inputs[nm])
    base.update(consts)
    in_maps = []
    for b in range(8):
        m = dict(base)
        m["xt"] = np.ascontiguousarray(x[b].T)
        in_maps.append(m)
    res = run_bass_kernel_spmd(nc, in_maps, core_ids=list(range(8)))
    out = np.stack([np.asarray(r["xo"]).T for r in res.results])
    return out.astype(np.float32)


# version 08
# speedup vs baseline: 1.0865x; 1.0107x over previous
import numpy as np
import ml_dtypes
import concourse.bass as bass
import concourse.mybir as mybir
from concourse.bass_utils import run_bass_kernel_spmd

F32 = mybir.dt.float32
BF16 = mybir.dt.bfloat16
AF = mybir.ActivationFunctionType
ALU = mybir.AluOpType
AXX = mybir.AxisListType.X

D = 1024
T = 2048
KD = 8
DI = 2048
KI = 16
NH = 8
DH = 64
WIN = 512
HID = 2730
DTR = 64
DST = 16
NCH = 4
TC = 512
TH = 1024
EPS = 1e-5
QK_SCALE = 8.0
HT = [(j * 128, min(128, HID - j * 128)) for j in range(22)]
XSW = T + 8


def _f(x):
    return np.ascontiguousarray(np.asarray(x), dtype=np.float32)


def _bf(x):
    return np.ascontiguousarray(np.asarray(x, dtype=np.float32).astype(ml_dtypes.bfloat16))


def host_consts():
    c = {}
    c["cI"] = _bf(np.eye(128))
    R = np.zeros((128, 128), np.float32)
    for blk in range(2):
        for m in range(64):
            if m < 32:
                R[blk * 64 + m + 32, blk * 64 + m] = -1.0
            else:
                R[blk * 64 + m - 32, blk * 64 + m] = 1.0
    c["cR"] = _bf(R)
    half = np.arange(0, DH, 2, dtype=np.float64)
    inv_freq = 1.0 / 10000.0 ** (half / DH)
    t = np.arange(T, dtype=np.float64)
    freqs = np.concatenate([t[:, None] * inv_freq[None, :]] * 2, -1)
    base = (half + 0.4 * DH) / (1.4 * DH)
    power = (t - T // 2) / float(WIN // 2)
    scale = np.concatenate([base[None, :] ** power[:, None]] * 2, -1)
    cos, sin = np.cos(freqs), np.sin(freqs)
    rep = lambda a: np.concatenate([a.T, a.T], 0)
    c["cT1q"] = _bf(rep(cos * scale))
    c["cT2q"] = _bf(rep(sin * scale))
    c["cT1k"] = _bf(rep(cos / scale))
    c["cT2k"] = _bf(rep(sin / scale))
    idx = np.arange(128)
    MgA = np.zeros((128, 640), np.float32)
    for r in range(128):
        kr = np.arange(640)
        MgA[r, :] = np.where((kr >= r) & (kr <= r + 512), 0.0, -1.0e30)
    c["cMgA"] = _bf(MgA)
    c["cM0A"] = _bf(np.where(idx[None, :] <= idx[:, None], 0.0, -1.0e30))
    c["cOnesB"] = _bf(np.ones((128, 1)))
    c["cOnesRowF"] = _f(np.ones((128, 128)))
    c["cOnesRowB"] = _bf(np.ones((1, 128)))
    B2 = np.zeros((128, 2), np.float32)
    B2[:64, 0] = 1.0
    B2[64:, 1] = 1.0
    c["cB2b"] = _bf(B2)
    B2T = np.zeros((2, 128), np.float32)
    B2T[0, :64] = 1.0
    B2T[1, 64:] = 1.0
    c["cB2T"] = _bf(B2T)
    c["cMisc"] = _f(np.tile(np.array([[EPS, 0.0]], np.float32), (128, 1)))
    return c


CONST_SPECS = {
    "cI": ([128, 128], BF16), "cR": ([128, 128], BF16),
    "cMgA": ([128, 640], BF16), "cM0A": ([128, 128], BF16),
    "cOnesB": ([128, 1], BF16), "cOnesRowF": ([128, 128], F32),
    "cOnesRowB": ([1, 128], BF16),
    "cB2b": ([128, 2], BF16), "cB2T": ([2, 128], BF16),
    "cMisc": ([128, 2], F32),
}
ROT_SPECS = {
    "cT1q": ([128, T], BF16), "cT2q": ([128, T], BF16),
    "cT1k": ([128, T], BF16), "cT2k": ([128, T], BF16),
}

WEIGHT_SPECS = {
    "L0_attn_gamma": [D], "L0_attn_beta": [D],
    "L0_attn_in_w": [D, 2 * DI],
    "L0_attn_conv_w": [DI, 4], "L0_attn_conv_b": [DI],
    "L0_attn_xproj_w": [DI, 96], "L0_attn_dt_w": [DTR, DI], "L0_attn_dt_b": [DI],
    "L0_attn_A_log": [DI, DST], "L0_attn_D_skip": [DI],
    "L0_attn_conv_w_b": [DI, 4], "L0_attn_conv_b_b": [DI],
    "L0_attn_xproj_w_b": [DI, 96], "L0_attn_dt_w_b": [DTR, DI], "L0_attn_dt_b_b": [DI],
    "L0_attn_A_log_b": [DI, DST], "L0_attn_D_skip_b": [DI],
    "L0_attn_merge_w": [2 * DI, DI], "L0_attn_out_w": [DI, D],
    "L0_ffn_wqkv": [2, D, 1536], "L0_ffn_q_scale": [2, DH], "L0_ffn_k_scale": [2, DH],
    "L0_ffn_wo": [2, 512, D], "L0_ffn_ln_g": [2, D], "L0_ffn_ln_b": [2, D],
    "L0_ffn_w1": [2, D, 2 * HID], "L0_ffn_b1": [2, 2 * HID],
    "L0_ffn_w2": [2, HID, D], "L0_ffn_b2": [2, D],
}


class EngProxy:

    _PASS = {"wait_ge", "drain", "nop", "dma_start", "sem_inc"}

    def __init__(self, prog, key, raw):
        self.prog = prog
        self.key = key
        self.raw = raw

    def __getattr__(self, name):
        fn = getattr(self.raw, name)
        if name in self._PASS or not callable(fn):
            return fn
        key = self.key
        prog = self.prog

        def wrapped(*args, **kwargs):
            aps = [a for a in args if isinstance(a, bass.AP)]
            aps += [v for v in kwargs.values() if isinstance(v, bass.AP)]
            dirty = prog.dirty[key]
            if aps:
                prog.sync_marks(self.raw, aps[0])
                ins_names = {a.tensor.name for a in aps[1:]}
                if ins_names & dirty:
                    self.raw.drain()
                    dirty.clear()
            instr = fn(*args, **kwargs)
            if aps:
                dirty.add(aps[0].tensor.name)
            return instr

        return wrapped


class Prog:
    def __init__(self, nc, want_dumps=()):
        self.nc = nc
        self._cms = {}
        self.sem = {}
        self.cnt = {}
        for name in ["t", "v", "s", "g", "y", "dy", "dg", "ds"]:
            cm = nc.semaphore("sem_" + name)
            self.sem[name] = cm.__enter__()
            self._cms["sem_" + name] = cm
            self.cnt[name] = 0
        self.PE = nc.tensor
        self.V = EngProxy(self, "v", nc.vector)
        self.S = EngProxy(self, "s", nc.scalar)
        self.G = EngProxy(self, "g", nc.gpsimd)
        self.SY = nc.sync
        self.dirty = {"v": set(), "s": set(), "g": set()}
        self._eng = {"t": self.PE, "v": self.V, "s": self.S, "g": self.G, "y": self.SY}
        self.pending_sync = {}
        self.dumps = []
        self.dump_off = 0
        self.want_dumps = set(want_dumps)
        self.dbg = None
        self.consts = {}
        self.dram = {}
        self.par = {}
        self.uid = 0
        self.stop_flag = False
        self.stop_after = None

    ARENA_BYTES = 206 * 1024

    def _ensure_arena(self):
        if not hasattr(self, "arena_base"):
            cm = self.nc.sbuf_tensor("big_arena", [128, self.ARENA_BYTES], mybir.dt.uint8)
            h = cm.__enter__()
            self._cms["big_arena"] = cm
            self.arena_base = self.nc.lookup_mloc(h).addr
            self.freelist = [(0, self.ARENA_BYTES)]
            self.alloc_info = {}

    def sbuf(self, name, shape, dtype, hi=False):
        self._ensure_arena()
        per = int(np.prod(shape[1:])) * mybir.dt.size(dtype)
        per = (per + 31) // 32 * 32
        ranges = list(enumerate(self.freelist))
        if hi:
            ranges = ranges[::-1]
        for idx, (off, sz) in ranges:
            if sz >= per:
                break
        else:
            amap = sorted((v[0], v[1], k) for k, v in self.alloc_info.items())
            raise MemoryError(f"SBUF arena full for {name} ({per}B): {self.freelist}\n" +
                              "\n".join(f"  {o:7d} +{s:6d} {k}" for o, s, k in amap))
        if sz == per:
            del self.freelist[idx]
        elif hi:
            off = off + sz - per
            self.freelist[idx] = (self.freelist[idx][0], sz - per)
        else:
            self.freelist[idx] = (off + per, sz - per)
        self.uid += 1
        name = f"{name}_{self.uid}"
        h = self.nc.virtual_sbuf_tensor(name, list(shape), dtype)
        self.nc._tensor(h.name, list(shape), dtype, type="SB")
        mloc = self.nc.lookup_mloc(h)
        mloc.allocated = True
        mloc.addr = self.arena_base + off
        self.alloc_info[h.name] = (off, per)
        self.pending_sync[h.name] = dict(self.cnt)
        return h

    def psum(self, name, shape, dtype=F32):
        self.uid += 1
        name = f"{name}_{self.uid}"
        esz = mybir.dt.size(dtype)
        free = int(np.prod(shape[1:]))
        pad = (-(free * esz) % 2048) // esz
        cm = self.nc.psum_tensor(name, [128, free + pad], dtype)
        t = cm.__enter__()
        self._cms[name] = (self.uid, cm)
        return t

    def free(self, *handles):
        psums = []
        for h in handles:
            name = h.name if hasattr(h, "name") else h.tensor.name
            if name in self.alloc_info:
                off, per = self.alloc_info.pop(name)
                self.freelist.append((off, per))
            else:
                psums.append(name)
        self.freelist.sort()
        merged = []
        for off, sz in self.freelist:
            if merged and merged[-1][0] + merged[-1][1] == off:
                merged[-1] = (merged[-1][0], merged[-1][1] + sz)
            else:
                merged.append((off, sz))
        self.freelist = [tuple(x) for x in merged]
        for name in sorted(psums, key=lambda n: self._cms[n][0], reverse=True):
            _, cm = self._cms.pop(name)
            cm.__exit__(None, None, None)

    def tick(self, name, instr, k=1):
        if name in ("v", "s", "g"):
            instr = self._eng[name].raw.drain()
            self.dirty[name].clear()
        elif name == "t":
            instr = self.PE.drain()
        instr.then_inc(self.sem[name], k)
        self.cnt[name] += k

    def wait(self, who, dep, val=None):
        self._eng[who].wait_ge(self.sem[dep], self.cnt[dep] if val is None else val)

    def waitv(self, who, dep, val):
        if val:
            self._eng[who].wait_ge(self.sem[dep], val)

    def sync_marks(self, eng, out_ap):
        name = out_ap.tensor.name
        marks = self.pending_sync.pop(name, None)
        if marks:
            for s, v in marks.items():
                if v > 0:
                    eng.wait_ge(self.sem[s], v)

    def sdma(self, qname, sname, out, in_, slow=False):
        if sname not in self.sem:
            cm = self.nc.semaphore("sem_" + sname)
            self.sem[sname] = cm.__enter__()
            self._cms["sem_" + sname] = cm
            self.cnt[sname] = 0
        if hasattr(out, "tensor") and out.tensor.name in self.pending_sync:
            self.sync_marks(self._eng[qname] if qname == "y" else self._eng[qname].raw, out)
        i = self._eng[qname].dma_start(out, in_, allow_slow_non_contiguous=slow)
        i.then_inc(self.sem[sname], 16)
        self.cnt[sname] += 16
        return i

    def dma(self, qname, out, in_, slow=False):
        if hasattr(out, "tensor") and out.tensor.name in self.pending_sync:
            self.sync_marks(self._eng[qname] if qname == "y" else self._eng[qname].raw, out)
        i = self._eng[qname].dma_start(out, in_, allow_slow_non_contiguous=slow)
        self.tick("d" + qname, i, 16)
        return i

    def barrier(self, engines="tvsgy"):
        marks = {}
        for e in engines:
            i = self._eng[e].nop()
            self.tick(e, i, 1)
            marks[e] = self.cnt[e]
        for e in engines:
            for o in engines:
                if o != e:
                    self._eng[e].wait_ge(self.sem[o], marks[o])
            for q in ["dy", "dg", "ds"]:
                self._eng[e].wait_ge(self.sem[q], self.cnt[q])

    def dump(self, tag, ap, producer="v", parts=128):
        if tag not in self.want_dumps:
            return
        w = int(np.prod(ap.shape)) // int(ap.shape[0])
        q = "y" if "float32" in str(ap.dtype) else "g"
        for e in "tvsg":
            self.tick(e, self._eng[e].nop())
            self.wait(q, e)
        self.dma(q, self.dbg[0:parts, self.dump_off:self.dump_off + w], ap)
        self.pending_sync[ap.tensor.name] = dict(self.cnt)
        self.dumps.append((tag, self.dump_off, parts, w, str(ap.dtype)))
        self.dump_off += w

    def stop(self, stage):
        if self.stop_after is not None and stage == self.stop_after:
            self.stop_flag = True
        return self.stop_flag



def emit_ln(P, src_tiles, out_writer, gamma_ap=None, beta_ap=None, fuse_double=False,
            keep_ab=False, direct_dst=None):
    need_cast = "float32" in str(src_tiles[0].dtype)
    sq = P.sbuf("ln_sq", [128, 2 * TC], BF16)
    xb = P.sbuf("ln_xb", [128, 2 * TC], BF16) if need_cast else None
    rows = P.sbuf("ln_rows", [1, 3 * T], F32)
    ab = P.sbuf("ln_ab", [128, 2 * T], F32)
    ab2 = P.sbuf("ln_ab2", [128, 2 * T], F32) if (fuse_double and keep_ab) else None
    ps_s = P.psum("ln_ps_s", [1, TC], F32)
    ps_q = P.psum("ln_ps_q", [1, TC], F32)
    ps_b = P.psum("ln_ps_b", [128, TC], F32)
    ones = P.consts["cOnesB"]

    for c in range(NCH):
        csl = slice(c * TC, (c + 1) * TC)
        for kt in range(KD):
            par = kt % 2
            psl = slice(par * TC, (par + 1) * TC)
            src = src_tiles[kt][:, csl]
            P.wait("s", "t")
            if need_cast:
                i = P.S.activation(xb[:, psl], src, AF.Identity)
                P.tick("s", i)
                xsrc = xb[:, psl]
            else:
                xsrc = src
            i = P.S.activation(sq[:, psl], src, AF.Square)
            P.tick("s", i)
            P.wait("t", "s")
            i = P.PE.matmul(ps_s[0:1, :], ones[:, 0:1], xsrc, start=(kt == 0), stop=(kt == KD - 1))
            i = P.PE.matmul(ps_q[0:1, :], ones[:, 0:1], sq[:, psl], start=(kt == 0), stop=(kt == KD - 1))
            P.tick("t", i)
        P.wait("v", "t")
        i = P.V.tensor_copy(rows[0:1, csl], ps_s[0:1, :])
        i = P.V.tensor_copy(rows[0:1, T + c * TC:T + (c + 1) * TC], ps_q[0:1, :])
        P.tick("v", i)
        P.wait("t", "v")

    muA = rows[0:1, 0:T]
    rB = rows[0:1, T:2 * T]
    bZ = rows[0:1, 2 * T:3 * T]
    vB = rB
    i = P.V.tensor_scalar_mul(muA, rows[0:1, 0:T], 1.0 / D)
    P.tick("v", i)
    P.wait("s", "v")
    i = P.S.activation(bZ, muA, AF.Square)
    P.tick("s", i)
    P.wait("v", "s")
    i = P.V.scalar_tensor_tensor(vB, vB, 1.0 / D, bZ, ALU.mult, ALU.subtract)
    P.tick("v", i)

    brow = P.sbuf("ln_brow", [1, T], F32)

    def bcast_pair(dst_ab):
        P.wait("v", "s")
        i = P.V.tensor_mul(brow[0:1, :], muA, rB_cur[0])
        P.tick("v", i)
        P.wait("y", "v")
        P.dma("y", P.dram["scr_rows"][0:1, :], rB_cur[0])
        P.dma("y", P.dram["scr_rows"][1:2, :], brow[0:1, :])
        P.wait("g", "dy")
        P.dma("g", dst_ab[:, 0:T], P.dram["scr_rows"][0:1, :].broadcast_to([128, T]))
        P.dma("g", dst_ab[:, T:2 * T], P.dram["scr_rows"][1:2, :].broadcast_to([128, T]))
        P.wait("v", "dg")
        P.wait("s", "dg")

    P.wait("s", "v")
    i = P.S.activation(bZ, vB, AF.Ln, bias=EPS)
    P.S.drain()
    i = P.S.activation(bZ, bZ, AF.Exp, scale=-0.5)
    P.tick("s", i)
    rB_cur = [bZ]
    if fuse_double:
        if ab2 is not None:
            bcast_pair(ab2)
        w1 = P.sbuf("ln_w1", [1, 2 * T], F32)
        wa = w1[0:1, 0:T]
        wb = w1[0:1, T:2 * T]
        i = P.S.activation(wa, vB, AF.Ln)
        i = P.S.activation(wb, vB, AF.Ln, bias=EPS)
        P.tick("s", i)
        P.wait("v", "s")
        i = P.V.tensor_sub(wa, wa, wb)
        P.tick("v", i)
        P.wait("s", "v")
        i = P.S.activation(wa, wa, AF.Exp)
        P.tick("s", i)
        P.wait("v", "s")
        i = P.V.tensor_scalar_add(wa, wa, EPS)
        P.tick("v", i)
        P.wait("s", "v")
        i = P.S.activation(wa, wa, AF.Ln)
        P.S.drain()
        i = P.S.activation(wa, wa, AF.Exp, scale=-0.5)
        P.tick("s", i)
        P.wait("v", "s")
        i = P.V.tensor_mul(bZ, bZ, wa)
        P.tick("v", i)
        P.free(w1)
    bcast_pair(ab)

    P.dump("lnab", ab[:, :], "v")
    a_b = ab[:, 0:T]
    b_b = ab[:, T:2 * T]
    tmp = P.sbuf("ln_tmp", [128, T], F32) if direct_dst is None else None
    for kt in range(KD):
        P.wait("v", "s")
        dst = tmp[:, :] if direct_dst is None else direct_dst(kt)
        i = P.V.tensor_mul(dst, src_tiles[kt], a_b)
        i = P.V.tensor_sub(dst, dst, b_b)
        P.tick("v", i)
        if out_writer is not None:
            out_writer(kt, dst,
                       None if gamma_ap is None else gamma_ap[:, kt:kt + 1],
                       None if beta_ap is None else beta_ap[:, kt:kt + 1])
    P.free(sq, rows, brow, ps_s, ps_q, ps_b)
    if tmp is not None:
        P.free(tmp)
    if xb is not None:
        P.free(xb)
    if keep_ab:
        return ab, ab2
    P.free(ab)
    return None


def emit_shift(P, xp, dst):
    for kt in range(KD):
        if kt < 4:
            i = P.S.activation(dst[:, kt * T:(kt + 1) * T], xp[:, kt * T:(kt + 1) * T], AF.Identity)
        else:
            i = P.S.activation(dst[:, kt * T + 1:(kt + 1) * T], xp[:, kt * T:kt * T + T - 1],
                               AF.Identity)
        P.tick("s", i)
    P.wait("v", "s")
    for kt in range(4, KD):
        i = P.V.memset(dst[:, kt * T:kt * T + 1], 0.0)
    P.tick("v", i)



def emit_ffn_layer(P, l, xp, wsb, dump_pref=""):
    xpt = lambda kt: xp[:, kt * T:(kt + 1) * T]

    sh = P.sbuf("ffn_sh", [128, KD * T], BF16, hi=True)
    emit_shift(P, xp, sh)
    P.dump(dump_pref + "shift1", sh[:, :], "v")
    wqkv = P.sbuf("ffc_wqkv", [128, KD * 1536], BF16)
    P.wait("g", "t")
    P.sdma("g", "wqS", wqkv[:, :].rearrange("p (kt m) -> p kt m", kt=KD),
           P.dram["L0_ffn_wqkv"][l].rearrange("(kt p) m -> p kt m", p=128))

    q4 = P.sbuf("ffn_q4", [128, 4 * T], BF16)
    k4 = P.sbuf("ffn_k4", [128, 4 * T], BF16)
    sqb = P.sbuf("ffn_sqb", [128, TC], BF16)
    tt1 = P.sbuf("ffn_tt1", [128, TC], F32)
    s2 = P.sbuf("ffn_s2", [2, T], F32)
    rbf2 = P.sbuf("ffn_rbf2", [2, T], BF16)
    rb = P.sbuf("ffn_rb", [128, T], BF16)
    qn = P.sbuf("ffn_qn", [128, T], BF16)
    t2t = P.sbuf("ffn_t2t", [128, TC], F32)
    ps_m = P.psum("ffn_psm", [128, TC], F32)
    ps_m2 = P.psum("ffn_psm2", [128, TC], F32)
    ps_n = P.psum("ffn_psn", [2, TC], F32)
    ps_r = P.psum("ffn_psr", [128, TC], F32)
    P.wait("t", "wqS")
    rtab = P.sbuf("ffn_rtab", [128, 2 * T], BF16)

    snap = {}
    for side, (out4, scl, tn1, tn2) in enumerate(
            [(q4, wsb["qs"], "cT1q", "cT2q"),
             (k4, wsb["ks"], "cT1k", "cT2k")]):
        t1c = rtab[:, 0:T]
        t2c = rtab[:, T:2 * T]
        P.wait("y", "v")
        P.sdma("y", "rtS", t1c, P.dram[tn1])
        P.sdma("y", "rtS", t2c, P.dram[tn2])
        P.wait("v", "rtS")
        for mt in range(4):
            col0 = side * 512 + mt * 128
            for c in range(NCH):
                csl = slice(c * TC, (c + 1) * TC)
                psx = ps_m if c % 2 == 0 else ps_m2
                P.waitv("t", "v", snap.get(("qv", c % 2)))
                P.waitv("t", "s", snap.get(("qs", c % 2)))
                for kt in range(KD):
                    i = P.PE.matmul(psx[:, :], wqkv[:, kt * 1536 + col0:kt * 1536 + col0 + 128],
                                    sh[:, kt * T + c * TC:kt * T + (c + 1) * TC],
                                    start=(kt == 0), stop=(kt == KD - 1))
                P.tick("t", i)
                P.wait("s", "t")
                i = P.S.activation(sqb[:, :], psx[:, :], AF.Square)
                P.tick("s", i)
                snap[("qs", c % 2)] = P.cnt["s"]
                P.wait("t", "s")
                P.waitv("t", "v", snap.get("psn_v"))
                i = P.PE.matmul(ps_n[0:2, :], P.consts["cB2b"][:, :], sqb[:, :], start=True, stop=True)
                P.tick("t", i)
                P.wait("v", "t")
                i = P.V.tensor_copy(s2[:, csl], ps_n[0:2, :])
                i = P.V.tensor_scalar_mul(qn[:, csl], psx[:, :], scl)
                P.tick("v", i)
                snap[("qv", c % 2)] = P.cnt["v"]
                snap["psn_v"] = P.cnt["v"]
            i = P.V.tensor_scalar_max(s2[:, :], s2[:, :], 1e-24)
            P.tick("v", i)
            P.wait("s", "v")
            i = P.S.activation(s2[:, :], s2[:, :], AF.Ln)
            P.S.drain()
            i = P.S.activation(rbf2[:, :], s2[:, :], AF.Exp, scale=-0.5)
            P.tick("s", i)
            for c in range(NCH):
                csl = slice(c * TC, (c + 1) * TC)
                P.wait("t", "s")
                P.wait("t", "v")
                i = P.PE.matmul(ps_r[:, :], P.consts["cB2T"][:, :], rbf2[:, csl],
                                start=True, stop=True)
                P.tick("t", i)
                P.wait("v", "t")
                i = P.V.tensor_copy(rb[:, csl], ps_r[:, :])
                P.tick("v", i)
                P.wait("t", "v")
            for c in range(NCH):
                csl = slice(c * TC, (c + 1) * TC)
                P.wait("t", "v")
                i = P.PE.matmul(ps_r[:, :], P.consts["cR"][:, :], qn[:, csl], start=True, stop=True)
                P.tick("t", i)
                P.wait("v", "t")
                i = P.V.tensor_tensor(t2t[:, :], ps_r[:, :], t2c[:, csl], ALU.mult)
                P.tick("v", i)
                P.wait("t", "v")
                i = P.V.tensor_tensor(tt1[:, :], qn[:, csl], t1c[:, csl], ALU.mult)
                i = P.V.tensor_add(tt1[:, :], tt1[:, :], t2t[:, :])
                i = P.V.tensor_tensor(out4[:, mt * T + c * TC:mt * T + (c + 1) * TC],
                                      tt1[:, :], rb[:, csl], ALU.mult)
                P.tick("v", i)
    P.dump(dump_pref + "q4", q4[:, :], "v")
    P.dump(dump_pref + "k4", k4[:, :], "v")
    if P.stop(dump_pref + "qk"):
        return xp

    P.free(sqb, tt1, s2, rbf2, rb, qn, t2t, rtab, ps_n, ps_r)
    vtm = P.sbuf("ffn_vtm", [128, 16 * 512], BF16)
    if P.stop(dump_pref + "v0"):
        return xp
    snap = {}
    for tt in range(16):
        psx = ps_m if tt % 2 == 0 else ps_m2
        P.waitv("t", "v", snap.get(tt % 2))
        for kt in range(KD):
            i = P.PE.matmul(psx[:, :], sh[:, kt * T + tt * 128:kt * T + (tt + 1) * 128],
                            wqkv[:, kt * 1536 + 1024:kt * 1536 + 1536],
                            start=(kt == 0), stop=(kt == KD - 1))
        P.tick("t", i)
        P.wait("v", "t")
        i = P.V.tensor_copy(vtm[:, tt * 512:tt * 512 + 512], psx[:, :])
        P.tick("v", i)
        snap[tt % 2] = P.cnt["v"]
    P.free(sh, wqkv, ps_m, ps_m2)
    P.dump(dump_pref + "vtm", vtm[:, :], "s")
    if P.stop(dump_pref + "v"):
        return xp

    nmax = P.sbuf("ffn_nmax", [128, 2], F32)
    smf = P.sbuf("ffn_smf", [128, 2 * 640], F32)
    Pb = P.sbuf("ffn_P", [128, 2 * 640], BF16)
    PT = P.sbuf("ffn_PT", [128, 2 * 8 * 512 + 512], BF16)
    rsr = P.sbuf("ffn_rsr", [128, TC], F32)
    rbq = P.sbuf("ffn_rbq", [128, TC], F32)
    osb = P.sbuf("ffn_o", [128, 4 * T], BF16)
    ps_sim = [P.psum("ffn_pssimA", [128, 640], F32), P.psum("ffn_pssimB", [128, 640], F32)]
    ps_pt = [P.psum("ffn_psptA", [128, 640], BF16), P.psum("ffn_psptB", [128, 640], BF16)]
    ps_o = P.psum("ffn_pso", [128, TC], F32)
    ps_rr = P.psum("ffn_psrr", [128, TC], F32)

    wins = [(h, qw) for h in range(NH) for qw in range(4)]
    steps = [(w, qt) for w in range(len(wins)) for qt in range(4)]
    sn = {}
    pend = [None]
    P.wait("t", "v")

    def meta(s):
        w, qt = steps[s]
        h, qw = wins[w]
        rt, po = h // 2, (h % 2) * 64
        if qw == 0:
            nblk, ktok0 = qt + 1, 0
        else:
            nblk, ktok0 = 5, (qw - 1) * 4 + qt
        return w, qt, h, qw, rt, po, nblk, ktok0, nblk * 128

    def stA(s):
        w, qt, h, qw, rt, po, nblk, ktok0, width = meta(s)
        par = s % 2
        i_abs = qw * 4 + qt
        P.waitv("t", "v", sn.get(("simfree", par)))
        lhs = q4[po:po + 64, rt * T + i_abs * 128:rt * T + (i_abs + 1) * 128]
        off = 0
        while off < width:
            n = min(TC, width - off)
            i = P.PE.matmul(ps_sim[par][:, off:off + n], lhs,
                            k4[po:po + 64,
                               rt * T + ktok0 * 128 + off:rt * T + ktok0 * 128 + off + n],
                            start=True, stop=True)
            off += n
        P.tick("t", i)
        sn[("simdone", par)] = P.cnt["t"]

    def stB(s):
        w, qt, h, qw, rt, po, nblk, ktok0, width = meta(s)
        par = s % 2
        sm = smf[:, par * 640:par * 640 + width]
        pssim = ps_sim[par]
        P.waitv("v", "t", sn[("simdone", par)])
        P.waitv("v", "s", sn.get(("sexp", par)))
        if qw == 0:
            if nblk > 1:
                i = P.V.tensor_scalar_mul(sm[:, 0:width - 128], pssim[:, 0:width - 128], QK_SCALE)
            i = P.V.scalar_tensor_tensor(sm[:, width - 128:width], pssim[:, width - 128:width],
                                         QK_SCALE, P.consts["cM0A"][:, :], ALU.mult, ALU.add)
        else:
            i = P.V.scalar_tensor_tensor(sm, pssim[:, 0:width], QK_SCALE,
                                         P.consts["cMgA"][:, 0:width], ALU.mult, ALU.add)
        i = P.V.reduce_max(nmax[:, par:par + 1], sm, axis=AXX, negate=True)
        P.tick("v", i)
        sn[("simfree", par)] = P.cnt["v"]
        sn[("smf", par)] = P.cnt["v"]

    def stX(s):
        w, qt, h, qw, rt, po, nblk, ktok0, width = meta(s)
        par = s % 2
        P.waitv("s", "v", sn[("smf", par)])
        P.waitv("s", "t", sn.get(("pbfree", par)))
        i = P.S.activation(Pb[:, par * 640:par * 640 + width],
                           smf[:, par * 640:par * 640 + width], AF.Exp,
                           bias=nmax[:, par:par + 1], scale=1.0)
        P.tick("s", i)
        sn[("sexp", par)] = P.cnt["s"]

    def stC(s):
        w, qt, h, qw, rt, po, nblk, ktok0, width = meta(s)
        par = s % 2
        P.waitv("t", "s", sn[("sexp", par)])
        P.waitv("t", "v", sn.get(("ptpsfree", par)))
        for j in range(nblk):
            i = P.PE.transpose(ps_pt[par][:, j * 128:(j + 1) * 128],
                               Pb[:, par * 640 + j * 128:par * 640 + (j + 1) * 128],
                               P.consts["cI"][:, :])
        P.tick("t", i)
        sn[("tdone", par)] = P.cnt["t"]
        sn[("pbfree", par)] = P.cnt["t"]

    def stE(s):
        w, qt, h, qw, rt, po, nblk, ktok0, width = meta(s)
        b = (po + 64) % 128
        PTb = PT[:, (w % 2) * 4096:(w % 2 + 1) * 4096]
        P.waitv("t", "v", P.cnt["v"])
        P.waitv("t", "v", sn.get("osbdone"))
        slots = list(range(4, 8)) if qw == 0 else list(range(0, 8))
        first = True
        for s_ in slots:
            vt = (s_ - 4) if qw == 0 else (qw - 1) * 4 + s_
            lhsv = vtm[:, vt * 512 + h * 64:vt * 512 + h * 64 + 64]
            i = P.PE.matmul(ps_o[po:po + 64, :], lhsv, PTb[:, s_ * 512:(s_ + 1) * 512],
                            start=first, stop=(s_ == slots[-1]), tile_position=(0, po))
            i = P.PE.matmul(ps_rr[b:b + 1, :], P.consts["cOnesB"][:, 0:1],
                            PTb[:, s_ * 512:(s_ + 1) * 512],
                            start=first, stop=(s_ == slots[-1]), tile_position=(0, b))
            first = False
        P.tick("t", i)
        sn[("avdone", w % 2)] = P.cnt["t"]
        P.waitv("s", "t", sn[("avdone", w % 2)])
        P.waitv("s", "t", sn.get(("rbdone", b)))
        i = P.S.activation(rsr[b:b + 1, :], ps_rr[b:b + 1, :], AF.Ln)
        P.S.drain()
        i = P.S.activation(rsr[b:b + 1, :], rsr[b:b + 1, :], AF.Exp, scale=-1.0)
        P.tick("s", i)
        pend[0] = (w, rt, po, qw, b, P.cnt["s"])

    def stD(s):
        w, qt, h, qw, rt, po, nblk, ktok0, width = meta(s)
        par = s % 2
        PTb = PT[:, (w % 2) * 4096:(w % 2 + 1) * 4096]
        P.waitv("v", "t", sn[("tdone", par)])
        P.waitv("v", "t", sn.get(("avdone", w % 2)))
        qcol = qt * 128
        base = (w % 2) * 4096
        slot0 = qt if qw > 0 else 4
        d0 = base + slot0 * 512 + qcol
        i = P.V.tensor_copy(
            PT[:, d0:d0 + nblk * 512].rearrange("p (j t) -> p j t", j=nblk)[:, :, 0:128],
            ps_pt[par][:, 0:nblk * 128].rearrange("p (j t) -> p j t", j=nblk))
        lo = 0 if qw > 0 else 4
        for a, b2 in [(lo, slot0), (slot0 + nblk, 8)]:
            if b2 > a:
                m0 = base + a * 512 + qcol
                i = P.V.memset(
                    PT[:, m0:m0 + (b2 - a) * 512].rearrange("p (j t) -> p j t",
                                                            j=b2 - a)[:, :, 0:128], 0.0)
        P.tick("v", i)
        sn[("ptpsfree", par)] = P.cnt["v"]
        if qt == 3:
            stE(s)

    def emit_rb_osb():
        w, rt, po, qw, b, rsrs = pend[0]
        pend[0] = None
        P.waitv("t", "s", rsrs)
        i = P.PE.matmul(ps_rr[po:po + 64, :], P.consts["cOnesRowF"][b:b + 1, 0:64],
                        rsr[b:b + 1, :], start=True, stop=True, tile_position=(b, po))
        P.tick("t", i)
        sn[("rbdone", b)] = P.cnt["t"]
        P.waitv("v", "t", sn[("rbdone", b)])
        i = P.V.tensor_copy(rbq[po:po + 64, :], ps_rr[po:po + 64, :])
        i = P.V.tensor_tensor(osb[po:po + 64, rt * T + qw * TC:rt * T + (qw + 1) * TC],
                              ps_o[po:po + 64, :], rbq[po:po + 64, :], ALU.mult)
        P.tick("v", i)
        sn["osbdone"] = P.cnt["v"]

    NS = len(steps)
    for it in range(NS + 5):
        if it < NS:
            stA(it)
        if pend[0] is not None:
            emit_rb_osb()
        if 1 <= it and it - 1 < NS:
            stB(it - 1)
        if 2 <= it and it - 2 < NS:
            stX(it - 2)
        if 3 <= it and it - 3 < NS:
            stC(it - 3)
        if 4 <= it and it - 4 < NS:
            stD(it - 4)
    if pend[0] is not None:
        emit_rb_osb()
    P.wait("t", "v")
    P.free(q4, k4, vtm, nmax, smf, Pb, PT, rsr, rbq)
    P.free(ps_sim[0], ps_sim[1], ps_pt[0], ps_pt[1], ps_o, ps_rr)
    P.dump(dump_pref + "o", osb[:, :], "v")
    if P.stop(dump_pref + "attn"):
        return xp

    wo = P.sbuf("ffc_wo", [128, 4 * D], BF16)
    P.sdma("g", "woS", wo[:, :].rearrange("p (ik m) -> p ik m", ik=4),
           P.dram["L0_ffn_wo"][l].rearrange("(ik p) m -> p ik m", p=128))
    ps_a = P.psum("ffn_psa", [128, TC], F32)
    ps_b2 = P.psum("ffn_psb", [128, TC], F32)
    P.wait("t", "woS")
    snap = {}
    for kt in range(KD):
        for c in range(NCH):
            par = (kt * NCH + c) % 2
            pss = ps_a if par == 0 else ps_b2
            P.waitv("t", "v", snap.get(par))
            for ik in range(4):
                i = P.PE.matmul(pss[:, :], wo[:, ik * D + kt * 128:ik * D + (kt + 1) * 128],
                                osb[:, ik * T + c * TC:ik * T + (c + 1) * TC],
                                start=(ik == 0), stop=(ik == 3))
            P.tick("t", i)
            P.wait("v", "t")
            i = P.V.tensor_add(xpt(kt)[:, c * TC:(c + 1) * TC],
                               xpt(kt)[:, c * TC:(c + 1) * TC], pss[:, :])
            P.tick("v", i)
            snap[par] = P.cnt["v"]
    P.free(osb, wo, ps_a, ps_b2)
    P.dump(dump_pref + "x_attn", xp[:, :], "v")
    if P.stop(dump_pref + "wo"):
        return xp

    lnx = P.sbuf("ffn_lnx", [128, KD * T], BF16, hi=True)
    sh2 = P.sbuf("ffn_sh2", [128, KD * T], BF16, hi=True)
    emit_shift(P, xp, sh2)

    def wr(kt, tmp, g, b):
        P.wait("s", "v")
        i = P.S.activation(lnx[:, kt * T:(kt + 1) * T], tmp[:, :], AF.Identity, bias=b, scale=g)
        P.tick("s", i)

    emit_ln(P, [sh2[:, kt * T:(kt + 1) * T] for kt in range(KD)], wr,
            gamma_ap=wsb["lng"], beta_ap=wsb["lnb"])
    P.free(sh2)
    P.dump(dump_pref + "lnx2", lnx[:, :], "s")
    P.tick("v", P.V.nop())
    P.wait("y", "v")
    P.dma("y", P.dram["scr_xp"].rearrange("(kt p) w -> p kt w", p=128),
          xp[:, :].rearrange("p (kt w) -> p kt w", kt=KD))
    P.wait("y", "dy")
    P.free(xp)
    if P.stop(dump_pref + "ln2i"):
        return None

    gated = P.sbuf("ffn_gated", [128, 22 * T], BF16)
    w1a = P.sbuf("ffn_w1a", [128, 2 * KD * 128], BF16)
    w1g = P.sbuf("ffn_w1g", [128, 2 * KD * 128], BF16)
    ps_w = [P.psum("ffn_psw%d" % k, [128, TC], F32) for k in range(4)]
    w1d = P.dram["L0_ffn_w1"]
    snap = {}
    for j, (off, sz) in enumerate(HT):
        par = j % 2
        base = par * KD * 128
        sname = "w1p%d" % par
        P.waitv("g", "t", snap.get(("t", par)))
        P.sdma("g", sname, w1a[:, base:base + KD * sz].rearrange("p (kt m) -> p kt m", kt=KD),
               w1d[l, :, off:off + sz].rearrange("(kt p) m -> p kt m", p=128))
        P.sdma("g", sname, w1g[:, base:base + KD * sz].rearrange("p (kt m) -> p kt m", kt=KD),
               w1d[l, :, HID + off:HID + off + sz].rearrange("(kt p) m -> p kt m", p=128))
        dgs = P.cnt[sname]
        for c in range(NCH):
            cp = c % 2
            ps_wa, ps_wg = ps_w[cp * 2], ps_w[cp * 2 + 1]
            P.waitv("t", sname, dgs)
            P.waitv("t", "v", snap.get(("v", cp)))
            for kt in range(KD):
                i = P.PE.matmul(ps_wa[0:sz, :], w1a[:, base + kt * sz:base + (kt + 1) * sz],
                                lnx[:, kt * T + c * TC:kt * T + (c + 1) * TC],
                                start=(kt == 0), stop=(kt == KD - 1))
            for kt in range(KD):
                i = P.PE.matmul(ps_wg[0:sz, :], w1g[:, base + kt * sz:base + (kt + 1) * sz],
                                lnx[:, kt * T + c * TC:kt * T + (c + 1) * TC],
                                start=(kt == 0), stop=(kt == KD - 1))
            P.tick("t", i)
            gsl = gated[0:sz, j * T + c * TC:j * T + (c + 1) * TC]
            P.wait("s", "t")
            i = P.S.activation(gsl, ps_wg[0:sz, :], AF.Gelu, bias=wsb["b1g"][0:sz, j:j + 1])
            P.tick("s", i)
            P.wait("v", "s")
            i = P.V.scalar_tensor_tensor(gsl, ps_wa[0:sz, :], wsb["b1a"][0:sz, j:j + 1], gsl,
                                         ALU.add, ALU.mult)
            P.tick("v", i)
            snap[("v", cp)] = P.cnt["v"]
        snap[("t", par)] = P.cnt["t"]
    P.free(w1a, w1g, *ps_w)
    P.free(lnx)
    P.dump(dump_pref + "gated", gated[:, :], "v")
    if P.stop(dump_pref + "gated"):
        return None

    xp = P.sbuf("xp_arena", [128, KD * T], F32, hi=True)
    P.dma("y", xp[:, :].rearrange("p (kt w) -> p kt w", kt=KD),
          P.dram["scr_xp"].rearrange("(kt p) w -> p kt w", p=128))
    w2s = P.sbuf("ffn_w2s", [128, 2 * 22 * 128], BF16)
    ps_a2 = P.psum("ffn_ps2a", [128, TC], F32)
    ps_b3 = P.psum("ffn_ps2b", [128, TC], F32)
    w2d = P.dram["L0_ffn_w2"]
    P.wait("v", "dy")
    snap = {}
    for m in range(KD):
        par = m % 2
        base = par * 22 * 128
        P.waitv("g", "t", snap.get(("t", par)))
        sname = "w2p%d" % par
        for j, (off, sz) in enumerate(HT):
            P.sdma("g", sname, w2s[0:sz, base + j * 128:base + (j + 1) * 128],
                   w2d[l, off:off + sz, m * 128:(m + 1) * 128])
        snap[("dg", par)] = (sname, P.cnt[sname])
        for c in range(NCH):
            cp = c % 2
            pss = ps_a2 if cp == 0 else ps_b3
            P.waitv("t", *snap[("dg", par)])
            P.waitv("t", "v", snap.get(("v", cp)))
            for j, (off, sz) in enumerate(HT):
                i = P.PE.matmul(pss[:, :], w2s[0:sz, base + j * 128:base + j * 128 + 128],
                                gated[0:sz, j * T + c * TC:j * T + (c + 1) * TC],
                                start=(j == 0), stop=(j == 21))
            P.tick("t", i)
            P.wait("v", "t")
            i = P.V.scalar_tensor_tensor(xpt(m)[:, c * TC:(c + 1) * TC], pss[:, :],
                                         wsb["b2"][:, m:m + 1], xpt(m)[:, c * TC:(c + 1) * TC],
                                         ALU.add, ALU.add)
            P.tick("v", i)
            snap[("v", cp)] = P.cnt["v"]
        snap[("t", par)] = P.cnt["t"]
    P.free(w2s, ps_a2, ps_b3, gated)
    P.dump(dump_pref + f"x_l{l}", xp[:, :], "v")
    return xp


def emit_ffn_call(P, xp, dump_pref=""):
    for l in range(2):
        wsb = {
            "qs": P.par["qs"][:, l:l + 1], "ks": P.par["ks"][:, l:l + 1],
            "lng": P.par["lng"][:, l * KD:(l + 1) * KD], "lnb": P.par["lnb"][:, l * KD:(l + 1) * KD],
            "b1a": P.par["b1a"][:, l * 22:(l + 1) * 22], "b1g": P.par["b1g"][:, l * 22:(l + 1) * 22],
            "b2": P.par["b2"][:, l * KD:(l + 1) * KD],
        }
        P.barrier()
        xp = emit_ffn_layer(P, l, xp, wsb, dump_pref=dump_pref + f"l{l}_")
        if P.stop_flag:
            break
        P.barrier()
        if P.stop(dump_pref + f"l{l}"):
            break
    return xp



def emit_mamba(P, lnx, dump_pref=""):
    xsp = P.sbuf("mb_xsp", [128, 2 * TC], BF16)
    ztile = P.sbuf("mb_zt", [128, KI * 4], BF16)
    i = P.V.memset(ztile[:, :], 0.0)
    P.tick("v", i)
    P.wait("y", "v")
    P.sdma("y", "xpadS", P.dram["scr_xs"][:, 0:4].rearrange("(kt p) w -> p kt w", p=128),
           ztile[:, :].rearrange("p (kt w) -> p kt w", kt=KI))
    P.sdma("y", "xpadS", P.dram["scr_xs"][:, XSW - 4:XSW].rearrange("(kt p) w -> p kt w", p=128),
           ztile[:, :].rearrange("p (kt w) -> p kt w", kt=KI))
    inw = P.sbuf("mb_inw", [128, 2 * KD * 128], BF16)
    zbuf = P.sbuf("mb_zb", [128, 2 * TC], BF16)
    ps_i = P.psum("mb_psi", [128, TC], F32)
    ps_i2 = P.psum("mb_psi2", [128, TC], F32)
    inwd = P.dram["L0_attn_in_w"]
    snap = {}
    for m in range(2 * KI):
        par = m % 2
        base = par * KD * 128
        P.waitv("g", "t", snap.get(("t", par)))
        sname = "inwp%d" % par
        P.sdma("g", sname, inw[:, base:base + KD * 128].rearrange("p (kt mm) -> p kt mm", kt=KD),
               inwd[:, m * 128:(m + 1) * 128].rearrange("(kt p) mm -> p kt mm", p=128))
        snap[("dg", par)] = (sname, P.cnt[sname])
        for c in range(NCH):
            cp = c % 2
            pss = ps_i if cp == 0 else ps_i2
            P.waitv("t", *snap[("dg", par)])
            P.waitv("t", "s", snap.get(("s", cp)))
            for kt in range(KD):
                i = P.PE.matmul(pss[:, :], inw[:, base + kt * 128:base + (kt + 1) * 128],
                                lnx[:, kt * T + c * TC:kt * T + (c + 1) * TC],
                                start=(kt == 0), stop=(kt == KD - 1))
            P.tick("t", i)
            P.wait("s", "t")
            if m < KI:
                xp2 = c % 2
                xn = "xswp%d" % xp2
                P.waitv("s", xn, snap.get(("xdy", xp2)))
                i = P.S.activation(xsp[:, xp2 * TC:(xp2 + 1) * TC], pss[:, :], AF.Identity)
                P.tick("s", i)
                P.wait("y", "s")
                P.sdma("y", xn, P.dram["scr_xs"][m * 128:(m + 1) * 128, 4 + c * TC:4 + (c + 1) * TC],
                       xsp[:, xp2 * TC:(xp2 + 1) * TC])
                snap[("xdy", xp2)] = P.cnt[xn]
            else:
                zp = c % 2
                zn = "zspp%d" % zp
                P.waitv("s", zn, snap.get(("zdy", zp)))
                i = P.S.activation(zbuf[:, zp * TC:(zp + 1) * TC], pss[:, :], AF.Silu)
                P.tick("s", i)
                P.wait("y", "s")
                P.sdma("y", zn, P.dram["scr_z"][(m - KI) * 128:(m - KI + 1) * 128, c * TC:(c + 1) * TC],
                       zbuf[:, zp * TC:(zp + 1) * TC])
                snap[("zdy", zp)] = P.cnt[zn]
            snap[("s", cp)] = P.cnt["s"]
        snap[("t", par)] = P.cnt["t"]
    P.free(inw, zbuf, ps_i, ps_i2, lnx, xsp, ztile)
    xsw_fin = (P.cnt.get("xswp0", 0), P.cnt.get("xswp1", 0), P.cnt.get("xpadS", 0))
    P.barrier()
    if P.stop("mamba_xs"):
        return

    for dirn in range(2):
        sfx = "" if dirn == 0 else "_b"
        wn = sfx
        pA = P.par["A" + sfx]
        pcw = P.par["cw" + sfx]
        pcb = P.par["cb" + sfx]
        pdtb = P.par["dtb" + sfx]
        pD = P.par["Dsk" + sfx]
        xst = P.sbuf("mb_xst", [128, 2 * XSW], BF16)
        xld = {}
        vconv = {}
        P.waitv("y", "xswp0", xsw_fin[0])
        P.waitv("y", "xswp1", xsw_fin[1])
        P.waitv("y", "xpadS", xsw_fin[2])

        def load_xs(kt):
            par = kt % 2
            xn = "xsL%d" % par
            P.waitv("y", "v", vconv.get(par))
            P.sdma("y", xn, xst[:, par * XSW:(par + 1) * XSW],
                   P.dram["scr_xs"][kt * 128:(kt + 1) * 128, :])
            xld[par] = P.cnt[xn]

        xprojT = P.sbuf("mb_xproj", [128, KI * 96], BF16)
        dtwT = P.sbuf("mb_dtw", [64, KI * 128], BF16)
        P.dma("g", xprojT[:, :].rearrange("p (kt m) -> p kt m", kt=KI),
              P.dram["L0_attn_xproj_w" + wn].rearrange("(kt p) m -> p kt m", p=128))
        P.dma("g", dtwT[:, :], P.dram["L0_attn_dt_w" + wn])
        xproj = xprojT[:, :]
        dtw = dtwT[:, :]
        ydram = P.dram["scr_yf" if dirn == 0 else "scr_yb"]

        def conv_tile(kt, dst_f32, t0, tw):
            tb = (kt % 2) * XSW
            P.waitv("v", "xsL%d" % (kt % 2), xld[kt % 2])
            for k in range(4):
                if dirn == 0:
                    src = xst[:, tb + 1 + k + t0:tb + 1 + k + t0 + tw]
                else:
                    st = tb + T + 6 - k - t0
                    src = xst[:, st:st - tw:-1]
                w_ap = pcw[:, kt * 4 + k:kt * 4 + k + 1]
                if k == 0:
                    i = P.V.tensor_scalar_mul(dst_f32, src, w_ap)
                else:
                    i = P.V.scalar_tensor_tensor(dst_f32, src, w_ap, dst_f32, ALU.mult, ALU.add)
            P.tick("v", i)

        xcb = P.sbuf("mbA_xc", [128, 2 * T], BF16)
        accA = P.sbuf("mbA_acc", [128, T], F32)
        dtpre = P.sbuf("mbA_dtpre", [64, T], BF16)
        bcrow = P.sbuf("mbA_bcrow", [48, T], BF16)
        ps_dt = P.psum("mbA_psdt", [64, T], F32)
        ps_bc = P.psum("mbA_psbc", [48, T], F32)
        load_xs(0)
        for kt in range(KI):
            par = kt % 2
            P.wait("v", "t")
            P.wait("v", "s")
            conv_tile(kt, accA[:, :], 0, T)
            vconv[kt % 2] = P.cnt["v"]
            if kt + 1 < KI:
                load_xs(kt + 1)
            P.wait("s", "v")
            P.wait("s", "t")
            i = P.S.activation(xcb[:, par * T:(par + 1) * T], accA[:, :], AF.Silu,
                               bias=pcb[:, kt:kt + 1])
            P.tick("s", i)
            P.wait("t", "s")
            for c in range(NCH):
                csl = slice(c * TC, (c + 1) * TC)
                mv = xcb[:, par * T + c * TC:par * T + (c + 1) * TC]
                i = P.PE.matmul(ps_dt[0:64, csl], xproj[:, kt * 96:kt * 96 + 64], mv,
                                start=(kt == 0), stop=(kt == KI - 1))
                i = P.PE.matmul(ps_bc[0:16, csl], xproj[:, kt * 96 + 64:kt * 96 + 80], mv,
                                start=(kt == 0), stop=(kt == KI - 1))
                i = P.PE.matmul(ps_bc[32:48, csl], xproj[:, kt * 96 + 80:kt * 96 + 96], mv,
                                start=(kt == 0), stop=(kt == KI - 1), tile_position=(0, 32))
            P.tick("t", i)
        P.wait("s", "t")
        i = P.S.activation(dtpre[:, :], ps_dt[0:64, :], AF.Identity)
        i = P.S.activation(bcrow[0:16, :], ps_bc[0:16, :], AF.Identity)
        i = P.S.activation(bcrow[32:48, :], ps_bc[32:48, :], AF.Identity)
        P.tick("s", i)
        P.wait("y", "s")
        P.dma("y", P.dram["scr_b"], bcrow[0:16, :])
        P.dma("y", P.dram["scr_c"], bcrow[32:48, :])
        P.wait("y", "dy")
        P.free(xcb, accA, ps_bc, bcrow)
        P.dump(dump_pref + "dtpre" + sfx, dtpre[:, :], "s", parts=64)

        NGS = 4
        NGRP = DST // NGS
        SEGW = TH + 1
        GWS = NGS * SEGW
        state = P.sbuf("mb_state", [128, KI * DST], F32)
        bcb = P.sbuf("mb_bcb", [128, 2 * DST * TH], BF16)
        dts = P.sbuf("mb_dts", [128, TH], F32)
        xcs = P.sbuf("mb_xcs", [128, TH], BF16)
        us = P.sbuf("mb_us", [128, TH], BF16)
        NB = 3
        dAg = P.sbuf("mb_dAg", [128, 2 * GWS], F32)
        dbug = P.sbuf("mb_dbug", [128, NB * GWS], BF16)
        cyg = P.sbuf("mb_cyg", [128, NGS * TH], BF16)
        tmp2 = P.sbuf("mb_tmp2", [128, 2 * TH], BF16)
        hbg = P.sbuf("mb_hbg", [128, GWS], BF16)
        yg = P.sbuf("mb_yg", [128, NGRP * 2 * TH], BF16)
        ysum = P.sbuf("mb_ysum", [128, TH], BF16)
        yout = P.sbuf("mb_yout", [128, TH], BF16)
        yrev = P.sbuf("mb_yrev", [128, TH], BF16) if dirn == 1 else None
        ps_d2 = P.psum("mb_psd2", [128, TC], F32)
        acc2 = P.sbuf("mb_acc2", [128, 2 * TH], F32)
        accS = acc2[:, 0:TH]
        sp2 = acc2[:, TH:2 * TH]
        i = P.V.memset(dAg[:, 0:2 * GWS:SEGW], 0.0)
        P.tick("v", i)
        for nb in range(NB):
            i = P.G.memset(dbug[:, nb * GWS:(nb + 1) * GWS:SEGW], 0.0)
        P.tick("g", i)

        gctr = 0
        snap_scan = None
        snap_scan_p = {}
        state_v = None
        for half in range(2):
            t0 = half * TH
            P.wait("y", "g")
            P.wait("y", "v")
            for n in range(DST):
                P.sdma("y", "bcbS", bcb[:, n * TH:(n + 1) * TH],
                       P.dram["scr_b"][n:n + 1, t0:t0 + TH].broadcast_to([128, TH]))
                P.sdma("y", "bcbS", bcb[:, (DST + n) * TH:(DST + n + 1) * TH],
                       P.dram["scr_c"][n:n + 1, t0:t0 + TH].broadcast_to([128, TH]))
            bc_dy = P.cnt["bcbS"]
            load_xs(0)
            for kt in range(KI):
                P.wait("v", "s")
                P.wait("v", "g")
                conv_tile(kt, accS, t0, TH)
                vconv[kt % 2] = P.cnt["v"]
                if kt + 1 < KI:
                    load_xs(kt + 1)
                P.wait("s", "v")
                i = P.S.activation(xcs[:, :], accS, AF.Silu, bias=pcb[:, kt:kt + 1])
                P.tick("s", i)
                for c in range(2):
                    P.wait("t", "v")
                    P.wait("t", "s")
                    i = P.PE.matmul(ps_d2[:, :], dtw[:, kt * 128:(kt + 1) * 128],
                                    dtpre[:, t0 + c * TC:t0 + (c + 1) * TC], start=True, stop=True)
                    P.tick("t", i)
                    P.wait("v", "t")
                    i = P.V.tensor_scalar(accS[:, c * TC:(c + 1) * TC], ps_d2[:, :],
                                          pdtb[:, kt:kt + 1], 0.0, ALU.add, ALU.max)
                    i = P.V.tensor_scalar(sp2[:, c * TC:(c + 1) * TC], ps_d2[:, :],
                                          pdtb[:, kt:kt + 1], None, ALU.add)
                    P.tick("v", i)
                P.wait("s", "v")
                i = P.S.activation(sp2, sp2, AF.Abs)
                P.S.drain()
                i = P.S.activation(sp2, sp2, AF.Exp, scale=-1.0)
                P.S.drain()
                i = P.S.activation(sp2, sp2, AF.Ln, bias=1.0)
                P.tick("s", i)
                P.wait("v", "s")
                i = P.V.tensor_add(dts[:, :], accS, sp2)
                i = P.V.tensor_tensor(us[:, :], dts[:, :], xcs[:, :], ALU.mult)
                P.tick("v", i)
                us_v = P.cnt["v"]
                for g in range(NGRP):
                    gp = gctr % NB
                    ap = gctr % 2
                    gctr += 1
                    n0 = g * NGS
                    s0 = kt * DST + n0
                    P.waitv("s", "v", max(us_v, snap_scan_p.get(("a", ap)) or 0))
                    for j in range(NGS):
                        i = P.S.activation(dAg[:, ap * GWS + j * SEGW + 1:ap * GWS + (j + 1) * SEGW],
                                           dts[:, :], AF.Exp, scale=pA[:, s0 + j:s0 + j + 1])
                    P.tick("s", i)
                    sexp_g = P.cnt["s"]
                    P.waitv("g", "v", max(us_v, snap_scan_p.get(("b", gp)) or 0, state_v or 0))
                    P.waitv("g", "bcbS", bc_dy)
                    for j in range(NGS):
                        i = P.G.tensor_tensor(dbug[:, gp * GWS + j * SEGW + 1:gp * GWS + (j + 1) * SEGW],
                                              us[:, :], bcb[:, (n0 + j) * TH:(n0 + j + 1) * TH],
                                              ALU.mult)
                    if half == 1:
                        i = P.G.tensor_copy(dbug[:, gp * GWS:gp * GWS + GWS:SEGW],
                                            state[:, s0:s0 + NGS])
                    P.tick("g", i)
                    gfill_g = P.cnt["g"]
                    P.waitv("v", "s", sexp_g)
                    P.waitv("v", "g", gfill_g)
                    i = P.V.tensor_tensor_scan(hbg[:, :], dAg[:, ap * GWS:(ap + 1) * GWS],
                                               dbug[:, gp * GWS:(gp + 1) * GWS],
                                               0.0, ALU.mult, ALU.add)
                    P.tick("v", i)
                    snap_scan = P.cnt["v"]
                    snap_scan_p[("a", ap)] = snap_scan
                    snap_scan_p[("b", gp)] = snap_scan
                    if half == 0:
                        i = P.V.tensor_copy(state[:, s0:s0 + NGS], hbg[:, TH:GWS:SEGW])
                    i = P.V.tensor_tensor(
                        cyg[:, :].rearrange("p (n t) -> p n t", n=NGS),
                        hbg[:, :].rearrange("p (n t) -> p n t", n=NGS)[:, :, 1:SEGW],
                        bcb[:, DST * TH + n0 * TH:DST * TH + (n0 + NGS) * TH]
                           .rearrange("p (n t) -> p n t", n=NGS),
                        ALU.mult)
                    i = P.V.tensor_add(yg[:, g * 2 * TH:(g + 1) * 2 * TH],
                                       cyg[:, 0:2 * TH], cyg[:, 2 * TH:4 * TH])
                    P.tick("v", i)
                if "yspS" in P.cnt:
                    P.waitv("v", "yspS", P.cnt["yspS"])
                i = P.V.tensor_add(tmp2[:, :], yg[:, 0:2 * TH], yg[:, 2 * TH:4 * TH])
                i = P.V.tensor_add(cyg[:, 0:2 * TH], yg[:, 4 * TH:6 * TH], yg[:, 6 * TH:8 * TH])
                i = P.V.tensor_add(cyg[:, 2 * TH:4 * TH], tmp2[:, :], cyg[:, 0:2 * TH])
                i = P.V.tensor_add(ysum[:, :], cyg[:, 2 * TH:3 * TH], cyg[:, 3 * TH:4 * TH])
                i = P.V.scalar_tensor_tensor(yout[:, :], xcs[:, :], pD[:, kt:kt + 1], ysum[:, :],
                                             ALU.mult, ALU.add)
                P.tick("v", i)
                if dirn == 0:
                    P.wait("y", "v")
                    P.sdma("y", "yspS", ydram[kt * 128:(kt + 1) * 128, t0:t0 + TH], yout[:, :])
                else:
                    i = P.V.tensor_copy(yrev[:, :], yout[:, TH - 1::-1])
                    P.tick("v", i)
                    P.wait("y", "v")
                    P.sdma("y", "yspS", ydram[kt * 128:(kt + 1) * 128, T - t0 - TH:T - t0], yrev[:, :])
            if half == 0:
                state_v = P.cnt["v"]
        P.free(dtpre, ps_dt, state, bcb, dts, xcs, us, dAg, dbug, cyg, tmp2, acc2,
               hbg, yg, ysum, yout, ps_d2, xprojT, dtwT, xst)
        if yrev is not None:
            P.free(yrev)
        P.barrier()
        if P.stop("mamba_dir%d" % dirn):
            return

    ych = P.sbuf("mg_ych", [128, 32 * TC], BF16)
    mw = P.sbuf("mg_mw", [128, 2 * 32 * 128], BF16)
    zs = P.sbuf("mg_zs", [128, 2 * TC], BF16)
    ymb = P.sbuf("mg_ym", [128, 2 * TC], BF16)
    ps_g = P.psum("mg_psg", [128, TC], F32)
    ps_g2 = P.psum("mg_psg2", [128, TC], F32)
    mwd = P.dram["L0_attn_merge_w"]
    snap = {}
    for c in range(NCH):
        P.wait("y", "t")
        P.sdma("y", "ychS", ych[:, 0:16 * TC].rearrange("p (kt w) -> p kt w", kt=KI),
               P.dram["scr_yf"][:, c * TC:(c + 1) * TC].rearrange("(kt p) w -> p kt w", p=128))
        P.sdma("y", "ychS", ych[:, 16 * TC:32 * TC].rearrange("p (kt w) -> p kt w", kt=KI),
               P.dram["scr_yb"][:, c * TC:(c + 1) * TC].rearrange("(kt p) w -> p kt w", p=128))
        ych_dy = P.cnt["ychS"]
        for m in range(KI):
            par = m % 2
            base = par * 32 * 128
            P.waitv("g", "t", snap.get(("t", par)))
            mn = "mwp%d" % par
            P.sdma("g", mn, mw[:, base:base + 32 * 128].rearrange("p (kt mm) -> p kt mm", kt=32),
                   mwd[:, m * 128:(m + 1) * 128].rearrange("(kt p) mm -> p kt mm", p=128))
            zn2 = "mzsp%d" % par
            P.waitv("y", "v", snap.get(("v", par)))
            P.sdma("y", zn2, zs[:, par * TC:(par + 1) * TC],
                   P.dram["scr_z"][m * 128:(m + 1) * 128, c * TC:(c + 1) * TC])
            pss = ps_g if par == 0 else ps_g2
            P.waitv("t", mn, P.cnt[mn])
            P.waitv("t", "ychS", ych_dy)
            P.waitv("t", "v", snap.get(("v", par)))
            for kt in range(32):
                i = P.PE.matmul(pss[:, :], mw[:, base + kt * 128:base + (kt + 1) * 128],
                                ych[:, kt * TC:(kt + 1) * TC], start=(kt == 0), stop=(kt == 31))
            P.tick("t", i)
            P.wait("v", "t")
            P.waitv("v", zn2, P.cnt[zn2])
            ymn = "ymbp%d" % par
            if ymn in P.cnt:
                P.waitv("v", ymn, P.cnt[ymn])
            i = P.V.tensor_tensor(ymb[:, par * TC:(par + 1) * TC], pss[:, :],
                                  zs[:, par * TC:(par + 1) * TC], ALU.mult)
            P.tick("v", i)
            snap[("v", par)] = P.cnt["v"]
            snap[("t", par)] = P.cnt["t"]
            P.wait("y", "v")
            P.sdma("y", ymn, P.dram["scr_ym"][m * 128:(m + 1) * 128, c * TC:(c + 1) * TC],
                   ymb[:, par * TC:(par + 1) * TC])
    P.free(ych, mw, zs, ymb, ps_g, ps_g2)
    P.barrier()
    if P.stop("mamba_merge"):
        return


def emit_outw_and_residual(P, x, ab2):
    ymch = P.sbuf("ow_ymch", [128, KI * TC], BF16)
    ow = P.sbuf("ow_w", [128, 2 * KI * 128], BF16)
    ps_o = P.psum("ow_ps", [128, TC], F32)
    ps_o2 = P.psum("ow_ps2", [128, TC], F32)
    tmp = P.sbuf("ow_tmp", [128, TC], F32)
    owd = P.dram["L0_attn_out_w"]
    a_b = ab2[:, 0:T]
    b_b = ab2[:, T:2 * T]
    snap = {}
    for c in range(NCH):
        P.wait("y", "t")
        P.sdma("y", "ymchS", ymch[:, :].rearrange("p (kt w) -> p kt w", kt=KI),
               P.dram["scr_ym"][:, c * TC:(c + 1) * TC].rearrange("(kt p) w -> p kt w", p=128))
        ych_dy = P.cnt["ymchS"]
        for mo in range(KD):
            par = mo % 2
            base = par * KI * 128
            P.waitv("g", "t", snap.get(("t", par)))
            on = "owp%d" % par
            P.sdma("g", on, ow[:, base:base + KI * 128].rearrange("p (kt mm) -> p kt mm", kt=KI),
                   owd[:, mo * 128:(mo + 1) * 128].rearrange("(kt p) mm -> p kt mm", p=128))
            pss = ps_o if par == 0 else ps_o2
            P.waitv("t", on, P.cnt[on])
            P.waitv("t", "ymchS", ych_dy)
            P.waitv("t", "v", snap.get(("v", par)))
            for kt in range(KI):
                i = P.PE.matmul(pss[:, :], ow[:, base + kt * 128:base + (kt + 1) * 128],
                                ymch[:, kt * TC:(kt + 1) * TC], start=(kt == 0), stop=(kt == KI - 1))
            P.tick("t", i)
            xsl = x[:, mo * T + c * TC:mo * T + (c + 1) * TC]
            P.wait("v", "t")
            i = P.V.tensor_tensor(tmp[:, :], xsl, a_b[:, c * TC:(c + 1) * TC], ALU.mult)
            i = P.V.tensor_sub(tmp[:, :], tmp[:, :], b_b[:, c * TC:(c + 1) * TC])
            i = P.V.tensor_add(tmp[:, :], tmp[:, :], pss[:, :])
            i = P.V.tensor_add(xsl, xsl, tmp[:, :])
            P.tick("v", i)
            snap[("v", par)] = P.cnt["v"]
            snap[("t", par)] = P.cnt["t"]
    P.free(ymch, ow, ps_o, ps_o2, tmp)


def self_val(P, e):
    return P.cnt[e]



def build_program(want_dumps=(), stop_after=None):
    nc = bass.Bass("TRN2", target_bir_lowering=False)
    P = Prog(nc, want_dumps)
    P.stop_after = stop_after

    P.dram["xt"] = nc.dram_tensor("xt", [D, T], F32, kind="ExternalInput").ap()
    P.dram["xo"] = nc.dram_tensor("xo", [D, T], F32, kind="ExternalOutput").ap()
    for nm, (shape, dt) in {**CONST_SPECS, **ROT_SPECS}.items():
        P.dram[nm] = nc.dram_tensor(nm, shape, dt, kind="ExternalInput").ap()
    for nm, shape in WEIGHT_SPECS.items():
        P.dram[nm] = nc.dram_tensor(nm, shape, F32, kind="ExternalInput").ap()
    for nm, shape, dt in [
        ("scr_x", [D, T], F32), ("scr_z", [DI, T], BF16),
        ("scr_yf", [DI, T], BF16), ("scr_yb", [DI, T], BF16),
        ("scr_ym", [DI, T], BF16), ("scr_b", [DST, T], BF16), ("scr_c", [DST, T], BF16),
        ("scr_ab", [128, 2 * T], F32), ("scr_rows", [2, T], F32),
        ("scr_xp", [D, T], F32), ("scr_xs", [DI, XSW], BF16),
    ]:
        P.dram[nm] = nc.dram_tensor(nm, shape, dt, kind="Internal").ap()
    if want_dumps:
        P.dbg = nc.dram_tensor("dbg", [128, 400000], F32, kind="ExternalOutput").ap()

    for nm, (shape, dt) in CONST_SPECS.items():
        P.consts[nm] = P.sbuf(nm, shape, dt)
        P.dma("y", P.consts[nm][:, :], P.dram[nm])
    nc.const_aps.register(F32, EPS, P.consts["cMisc"][:, 0:1])

    def par_alloc(nm, shape, dtype=F32):
        P.par[nm] = P.sbuf("par_" + nm, shape, dtype)
        return P.par[nm]

    qsd, ksd = P.dram["L0_ffn_q_scale"], P.dram["L0_ffn_k_scale"]
    par_alloc("qs", [128, 2]); par_alloc("ks", [128, 2])
    for l in range(2):
        for half in range(2):
            P.dma("y", P.par["qs"][half * 64:(half + 1) * 64, l:l + 1],
                  qsd[l, :].rearrange("(d one) -> d one", one=1))
            P.dma("y", P.par["ks"][half * 64:(half + 1) * 64, l:l + 1],
                  ksd[l, :].rearrange("(d one) -> d one", one=1))
    par_alloc("lng", [128, 2 * KD]); par_alloc("lnb", [128, 2 * KD])
    par_alloc("b2", [128, 2 * KD])
    for l in range(2):
        P.dma("y", P.par["lng"][:, l * KD:(l + 1) * KD],
              P.dram["L0_ffn_ln_g"][l].rearrange("(kt p) -> p kt", p=128), slow=True)
        P.dma("y", P.par["lnb"][:, l * KD:(l + 1) * KD],
              P.dram["L0_ffn_ln_b"][l].rearrange("(kt p) -> p kt", p=128), slow=True)
        P.dma("y", P.par["b2"][:, l * KD:(l + 1) * KD],
              P.dram["L0_ffn_b2"][l].rearrange("(kt p) -> p kt", p=128), slow=True)
    par_alloc("b1a", [128, 2 * 22]); par_alloc("b1g", [128, 2 * 22])
    for l in range(2):
        for which, off0 in (("b1a", 0), ("b1g", HID)):
            P.dma("y", P.par[which][:, l * 22:l * 22 + 21],
                  P.dram["L0_ffn_b1"][l, off0:off0 + 2688].rearrange("(j p) -> p j", p=128), slow=True)
            P.dma("y", P.par[which][0:42, l * 22 + 21:l * 22 + 22],
                  P.dram["L0_ffn_b1"][l, off0 + 2688:off0 + 2730].rearrange("(d one) -> d one", one=1))
    par_alloc("gamma", [128, KD]); par_alloc("beta", [128, KD])
    P.dma("y", P.par["gamma"][:, :], P.dram["L0_attn_gamma"].rearrange("(kt p) -> p kt", p=128), slow=True)
    P.dma("y", P.par["beta"][:, :], P.dram["L0_attn_beta"].rearrange("(kt p) -> p kt", p=128), slow=True)
    for sfx, wn in (("", ""), ("_b", "_b")):
        par_alloc("A" + sfx, [128, KI * DST])
        P.dma("y", P.par["A" + sfx][:, :].rearrange("p (kt n) -> p kt n", kt=KI),
              P.dram["L0_attn_A_log" + wn].rearrange("(kt p) n -> p kt n", p=128))
        par_alloc("cw" + sfx, [128, KI * 4])
        P.dma("y", P.par["cw" + sfx][:, :].rearrange("p (kt k) -> p kt k", kt=KI),
              P.dram["L0_attn_conv_w" + wn].rearrange("(kt p) k -> p kt k", p=128))
        for pn, dn in (("cb", "conv_b"), ("dtb", "dt_b"), ("Dsk", "D_skip")):
            par_alloc(pn + sfx, [128, KI])
            P.dma("y", P.par[pn + sfx][:, :],
                  P.dram["L0_attn_" + dn + wn].rearrange("(kt p) -> p kt", p=128), slow=True)
    P.barrier()
    for sfx in ("", "_b"):
        i = P.S.activation(P.par["A" + sfx][:, :], P.par["A" + sfx][:, :], AF.Exp)
        P.tick("s", i)
        P.wait("v", "s")
        i = P.V.tensor_scalar_mul(P.par["A" + sfx][:, :], P.par["A" + sfx][:, :], -1.0)
        P.tick("v", i)
    P.barrier()

    x = P.sbuf("x_arena", [128, KD * T], F32, hi=True)
    P.dma("y", x[:, :].rearrange("p (kt w) -> p kt w", kt=KD),
          P.dram["xt"].rearrange("(kt p) w -> p kt w", p=128))
    P.barrier()
    xtiles = lambda xx: [xx[:, kt * T:(kt + 1) * T] for kt in range(KD)]

    def run_ffn_macaron(tagp):
        xp = P.sbuf("xp_arena", [128, KD * T], F32, hi=True)
        emit_ln(P, xtiles(x), None, direct_dst=lambda kt: xp[:, kt * T:(kt + 1) * T])
        P.dump(tagp + "ln", xp[:, :], "v")
        P.wait("y", "v")
        P.dma("y", P.dram["scr_x"].rearrange("(kt p) w -> p kt w", p=128),
              x[:, :].rearrange("p (kt w) -> p kt w", kt=KD))
        P.barrier()
        P.free(x)
        xp = emit_ffn_call(P, xp, dump_pref=tagp)
        if P.stop_flag:
            return xp, None
        x2 = P.sbuf("x_arena", [128, KD * T], F32, hi=True)
        P.dma("y", x2[:, :].rearrange("p (kt w) -> p kt w", kt=KD),
              P.dram["scr_x"].rearrange("(kt p) w -> p kt w", p=128))
        P.barrier()
        for kt in range(KD):
            i = P.V.scalar_tensor_tensor(x2[:, kt * T:(kt + 1) * T], xp[:, kt * T:(kt + 1) * T],
                                         0.5, x2[:, kt * T:(kt + 1) * T], ALU.mult, ALU.add)
            P.tick("v", i)
        P.free(xp)
        P.dump(tagp + "x_out", x2[:, :], "v")
        return None, x2

    if stop_after == "ln1_only":
        P.dump("x0", x[:, :], "v")
        xp = P.sbuf("xp_arena", [128, KD * T], F32, hi=True)
        emit_ln(P, xtiles(x), None, direct_dst=lambda kt: xp[:, kt * T:(kt + 1) * T])
        P.dump("ln1", xp[:, :], "v")
        P.barrier()
        P.dump("ln1b", xp[:, :], "v")
        P.stop_flag = True
        finish(P)
        return nc, P
    xp, x = run_ffn_macaron("f1_")
    if P.stop_flag:
        finish(P)
        return nc, P

    lnm = P.sbuf("mb_lnm", [128, KD * T], BF16, hi=True)

    def wrm(kt, tmp, g, b):
        P.wait("s", "v")
        i = P.S.activation(lnm[:, kt * T:(kt + 1) * T], tmp[:, :], AF.Identity, bias=b, scale=g)
        P.tick("s", i)

    ab, ab2 = emit_ln(P, xtiles(x), wrm, gamma_ap=P.par["gamma"], beta_ap=P.par["beta"],
                      fuse_double=True, keep_ab=True)
    P.dump("mb_lnm", lnm[:, :], "s")
    P.wait("y", "v")
    P.dma("y", P.dram["scr_x"].rearrange("(kt p) w -> p kt w", p=128),
          x[:, :].rearrange("p (kt w) -> p kt w", kt=KD))
    P.dma("y", P.dram["scr_ab"], ab2[:, :])
    P.barrier()
    P.free(x, ab, ab2)
    if P.stop("ln2"):
        finish(P)
        return nc, P
    emit_mamba(P, lnm, dump_pref="mb_")
    if P.stop_flag:
        finish(P)
        return nc, P
    x = P.sbuf("x_arena", [128, KD * T], F32, hi=True)
    P.dma("y", x[:, :].rearrange("p (kt w) -> p kt w", kt=KD),
          P.dram["scr_x"].rearrange("(kt p) w -> p kt w", p=128))
    ab2 = P.sbuf("ow_ab2", [128, 2 * T], F32)
    P.dma("y", ab2[:, :], P.dram["scr_ab"])
    P.barrier()
    emit_outw_and_residual(P, x, ab2)
    P.free(ab2)
    P.dump("mb_x_out", x[:, :], "v")
    P.barrier()
    if P.stop("mamba"):
        finish(P)
        return nc, P

    xp, x = run_ffn_macaron("f2_")
    if P.stop_flag:
        finish(P)
        return nc, P

    P.wait("y", "v")
    P.dma("y", P.dram["xo"].rearrange("(kt p) w -> p kt w", p=128),
          x[:, :].rearrange("p (kt w) -> p kt w", kt=KD))
    finish(P)
    return nc, P


def finish(P):
    P.barrier()
    for q in ["dy", "dg", "ds"]:
        P.SY.wait_ge(P.sem[q], P.cnt[q])



_BUILD_CACHE = {}


def kernel(**inputs):
    key = ("plain",)
    if key not in _BUILD_CACHE:
        _BUILD_CACHE[key] = build_program()
    nc, P = _BUILD_CACHE[key]
    consts = host_consts()
    x = np.asarray(inputs["L0_attn_x"], np.float32)
    base = {}
    for nm in WEIGHT_SPECS:
        base[nm] = _f(inputs[nm])
    base.update(consts)
    in_maps = []
    for b in range(8):
        m = dict(base)
        m["xt"] = np.ascontiguousarray(x[b].T)
        in_maps.append(m)
    res = run_bass_kernel_spmd(nc, in_maps, core_ids=list(range(8)))
    out = np.stack([np.asarray(r["xo"]).T for r in res.results])
    return out.astype(np.float32)


# version 10
# speedup vs baseline: 1.0889x; 1.0022x over previous
import numpy as np
import ml_dtypes
import concourse.bass as bass
import concourse.mybir as mybir
from concourse.bass_utils import run_bass_kernel_spmd

F32 = mybir.dt.float32
BF16 = mybir.dt.bfloat16
AF = mybir.ActivationFunctionType
ALU = mybir.AluOpType
AXX = mybir.AxisListType.X

D = 1024
T = 2048
KD = 8
DI = 2048
KI = 16
NH = 8
DH = 64
WIN = 512
HID = 2730
DTR = 64
DST = 16
NCH = 4
TC = 512
TH = 1024
EPS = 1e-5
QK_SCALE = 8.0
HT = [(j * 128, min(128, HID - j * 128)) for j in range(22)]
XSW = T + 8


def _f(x):
    return np.ascontiguousarray(np.asarray(x), dtype=np.float32)


def _bf(x):
    return np.ascontiguousarray(np.asarray(x, dtype=np.float32).astype(ml_dtypes.bfloat16))


def host_consts():
    c = {}
    c["cI"] = _bf(np.eye(128))
    R = np.zeros((128, 128), np.float32)
    for blk in range(2):
        for m in range(64):
            if m < 32:
                R[blk * 64 + m + 32, blk * 64 + m] = -1.0
            else:
                R[blk * 64 + m - 32, blk * 64 + m] = 1.0
    c["cR"] = _bf(R)
    half = np.arange(0, DH, 2, dtype=np.float64)
    inv_freq = 1.0 / 10000.0 ** (half / DH)
    t = np.arange(T, dtype=np.float64)
    freqs = np.concatenate([t[:, None] * inv_freq[None, :]] * 2, -1)
    base = (half + 0.4 * DH) / (1.4 * DH)
    power = (t - T // 2) / float(WIN // 2)
    scale = np.concatenate([base[None, :] ** power[:, None]] * 2, -1)
    cos, sin = np.cos(freqs), np.sin(freqs)
    rep = lambda a: np.concatenate([a.T, a.T], 0)
    c["cT1q"] = _bf(rep(cos * scale))
    c["cT2q"] = _bf(rep(sin * scale))
    c["cT1k"] = _bf(rep(cos / scale))
    c["cT2k"] = _bf(rep(sin / scale))
    idx = np.arange(128)
    MgA = np.zeros((128, 640), np.float32)
    for r in range(128):
        kr = np.arange(640)
        MgA[r, :] = np.where((kr >= r) & (kr <= r + 512), 0.0, -1.0e30)
    c["cMgA"] = _bf(MgA)
    c["cM0A"] = _bf(np.where(idx[None, :] <= idx[:, None], 0.0, -1.0e30))
    c["cOnesB"] = _bf(np.ones((128, 1)))
    c["cOnesRowF"] = _f(np.ones((128, 128)))
    c["cOnesRowB"] = _bf(np.ones((1, 128)))
    B2 = np.zeros((128, 2), np.float32)
    B2[:64, 0] = 1.0
    B2[64:, 1] = 1.0
    c["cB2b"] = _bf(B2)
    B2T = np.zeros((2, 128), np.float32)
    B2T[0, :64] = 1.0
    B2T[1, 64:] = 1.0
    c["cB2T"] = _bf(B2T)
    c["cMisc"] = _f(np.tile(np.array([[EPS, 0.0]], np.float32), (128, 1)))
    return c


CONST_SPECS = {
    "cI": ([128, 128], BF16), "cR": ([128, 128], BF16),
    "cMgA": ([128, 640], BF16), "cM0A": ([128, 128], BF16),
    "cOnesB": ([128, 1], BF16), "cOnesRowF": ([128, 128], F32),
    "cOnesRowB": ([1, 128], BF16),
    "cB2b": ([128, 2], BF16), "cB2T": ([2, 128], BF16),
    "cMisc": ([128, 2], F32),
}
ROT_SPECS = {
    "cT1q": ([128, T], BF16), "cT2q": ([128, T], BF16),
    "cT1k": ([128, T], BF16), "cT2k": ([128, T], BF16),
}

WEIGHT_SPECS = {
    "L0_attn_gamma": [D], "L0_attn_beta": [D],
    "L0_attn_in_w": [D, 2 * DI],
    "L0_attn_conv_w": [DI, 4], "L0_attn_conv_b": [DI],
    "L0_attn_xproj_w": [DI, 96], "L0_attn_dt_w": [DTR, DI], "L0_attn_dt_b": [DI],
    "L0_attn_A_log": [DI, DST], "L0_attn_D_skip": [DI],
    "L0_attn_conv_w_b": [DI, 4], "L0_attn_conv_b_b": [DI],
    "L0_attn_xproj_w_b": [DI, 96], "L0_attn_dt_w_b": [DTR, DI], "L0_attn_dt_b_b": [DI],
    "L0_attn_A_log_b": [DI, DST], "L0_attn_D_skip_b": [DI],
    "L0_attn_merge_w": [2 * DI, DI], "L0_attn_out_w": [DI, D],
    "L0_ffn_wqkv": [2, D, 1536], "L0_ffn_q_scale": [2, DH], "L0_ffn_k_scale": [2, DH],
    "L0_ffn_wo": [2, 512, D], "L0_ffn_ln_g": [2, D], "L0_ffn_ln_b": [2, D],
    "L0_ffn_w1": [2, D, 2 * HID], "L0_ffn_b1": [2, 2 * HID],
    "L0_ffn_w2": [2, HID, D], "L0_ffn_b2": [2, D],
}


class EngProxy:

    _PASS = {"wait_ge", "drain", "nop", "dma_start", "sem_inc"}

    def __init__(self, prog, key, raw):
        self.prog = prog
        self.key = key
        self.raw = raw

    def __getattr__(self, name):
        fn = getattr(self.raw, name)
        if name in self._PASS or not callable(fn):
            return fn
        key = self.key
        prog = self.prog

        def wrapped(*args, **kwargs):
            aps = [a for a in args if isinstance(a, bass.AP)]
            aps += [v for v in kwargs.values() if isinstance(v, bass.AP)]
            dirty = prog.dirty[key]
            if aps:
                prog.sync_marks(self.raw, aps[0])
                ins_names = {a.tensor.name for a in aps[1:]}
                if ins_names & dirty:
                    self.raw.drain()
                    dirty.clear()
            instr = fn(*args, **kwargs)
            if aps:
                dirty.add(aps[0].tensor.name)
            return instr

        return wrapped


class Prog:
    def __init__(self, nc, want_dumps=()):
        self.nc = nc
        self._cms = {}
        self.sem = {}
        self.cnt = {}
        for name in ["t", "v", "s", "g", "y", "dy", "dg", "ds"]:
            cm = nc.semaphore("sem_" + name)
            self.sem[name] = cm.__enter__()
            self._cms["sem_" + name] = cm
            self.cnt[name] = 0
        self.PE = nc.tensor
        self.V = EngProxy(self, "v", nc.vector)
        self.S = EngProxy(self, "s", nc.scalar)
        self.G = EngProxy(self, "g", nc.gpsimd)
        self.SY = nc.sync
        self.dirty = {"v": set(), "s": set(), "g": set()}
        self._eng = {"t": self.PE, "v": self.V, "s": self.S, "g": self.G, "y": self.SY}
        self.pending_sync = {}
        self.dumps = []
        self.dump_off = 0
        self.want_dumps = set(want_dumps)
        self.dbg = None
        self.consts = {}
        self.dram = {}
        self.par = {}
        self.uid = 0
        self.stop_flag = False
        self.stop_after = None

    ARENA_BYTES = 206 * 1024

    def _ensure_arena(self):
        if not hasattr(self, "arena_base"):
            cm = self.nc.sbuf_tensor("big_arena", [128, self.ARENA_BYTES], mybir.dt.uint8)
            h = cm.__enter__()
            self._cms["big_arena"] = cm
            self.arena_base = self.nc.lookup_mloc(h).addr
            self.freelist = [(0, self.ARENA_BYTES)]
            self.alloc_info = {}

    def sbuf(self, name, shape, dtype, hi=False):
        self._ensure_arena()
        per = int(np.prod(shape[1:])) * mybir.dt.size(dtype)
        per = (per + 31) // 32 * 32
        ranges = list(enumerate(self.freelist))
        if hi:
            ranges = ranges[::-1]
        for idx, (off, sz) in ranges:
            if sz >= per:
                break
        else:
            amap = sorted((v[0], v[1], k) for k, v in self.alloc_info.items())
            raise MemoryError(f"SBUF arena full for {name} ({per}B): {self.freelist}\n" +
                              "\n".join(f"  {o:7d} +{s:6d} {k}" for o, s, k in amap))
        if sz == per:
            del self.freelist[idx]
        elif hi:
            off = off + sz - per
            self.freelist[idx] = (self.freelist[idx][0], sz - per)
        else:
            self.freelist[idx] = (off + per, sz - per)
        self.uid += 1
        name = f"{name}_{self.uid}"
        h = self.nc.virtual_sbuf_tensor(name, list(shape), dtype)
        self.nc._tensor(h.name, list(shape), dtype, type="SB")
        mloc = self.nc.lookup_mloc(h)
        mloc.allocated = True
        mloc.addr = self.arena_base + off
        self.alloc_info[h.name] = (off, per)
        self.pending_sync[h.name] = dict(self.cnt)
        return h

    def psum(self, name, shape, dtype=F32):
        self.uid += 1
        name = f"{name}_{self.uid}"
        esz = mybir.dt.size(dtype)
        free = int(np.prod(shape[1:]))
        pad = (-(free * esz) % 2048) // esz
        cm = self.nc.psum_tensor(name, [128, free + pad], dtype)
        t = cm.__enter__()
        self._cms[name] = (self.uid, cm)
        return t

    def free(self, *handles):
        psums = []
        for h in handles:
            name = h.name if hasattr(h, "name") else h.tensor.name
            if name in self.alloc_info:
                off, per = self.alloc_info.pop(name)
                self.freelist.append((off, per))
            else:
                psums.append(name)
        self.freelist.sort()
        merged = []
        for off, sz in self.freelist:
            if merged and merged[-1][0] + merged[-1][1] == off:
                merged[-1] = (merged[-1][0], merged[-1][1] + sz)
            else:
                merged.append((off, sz))
        self.freelist = [tuple(x) for x in merged]
        for name in sorted(psums, key=lambda n: self._cms[n][0], reverse=True):
            _, cm = self._cms.pop(name)
            cm.__exit__(None, None, None)

    def tick(self, name, instr, k=1):
        if name in ("v", "s", "g"):
            instr = self._eng[name].raw.drain()
            self.dirty[name].clear()
        elif name == "t":
            instr = self.PE.drain()
        instr.then_inc(self.sem[name], k)
        self.cnt[name] += k

    def wait(self, who, dep, val=None):
        self._eng[who].wait_ge(self.sem[dep], self.cnt[dep] if val is None else val)

    def waitv(self, who, dep, val):
        if val:
            self._eng[who].wait_ge(self.sem[dep], val)

    def sync_marks(self, eng, out_ap):
        name = out_ap.tensor.name
        marks = self.pending_sync.pop(name, None)
        if marks:
            for s, v in marks.items():
                if v > 0:
                    eng.wait_ge(self.sem[s], v)

    def sdma(self, qname, sname, out, in_, slow=False):
        if sname not in self.sem:
            cm = self.nc.semaphore("sem_" + sname)
            self.sem[sname] = cm.__enter__()
            self._cms["sem_" + sname] = cm
            self.cnt[sname] = 0
        if hasattr(out, "tensor") and out.tensor.name in self.pending_sync:
            self.sync_marks(self._eng[qname] if qname == "y" else self._eng[qname].raw, out)
        i = self._eng[qname].dma_start(out, in_, allow_slow_non_contiguous=slow)
        i.then_inc(self.sem[sname], 16)
        self.cnt[sname] += 16
        return i

    def dma(self, qname, out, in_, slow=False):
        if hasattr(out, "tensor") and out.tensor.name in self.pending_sync:
            self.sync_marks(self._eng[qname] if qname == "y" else self._eng[qname].raw, out)
        i = self._eng[qname].dma_start(out, in_, allow_slow_non_contiguous=slow)
        self.tick("d" + qname, i, 16)
        return i

    def barrier(self, engines="tvsgy"):
        marks = {}
        for e in engines:
            i = self._eng[e].nop()
            self.tick(e, i, 1)
            marks[e] = self.cnt[e]
        for e in engines:
            for o in engines:
                if o != e:
                    self._eng[e].wait_ge(self.sem[o], marks[o])
            for q in ["dy", "dg", "ds"]:
                self._eng[e].wait_ge(self.sem[q], self.cnt[q])

    def dump(self, tag, ap, producer="v", parts=128):
        if tag not in self.want_dumps:
            return
        w = int(np.prod(ap.shape)) // int(ap.shape[0])
        q = "y" if "float32" in str(ap.dtype) else "g"
        for e in "tvsg":
            self.tick(e, self._eng[e].nop())
            self.wait(q, e)
        self.dma(q, self.dbg[0:parts, self.dump_off:self.dump_off + w], ap)
        self.pending_sync[ap.tensor.name] = dict(self.cnt)
        self.dumps.append((tag, self.dump_off, parts, w, str(ap.dtype)))
        self.dump_off += w

    def stop(self, stage):
        if self.stop_after is not None and stage == self.stop_after:
            self.stop_flag = True
        return self.stop_flag



def emit_ln(P, src_tiles, out_writer, gamma_ap=None, beta_ap=None, fuse_double=False,
            keep_ab=False, direct_dst=None):
    need_cast = "float32" in str(src_tiles[0].dtype)
    sq = P.sbuf("ln_sq", [128, 2 * TC], BF16)
    xb = P.sbuf("ln_xb", [128, 2 * TC], BF16) if need_cast else None
    rows = P.sbuf("ln_rows", [1, 3 * T], F32)
    ab = P.sbuf("ln_ab", [128, 2 * T], F32)
    ab2 = P.sbuf("ln_ab2", [128, 2 * T], F32) if (fuse_double and keep_ab) else None
    ps_s = P.psum("ln_ps_s", [1, TC], F32)
    ps_q = P.psum("ln_ps_q", [1, TC], F32)
    ps_b = P.psum("ln_ps_b", [128, TC], F32)
    ones = P.consts["cOnesB"]

    for c in range(NCH):
        csl = slice(c * TC, (c + 1) * TC)
        for kt in range(KD):
            par = kt % 2
            psl = slice(par * TC, (par + 1) * TC)
            src = src_tiles[kt][:, csl]
            P.wait("s", "t")
            if need_cast:
                i = P.S.activation(xb[:, psl], src, AF.Identity)
                P.tick("s", i)
                xsrc = xb[:, psl]
            else:
                xsrc = src
            i = P.S.activation(sq[:, psl], src, AF.Square)
            P.tick("s", i)
            P.wait("t", "s")
            i = P.PE.matmul(ps_s[0:1, :], ones[:, 0:1], xsrc, start=(kt == 0), stop=(kt == KD - 1))
            i = P.PE.matmul(ps_q[0:1, :], ones[:, 0:1], sq[:, psl], start=(kt == 0), stop=(kt == KD - 1))
            P.tick("t", i)
        P.wait("v", "t")
        i = P.V.tensor_copy(rows[0:1, csl], ps_s[0:1, :])
        i = P.V.tensor_copy(rows[0:1, T + c * TC:T + (c + 1) * TC], ps_q[0:1, :])
        P.tick("v", i)
        P.wait("t", "v")

    muA = rows[0:1, 0:T]
    rB = rows[0:1, T:2 * T]
    bZ = rows[0:1, 2 * T:3 * T]
    vB = rB
    i = P.V.tensor_scalar_mul(muA, rows[0:1, 0:T], 1.0 / D)
    P.tick("v", i)
    P.wait("s", "v")
    i = P.S.activation(bZ, muA, AF.Square)
    P.tick("s", i)
    P.wait("v", "s")
    i = P.V.scalar_tensor_tensor(vB, vB, 1.0 / D, bZ, ALU.mult, ALU.subtract)
    P.tick("v", i)

    brow = P.sbuf("ln_brow", [1, T], F32)

    def bcast_pair(dst_ab):
        P.wait("v", "s")
        i = P.V.tensor_mul(brow[0:1, :], muA, rB_cur[0])
        P.tick("v", i)
        P.wait("y", "v")
        P.dma("y", P.dram["scr_rows"][0:1, :], rB_cur[0])
        P.dma("y", P.dram["scr_rows"][1:2, :], brow[0:1, :])
        P.wait("g", "dy")
        P.dma("g", dst_ab[:, 0:T], P.dram["scr_rows"][0:1, :].broadcast_to([128, T]))
        P.dma("g", dst_ab[:, T:2 * T], P.dram["scr_rows"][1:2, :].broadcast_to([128, T]))
        P.wait("v", "dg")
        P.wait("s", "dg")

    P.wait("s", "v")
    i = P.S.activation(bZ, vB, AF.Ln, bias=EPS)
    P.S.drain()
    i = P.S.activation(bZ, bZ, AF.Exp, scale=-0.5)
    P.tick("s", i)
    rB_cur = [bZ]
    if fuse_double:
        if ab2 is not None:
            bcast_pair(ab2)
        w1 = P.sbuf("ln_w1", [1, 2 * T], F32)
        wa = w1[0:1, 0:T]
        wb = w1[0:1, T:2 * T]
        i = P.S.activation(wa, vB, AF.Ln)
        i = P.S.activation(wb, vB, AF.Ln, bias=EPS)
        P.tick("s", i)
        P.wait("v", "s")
        i = P.V.tensor_sub(wa, wa, wb)
        P.tick("v", i)
        P.wait("s", "v")
        i = P.S.activation(wa, wa, AF.Exp)
        P.tick("s", i)
        P.wait("v", "s")
        i = P.V.tensor_scalar_add(wa, wa, EPS)
        P.tick("v", i)
        P.wait("s", "v")
        i = P.S.activation(wa, wa, AF.Ln)
        P.S.drain()
        i = P.S.activation(wa, wa, AF.Exp, scale=-0.5)
        P.tick("s", i)
        P.wait("v", "s")
        i = P.V.tensor_mul(bZ, bZ, wa)
        P.tick("v", i)
        P.free(w1)
    bcast_pair(ab)

    P.dump("lnab", ab[:, :], "v")
    a_b = ab[:, 0:T]
    b_b = ab[:, T:2 * T]
    tmp = P.sbuf("ln_tmp", [128, T], F32) if direct_dst is None else None
    for kt in range(KD):
        P.wait("v", "s")
        dst = tmp[:, :] if direct_dst is None else direct_dst(kt)
        i = P.V.tensor_mul(dst, src_tiles[kt], a_b)
        i = P.V.tensor_sub(dst, dst, b_b)
        P.tick("v", i)
        if out_writer is not None:
            out_writer(kt, dst,
                       None if gamma_ap is None else gamma_ap[:, kt:kt + 1],
                       None if beta_ap is None else beta_ap[:, kt:kt + 1])
    P.free(sq, rows, brow, ps_s, ps_q, ps_b)
    if tmp is not None:
        P.free(tmp)
    if xb is not None:
        P.free(xb)
    if keep_ab:
        return ab, ab2
    P.free(ab)
    return None


def emit_shift(P, xp, dst):
    for kt in range(KD):
        if kt < 4:
            i = P.S.activation(dst[:, kt * T:(kt + 1) * T], xp[:, kt * T:(kt + 1) * T], AF.Identity)
        else:
            i = P.S.activation(dst[:, kt * T + 1:(kt + 1) * T], xp[:, kt * T:kt * T + T - 1],
                               AF.Identity)
        P.tick("s", i)
    P.wait("v", "s")
    for kt in range(4, KD):
        i = P.V.memset(dst[:, kt * T:kt * T + 1], 0.0)
    P.tick("v", i)



def emit_ffn_layer(P, l, xp, wsb, dump_pref=""):
    xpt = lambda kt: xp[:, kt * T:(kt + 1) * T]

    sh = P.sbuf("ffn_sh", [128, KD * T], BF16, hi=True)
    emit_shift(P, xp, sh)
    P.dump(dump_pref + "shift1", sh[:, :], "v")
    wqkv = P.sbuf("ffc_wqkv", [128, KD * 1536], BF16)
    P.wait("g", "t")
    P.sdma("g", "wqS", wqkv[:, :].rearrange("p (kt m) -> p kt m", kt=KD),
           P.dram["L0_ffn_wqkv"][l].rearrange("(kt p) m -> p kt m", p=128))

    q4 = P.sbuf("ffn_q4", [128, 4 * T], BF16)
    k4 = P.sbuf("ffn_k4", [128, 4 * T], BF16)
    sqb = P.sbuf("ffn_sqb", [128, TC], BF16)
    tt1 = P.sbuf("ffn_tt1", [128, TC], F32)
    s2 = P.sbuf("ffn_s2", [2, T], F32)
    rbf2 = P.sbuf("ffn_rbf2", [2, T], BF16)
    rb = P.sbuf("ffn_rb", [128, T], BF16)
    qn = P.sbuf("ffn_qn", [128, T], BF16)
    t2t = P.sbuf("ffn_t2t", [128, TC], F32)
    ps_m = P.psum("ffn_psm", [128, TC], F32)
    ps_m2 = P.psum("ffn_psm2", [128, TC], F32)
    ps_n = P.psum("ffn_psn", [2, TC], F32)
    ps_r = P.psum("ffn_psr", [128, TC], F32)
    P.wait("t", "wqS")
    rtab = P.sbuf("ffn_rtab", [128, 2 * T], BF16)

    snap = {}
    for side, (out4, scl, tn1, tn2) in enumerate(
            [(q4, wsb["qs"], "cT1q", "cT2q"),
             (k4, wsb["ks"], "cT1k", "cT2k")]):
        t1c = rtab[:, 0:T]
        t2c = rtab[:, T:2 * T]
        P.wait("y", "v")
        P.sdma("y", "rtS", t1c, P.dram[tn1])
        P.sdma("y", "rtS", t2c, P.dram[tn2])
        P.wait("v", "rtS")
        for mt in range(4):
            col0 = side * 512 + mt * 128
            for c in range(NCH):
                csl = slice(c * TC, (c + 1) * TC)
                psx = ps_m if c % 2 == 0 else ps_m2
                P.waitv("t", "v", snap.get(("qv", c % 2)))
                P.waitv("t", "s", snap.get(("qs", c % 2)))
                for kt in range(KD):
                    i = P.PE.matmul(psx[:, :], wqkv[:, kt * 1536 + col0:kt * 1536 + col0 + 128],
                                    sh[:, kt * T + c * TC:kt * T + (c + 1) * TC],
                                    start=(kt == 0), stop=(kt == KD - 1))
                P.tick("t", i)
                P.wait("s", "t")
                i = P.S.activation(sqb[:, :], psx[:, :], AF.Square)
                P.tick("s", i)
                snap[("qs", c % 2)] = P.cnt["s"]
                P.wait("t", "s")
                P.waitv("t", "v", snap.get("psn_v"))
                i = P.PE.matmul(ps_n[0:2, :], P.consts["cB2b"][:, :], sqb[:, :], start=True, stop=True)
                P.tick("t", i)
                P.wait("v", "t")
                i = P.V.tensor_copy(s2[:, csl], ps_n[0:2, :])
                i = P.V.tensor_scalar_mul(qn[:, csl], psx[:, :], scl)
                P.tick("v", i)
                snap[("qv", c % 2)] = P.cnt["v"]
                snap["psn_v"] = P.cnt["v"]
            i = P.V.tensor_scalar_max(s2[:, :], s2[:, :], 1e-24)
            P.tick("v", i)
            P.wait("s", "v")
            i = P.S.activation(s2[:, :], s2[:, :], AF.Ln)
            P.S.drain()
            i = P.S.activation(rbf2[:, :], s2[:, :], AF.Exp, scale=-0.5)
            P.tick("s", i)
            for c in range(NCH):
                csl = slice(c * TC, (c + 1) * TC)
                P.wait("t", "s")
                P.wait("t", "v")
                i = P.PE.matmul(ps_r[:, :], P.consts["cB2T"][:, :], rbf2[:, csl],
                                start=True, stop=True)
                P.tick("t", i)
                P.wait("v", "t")
                i = P.V.tensor_copy(rb[:, csl], ps_r[:, :])
                P.tick("v", i)
                P.wait("t", "v")
            for c in range(NCH):
                csl = slice(c * TC, (c + 1) * TC)
                P.wait("t", "v")
                i = P.PE.matmul(ps_r[:, :], P.consts["cR"][:, :], qn[:, csl], start=True, stop=True)
                P.tick("t", i)
                P.wait("v", "t")
                i = P.V.tensor_tensor(t2t[:, :], ps_r[:, :], t2c[:, csl], ALU.mult)
                P.tick("v", i)
                P.wait("t", "v")
                i = P.V.tensor_tensor(tt1[:, :], qn[:, csl], t1c[:, csl], ALU.mult)
                i = P.V.tensor_add(tt1[:, :], tt1[:, :], t2t[:, :])
                i = P.V.tensor_tensor(out4[:, mt * T + c * TC:mt * T + (c + 1) * TC],
                                      tt1[:, :], rb[:, csl], ALU.mult)
                P.tick("v", i)
    P.dump(dump_pref + "q4", q4[:, :], "v")
    P.dump(dump_pref + "k4", k4[:, :], "v")
    if P.stop(dump_pref + "qk"):
        return xp

    P.free(sqb, tt1, s2, rbf2, rb, qn, t2t, rtab, ps_n, ps_r)
    vtm = P.sbuf("ffn_vtm", [128, 16 * 512], BF16)
    if P.stop(dump_pref + "v0"):
        return xp
    snap = {}
    for tt in range(16):
        psx = ps_m if tt % 2 == 0 else ps_m2
        P.waitv("t", "v", snap.get(tt % 2))
        for kt in range(KD):
            i = P.PE.matmul(psx[:, :], sh[:, kt * T + tt * 128:kt * T + (tt + 1) * 128],
                            wqkv[:, kt * 1536 + 1024:kt * 1536 + 1536],
                            start=(kt == 0), stop=(kt == KD - 1))
        P.tick("t", i)
        P.wait("v", "t")
        i = P.V.tensor_copy(vtm[:, tt * 512:tt * 512 + 512], psx[:, :])
        P.tick("v", i)
        snap[tt % 2] = P.cnt["v"]
    P.free(sh, wqkv, ps_m, ps_m2)
    P.dump(dump_pref + "vtm", vtm[:, :], "s")
    if P.stop(dump_pref + "v"):
        return xp

    nmax = P.sbuf("ffn_nmax", [128, 2], F32)
    smf = P.sbuf("ffn_smf", [128, 2 * 640], F32)
    Pb = P.sbuf("ffn_P", [128, 2 * 640], BF16)
    PT = P.sbuf("ffn_PT", [128, 2 * 8 * 512 + 512], BF16)
    rsr = P.sbuf("ffn_rsr", [128, TC], F32)
    rbq = P.sbuf("ffn_rbq", [128, TC], F32)
    osb = P.sbuf("ffn_o", [128, 4 * T], BF16)
    ps_sim = [P.psum("ffn_pssimA", [128, 640], F32), P.psum("ffn_pssimB", [128, 640], F32)]
    ps_pt = [P.psum("ffn_psptA", [128, 640], BF16), P.psum("ffn_psptB", [128, 640], BF16)]
    ps_o = P.psum("ffn_pso", [128, TC], F32)
    ps_rr = P.psum("ffn_psrr", [128, TC], F32)

    wins = [(h, qw) for h in range(NH) for qw in range(4)]
    steps = [(w, qt) for w in range(len(wins)) for qt in range(4)]
    sn = {}
    pend = [None]
    P.wait("t", "v")

    def meta(s):
        w, qt = steps[s]
        h, qw = wins[w]
        rt, po = h // 2, (h % 2) * 64
        if qw == 0:
            nblk, ktok0 = qt + 1, 0
        else:
            nblk, ktok0 = 5, (qw - 1) * 4 + qt
        return w, qt, h, qw, rt, po, nblk, ktok0, nblk * 128

    def stA(s):
        w, qt, h, qw, rt, po, nblk, ktok0, width = meta(s)
        par = s % 2
        i_abs = qw * 4 + qt
        P.waitv("t", "v", sn.get(("simfree", par)))
        lhs = q4[po:po + 64, rt * T + i_abs * 128:rt * T + (i_abs + 1) * 128]
        off = 0
        while off < width:
            n = min(TC, width - off)
            i = P.PE.matmul(ps_sim[par][:, off:off + n], lhs,
                            k4[po:po + 64,
                               rt * T + ktok0 * 128 + off:rt * T + ktok0 * 128 + off + n],
                            start=True, stop=True)
            off += n
        P.tick("t", i)
        sn[("simdone", par)] = P.cnt["t"]

    def stB(s):
        w, qt, h, qw, rt, po, nblk, ktok0, width = meta(s)
        par = s % 2
        sm = smf[:, par * 640:par * 640 + width]
        pssim = ps_sim[par]
        P.waitv("v", "t", sn[("simdone", par)])
        P.waitv("v", "s", sn.get(("sexp", par)))
        if qw == 0:
            if nblk > 1:
                i = P.V.tensor_scalar_mul(sm[:, 0:width - 128], pssim[:, 0:width - 128], QK_SCALE)
            i = P.V.scalar_tensor_tensor(sm[:, width - 128:width], pssim[:, width - 128:width],
                                         QK_SCALE, P.consts["cM0A"][:, :], ALU.mult, ALU.add)
        else:
            i = P.V.scalar_tensor_tensor(sm, pssim[:, 0:width], QK_SCALE,
                                         P.consts["cMgA"][:, 0:width], ALU.mult, ALU.add)
        i = P.V.reduce_max(nmax[:, par:par + 1], sm, axis=AXX, negate=True)
        P.tick("v", i)
        sn[("simfree", par)] = P.cnt["v"]
        sn[("smf", par)] = P.cnt["v"]

    def stX(s):
        w, qt, h, qw, rt, po, nblk, ktok0, width = meta(s)
        par = s % 2
        P.waitv("s", "v", sn[("smf", par)])
        P.waitv("s", "t", sn.get(("pbfree", par)))
        i = P.S.activation(Pb[:, par * 640:par * 640 + width],
                           smf[:, par * 640:par * 640 + width], AF.Exp,
                           bias=nmax[:, par:par + 1], scale=1.0)
        P.tick("s", i)
        sn[("sexp", par)] = P.cnt["s"]

    def stC(s):
        w, qt, h, qw, rt, po, nblk, ktok0, width = meta(s)
        par = s % 2
        P.waitv("t", "s", sn[("sexp", par)])
        P.waitv("t", "v", sn.get(("ptpsfree", par)))
        for j in range(nblk):
            i = P.PE.transpose(ps_pt[par][:, j * 128:(j + 1) * 128],
                               Pb[:, par * 640 + j * 128:par * 640 + (j + 1) * 128],
                               P.consts["cI"][:, :])
        P.tick("t", i)
        sn[("tdone", par)] = P.cnt["t"]
        sn[("pbfree", par)] = P.cnt["t"]

    def stE(s):
        w, qt, h, qw, rt, po, nblk, ktok0, width = meta(s)
        b = (po + 64) % 128
        PTb = PT[:, (w % 2) * 4096:(w % 2 + 1) * 4096]
        P.waitv("t", "v", P.cnt["v"])
        P.waitv("t", "v", sn.get("osbdone"))
        slots = list(range(4, 8)) if qw == 0 else list(range(0, 8))
        first = True
        for s_ in slots:
            vt = (s_ - 4) if qw == 0 else (qw - 1) * 4 + s_
            lhsv = vtm[:, vt * 512 + h * 64:vt * 512 + h * 64 + 64]
            i = P.PE.matmul(ps_o[po:po + 64, :], lhsv, PTb[:, s_ * 512:(s_ + 1) * 512],
                            start=first, stop=(s_ == slots[-1]), tile_position=(0, po))
            i = P.PE.matmul(ps_rr[b:b + 1, :], P.consts["cOnesB"][:, 0:1],
                            PTb[:, s_ * 512:(s_ + 1) * 512],
                            start=first, stop=(s_ == slots[-1]), tile_position=(0, b))
            first = False
        P.tick("t", i)
        sn[("avdone", w % 2)] = P.cnt["t"]
        P.waitv("s", "t", sn[("avdone", w % 2)])
        P.waitv("s", "t", sn.get(("rbdone", b)))
        i = P.S.activation(rsr[b:b + 1, :], ps_rr[b:b + 1, :], AF.Ln)
        P.S.drain()
        i = P.S.activation(rsr[b:b + 1, :], rsr[b:b + 1, :], AF.Exp, scale=-1.0)
        P.tick("s", i)
        pend[0] = (w, rt, po, qw, b, P.cnt["s"])

    def stD(s):
        w, qt, h, qw, rt, po, nblk, ktok0, width = meta(s)
        par = s % 2
        PTb = PT[:, (w % 2) * 4096:(w % 2 + 1) * 4096]
        P.waitv("v", "t", sn[("tdone", par)])
        P.waitv("v", "t", sn.get(("avdone", w % 2)))
        qcol = qt * 128
        base = (w % 2) * 4096
        slot0 = qt if qw > 0 else 4
        d0 = base + slot0 * 512 + qcol
        i = P.V.tensor_copy(
            PT[:, d0:d0 + nblk * 512].rearrange("p (j t) -> p j t", j=nblk)[:, :, 0:128],
            ps_pt[par][:, 0:nblk * 128].rearrange("p (j t) -> p j t", j=nblk))
        lo = 0 if qw > 0 else 4
        for a, b2 in [(lo, slot0), (slot0 + nblk, 8)]:
            if b2 > a:
                m0 = base + a * 512 + qcol
                i = P.V.memset(
                    PT[:, m0:m0 + (b2 - a) * 512].rearrange("p (j t) -> p j t",
                                                            j=b2 - a)[:, :, 0:128], 0.0)
        P.tick("v", i)
        sn[("ptpsfree", par)] = P.cnt["v"]
        if qt == 3:
            stE(s)

    def emit_rb_osb():
        w, rt, po, qw, b, rsrs = pend[0]
        pend[0] = None
        P.waitv("t", "s", rsrs)
        i = P.PE.matmul(ps_rr[po:po + 64, :], P.consts["cOnesRowF"][b:b + 1, 0:64],
                        rsr[b:b + 1, :], start=True, stop=True, tile_position=(b, po))
        P.tick("t", i)
        sn[("rbdone", b)] = P.cnt["t"]
        P.waitv("v", "t", sn[("rbdone", b)])
        i = P.V.tensor_copy(rbq[po:po + 64, :], ps_rr[po:po + 64, :])
        i = P.V.tensor_tensor(osb[po:po + 64, rt * T + qw * TC:rt * T + (qw + 1) * TC],
                              ps_o[po:po + 64, :], rbq[po:po + 64, :], ALU.mult)
        P.tick("v", i)
        sn["osbdone"] = P.cnt["v"]

    NS = len(steps)
    for it in range(NS + 5):
        if it < NS:
            stA(it)
        if pend[0] is not None:
            emit_rb_osb()
        if 1 <= it and it - 1 < NS:
            stB(it - 1)
        if 2 <= it and it - 2 < NS:
            stX(it - 2)
        if 3 <= it and it - 3 < NS:
            stC(it - 3)
        if 4 <= it and it - 4 < NS:
            stD(it - 4)
    if pend[0] is not None:
        emit_rb_osb()
    P.wait("t", "v")
    P.free(q4, k4, vtm, nmax, smf, Pb, PT, rsr, rbq)
    P.free(ps_sim[0], ps_sim[1], ps_pt[0], ps_pt[1], ps_o, ps_rr)
    P.dump(dump_pref + "o", osb[:, :], "v")
    if P.stop(dump_pref + "attn"):
        return xp

    wo = P.sbuf("ffc_wo", [128, 4 * D], BF16)
    P.sdma("g", "woS", wo[:, :].rearrange("p (ik m) -> p ik m", ik=4),
           P.dram["L0_ffn_wo"][l].rearrange("(ik p) m -> p ik m", p=128))
    ps_a = P.psum("ffn_psa", [128, TC], F32)
    ps_b2 = P.psum("ffn_psb", [128, TC], F32)
    P.wait("t", "woS")
    snap = {}
    for kt in range(KD):
        for c in range(NCH):
            par = (kt * NCH + c) % 2
            pss = ps_a if par == 0 else ps_b2
            P.waitv("t", "v", snap.get(par))
            for ik in range(4):
                i = P.PE.matmul(pss[:, :], wo[:, ik * D + kt * 128:ik * D + (kt + 1) * 128],
                                osb[:, ik * T + c * TC:ik * T + (c + 1) * TC],
                                start=(ik == 0), stop=(ik == 3))
            P.tick("t", i)
            P.wait("v", "t")
            i = P.V.tensor_add(xpt(kt)[:, c * TC:(c + 1) * TC],
                               xpt(kt)[:, c * TC:(c + 1) * TC], pss[:, :])
            P.tick("v", i)
            snap[par] = P.cnt["v"]
    P.free(osb, wo, ps_a, ps_b2)
    P.dump(dump_pref + "x_attn", xp[:, :], "v")
    if P.stop(dump_pref + "wo"):
        return xp

    lnx = P.sbuf("ffn_lnx", [128, KD * T], BF16, hi=True)
    sh2 = P.sbuf("ffn_sh2", [128, KD * T], BF16, hi=True)
    emit_shift(P, xp, sh2)

    def wr(kt, tmp, g, b):
        P.wait("s", "v")
        i = P.S.activation(lnx[:, kt * T:(kt + 1) * T], tmp[:, :], AF.Identity, bias=b, scale=g)
        P.tick("s", i)

    emit_ln(P, [sh2[:, kt * T:(kt + 1) * T] for kt in range(KD)], wr,
            gamma_ap=wsb["lng"], beta_ap=wsb["lnb"])
    P.free(sh2)
    P.dump(dump_pref + "lnx2", lnx[:, :], "s")
    P.tick("v", P.V.nop())
    P.wait("y", "v")
    P.dma("y", P.dram["scr_xp"].rearrange("(kt p) w -> p kt w", p=128),
          xp[:, :].rearrange("p (kt w) -> p kt w", kt=KD))
    P.wait("y", "dy")
    P.free(xp)
    if P.stop(dump_pref + "ln2i"):
        return None

    gated = P.sbuf("ffn_gated", [128, 22 * T], BF16)
    w1a = P.sbuf("ffn_w1a", [128, 2 * KD * 128], BF16)
    w1g = P.sbuf("ffn_w1g", [128, 2 * KD * 128], BF16)
    ps_w = [P.psum("ffn_psw%d" % k, [128, TC], F32) for k in range(4)]
    w1d = P.dram["L0_ffn_w1"]
    snap = {}
    for j, (off, sz) in enumerate(HT):
        par = j % 2
        base = par * KD * 128
        sname = "w1p%d" % par
        P.waitv("g", "t", snap.get(("t", par)))
        P.sdma("g", sname, w1a[:, base:base + KD * sz].rearrange("p (kt m) -> p kt m", kt=KD),
               w1d[l, :, off:off + sz].rearrange("(kt p) m -> p kt m", p=128))
        P.sdma("g", sname, w1g[:, base:base + KD * sz].rearrange("p (kt m) -> p kt m", kt=KD),
               w1d[l, :, HID + off:HID + off + sz].rearrange("(kt p) m -> p kt m", p=128))
        dgs = P.cnt[sname]
        for c in range(NCH):
            cp = c % 2
            ps_wa, ps_wg = ps_w[cp * 2], ps_w[cp * 2 + 1]
            P.waitv("t", sname, dgs)
            P.waitv("t", "v", snap.get(("v", cp)))
            for kt in range(KD):
                i = P.PE.matmul(ps_wa[0:sz, :], w1a[:, base + kt * sz:base + (kt + 1) * sz],
                                lnx[:, kt * T + c * TC:kt * T + (c + 1) * TC],
                                start=(kt == 0), stop=(kt == KD - 1))
            for kt in range(KD):
                i = P.PE.matmul(ps_wg[0:sz, :], w1g[:, base + kt * sz:base + (kt + 1) * sz],
                                lnx[:, kt * T + c * TC:kt * T + (c + 1) * TC],
                                start=(kt == 0), stop=(kt == KD - 1))
            P.tick("t", i)
            gsl = gated[0:sz, j * T + c * TC:j * T + (c + 1) * TC]
            P.wait("s", "t")
            i = P.S.activation(gsl, ps_wg[0:sz, :], AF.Gelu, bias=wsb["b1g"][0:sz, j:j + 1])
            P.tick("s", i)
            P.wait("v", "s")
            i = P.V.scalar_tensor_tensor(gsl, ps_wa[0:sz, :], wsb["b1a"][0:sz, j:j + 1], gsl,
                                         ALU.add, ALU.mult)
            P.tick("v", i)
            snap[("v", cp)] = P.cnt["v"]
        snap[("t", par)] = P.cnt["t"]
    P.free(w1a, w1g, *ps_w)
    P.free(lnx)
    P.dump(dump_pref + "gated", gated[:, :], "v")
    if P.stop(dump_pref + "gated"):
        return None

    xp = P.sbuf("xp_arena", [128, KD * T], F32, hi=True)
    P.dma("y", xp[:, :].rearrange("p (kt w) -> p kt w", kt=KD),
          P.dram["scr_xp"].rearrange("(kt p) w -> p kt w", p=128))
    w2s = P.sbuf("ffn_w2s", [128, 2 * 22 * 128], BF16)
    ps_a2 = P.psum("ffn_ps2a", [128, TC], F32)
    ps_b3 = P.psum("ffn_ps2b", [128, TC], F32)
    w2d = P.dram["L0_ffn_w2"]
    P.wait("v", "dy")
    snap = {}
    for m in range(KD):
        par = m % 2
        base = par * 22 * 128
        P.waitv("g", "t", snap.get(("t", par)))
        sname = "w2p%d" % par
        for j, (off, sz) in enumerate(HT):
            P.sdma("g", sname, w2s[0:sz, base + j * 128:base + (j + 1) * 128],
                   w2d[l, off:off + sz, m * 128:(m + 1) * 128])
        snap[("dg", par)] = (sname, P.cnt[sname])
        for c in range(NCH):
            cp = c % 2
            pss = ps_a2 if cp == 0 else ps_b3
            P.waitv("t", *snap[("dg", par)])
            P.waitv("t", "v", snap.get(("v", cp)))
            for j, (off, sz) in enumerate(HT):
                i = P.PE.matmul(pss[:, :], w2s[0:sz, base + j * 128:base + j * 128 + 128],
                                gated[0:sz, j * T + c * TC:j * T + (c + 1) * TC],
                                start=(j == 0), stop=(j == 21))
            P.tick("t", i)
            P.wait("v", "t")
            i = P.V.scalar_tensor_tensor(xpt(m)[:, c * TC:(c + 1) * TC], pss[:, :],
                                         wsb["b2"][:, m:m + 1], xpt(m)[:, c * TC:(c + 1) * TC],
                                         ALU.add, ALU.add)
            P.tick("v", i)
            snap[("v", cp)] = P.cnt["v"]
        snap[("t", par)] = P.cnt["t"]
    P.free(w2s, ps_a2, ps_b3, gated)
    P.dump(dump_pref + f"x_l{l}", xp[:, :], "v")
    return xp


def emit_ffn_call(P, xp, dump_pref=""):
    for l in range(2):
        wsb = {
            "qs": P.par["qs"][:, l:l + 1], "ks": P.par["ks"][:, l:l + 1],
            "lng": P.par["lng"][:, l * KD:(l + 1) * KD], "lnb": P.par["lnb"][:, l * KD:(l + 1) * KD],
            "b1a": P.par["b1a"][:, l * 22:(l + 1) * 22], "b1g": P.par["b1g"][:, l * 22:(l + 1) * 22],
            "b2": P.par["b2"][:, l * KD:(l + 1) * KD],
        }
        P.barrier()
        xp = emit_ffn_layer(P, l, xp, wsb, dump_pref=dump_pref + f"l{l}_")
        if P.stop_flag:
            break
        P.barrier()
        if P.stop(dump_pref + f"l{l}"):
            break
    return xp



def emit_mamba(P, lnx, dump_pref=""):
    xsp = P.sbuf("mb_xsp", [128, 2 * TC], BF16)
    ztile = P.sbuf("mb_zt", [128, KI * 4], BF16)
    i = P.V.memset(ztile[:, :], 0.0)
    P.tick("v", i)
    P.wait("y", "v")
    P.sdma("y", "xpadS", P.dram["scr_xs"][:, 0:4].rearrange("(kt p) w -> p kt w", p=128),
           ztile[:, :].rearrange("p (kt w) -> p kt w", kt=KI))
    P.sdma("y", "xpadS", P.dram["scr_xs"][:, XSW - 4:XSW].rearrange("(kt p) w -> p kt w", p=128),
           ztile[:, :].rearrange("p (kt w) -> p kt w", kt=KI))
    inw = P.sbuf("mb_inw", [128, 2 * KD * 128], BF16)
    zbuf = P.sbuf("mb_zb", [128, 2 * TC], BF16)
    ps_i = P.psum("mb_psi", [128, TC], F32)
    ps_i2 = P.psum("mb_psi2", [128, TC], F32)
    inwd = P.dram["L0_attn_in_w"]
    snap = {}
    for m in range(2 * KI):
        par = m % 2
        base = par * KD * 128
        P.waitv("g", "t", snap.get(("t", par)))
        sname = "inwp%d" % par
        P.sdma("g", sname, inw[:, base:base + KD * 128].rearrange("p (kt mm) -> p kt mm", kt=KD),
               inwd[:, m * 128:(m + 1) * 128].rearrange("(kt p) mm -> p kt mm", p=128))
        snap[("dg", par)] = (sname, P.cnt[sname])
        for c in range(NCH):
            cp = c % 2
            pss = ps_i if cp == 0 else ps_i2
            P.waitv("t", *snap[("dg", par)])
            P.waitv("t", "s", snap.get(("s", cp)))
            for kt in range(KD):
                i = P.PE.matmul(pss[:, :], inw[:, base + kt * 128:base + (kt + 1) * 128],
                                lnx[:, kt * T + c * TC:kt * T + (c + 1) * TC],
                                start=(kt == 0), stop=(kt == KD - 1))
            P.tick("t", i)
            P.wait("s", "t")
            if m < KI:
                xp2 = c % 2
                xn = "xswp%d" % xp2
                P.waitv("s", xn, snap.get(("xdy", xp2)))
                i = P.S.activation(xsp[:, xp2 * TC:(xp2 + 1) * TC], pss[:, :], AF.Identity)
                P.tick("s", i)
                P.wait("y", "s")
                P.sdma("y", xn, P.dram["scr_xs"][m * 128:(m + 1) * 128, 4 + c * TC:4 + (c + 1) * TC],
                       xsp[:, xp2 * TC:(xp2 + 1) * TC])
                snap[("xdy", xp2)] = P.cnt[xn]
            else:
                zp = c % 2
                zn = "zspp%d" % zp
                P.waitv("s", zn, snap.get(("zdy", zp)))
                i = P.S.activation(zbuf[:, zp * TC:(zp + 1) * TC], pss[:, :], AF.Silu)
                P.tick("s", i)
                P.wait("y", "s")
                P.sdma("y", zn, P.dram["scr_z"][(m - KI) * 128:(m - KI + 1) * 128, c * TC:(c + 1) * TC],
                       zbuf[:, zp * TC:(zp + 1) * TC])
                snap[("zdy", zp)] = P.cnt[zn]
            snap[("s", cp)] = P.cnt["s"]
        snap[("t", par)] = P.cnt["t"]
    P.free(inw, zbuf, ps_i, ps_i2, lnx, xsp, ztile)
    xsw_fin = (P.cnt.get("xswp0", 0), P.cnt.get("xswp1", 0), P.cnt.get("xpadS", 0))
    P.barrier()
    if P.stop("mamba_xs"):
        return

    for dirn in range(2):
        sfx = "" if dirn == 0 else "_b"
        wn = sfx
        pA = P.par["A" + sfx]
        pcw = P.par["cw" + sfx]
        pcb = P.par["cb" + sfx]
        pdtb = P.par["dtb" + sfx]
        pD = P.par["Dsk" + sfx]
        xst = P.sbuf("mb_xst", [128, 2 * XSW], BF16)
        xld = {}
        vconv = {}
        P.waitv("y", "xswp0", xsw_fin[0])
        P.waitv("y", "xswp1", xsw_fin[1])
        P.waitv("y", "xpadS", xsw_fin[2])

        def load_xs(kt):
            par = kt % 2
            xn = "xsL%d" % par
            P.waitv("y", "v", vconv.get(par))
            P.sdma("y", xn, xst[:, par * XSW:(par + 1) * XSW],
                   P.dram["scr_xs"][kt * 128:(kt + 1) * 128, :])
            xld[par] = P.cnt[xn]

        xprojT = P.sbuf("mb_xproj", [128, KI * 96], BF16)
        dtwT = P.sbuf("mb_dtw", [64, KI * 128], BF16)
        P.dma("g", xprojT[:, :].rearrange("p (kt m) -> p kt m", kt=KI),
              P.dram["L0_attn_xproj_w" + wn].rearrange("(kt p) m -> p kt m", p=128))
        P.dma("g", dtwT[:, :], P.dram["L0_attn_dt_w" + wn])
        xproj = xprojT[:, :]
        dtw = dtwT[:, :]
        ydram = P.dram["scr_yf" if dirn == 0 else "scr_yb"]

        def conv_tile(kt, dst_f32, t0, tw):
            tb = (kt % 2) * XSW
            P.waitv("v", "xsL%d" % (kt % 2), xld[kt % 2])
            for k in range(4):
                if dirn == 0:
                    src = xst[:, tb + 1 + k + t0:tb + 1 + k + t0 + tw]
                else:
                    st = tb + T + 6 - k - t0
                    src = xst[:, st:st - tw:-1]
                w_ap = pcw[:, kt * 4 + k:kt * 4 + k + 1]
                if k == 0:
                    i = P.V.tensor_scalar_mul(dst_f32, src, w_ap)
                else:
                    i = P.V.scalar_tensor_tensor(dst_f32, src, w_ap, dst_f32, ALU.mult, ALU.add)
            P.tick("v", i)

        xcb = P.sbuf("mbA_xc", [128, 2 * T], BF16)
        accA = P.sbuf("mbA_acc", [128, T], F32)
        dtpre = P.sbuf("mbA_dtpre", [64, T], BF16)
        bcrow = P.sbuf("mbA_bcrow", [48, T], BF16)
        ps_dt = P.psum("mbA_psdt", [64, T], F32)
        ps_bc = P.psum("mbA_psbc", [48, T], F32)
        load_xs(0)
        for kt in range(KI):
            par = kt % 2
            P.wait("v", "t")
            P.wait("v", "s")
            conv_tile(kt, accA[:, :], 0, T)
            vconv[kt % 2] = P.cnt["v"]
            if kt + 1 < KI:
                load_xs(kt + 1)
            P.wait("s", "v")
            P.wait("s", "t")
            i = P.S.activation(xcb[:, par * T:(par + 1) * T], accA[:, :], AF.Silu,
                               bias=pcb[:, kt:kt + 1])
            P.tick("s", i)
            P.wait("t", "s")
            for c in range(NCH):
                csl = slice(c * TC, (c + 1) * TC)
                mv = xcb[:, par * T + c * TC:par * T + (c + 1) * TC]
                i = P.PE.matmul(ps_dt[0:64, csl], xproj[:, kt * 96:kt * 96 + 64], mv,
                                start=(kt == 0), stop=(kt == KI - 1))
                i = P.PE.matmul(ps_bc[0:16, csl], xproj[:, kt * 96 + 64:kt * 96 + 80], mv,
                                start=(kt == 0), stop=(kt == KI - 1))
                i = P.PE.matmul(ps_bc[32:48, csl], xproj[:, kt * 96 + 80:kt * 96 + 96], mv,
                                start=(kt == 0), stop=(kt == KI - 1), tile_position=(0, 32))
            P.tick("t", i)
        P.wait("s", "t")
        i = P.S.activation(dtpre[:, :], ps_dt[0:64, :], AF.Identity)
        i = P.S.activation(bcrow[0:16, :], ps_bc[0:16, :], AF.Identity)
        i = P.S.activation(bcrow[32:48, :], ps_bc[32:48, :], AF.Identity)
        P.tick("s", i)
        P.wait("y", "s")
        P.dma("y", P.dram["scr_b"], bcrow[0:16, :])
        P.dma("y", P.dram["scr_c"], bcrow[32:48, :])
        P.wait("y", "dy")
        P.free(xcb, accA, ps_bc, bcrow)
        P.dump(dump_pref + "dtpre" + sfx, dtpre[:, :], "s", parts=64)

        NGS = 4
        NGRP = DST // NGS
        SEGW = TH + 1
        GWS = NGS * SEGW
        state = P.sbuf("mb_state", [128, KI * DST], F32)
        bcb = P.sbuf("mb_bcb", [128, 2 * DST * TH], BF16)
        dts = P.sbuf("mb_dts", [128, TH], F32)
        xcs2 = P.sbuf("mb_xcs", [128, 2 * TH], BF16)
        us = P.sbuf("mb_us", [128, TH], BF16)
        NB = 3
        dAg = P.sbuf("mb_dAg", [128, 2 * GWS], F32)
        dbug = P.sbuf("mb_dbug", [128, NB * GWS], BF16)
        cyg = P.sbuf("mb_cyg", [128, NGS * TH], BF16)
        hbg = P.sbuf("mb_hbg", [128, GWS], BF16)
        yg = P.sbuf("mb_yg", [128, NGRP * 2 * TH], BF16)
        ysum = P.sbuf("mb_ysum", [128, TH], BF16)
        yout = P.sbuf("mb_yout", [128, TH], BF16)
        yrev = P.sbuf("mb_yrev", [128, TH], BF16) if dirn == 1 else None
        ps_d2 = P.psum("mb_psd2", [128, TC], F32)
        acc2 = P.sbuf("mb_acc2", [128, 2 * TH], F32)
        accS = acc2[:, 0:TH]
        sp2 = acc2[:, TH:2 * TH]
        i = P.V.memset(dAg[:, 0:2 * GWS:SEGW], 0.0)
        P.tick("v", i)
        for nb in range(NB):
            i = P.G.memset(dbug[:, nb * GWS:(nb + 1) * GWS:SEGW], 0.0)
        P.tick("g", i)

        gctr = 0
        snap_scan = None
        snap_scan_p = {}
        state_v = None
        for half in range(2):
            t0 = half * TH
            P.wait("y", "g")
            P.wait("y", "v")
            for n in range(DST):
                P.sdma("y", "bcbS", bcb[:, n * TH:(n + 1) * TH],
                       P.dram["scr_b"][n:n + 1, t0:t0 + TH].broadcast_to([128, TH]))
                P.sdma("y", "bcbS", bcb[:, (DST + n) * TH:(DST + n + 1) * TH],
                       P.dram["scr_c"][n:n + 1, t0:t0 + TH].broadcast_to([128, TH]))
            bc_dy = P.cnt["bcbS"]
            load_xs(0)
            usv_h = [0]

            def chain(kt):
                P.wait("v", "s")
                P.wait("v", "g")
                conv_tile(kt, accS, t0, TH)
                vconv[kt % 2] = P.cnt["v"]
                if kt + 1 < KI:
                    load_xs(kt + 1)
                P.wait("s", "v")
                i = P.S.activation(xcs2[:, (kt % 2) * TH:(kt % 2 + 1) * TH], accS, AF.Silu,
                                   bias=pcb[:, kt:kt + 1])
                P.tick("s", i)
                for c in range(2):
                    P.wait("t", "v")
                    P.wait("t", "s")
                    i = P.PE.matmul(ps_d2[:, :], dtw[:, kt * 128:(kt + 1) * 128],
                                    dtpre[:, t0 + c * TC:t0 + (c + 1) * TC], start=True, stop=True)
                    P.tick("t", i)
                    P.wait("v", "t")
                    i = P.V.tensor_scalar(accS[:, c * TC:(c + 1) * TC], ps_d2[:, :],
                                          pdtb[:, kt:kt + 1], 0.0, ALU.add, ALU.max)
                    i = P.V.tensor_scalar(sp2[:, c * TC:(c + 1) * TC], ps_d2[:, :],
                                          pdtb[:, kt:kt + 1], None, ALU.add)
                    P.tick("v", i)
                P.wait("s", "v")
                i = P.S.activation(sp2, sp2, AF.Abs)
                P.S.drain()
                i = P.S.activation(sp2, sp2, AF.Exp, scale=-1.0)
                P.S.drain()
                i = P.S.activation(sp2, sp2, AF.Ln, bias=1.0)
                P.tick("s", i)
                P.wait("v", "s")
                i = P.V.tensor_add(dts[:, :], accS, sp2)
                i = P.V.tensor_tensor(us[:, :], dts[:, :],
                                      xcs2[:, (kt % 2) * TH:(kt % 2 + 1) * TH], ALU.mult)
                P.tick("v", i)
                usv_h[0] = P.cnt["v"]

            chain(0)
            for kt in range(KI):
                us_v = usv_h[0]
                xc = xcs2[:, (kt % 2) * TH:(kt % 2 + 1) * TH]
                for g in range(NGRP):
                    gp = gctr % NB
                    ap = gctr % 2
                    gctr += 1
                    n0 = g * NGS
                    s0 = kt * DST + n0
                    P.waitv("s", "v", max(us_v, snap_scan_p.get(("a", ap)) or 0))
                    for j in range(NGS):
                        i = P.S.activation(dAg[:, ap * GWS + j * SEGW + 1:ap * GWS + (j + 1) * SEGW],
                                           dts[:, :], AF.Exp, scale=pA[:, s0 + j:s0 + j + 1])
                    P.tick("s", i)
                    sexp_g = P.cnt["s"]
                    P.waitv("g", "v", max(us_v, snap_scan_p.get(("b", gp)) or 0, state_v or 0))
                    P.waitv("g", "bcbS", bc_dy)
                    for j in range(NGS):
                        i = P.G.tensor_tensor(dbug[:, gp * GWS + j * SEGW + 1:gp * GWS + (j + 1) * SEGW],
                                              us[:, :], bcb[:, (n0 + j) * TH:(n0 + j + 1) * TH],
                                              ALU.mult)
                    if half == 1:
                        i = P.G.tensor_copy(dbug[:, gp * GWS:gp * GWS + GWS:SEGW],
                                            state[:, s0:s0 + NGS])
                    P.tick("g", i)
                    gfill_g = P.cnt["g"]
                    P.waitv("v", "s", sexp_g)
                    P.waitv("v", "g", gfill_g)
                    i = P.V.tensor_tensor_scan(hbg[:, :], dAg[:, ap * GWS:(ap + 1) * GWS],
                                               dbug[:, gp * GWS:(gp + 1) * GWS],
                                               0.0, ALU.mult, ALU.add)
                    P.tick("v", i)
                    snap_scan = P.cnt["v"]
                    snap_scan_p[("a", ap)] = snap_scan
                    snap_scan_p[("b", gp)] = snap_scan
                    if half == 0:
                        i = P.V.tensor_copy(state[:, s0:s0 + NGS], hbg[:, TH:GWS:SEGW])
                    i = P.V.tensor_tensor(
                        cyg[:, :].rearrange("p (n t) -> p n t", n=NGS),
                        hbg[:, :].rearrange("p (n t) -> p n t", n=NGS)[:, :, 1:SEGW],
                        bcb[:, DST * TH + n0 * TH:DST * TH + (n0 + NGS) * TH]
                           .rearrange("p (n t) -> p n t", n=NGS),
                        ALU.mult)
                    i = P.V.tensor_add(yg[:, g * 2 * TH:(g + 1) * 2 * TH],
                                       cyg[:, 0:2 * TH], cyg[:, 2 * TH:4 * TH])
                    P.tick("v", i)
                if kt + 1 < KI:
                    chain(kt + 1)
                if "yspS" in P.cnt:
                    P.waitv("v", "yspS", P.cnt["yspS"])
                i = P.V.tensor_add(cyg[:, 0:2 * TH], yg[:, 0:2 * TH], yg[:, 2 * TH:4 * TH])
                i = P.V.tensor_add(cyg[:, 2 * TH:4 * TH], yg[:, 4 * TH:6 * TH], yg[:, 6 * TH:8 * TH])
                i = P.V.tensor_add(yg[:, 0:2 * TH], cyg[:, 0:2 * TH], cyg[:, 2 * TH:4 * TH])
                i = P.V.tensor_add(ysum[:, :], yg[:, 0:TH], yg[:, TH:2 * TH])
                i = P.V.scalar_tensor_tensor(yout[:, :], xc, pD[:, kt:kt + 1], ysum[:, :],
                                             ALU.mult, ALU.add)
                P.tick("v", i)
                if dirn == 0:
                    P.wait("y", "v")
                    P.sdma("y", "yspS", ydram[kt * 128:(kt + 1) * 128, t0:t0 + TH], yout[:, :])
                else:
                    i = P.V.tensor_copy(yrev[:, :], yout[:, TH - 1::-1])
                    P.tick("v", i)
                    P.wait("y", "v")
                    P.sdma("y", "yspS", ydram[kt * 128:(kt + 1) * 128, T - t0 - TH:T - t0], yrev[:, :])
            if half == 0:
                state_v = P.cnt["v"]
        P.free(dtpre, ps_dt, state, bcb, dts, xcs2, us, dAg, dbug, cyg, acc2,
               hbg, yg, ysum, yout, ps_d2, xprojT, dtwT, xst)
        if yrev is not None:
            P.free(yrev)
        P.barrier()
        if P.stop("mamba_dir%d" % dirn):
            return

    ych = P.sbuf("mg_ych", [128, 32 * TC], BF16)
    mw = P.sbuf("mg_mw", [128, 2 * 32 * 128], BF16)
    zs = P.sbuf("mg_zs", [128, 2 * TC], BF16)
    ymb = P.sbuf("mg_ym", [128, 2 * TC], BF16)
    ps_g = P.psum("mg_psg", [128, TC], F32)
    ps_g2 = P.psum("mg_psg2", [128, TC], F32)
    mwd = P.dram["L0_attn_merge_w"]
    snap = {}
    for c in range(NCH):
        P.wait("y", "t")
        P.sdma("y", "ychS", ych[:, 0:16 * TC].rearrange("p (kt w) -> p kt w", kt=KI),
               P.dram["scr_yf"][:, c * TC:(c + 1) * TC].rearrange("(kt p) w -> p kt w", p=128))
        P.sdma("y", "ychS", ych[:, 16 * TC:32 * TC].rearrange("p (kt w) -> p kt w", kt=KI),
               P.dram["scr_yb"][:, c * TC:(c + 1) * TC].rearrange("(kt p) w -> p kt w", p=128))
        ych_dy = P.cnt["ychS"]
        for m in range(KI):
            par = m % 2
            base = par * 32 * 128
            P.waitv("g", "t", snap.get(("t", par)))
            mn = "mwp%d" % par
            P.sdma("g", mn, mw[:, base:base + 32 * 128].rearrange("p (kt mm) -> p kt mm", kt=32),
                   mwd[:, m * 128:(m + 1) * 128].rearrange("(kt p) mm -> p kt mm", p=128))
            zn2 = "mzsp%d" % par
            P.waitv("y", "v", snap.get(("v", par)))
            P.sdma("y", zn2, zs[:, par * TC:(par + 1) * TC],
                   P.dram["scr_z"][m * 128:(m + 1) * 128, c * TC:(c + 1) * TC])
            pss = ps_g if par == 0 else ps_g2
            P.waitv("t", mn, P.cnt[mn])
            P.waitv("t", "ychS", ych_dy)
            P.waitv("t", "v", snap.get(("v", par)))
            for kt in range(32):
                i = P.PE.matmul(pss[:, :], mw[:, base + kt * 128:base + (kt + 1) * 128],
                                ych[:, kt * TC:(kt + 1) * TC], start=(kt == 0), stop=(kt == 31))
            P.tick("t", i)
            P.wait("v", "t")
            P.waitv("v", zn2, P.cnt[zn2])
            ymn = "ymbp%d" % par
            if ymn in P.cnt:
                P.waitv("v", ymn, P.cnt[ymn])
            i = P.V.tensor_tensor(ymb[:, par * TC:(par + 1) * TC], pss[:, :],
                                  zs[:, par * TC:(par + 1) * TC], ALU.mult)
            P.tick("v", i)
            snap[("v", par)] = P.cnt["v"]
            snap[("t", par)] = P.cnt["t"]
            P.wait("y", "v")
            P.sdma("y", ymn, P.dram["scr_ym"][m * 128:(m + 1) * 128, c * TC:(c + 1) * TC],
                   ymb[:, par * TC:(par + 1) * TC])
    P.free(ych, mw, zs, ymb, ps_g, ps_g2)
    P.barrier()
    if P.stop("mamba_merge"):
        return


def emit_outw_and_residual(P, x, ab2):
    ymch = P.sbuf("ow_ymch", [128, KI * TC], BF16)
    ow = P.sbuf("ow_w", [128, 2 * KI * 128], BF16)
    ps_o = P.psum("ow_ps", [128, TC], F32)
    ps_o2 = P.psum("ow_ps2", [128, TC], F32)
    tmp = P.sbuf("ow_tmp", [128, TC], F32)
    owd = P.dram["L0_attn_out_w"]
    a_b = ab2[:, 0:T]
    b_b = ab2[:, T:2 * T]
    snap = {}
    for c in range(NCH):
        P.wait("y", "t")
        P.sdma("y", "ymchS", ymch[:, :].rearrange("p (kt w) -> p kt w", kt=KI),
               P.dram["scr_ym"][:, c * TC:(c + 1) * TC].rearrange("(kt p) w -> p kt w", p=128))
        ych_dy = P.cnt["ymchS"]
        for mo in range(KD):
            par = mo % 2
            base = par * KI * 128
            P.waitv("g", "t", snap.get(("t", par)))
            on = "owp%d" % par
            P.sdma("g", on, ow[:, base:base + KI * 128].rearrange("p (kt mm) -> p kt mm", kt=KI),
                   owd[:, mo * 128:(mo + 1) * 128].rearrange("(kt p) mm -> p kt mm", p=128))
            pss = ps_o if par == 0 else ps_o2
            P.waitv("t", on, P.cnt[on])
            P.waitv("t", "ymchS", ych_dy)
            P.waitv("t", "v", snap.get(("v", par)))
            for kt in range(KI):
                i = P.PE.matmul(pss[:, :], ow[:, base + kt * 128:base + (kt + 1) * 128],
                                ymch[:, kt * TC:(kt + 1) * TC], start=(kt == 0), stop=(kt == KI - 1))
            P.tick("t", i)
            xsl = x[:, mo * T + c * TC:mo * T + (c + 1) * TC]
            P.wait("v", "t")
            i = P.V.tensor_tensor(tmp[:, :], xsl, a_b[:, c * TC:(c + 1) * TC], ALU.mult)
            i = P.V.tensor_sub(tmp[:, :], tmp[:, :], b_b[:, c * TC:(c + 1) * TC])
            i = P.V.tensor_add(tmp[:, :], tmp[:, :], pss[:, :])
            i = P.V.tensor_add(xsl, xsl, tmp[:, :])
            P.tick("v", i)
            snap[("v", par)] = P.cnt["v"]
            snap[("t", par)] = P.cnt["t"]
    P.free(ymch, ow, ps_o, ps_o2, tmp)


def self_val(P, e):
    return P.cnt[e]



def build_program(want_dumps=(), stop_after=None):
    nc = bass.Bass("TRN2", target_bir_lowering=False)
    P = Prog(nc, want_dumps)
    P.stop_after = stop_after

    P.dram["xt"] = nc.dram_tensor("xt", [D, T], F32, kind="ExternalInput").ap()
    P.dram["xo"] = nc.dram_tensor("xo", [D, T], F32, kind="ExternalOutput").ap()
    for nm, (shape, dt) in {**CONST_SPECS, **ROT_SPECS}.items():
        P.dram[nm] = nc.dram_tensor(nm, shape, dt, kind="ExternalInput").ap()
    for nm, shape in WEIGHT_SPECS.items():
        P.dram[nm] = nc.dram_tensor(nm, shape, F32, kind="ExternalInput").ap()
    for nm, shape, dt in [
        ("scr_x", [D, T], F32), ("scr_z", [DI, T], BF16),
        ("scr_yf", [DI, T], BF16), ("scr_yb", [DI, T], BF16),
        ("scr_ym", [DI, T], BF16), ("scr_b", [DST, T], BF16), ("scr_c", [DST, T], BF16),
        ("scr_ab", [128, 2 * T], F32), ("scr_rows", [2, T], F32),
        ("scr_xp", [D, T], F32), ("scr_xs", [DI, XSW], BF16),
    ]:
        P.dram[nm] = nc.dram_tensor(nm, shape, dt, kind="Internal").ap()
    if want_dumps:
        P.dbg = nc.dram_tensor("dbg", [128, 400000], F32, kind="ExternalOutput").ap()

    for nm, (shape, dt) in CONST_SPECS.items():
        P.consts[nm] = P.sbuf(nm, shape, dt)
        P.dma("y", P.consts[nm][:, :], P.dram[nm])
    nc.const_aps.register(F32, EPS, P.consts["cMisc"][:, 0:1])

    def par_alloc(nm, shape, dtype=F32):
        P.par[nm] = P.sbuf("par_" + nm, shape, dtype)
        return P.par[nm]

    qsd, ksd = P.dram["L0_ffn_q_scale"], P.dram["L0_ffn_k_scale"]
    par_alloc("qs", [128, 2]); par_alloc("ks", [128, 2])
    for l in range(2):
        for half in range(2):
            P.dma("y", P.par["qs"][half * 64:(half + 1) * 64, l:l + 1],
                  qsd[l, :].rearrange("(d one) -> d one", one=1))
            P.dma("y", P.par["ks"][half * 64:(half + 1) * 64, l:l + 1],
                  ksd[l, :].rearrange("(d one) -> d one", one=1))
    par_alloc("lng", [128, 2 * KD]); par_alloc("lnb", [128, 2 * KD])
    par_alloc("b2", [128, 2 * KD])
    for l in range(2):
        P.dma("y", P.par["lng"][:, l * KD:(l + 1) * KD],
              P.dram["L0_ffn_ln_g"][l].rearrange("(kt p) -> p kt", p=128), slow=True)
        P.dma("y", P.par["lnb"][:, l * KD:(l + 1) * KD],
              P.dram["L0_ffn_ln_b"][l].rearrange("(kt p) -> p kt", p=128), slow=True)
        P.dma("y", P.par["b2"][:, l * KD:(l + 1) * KD],
              P.dram["L0_ffn_b2"][l].rearrange("(kt p) -> p kt", p=128), slow=True)
    par_alloc("b1a", [128, 2 * 22]); par_alloc("b1g", [128, 2 * 22])
    for l in range(2):
        for which, off0 in (("b1a", 0), ("b1g", HID)):
            P.dma("y", P.par[which][:, l * 22:l * 22 + 21],
                  P.dram["L0_ffn_b1"][l, off0:off0 + 2688].rearrange("(j p) -> p j", p=128), slow=True)
            P.dma("y", P.par[which][0:42, l * 22 + 21:l * 22 + 22],
                  P.dram["L0_ffn_b1"][l, off0 + 2688:off0 + 2730].rearrange("(d one) -> d one", one=1))
    par_alloc("gamma", [128, KD]); par_alloc("beta", [128, KD])
    P.dma("y", P.par["gamma"][:, :], P.dram["L0_attn_gamma"].rearrange("(kt p) -> p kt", p=128), slow=True)
    P.dma("y", P.par["beta"][:, :], P.dram["L0_attn_beta"].rearrange("(kt p) -> p kt", p=128), slow=True)
    for sfx, wn in (("", ""), ("_b", "_b")):
        par_alloc("A" + sfx, [128, KI * DST])
        P.dma("y", P.par["A" + sfx][:, :].rearrange("p (kt n) -> p kt n", kt=KI),
              P.dram["L0_attn_A_log" + wn].rearrange("(kt p) n -> p kt n", p=128))
        par_alloc("cw" + sfx, [128, KI * 4])
        P.dma("y", P.par["cw" + sfx][:, :].rearrange("p (kt k) -> p kt k", kt=KI),
              P.dram["L0_attn_conv_w" + wn].rearrange("(kt p) k -> p kt k", p=128))
        for pn, dn in (("cb", "conv_b"), ("dtb", "dt_b"), ("Dsk", "D_skip")):
            par_alloc(pn + sfx, [128, KI])
            P.dma("y", P.par[pn + sfx][:, :],
                  P.dram["L0_attn_" + dn + wn].rearrange("(kt p) -> p kt", p=128), slow=True)
    P.barrier()
    for sfx in ("", "_b"):
        i = P.S.activation(P.par["A" + sfx][:, :], P.par["A" + sfx][:, :], AF.Exp)
        P.tick("s", i)
        P.wait("v", "s")
        i = P.V.tensor_scalar_mul(P.par["A" + sfx][:, :], P.par["A" + sfx][:, :], -1.0)
        P.tick("v", i)
    P.barrier()

    x = P.sbuf("x_arena", [128, KD * T], F32, hi=True)
    P.dma("y", x[:, :].rearrange("p (kt w) -> p kt w", kt=KD),
          P.dram["xt"].rearrange("(kt p) w -> p kt w", p=128))
    P.barrier()
    xtiles = lambda xx: [xx[:, kt * T:(kt + 1) * T] for kt in range(KD)]

    def run_ffn_macaron(tagp):
        xp = P.sbuf("xp_arena", [128, KD * T], F32, hi=True)
        emit_ln(P, xtiles(x), None, direct_dst=lambda kt: xp[:, kt * T:(kt + 1) * T])
        P.dump(tagp + "ln", xp[:, :], "v")
        P.wait("y", "v")
        P.dma("y", P.dram["scr_x"].rearrange("(kt p) w -> p kt w", p=128),
              x[:, :].rearrange("p (kt w) -> p kt w", kt=KD))
        P.barrier()
        P.free(x)
        xp = emit_ffn_call(P, xp, dump_pref=tagp)
        if P.stop_flag:
            return xp, None
        x2 = P.sbuf("x_arena", [128, KD * T], F32, hi=True)
        P.dma("y", x2[:, :].rearrange("p (kt w) -> p kt w", kt=KD),
              P.dram["scr_x"].rearrange("(kt p) w -> p kt w", p=128))
        P.barrier()
        for kt in range(KD):
            i = P.V.scalar_tensor_tensor(x2[:, kt * T:(kt + 1) * T], xp[:, kt * T:(kt + 1) * T],
                                         0.5, x2[:, kt * T:(kt + 1) * T], ALU.mult, ALU.add)
            P.tick("v", i)
        P.free(xp)
        P.dump(tagp + "x_out", x2[:, :], "v")
        return None, x2

    if stop_after == "ln1_only":
        P.dump("x0", x[:, :], "v")
        xp = P.sbuf("xp_arena", [128, KD * T], F32, hi=True)
        emit_ln(P, xtiles(x), None, direct_dst=lambda kt: xp[:, kt * T:(kt + 1) * T])
        P.dump("ln1", xp[:, :], "v")
        P.barrier()
        P.dump("ln1b", xp[:, :], "v")
        P.stop_flag = True
        finish(P)
        return nc, P
    xp, x = run_ffn_macaron("f1_")
    if P.stop_flag:
        finish(P)
        return nc, P

    lnm = P.sbuf("mb_lnm", [128, KD * T], BF16, hi=True)

    def wrm(kt, tmp, g, b):
        P.wait("s", "v")
        i = P.S.activation(lnm[:, kt * T:(kt + 1) * T], tmp[:, :], AF.Identity, bias=b, scale=g)
        P.tick("s", i)

    ab, ab2 = emit_ln(P, xtiles(x), wrm, gamma_ap=P.par["gamma"], beta_ap=P.par["beta"],
                      fuse_double=True, keep_ab=True)
    P.dump("mb_lnm", lnm[:, :], "s")
    P.wait("y", "v")
    P.dma("y", P.dram["scr_x"].rearrange("(kt p) w -> p kt w", p=128),
          x[:, :].rearrange("p (kt w) -> p kt w", kt=KD))
    P.dma("y", P.dram["scr_ab"], ab2[:, :])
    P.barrier()
    P.free(x, ab, ab2)
    if P.stop("ln2"):
        finish(P)
        return nc, P
    emit_mamba(P, lnm, dump_pref="mb_")
    if P.stop_flag:
        finish(P)
        return nc, P
    x = P.sbuf("x_arena", [128, KD * T], F32, hi=True)
    P.dma("y", x[:, :].rearrange("p (kt w) -> p kt w", kt=KD),
          P.dram["scr_x"].rearrange("(kt p) w -> p kt w", p=128))
    ab2 = P.sbuf("ow_ab2", [128, 2 * T], F32)
    P.dma("y", ab2[:, :], P.dram["scr_ab"])
    P.barrier()
    emit_outw_and_residual(P, x, ab2)
    P.free(ab2)
    P.dump("mb_x_out", x[:, :], "v")
    P.barrier()
    if P.stop("mamba"):
        finish(P)
        return nc, P

    xp, x = run_ffn_macaron("f2_")
    if P.stop_flag:
        finish(P)
        return nc, P

    P.wait("y", "v")
    P.dma("y", P.dram["xo"].rearrange("(kt p) w -> p kt w", p=128),
          x[:, :].rearrange("p (kt w) -> p kt w", kt=KD))
    finish(P)
    return nc, P


def finish(P):
    P.barrier()
    for q in ["dy", "dg", "ds"]:
        P.SY.wait_ge(P.sem[q], P.cnt[q])



_BUILD_CACHE = {}


def kernel(**inputs):
    key = ("plain",)
    if key not in _BUILD_CACHE:
        _BUILD_CACHE[key] = build_program()
    nc, P = _BUILD_CACHE[key]
    consts = host_consts()
    x = np.asarray(inputs["L0_attn_x"], np.float32)
    base = {}
    for nm in WEIGHT_SPECS:
        base[nm] = _f(inputs[nm])
    base.update(consts)
    in_maps = []
    for b in range(8):
        m = dict(base)
        m["xt"] = np.ascontiguousarray(x[b].T)
        in_maps.append(m)
    res = run_bass_kernel_spmd(nc, in_maps, core_ids=list(range(8)))
    out = np.stack([np.asarray(r["xo"]).T for r in res.results])
    return out.astype(np.float32)


# version 11 (final)
# speedup vs baseline: 1.1046x; 1.0145x over previous
import numpy as np
import ml_dtypes
import concourse.bass as bass
import concourse.mybir as mybir
from concourse.bass_utils import run_bass_kernel_spmd

F32 = mybir.dt.float32
BF16 = mybir.dt.bfloat16
AF = mybir.ActivationFunctionType
ALU = mybir.AluOpType
AXX = mybir.AxisListType.X

D = 1024
T = 2048
KD = 8
DI = 2048
KI = 16
NH = 8
DH = 64
WIN = 512
HID = 2730
DTR = 64
DST = 16
NCH = 4
TC = 512
TH = 1024
EPS = 1e-5
QK_SCALE = 8.0
HT = [(j * 128, min(128, HID - j * 128)) for j in range(22)]
XSW = T + 8


def _f(x):
    return np.ascontiguousarray(np.asarray(x), dtype=np.float32)


def _bf(x):
    return np.ascontiguousarray(np.asarray(x, dtype=np.float32).astype(ml_dtypes.bfloat16))


def host_consts():
    c = {}
    c["cI"] = _bf(np.eye(128))
    R = np.zeros((128, 128), np.float32)
    for blk in range(2):
        for m in range(64):
            if m < 32:
                R[blk * 64 + m + 32, blk * 64 + m] = -1.0
            else:
                R[blk * 64 + m - 32, blk * 64 + m] = 1.0
    c["cR"] = _bf(R)
    half = np.arange(0, DH, 2, dtype=np.float64)
    inv_freq = 1.0 / 10000.0 ** (half / DH)
    t = np.arange(T, dtype=np.float64)
    freqs = np.concatenate([t[:, None] * inv_freq[None, :]] * 2, -1)
    base = (half + 0.4 * DH) / (1.4 * DH)
    power = (t - T // 2) / float(WIN // 2)
    scale = np.concatenate([base[None, :] ** power[:, None]] * 2, -1)
    cos, sin = np.cos(freqs), np.sin(freqs)
    rep = lambda a: np.concatenate([a.T, a.T], 0)
    c["cT1q"] = _bf(rep(cos * scale))
    c["cT2q"] = _bf(rep(sin * scale))
    c["cT1k"] = _bf(rep(cos / scale))
    c["cT2k"] = _bf(rep(sin / scale))
    idx = np.arange(128)
    MgA = np.zeros((128, 640), np.float32)
    for r in range(128):
        kr = np.arange(640)
        MgA[r, :] = np.where((kr >= r) & (kr <= r + 512), 0.0, -1.0e30)
    c["cMgA"] = _bf(MgA)
    c["cM0A"] = _bf(np.where(idx[None, :] <= idx[:, None], 0.0, -1.0e30))
    c["cOnesB"] = _bf(np.ones((128, 1)))
    c["cOnesRowF"] = _f(np.ones((128, 128)))
    c["cOnesRowB"] = _bf(np.ones((1, 128)))
    B2 = np.zeros((128, 2), np.float32)
    B2[:64, 0] = 1.0
    B2[64:, 1] = 1.0
    c["cB2b"] = _bf(B2)
    B2T = np.zeros((2, 128), np.float32)
    B2T[0, :64] = 1.0
    B2T[1, 64:] = 1.0
    c["cB2T"] = _bf(B2T)
    c["cMisc"] = _f(np.tile(np.array([[EPS, 0.0]], np.float32), (128, 1)))
    return c


CONST_SPECS = {
    "cI": ([128, 128], BF16), "cR": ([128, 128], BF16),
    "cMgA": ([128, 640], BF16), "cM0A": ([128, 128], BF16),
    "cOnesB": ([128, 1], BF16), "cOnesRowF": ([128, 128], F32),
    "cOnesRowB": ([1, 128], BF16),
    "cB2b": ([128, 2], BF16), "cB2T": ([2, 128], BF16),
    "cMisc": ([128, 2], F32),
}
ROT_SPECS = {
    "cT1q": ([128, T], BF16), "cT2q": ([128, T], BF16),
    "cT1k": ([128, T], BF16), "cT2k": ([128, T], BF16),
}

WEIGHT_SPECS = {
    "L0_attn_gamma": [D], "L0_attn_beta": [D],
    "L0_attn_in_w": [D, 2 * DI],
    "L0_attn_conv_w": [DI, 4], "L0_attn_conv_b": [DI],
    "L0_attn_xproj_w": [DI, 96], "L0_attn_dt_w": [DTR, DI], "L0_attn_dt_b": [DI],
    "L0_attn_A_log": [DI, DST], "L0_attn_D_skip": [DI],
    "L0_attn_conv_w_b": [DI, 4], "L0_attn_conv_b_b": [DI],
    "L0_attn_xproj_w_b": [DI, 96], "L0_attn_dt_w_b": [DTR, DI], "L0_attn_dt_b_b": [DI],
    "L0_attn_A_log_b": [DI, DST], "L0_attn_D_skip_b": [DI],
    "L0_attn_merge_w": [2 * DI, DI], "L0_attn_out_w": [DI, D],
    "L0_ffn_wqkv": [2, D, 1536], "L0_ffn_q_scale": [2, DH], "L0_ffn_k_scale": [2, DH],
    "L0_ffn_wo": [2, 512, D], "L0_ffn_ln_g": [2, D], "L0_ffn_ln_b": [2, D],
    "L0_ffn_w1": [2, D, 2 * HID], "L0_ffn_b1": [2, 2 * HID],
    "L0_ffn_w2": [2, HID, D], "L0_ffn_b2": [2, D],
}


class EngProxy:

    _PASS = {"wait_ge", "drain", "nop", "dma_start", "sem_inc"}

    def __init__(self, prog, key, raw):
        self.prog = prog
        self.key = key
        self.raw = raw

    def __getattr__(self, name):
        fn = getattr(self.raw, name)
        if name in self._PASS or not callable(fn):
            return fn
        key = self.key
        prog = self.prog

        def wrapped(*args, **kwargs):
            aps = [a for a in args if isinstance(a, bass.AP)]
            aps += [v for v in kwargs.values() if isinstance(v, bass.AP)]
            dirty = prog.dirty[key]
            if aps:
                prog.sync_marks(self.raw, aps[0])
                ins_names = {a.tensor.name for a in aps[1:]}
                if ins_names & dirty:
                    self.raw.drain()
                    dirty.clear()
            instr = fn(*args, **kwargs)
            if aps:
                dirty.add(aps[0].tensor.name)
            return instr

        return wrapped


class Prog:
    def __init__(self, nc, want_dumps=()):
        self.nc = nc
        self._cms = {}
        self.sem = {}
        self.cnt = {}
        for name in ["t", "v", "s", "g", "y", "dy", "dg", "ds"]:
            cm = nc.semaphore("sem_" + name)
            self.sem[name] = cm.__enter__()
            self._cms["sem_" + name] = cm
            self.cnt[name] = 0
        self.PE = nc.tensor
        self.V = EngProxy(self, "v", nc.vector)
        self.S = EngProxy(self, "s", nc.scalar)
        self.G = EngProxy(self, "g", nc.gpsimd)
        self.SY = nc.sync
        self.dirty = {"v": set(), "s": set(), "g": set()}
        self._eng = {"t": self.PE, "v": self.V, "s": self.S, "g": self.G, "y": self.SY}
        self.pending_sync = {}
        self.dumps = []
        self.dump_off = 0
        self.want_dumps = set(want_dumps)
        self.dbg = None
        self.consts = {}
        self.dram = {}
        self.par = {}
        self.uid = 0
        self.stop_flag = False
        self.stop_after = None

    ARENA_BYTES = 206 * 1024

    def _ensure_arena(self):
        if not hasattr(self, "arena_base"):
            cm = self.nc.sbuf_tensor("big_arena", [128, self.ARENA_BYTES], mybir.dt.uint8)
            h = cm.__enter__()
            self._cms["big_arena"] = cm
            self.arena_base = self.nc.lookup_mloc(h).addr
            self.freelist = [(0, self.ARENA_BYTES)]
            self.alloc_info = {}

    def sbuf(self, name, shape, dtype, hi=False):
        self._ensure_arena()
        per = int(np.prod(shape[1:])) * mybir.dt.size(dtype)
        per = (per + 31) // 32 * 32
        ranges = list(enumerate(self.freelist))
        if hi:
            ranges = ranges[::-1]
        for idx, (off, sz) in ranges:
            if sz >= per:
                break
        else:
            amap = sorted((v[0], v[1], k) for k, v in self.alloc_info.items())
            raise MemoryError(f"SBUF arena full for {name} ({per}B): {self.freelist}\n" +
                              "\n".join(f"  {o:7d} +{s:6d} {k}" for o, s, k in amap))
        if sz == per:
            del self.freelist[idx]
        elif hi:
            off = off + sz - per
            self.freelist[idx] = (self.freelist[idx][0], sz - per)
        else:
            self.freelist[idx] = (off + per, sz - per)
        self.uid += 1
        name = f"{name}_{self.uid}"
        h = self.nc.virtual_sbuf_tensor(name, list(shape), dtype)
        self.nc._tensor(h.name, list(shape), dtype, type="SB")
        mloc = self.nc.lookup_mloc(h)
        mloc.allocated = True
        mloc.addr = self.arena_base + off
        self.alloc_info[h.name] = (off, per)
        self.pending_sync[h.name] = dict(self.cnt)
        return h

    def psum(self, name, shape, dtype=F32):
        self.uid += 1
        name = f"{name}_{self.uid}"
        esz = mybir.dt.size(dtype)
        free = int(np.prod(shape[1:]))
        pad = (-(free * esz) % 2048) // esz
        cm = self.nc.psum_tensor(name, [128, free + pad], dtype)
        t = cm.__enter__()
        self._cms[name] = (self.uid, cm)
        return t

    def free(self, *handles):
        psums = []
        for h in handles:
            name = h.name if hasattr(h, "name") else h.tensor.name
            if name in self.alloc_info:
                off, per = self.alloc_info.pop(name)
                self.freelist.append((off, per))
            else:
                psums.append(name)
        self.freelist.sort()
        merged = []
        for off, sz in self.freelist:
            if merged and merged[-1][0] + merged[-1][1] == off:
                merged[-1] = (merged[-1][0], merged[-1][1] + sz)
            else:
                merged.append((off, sz))
        self.freelist = [tuple(x) for x in merged]
        for name in sorted(psums, key=lambda n: self._cms[n][0], reverse=True):
            _, cm = self._cms.pop(name)
            cm.__exit__(None, None, None)

    def tick(self, name, instr, k=1):
        if name in ("v", "s", "g"):
            instr = self._eng[name].raw.drain()
            self.dirty[name].clear()
        elif name == "t":
            instr = self.PE.drain()
        instr.then_inc(self.sem[name], k)
        self.cnt[name] += k

    def wait(self, who, dep, val=None):
        self._eng[who].wait_ge(self.sem[dep], self.cnt[dep] if val is None else val)

    def waitv(self, who, dep, val):
        if val:
            self._eng[who].wait_ge(self.sem[dep], val)

    def sync_marks(self, eng, out_ap):
        name = out_ap.tensor.name
        marks = self.pending_sync.pop(name, None)
        if marks:
            for s, v in marks.items():
                if v > 0:
                    eng.wait_ge(self.sem[s], v)

    def sdma(self, qname, sname, out, in_, slow=False):
        if sname not in self.sem:
            cm = self.nc.semaphore("sem_" + sname)
            self.sem[sname] = cm.__enter__()
            self._cms["sem_" + sname] = cm
            self.cnt[sname] = 0
        if hasattr(out, "tensor") and out.tensor.name in self.pending_sync:
            self.sync_marks(self._eng[qname] if qname == "y" else self._eng[qname].raw, out)
        i = self._eng[qname].dma_start(out, in_, allow_slow_non_contiguous=slow)
        i.then_inc(self.sem[sname], 16)
        self.cnt[sname] += 16
        return i

    def dma(self, qname, out, in_, slow=False):
        if hasattr(out, "tensor") and out.tensor.name in self.pending_sync:
            self.sync_marks(self._eng[qname] if qname == "y" else self._eng[qname].raw, out)
        i = self._eng[qname].dma_start(out, in_, allow_slow_non_contiguous=slow)
        self.tick("d" + qname, i, 16)
        return i

    def barrier(self, engines="tvsgy"):
        marks = {}
        for e in engines:
            i = self._eng[e].nop()
            self.tick(e, i, 1)
            marks[e] = self.cnt[e]
        for e in engines:
            for o in engines:
                if o != e:
                    self._eng[e].wait_ge(self.sem[o], marks[o])
            for q in ["dy", "dg", "ds"]:
                self._eng[e].wait_ge(self.sem[q], self.cnt[q])

    def dump(self, tag, ap, producer="v", parts=128):
        if tag not in self.want_dumps:
            return
        w = int(np.prod(ap.shape)) // int(ap.shape[0])
        q = "y" if "float32" in str(ap.dtype) else "g"
        for e in "tvsg":
            self.tick(e, self._eng[e].nop())
            self.wait(q, e)
        self.dma(q, self.dbg[0:parts, self.dump_off:self.dump_off + w], ap)
        self.pending_sync[ap.tensor.name] = dict(self.cnt)
        self.dumps.append((tag, self.dump_off, parts, w, str(ap.dtype)))
        self.dump_off += w

    def stop(self, stage):
        if self.stop_after is not None and stage == self.stop_after:
            self.stop_flag = True
        return self.stop_flag



def emit_ln(P, src_tiles, out_writer, gamma_ap=None, beta_ap=None, fuse_double=False,
            keep_ab=False, direct_dst=None):
    need_cast = "float32" in str(src_tiles[0].dtype)
    sq = P.sbuf("ln_sq", [128, 2 * TC], BF16)
    xb = P.sbuf("ln_xb", [128, 2 * TC], BF16) if need_cast else None
    rows = P.sbuf("ln_rows", [1, 3 * T], F32)
    ab = P.sbuf("ln_ab", [128, 2 * T], F32)
    ab2 = P.sbuf("ln_ab2", [128, 2 * T], F32) if (fuse_double and keep_ab) else None
    ps_s = P.psum("ln_ps_s", [1, TC], F32)
    ps_q = P.psum("ln_ps_q", [1, TC], F32)
    ps_b = P.psum("ln_ps_b", [128, TC], F32)
    ones = P.consts["cOnesB"]

    for c in range(NCH):
        csl = slice(c * TC, (c + 1) * TC)
        for kt in range(KD):
            par = kt % 2
            psl = slice(par * TC, (par + 1) * TC)
            src = src_tiles[kt][:, csl]
            P.wait("s", "t")
            if need_cast:
                i = P.S.activation(xb[:, psl], src, AF.Identity)
                P.tick("s", i)
                xsrc = xb[:, psl]
            else:
                xsrc = src
            i = P.S.activation(sq[:, psl], src, AF.Square)
            P.tick("s", i)
            P.wait("t", "s")
            i = P.PE.matmul(ps_s[0:1, :], ones[:, 0:1], xsrc, start=(kt == 0), stop=(kt == KD - 1))
            i = P.PE.matmul(ps_q[0:1, :], ones[:, 0:1], sq[:, psl], start=(kt == 0), stop=(kt == KD - 1))
            P.tick("t", i)
        P.wait("v", "t")
        i = P.V.tensor_copy(rows[0:1, csl], ps_s[0:1, :])
        i = P.V.tensor_copy(rows[0:1, T + c * TC:T + (c + 1) * TC], ps_q[0:1, :])
        P.tick("v", i)
        P.wait("t", "v")

    muA = rows[0:1, 0:T]
    rB = rows[0:1, T:2 * T]
    bZ = rows[0:1, 2 * T:3 * T]
    vB = rB
    i = P.V.tensor_scalar_mul(muA, rows[0:1, 0:T], 1.0 / D)
    P.tick("v", i)
    P.wait("s", "v")
    i = P.S.activation(bZ, muA, AF.Square)
    P.tick("s", i)
    P.wait("v", "s")
    i = P.V.scalar_tensor_tensor(vB, vB, 1.0 / D, bZ, ALU.mult, ALU.subtract)
    P.tick("v", i)

    brow = P.sbuf("ln_brow", [1, T], F32)

    def bcast_pair(dst_ab):
        P.wait("v", "s")
        i = P.V.tensor_mul(brow[0:1, :], muA, rB_cur[0])
        P.tick("v", i)
        P.wait("y", "v")
        P.dma("y", P.dram["scr_rows"][0:1, :], rB_cur[0])
        P.dma("y", P.dram["scr_rows"][1:2, :], brow[0:1, :])
        P.wait("g", "dy")
        P.dma("g", dst_ab[:, 0:T], P.dram["scr_rows"][0:1, :].broadcast_to([128, T]))
        P.dma("g", dst_ab[:, T:2 * T], P.dram["scr_rows"][1:2, :].broadcast_to([128, T]))
        P.wait("v", "dg")
        P.wait("s", "dg")

    P.wait("s", "v")
    i = P.S.activation(bZ, vB, AF.Ln, bias=EPS)
    P.S.drain()
    i = P.S.activation(bZ, bZ, AF.Exp, scale=-0.5)
    P.tick("s", i)
    rB_cur = [bZ]
    if fuse_double:
        if ab2 is not None:
            bcast_pair(ab2)
        w1 = P.sbuf("ln_w1", [1, 2 * T], F32)
        wa = w1[0:1, 0:T]
        wb = w1[0:1, T:2 * T]
        i = P.S.activation(wa, vB, AF.Ln)
        i = P.S.activation(wb, vB, AF.Ln, bias=EPS)
        P.tick("s", i)
        P.wait("v", "s")
        i = P.V.tensor_sub(wa, wa, wb)
        P.tick("v", i)
        P.wait("s", "v")
        i = P.S.activation(wa, wa, AF.Exp)
        P.tick("s", i)
        P.wait("v", "s")
        i = P.V.tensor_scalar_add(wa, wa, EPS)
        P.tick("v", i)
        P.wait("s", "v")
        i = P.S.activation(wa, wa, AF.Ln)
        P.S.drain()
        i = P.S.activation(wa, wa, AF.Exp, scale=-0.5)
        P.tick("s", i)
        P.wait("v", "s")
        i = P.V.tensor_mul(bZ, bZ, wa)
        P.tick("v", i)
        P.free(w1)
    bcast_pair(ab)

    P.dump("lnab", ab[:, :], "v")
    a_b = ab[:, 0:T]
    b_b = ab[:, T:2 * T]
    tmp = P.sbuf("ln_tmp", [128, T], F32) if direct_dst is None else None
    for kt in range(KD):
        P.wait("v", "s")
        dst = tmp[:, :] if direct_dst is None else direct_dst(kt)
        i = P.V.tensor_mul(dst, src_tiles[kt], a_b)
        i = P.V.tensor_sub(dst, dst, b_b)
        P.tick("v", i)
        if out_writer is not None:
            out_writer(kt, dst,
                       None if gamma_ap is None else gamma_ap[:, kt:kt + 1],
                       None if beta_ap is None else beta_ap[:, kt:kt + 1])
    P.free(sq, rows, brow, ps_s, ps_q, ps_b)
    if tmp is not None:
        P.free(tmp)
    if xb is not None:
        P.free(xb)
    if keep_ab:
        return ab, ab2
    P.free(ab)
    return None


def emit_shift(P, xp, dst):
    for kt in range(KD):
        if kt < 4:
            i = P.S.activation(dst[:, kt * T:(kt + 1) * T], xp[:, kt * T:(kt + 1) * T], AF.Identity)
        else:
            i = P.S.activation(dst[:, kt * T + 1:(kt + 1) * T], xp[:, kt * T:kt * T + T - 1],
                               AF.Identity)
        P.tick("s", i)
    P.wait("v", "s")
    for kt in range(4, KD):
        i = P.V.memset(dst[:, kt * T:kt * T + 1], 0.0)
    P.tick("v", i)



def emit_ffn_layer(P, l, xp, wsb, dump_pref=""):
    xpt = lambda kt: xp[:, kt * T:(kt + 1) * T]

    sh = P.sbuf("ffn_sh", [128, KD * T], BF16, hi=True)
    emit_shift(P, xp, sh)
    P.dump(dump_pref + "shift1", sh[:, :], "v")
    wqkv = P.sbuf("ffc_wqkv", [128, KD * 1536], BF16)
    P.wait("g", "t")
    P.sdma("g", "wqS", wqkv[:, :].rearrange("p (kt m) -> p kt m", kt=KD),
           P.dram["L0_ffn_wqkv"][l].rearrange("(kt p) m -> p kt m", p=128))

    q4 = P.sbuf("ffn_q4", [128, 4 * T], BF16)
    k4 = P.sbuf("ffn_k4", [128, 4 * T], BF16)
    sqb = P.sbuf("ffn_sqb", [128, TC], BF16)
    tt1 = P.sbuf("ffn_tt1", [128, TC], F32)
    s2 = P.sbuf("ffn_s2", [2, T], F32)
    rbf2 = P.sbuf("ffn_rbf2", [2, T], BF16)
    rb = P.sbuf("ffn_rb", [128, T], BF16)
    qn = P.sbuf("ffn_qn", [128, T], BF16)
    t2t = P.sbuf("ffn_t2t", [128, TC], F32)
    ps_m = P.psum("ffn_psm", [128, TC], F32)
    ps_m2 = P.psum("ffn_psm2", [128, TC], F32)
    ps_n = P.psum("ffn_psn", [2, TC], F32)
    ps_r = P.psum("ffn_psr", [128, TC], F32)
    P.wait("t", "wqS")
    rtab = P.sbuf("ffn_rtab", [128, 2 * T], BF16)

    snap = {}
    for side, (out4, scl, tn1, tn2) in enumerate(
            [(q4, wsb["qs"], "cT1q", "cT2q"),
             (k4, wsb["ks"], "cT1k", "cT2k")]):
        t1c = rtab[:, 0:T]
        t2c = rtab[:, T:2 * T]
        P.wait("y", "v")
        P.sdma("y", "rtS", t1c, P.dram[tn1])
        P.sdma("y", "rtS", t2c, P.dram[tn2])
        P.wait("v", "rtS")
        for mt in range(4):
            col0 = side * 512 + mt * 128
            for c in range(NCH):
                csl = slice(c * TC, (c + 1) * TC)
                psx = ps_m if c % 2 == 0 else ps_m2
                P.waitv("t", "v", snap.get(("qv", c % 2)))
                P.waitv("t", "s", snap.get(("qs", c % 2)))
                for kt in range(KD):
                    i = P.PE.matmul(psx[:, :], wqkv[:, kt * 1536 + col0:kt * 1536 + col0 + 128],
                                    sh[:, kt * T + c * TC:kt * T + (c + 1) * TC],
                                    start=(kt == 0), stop=(kt == KD - 1))
                P.tick("t", i)
                P.wait("s", "t")
                i = P.S.activation(sqb[:, :], psx[:, :], AF.Square)
                P.tick("s", i)
                snap[("qs", c % 2)] = P.cnt["s"]
                P.wait("t", "s")
                P.waitv("t", "v", snap.get("psn_v"))
                i = P.PE.matmul(ps_n[0:2, :], P.consts["cB2b"][:, :], sqb[:, :], start=True, stop=True)
                P.tick("t", i)
                P.wait("v", "t")
                i = P.V.tensor_copy(s2[:, csl], ps_n[0:2, :])
                i = P.V.tensor_scalar_mul(qn[:, csl], psx[:, :], scl)
                P.tick("v", i)
                snap[("qv", c % 2)] = P.cnt["v"]
                snap["psn_v"] = P.cnt["v"]
            i = P.V.tensor_scalar_max(s2[:, :], s2[:, :], 1e-24)
            P.tick("v", i)
            P.wait("s", "v")
            i = P.S.activation(s2[:, :], s2[:, :], AF.Ln)
            P.S.drain()
            i = P.S.activation(rbf2[:, :], s2[:, :], AF.Exp, scale=-0.5)
            P.tick("s", i)
            for c in range(NCH):
                csl = slice(c * TC, (c + 1) * TC)
                P.wait("t", "s")
                P.wait("t", "v")
                i = P.PE.matmul(ps_r[:, :], P.consts["cB2T"][:, :], rbf2[:, csl],
                                start=True, stop=True)
                P.tick("t", i)
                P.wait("v", "t")
                i = P.V.tensor_copy(rb[:, csl], ps_r[:, :])
                P.tick("v", i)
                P.wait("t", "v")
            for c in range(NCH):
                csl = slice(c * TC, (c + 1) * TC)
                P.wait("t", "v")
                i = P.PE.matmul(ps_r[:, :], P.consts["cR"][:, :], qn[:, csl], start=True, stop=True)
                P.tick("t", i)
                P.wait("v", "t")
                i = P.V.tensor_tensor(t2t[:, :], ps_r[:, :], t2c[:, csl], ALU.mult)
                P.tick("v", i)
                P.wait("t", "v")
                i = P.V.tensor_tensor(tt1[:, :], qn[:, csl], t1c[:, csl], ALU.mult)
                i = P.V.tensor_add(tt1[:, :], tt1[:, :], t2t[:, :])
                i = P.V.tensor_tensor(out4[:, mt * T + c * TC:mt * T + (c + 1) * TC],
                                      tt1[:, :], rb[:, csl], ALU.mult)
                P.tick("v", i)
    P.dump(dump_pref + "q4", q4[:, :], "v")
    P.dump(dump_pref + "k4", k4[:, :], "v")
    if P.stop(dump_pref + "qk"):
        return xp

    P.free(sqb, tt1, s2, rbf2, rb, qn, t2t, rtab, ps_n, ps_r)
    vtm = P.sbuf("ffn_vtm", [128, 16 * 512], BF16)
    if P.stop(dump_pref + "v0"):
        return xp
    snap = {}
    for tt in range(16):
        psx = ps_m if tt % 2 == 0 else ps_m2
        P.waitv("t", "v", snap.get(tt % 2))
        for kt in range(KD):
            i = P.PE.matmul(psx[:, :], sh[:, kt * T + tt * 128:kt * T + (tt + 1) * 128],
                            wqkv[:, kt * 1536 + 1024:kt * 1536 + 1536],
                            start=(kt == 0), stop=(kt == KD - 1))
        P.tick("t", i)
        P.wait("v", "t")
        i = P.V.tensor_copy(vtm[:, tt * 512:tt * 512 + 512], psx[:, :])
        P.tick("v", i)
        snap[tt % 2] = P.cnt["v"]
    P.free(sh, wqkv, ps_m, ps_m2)
    P.dump(dump_pref + "vtm", vtm[:, :], "s")
    if P.stop(dump_pref + "v"):
        return xp

    nmax = P.sbuf("ffn_nmax", [128, 2], F32)
    smf = P.sbuf("ffn_smf", [128, 2 * 640], F32)
    Pb = P.sbuf("ffn_P", [128, 2 * 640], BF16)
    PT = P.sbuf("ffn_PT", [128, 2 * 8 * 512 + 512], BF16)
    rsr = P.sbuf("ffn_rsr", [128, TC], F32)
    rbq = P.sbuf("ffn_rbq", [128, TC], F32)
    osb = P.sbuf("ffn_o", [128, 4 * T], BF16)
    ps_sim = [P.psum("ffn_pssimA", [128, 640], F32), P.psum("ffn_pssimB", [128, 640], F32)]
    ps_pt = [P.psum("ffn_psptA", [128, 640], BF16), P.psum("ffn_psptB", [128, 640], BF16)]
    ps_o = P.psum("ffn_pso", [128, TC], F32)
    ps_rr = P.psum("ffn_psrr", [128, TC], F32)

    wins = [(h, qw) for h in range(NH) for qw in range(4)]
    steps = [(w, qt) for w in range(len(wins)) for qt in range(4)]
    sn = {}
    pend = [None]
    P.wait("t", "v")

    def meta(s):
        w, qt = steps[s]
        h, qw = wins[w]
        rt, po = h // 2, (h % 2) * 64
        if qw == 0:
            nblk, ktok0 = qt + 1, 0
        else:
            nblk, ktok0 = 5, (qw - 1) * 4 + qt
        return w, qt, h, qw, rt, po, nblk, ktok0, nblk * 128

    def stA(s):
        w, qt, h, qw, rt, po, nblk, ktok0, width = meta(s)
        par = s % 2
        i_abs = qw * 4 + qt
        P.waitv("t", "v", sn.get(("simfree", par)))
        lhs = q4[po:po + 64, rt * T + i_abs * 128:rt * T + (i_abs + 1) * 128]
        off = 0
        while off < width:
            n = min(TC, width - off)
            i = P.PE.matmul(ps_sim[par][:, off:off + n], lhs,
                            k4[po:po + 64,
                               rt * T + ktok0 * 128 + off:rt * T + ktok0 * 128 + off + n],
                            start=True, stop=True)
            off += n
        P.tick("t", i)
        sn[("simdone", par)] = P.cnt["t"]

    def stB(s):
        w, qt, h, qw, rt, po, nblk, ktok0, width = meta(s)
        par = s % 2
        sm = smf[:, par * 640:par * 640 + width]
        pssim = ps_sim[par]
        P.waitv("v", "t", sn[("simdone", par)])
        P.waitv("v", "s", sn.get(("sexp", par)))
        if qw == 0:
            if nblk > 1:
                i = P.V.tensor_scalar_mul(sm[:, 0:width - 128], pssim[:, 0:width - 128], QK_SCALE)
            i = P.V.scalar_tensor_tensor(sm[:, width - 128:width], pssim[:, width - 128:width],
                                         QK_SCALE, P.consts["cM0A"][:, :], ALU.mult, ALU.add)
        else:
            i = P.V.scalar_tensor_tensor(sm, pssim[:, 0:width], QK_SCALE,
                                         P.consts["cMgA"][:, 0:width], ALU.mult, ALU.add)
        i = P.V.reduce_max(nmax[:, par:par + 1], sm, axis=AXX, negate=True)
        P.tick("v", i)
        sn[("simfree", par)] = P.cnt["v"]
        sn[("smf", par)] = P.cnt["v"]

    def stX(s):
        w, qt, h, qw, rt, po, nblk, ktok0, width = meta(s)
        par = s % 2
        P.waitv("s", "v", sn[("smf", par)])
        P.waitv("s", "t", sn.get(("pbfree", par)))
        i = P.S.activation(Pb[:, par * 640:par * 640 + width],
                           smf[:, par * 640:par * 640 + width], AF.Exp,
                           bias=nmax[:, par:par + 1], scale=1.0)
        P.tick("s", i)
        sn[("sexp", par)] = P.cnt["s"]

    def stC(s):
        w, qt, h, qw, rt, po, nblk, ktok0, width = meta(s)
        par = s % 2
        P.waitv("t", "s", sn[("sexp", par)])
        P.waitv("t", "s", sn.get(("ptpsfree", par)))
        for j in range(nblk):
            i = P.PE.transpose(ps_pt[par][:, j * 128:(j + 1) * 128],
                               Pb[:, par * 640 + j * 128:par * 640 + (j + 1) * 128],
                               P.consts["cI"][:, :])
        P.tick("t", i)
        sn[("tdone", par)] = P.cnt["t"]
        sn[("pbfree", par)] = P.cnt["t"]

    def stE(s):
        w, qt, h, qw, rt, po, nblk, ktok0, width = meta(s)
        b = (po + 64) % 128
        PTb = PT[:, (w % 2) * 4096:(w % 2 + 1) * 4096]
        P.waitv("t", "v", P.cnt["v"])
        P.waitv("t", "s", P.cnt["s"])
        P.waitv("t", "v", sn.get("osbdone"))
        slots = list(range(4, 8)) if qw == 0 else list(range(0, 8))
        first = True
        for s_ in slots:
            vt = (s_ - 4) if qw == 0 else (qw - 1) * 4 + s_
            lhsv = vtm[:, vt * 512 + h * 64:vt * 512 + h * 64 + 64]
            i = P.PE.matmul(ps_o[po:po + 64, :], lhsv, PTb[:, s_ * 512:(s_ + 1) * 512],
                            start=first, stop=(s_ == slots[-1]), tile_position=(0, po))
            i = P.PE.matmul(ps_rr[b:b + 1, :], P.consts["cOnesB"][:, 0:1],
                            PTb[:, s_ * 512:(s_ + 1) * 512],
                            start=first, stop=(s_ == slots[-1]), tile_position=(0, b))
            first = False
        P.tick("t", i)
        sn[("avdone", w % 2)] = P.cnt["t"]
        P.waitv("s", "t", sn[("avdone", w % 2)])
        P.waitv("s", "t", sn.get(("rbdone", b)))
        i = P.S.activation(rsr[b:b + 1, :], ps_rr[b:b + 1, :], AF.Ln)
        P.S.drain()
        i = P.S.activation(rsr[b:b + 1, :], rsr[b:b + 1, :], AF.Exp, scale=-1.0)
        P.tick("s", i)
        pend[0] = (w, rt, po, qw, b, P.cnt["s"])

    def stD(s):
        w, qt, h, qw, rt, po, nblk, ktok0, width = meta(s)
        par = s % 2
        PTb = PT[:, (w % 2) * 4096:(w % 2 + 1) * 4096]
        qcol = qt * 128
        base = (w % 2) * 4096
        slot0 = qt if qw > 0 else 4
        P.waitv("s", "t", sn[("tdone", par)])
        P.waitv("s", "t", sn.get(("avdone", w % 2)))
        for j in range(nblk):
            d0 = base + (slot0 + j) * 512 + qcol
            i2 = P.S.activation(PT[:, d0:d0 + 128], ps_pt[par][:, j * 128:(j + 1) * 128],
                                AF.Identity)
        P.tick("s", i2)
        sn[("ptpsfree", par)] = P.cnt["s"]
        P.waitv("v", "t", sn.get(("avdone", w % 2)))
        lo = 0 if qw > 0 else 4
        i = None
        for a, b2 in [(lo, slot0), (slot0 + nblk, 8)]:
            if b2 > a:
                m0 = base + a * 512 + qcol
                i = P.V.memset(
                    PT[:, m0:m0 + (b2 - a) * 512].rearrange("p (j t) -> p j t",
                                                            j=b2 - a)[:, :, 0:128], 0.0)
        if i is not None:
            P.tick("v", i)
        if qt == 3:
            stE(s)

    def emit_rb_osb():
        w, rt, po, qw, b, rsrs = pend[0]
        pend[0] = None
        P.waitv("t", "s", rsrs)
        i = P.PE.matmul(ps_rr[po:po + 64, :], P.consts["cOnesRowF"][b:b + 1, 0:64],
                        rsr[b:b + 1, :], start=True, stop=True, tile_position=(b, po))
        P.tick("t", i)
        sn[("rbdone", b)] = P.cnt["t"]
        P.waitv("v", "t", sn[("rbdone", b)])
        i = P.V.tensor_copy(rbq[po:po + 64, :], ps_rr[po:po + 64, :])
        i = P.V.tensor_tensor(osb[po:po + 64, rt * T + qw * TC:rt * T + (qw + 1) * TC],
                              ps_o[po:po + 64, :], rbq[po:po + 64, :], ALU.mult)
        P.tick("v", i)
        sn["osbdone"] = P.cnt["v"]

    NS = len(steps)
    for it in range(NS + 5):
        if it < NS:
            stA(it)
        if pend[0] is not None:
            emit_rb_osb()
        if 1 <= it and it - 1 < NS:
            stB(it - 1)
        if 2 <= it and it - 2 < NS:
            stX(it - 2)
        if 3 <= it and it - 3 < NS:
            stC(it - 3)
        if 4 <= it and it - 4 < NS:
            stD(it - 4)
    if pend[0] is not None:
        emit_rb_osb()
    P.wait("t", "v")
    P.free(q4, k4, vtm, nmax, smf, Pb, PT, rsr, rbq)
    P.free(ps_sim[0], ps_sim[1], ps_pt[0], ps_pt[1], ps_o, ps_rr)
    P.dump(dump_pref + "o", osb[:, :], "v")
    if P.stop(dump_pref + "attn"):
        return xp

    wo = P.sbuf("ffc_wo", [128, 4 * D], BF16)
    P.sdma("g", "woS", wo[:, :].rearrange("p (ik m) -> p ik m", ik=4),
           P.dram["L0_ffn_wo"][l].rearrange("(ik p) m -> p ik m", p=128))
    ps_a = P.psum("ffn_psa", [128, TC], F32)
    ps_b2 = P.psum("ffn_psb", [128, TC], F32)
    P.wait("t", "woS")
    snap = {}
    for kt in range(KD):
        for c in range(NCH):
            par = (kt * NCH + c) % 2
            pss = ps_a if par == 0 else ps_b2
            P.waitv("t", "v", snap.get(par))
            for ik in range(4):
                i = P.PE.matmul(pss[:, :], wo[:, ik * D + kt * 128:ik * D + (kt + 1) * 128],
                                osb[:, ik * T + c * TC:ik * T + (c + 1) * TC],
                                start=(ik == 0), stop=(ik == 3))
            P.tick("t", i)
            P.wait("v", "t")
            i = P.V.tensor_add(xpt(kt)[:, c * TC:(c + 1) * TC],
                               xpt(kt)[:, c * TC:(c + 1) * TC], pss[:, :])
            P.tick("v", i)
            snap[par] = P.cnt["v"]
    P.free(osb, wo, ps_a, ps_b2)
    P.dump(dump_pref + "x_attn", xp[:, :], "v")
    if P.stop(dump_pref + "wo"):
        return xp

    lnx = P.sbuf("ffn_lnx", [128, KD * T], BF16, hi=True)
    sh2 = P.sbuf("ffn_sh2", [128, KD * T], BF16, hi=True)
    emit_shift(P, xp, sh2)

    def wr(kt, tmp, g, b):
        P.wait("s", "v")
        i = P.S.activation(lnx[:, kt * T:(kt + 1) * T], tmp[:, :], AF.Identity, bias=b, scale=g)
        P.tick("s", i)

    emit_ln(P, [sh2[:, kt * T:(kt + 1) * T] for kt in range(KD)], wr,
            gamma_ap=wsb["lng"], beta_ap=wsb["lnb"])
    P.free(sh2)
    P.dump(dump_pref + "lnx2", lnx[:, :], "s")
    P.tick("v", P.V.nop())
    P.wait("y", "v")
    P.dma("y", P.dram["scr_xp"].rearrange("(kt p) w -> p kt w", p=128),
          xp[:, :].rearrange("p (kt w) -> p kt w", kt=KD))
    P.wait("y", "dy")
    P.free(xp)
    if P.stop(dump_pref + "ln2i"):
        return None

    gated = P.sbuf("ffn_gated", [128, 22 * T], BF16)
    w1a = P.sbuf("ffn_w1a", [128, 2 * KD * 128], BF16)
    w1g = P.sbuf("ffn_w1g", [128, 2 * KD * 128], BF16)
    ps_w = [P.psum("ffn_psw%d" % k, [128, TC], F32) for k in range(4)]
    w1d = P.dram["L0_ffn_w1"]
    snap = {}
    for j, (off, sz) in enumerate(HT):
        par = j % 2
        base = par * KD * 128
        sname = "w1p%d" % par
        P.waitv("g", "t", snap.get(("t", par)))
        P.sdma("g", sname, w1a[:, base:base + KD * sz].rearrange("p (kt m) -> p kt m", kt=KD),
               w1d[l, :, off:off + sz].rearrange("(kt p) m -> p kt m", p=128))
        P.sdma("g", sname, w1g[:, base:base + KD * sz].rearrange("p (kt m) -> p kt m", kt=KD),
               w1d[l, :, HID + off:HID + off + sz].rearrange("(kt p) m -> p kt m", p=128))
        dgs = P.cnt[sname]
        for c in range(NCH):
            cp = c % 2
            ps_wa, ps_wg = ps_w[cp * 2], ps_w[cp * 2 + 1]
            P.waitv("t", sname, dgs)
            P.waitv("t", "v", snap.get(("v", cp)))
            for kt in range(KD):
                i = P.PE.matmul(ps_wa[0:sz, :], w1a[:, base + kt * sz:base + (kt + 1) * sz],
                                lnx[:, kt * T + c * TC:kt * T + (c + 1) * TC],
                                start=(kt == 0), stop=(kt == KD - 1))
            for kt in range(KD):
                i = P.PE.matmul(ps_wg[0:sz, :], w1g[:, base + kt * sz:base + (kt + 1) * sz],
                                lnx[:, kt * T + c * TC:kt * T + (c + 1) * TC],
                                start=(kt == 0), stop=(kt == KD - 1))
            P.tick("t", i)
            gsl = gated[0:sz, j * T + c * TC:j * T + (c + 1) * TC]
            P.wait("s", "t")
            i = P.S.activation(gsl, ps_wg[0:sz, :], AF.Gelu, bias=wsb["b1g"][0:sz, j:j + 1])
            P.tick("s", i)
            P.wait("v", "s")
            i = P.V.scalar_tensor_tensor(gsl, ps_wa[0:sz, :], wsb["b1a"][0:sz, j:j + 1], gsl,
                                         ALU.add, ALU.mult)
            P.tick("v", i)
            snap[("v", cp)] = P.cnt["v"]
        snap[("t", par)] = P.cnt["t"]
    P.free(w1a, w1g, *ps_w)
    P.free(lnx)
    P.dump(dump_pref + "gated", gated[:, :], "v")
    if P.stop(dump_pref + "gated"):
        return None

    xp = P.sbuf("xp_arena", [128, KD * T], F32, hi=True)
    P.dma("y", xp[:, :].rearrange("p (kt w) -> p kt w", kt=KD),
          P.dram["scr_xp"].rearrange("(kt p) w -> p kt w", p=128))
    w2s = P.sbuf("ffn_w2s", [128, 2 * 22 * 128], BF16)
    ps_a2 = P.psum("ffn_ps2a", [128, TC], F32)
    ps_b3 = P.psum("ffn_ps2b", [128, TC], F32)
    w2d = P.dram["L0_ffn_w2"]
    P.wait("v", "dy")
    snap = {}
    for m in range(KD):
        par = m % 2
        base = par * 22 * 128
        P.waitv("g", "t", snap.get(("t", par)))
        sname = "w2p%d" % par
        for j, (off, sz) in enumerate(HT):
            P.sdma("g", sname, w2s[0:sz, base + j * 128:base + (j + 1) * 128],
                   w2d[l, off:off + sz, m * 128:(m + 1) * 128])
        snap[("dg", par)] = (sname, P.cnt[sname])
        for c in range(NCH):
            cp = c % 2
            pss = ps_a2 if cp == 0 else ps_b3
            P.waitv("t", *snap[("dg", par)])
            P.waitv("t", "v", snap.get(("v", cp)))
            for j, (off, sz) in enumerate(HT):
                i = P.PE.matmul(pss[:, :], w2s[0:sz, base + j * 128:base + j * 128 + 128],
                                gated[0:sz, j * T + c * TC:j * T + (c + 1) * TC],
                                start=(j == 0), stop=(j == 21))
            P.tick("t", i)
            P.wait("v", "t")
            i = P.V.scalar_tensor_tensor(xpt(m)[:, c * TC:(c + 1) * TC], pss[:, :],
                                         wsb["b2"][:, m:m + 1], xpt(m)[:, c * TC:(c + 1) * TC],
                                         ALU.add, ALU.add)
            P.tick("v", i)
            snap[("v", cp)] = P.cnt["v"]
        snap[("t", par)] = P.cnt["t"]
    P.free(w2s, ps_a2, ps_b3, gated)
    P.dump(dump_pref + f"x_l{l}", xp[:, :], "v")
    return xp


def emit_ffn_call(P, xp, dump_pref=""):
    for l in range(2):
        wsb = {
            "qs": P.par["qs"][:, l:l + 1], "ks": P.par["ks"][:, l:l + 1],
            "lng": P.par["lng"][:, l * KD:(l + 1) * KD], "lnb": P.par["lnb"][:, l * KD:(l + 1) * KD],
            "b1a": P.par["b1a"][:, l * 22:(l + 1) * 22], "b1g": P.par["b1g"][:, l * 22:(l + 1) * 22],
            "b2": P.par["b2"][:, l * KD:(l + 1) * KD],
        }
        P.barrier()
        xp = emit_ffn_layer(P, l, xp, wsb, dump_pref=dump_pref + f"l{l}_")
        if P.stop_flag:
            break
        P.barrier()
        if P.stop(dump_pref + f"l{l}"):
            break
    return xp



def emit_mamba(P, lnx, dump_pref=""):
    xsp = P.sbuf("mb_xsp", [128, 2 * TC], BF16)
    ztile = P.sbuf("mb_zt", [128, KI * 4], BF16)
    i = P.V.memset(ztile[:, :], 0.0)
    P.tick("v", i)
    P.wait("y", "v")
    P.sdma("y", "xpadS", P.dram["scr_xs"][:, 0:4].rearrange("(kt p) w -> p kt w", p=128),
           ztile[:, :].rearrange("p (kt w) -> p kt w", kt=KI))
    P.sdma("y", "xpadS", P.dram["scr_xs"][:, XSW - 4:XSW].rearrange("(kt p) w -> p kt w", p=128),
           ztile[:, :].rearrange("p (kt w) -> p kt w", kt=KI))
    inw = P.sbuf("mb_inw", [128, 2 * KD * 128], BF16)
    zbuf = P.sbuf("mb_zb", [128, 2 * TC], BF16)
    ps_i = P.psum("mb_psi", [128, TC], F32)
    ps_i2 = P.psum("mb_psi2", [128, TC], F32)
    inwd = P.dram["L0_attn_in_w"]
    snap = {}
    for m in range(2 * KI):
        par = m % 2
        base = par * KD * 128
        P.waitv("g", "t", snap.get(("t", par)))
        sname = "inwp%d" % par
        P.sdma("g", sname, inw[:, base:base + KD * 128].rearrange("p (kt mm) -> p kt mm", kt=KD),
               inwd[:, m * 128:(m + 1) * 128].rearrange("(kt p) mm -> p kt mm", p=128))
        snap[("dg", par)] = (sname, P.cnt[sname])
        for c in range(NCH):
            cp = c % 2
            pss = ps_i if cp == 0 else ps_i2
            P.waitv("t", *snap[("dg", par)])
            P.waitv("t", "s", snap.get(("s", cp)))
            for kt in range(KD):
                i = P.PE.matmul(pss[:, :], inw[:, base + kt * 128:base + (kt + 1) * 128],
                                lnx[:, kt * T + c * TC:kt * T + (c + 1) * TC],
                                start=(kt == 0), stop=(kt == KD - 1))
            P.tick("t", i)
            P.wait("s", "t")
            if m < KI:
                xp2 = c % 2
                xn = "xswp%d" % xp2
                P.waitv("s", xn, snap.get(("xdy", xp2)))
                i = P.S.activation(xsp[:, xp2 * TC:(xp2 + 1) * TC], pss[:, :], AF.Identity)
                P.tick("s", i)
                P.wait("y", "s")
                P.sdma("y", xn, P.dram["scr_xs"][m * 128:(m + 1) * 128, 4 + c * TC:4 + (c + 1) * TC],
                       xsp[:, xp2 * TC:(xp2 + 1) * TC])
                snap[("xdy", xp2)] = P.cnt[xn]
            else:
                zp = c % 2
                zn = "zspp%d" % zp
                P.waitv("s", zn, snap.get(("zdy", zp)))
                i = P.S.activation(zbuf[:, zp * TC:(zp + 1) * TC], pss[:, :], AF.Silu)
                P.tick("s", i)
                P.wait("y", "s")
                P.sdma("y", zn, P.dram["scr_z"][(m - KI) * 128:(m - KI + 1) * 128, c * TC:(c + 1) * TC],
                       zbuf[:, zp * TC:(zp + 1) * TC])
                snap[("zdy", zp)] = P.cnt[zn]
            snap[("s", cp)] = P.cnt["s"]
        snap[("t", par)] = P.cnt["t"]
    P.free(inw, zbuf, ps_i, ps_i2, lnx, xsp, ztile)
    xsw_fin = (P.cnt.get("xswp0", 0), P.cnt.get("xswp1", 0), P.cnt.get("xpadS", 0))
    P.barrier()
    if P.stop("mamba_xs"):
        return

    for dirn in range(2):
        sfx = "" if dirn == 0 else "_b"
        wn = sfx
        pA = P.par["A" + sfx]
        pcw = P.par["cw" + sfx]
        pcb = P.par["cb" + sfx]
        pdtb = P.par["dtb" + sfx]
        pD = P.par["Dsk" + sfx]
        xst = P.sbuf("mb_xst", [128, 2 * XSW], BF16)
        xld = {}
        vconv = {}
        P.waitv("y", "xswp0", xsw_fin[0])
        P.waitv("y", "xswp1", xsw_fin[1])
        P.waitv("y", "xpadS", xsw_fin[2])

        def load_xs(kt):
            par = kt % 2
            xn = "xsL%d" % par
            P.waitv("y", "v", vconv.get(par))
            P.sdma("y", xn, xst[:, par * XSW:(par + 1) * XSW],
                   P.dram["scr_xs"][kt * 128:(kt + 1) * 128, :])
            xld[par] = P.cnt[xn]

        xprojT = P.sbuf("mb_xproj", [128, KI * 96], BF16)
        dtwT = P.sbuf("mb_dtw", [64, KI * 128], BF16)
        P.dma("g", xprojT[:, :].rearrange("p (kt m) -> p kt m", kt=KI),
              P.dram["L0_attn_xproj_w" + wn].rearrange("(kt p) m -> p kt m", p=128))
        P.dma("g", dtwT[:, :], P.dram["L0_attn_dt_w" + wn])
        xproj = xprojT[:, :]
        dtw = dtwT[:, :]
        ydram = P.dram["scr_yf" if dirn == 0 else "scr_yb"]

        def conv_tile(kt, dst_f32, t0, tw):
            tb = (kt % 2) * XSW
            P.waitv("v", "xsL%d" % (kt % 2), xld[kt % 2])
            for k in range(4):
                if dirn == 0:
                    src = xst[:, tb + 1 + k + t0:tb + 1 + k + t0 + tw]
                else:
                    st = tb + T + 6 - k - t0
                    src = xst[:, st:st - tw:-1]
                w_ap = pcw[:, kt * 4 + k:kt * 4 + k + 1]
                if k == 0:
                    i = P.V.tensor_scalar_mul(dst_f32, src, w_ap)
                else:
                    i = P.V.scalar_tensor_tensor(dst_f32, src, w_ap, dst_f32, ALU.mult, ALU.add)
            P.tick("v", i)

        xcb = P.sbuf("mbA_xc", [128, 2 * T], BF16)
        accA = P.sbuf("mbA_acc", [128, T], F32)
        dtpre = P.sbuf("mbA_dtpre", [64, T], BF16)
        bcrow = P.sbuf("mbA_bcrow", [48, T], BF16)
        ps_dt = P.psum("mbA_psdt", [64, T], F32)
        ps_bc = P.psum("mbA_psbc", [48, T], F32)
        load_xs(0)
        for kt in range(KI):
            par = kt % 2
            P.wait("v", "t")
            P.wait("v", "s")
            conv_tile(kt, accA[:, :], 0, T)
            vconv[kt % 2] = P.cnt["v"]
            if kt + 1 < KI:
                load_xs(kt + 1)
            P.wait("s", "v")
            P.wait("s", "t")
            i = P.S.activation(xcb[:, par * T:(par + 1) * T], accA[:, :], AF.Silu,
                               bias=pcb[:, kt:kt + 1])
            P.tick("s", i)
            P.wait("t", "s")
            for c in range(NCH):
                csl = slice(c * TC, (c + 1) * TC)
                mv = xcb[:, par * T + c * TC:par * T + (c + 1) * TC]
                i = P.PE.matmul(ps_dt[0:64, csl], xproj[:, kt * 96:kt * 96 + 64], mv,
                                start=(kt == 0), stop=(kt == KI - 1))
                i = P.PE.matmul(ps_bc[0:16, csl], xproj[:, kt * 96 + 64:kt * 96 + 80], mv,
                                start=(kt == 0), stop=(kt == KI - 1))
                i = P.PE.matmul(ps_bc[32:48, csl], xproj[:, kt * 96 + 80:kt * 96 + 96], mv,
                                start=(kt == 0), stop=(kt == KI - 1), tile_position=(0, 32))
            P.tick("t", i)
        P.wait("s", "t")
        i = P.S.activation(dtpre[:, :], ps_dt[0:64, :], AF.Identity)
        i = P.S.activation(bcrow[0:16, :], ps_bc[0:16, :], AF.Identity)
        i = P.S.activation(bcrow[32:48, :], ps_bc[32:48, :], AF.Identity)
        P.tick("s", i)
        P.wait("y", "s")
        P.dma("y", P.dram["scr_b"], bcrow[0:16, :])
        P.dma("y", P.dram["scr_c"], bcrow[32:48, :])
        P.wait("y", "dy")
        P.free(xcb, accA, ps_bc, bcrow)
        P.dump(dump_pref + "dtpre" + sfx, dtpre[:, :], "s", parts=64)

        NGS = 4
        NGRP = DST // NGS
        SEGW = TH + 1
        GWS = NGS * SEGW
        state = P.sbuf("mb_state", [128, KI * DST], F32)
        bcb = P.sbuf("mb_bcb", [128, 2 * DST * TH], BF16)
        dts = P.sbuf("mb_dts", [128, TH], F32)
        xcs2 = P.sbuf("mb_xcs", [128, 2 * TH], BF16)
        us = P.sbuf("mb_us", [128, TH], BF16)
        NB = 3
        dAg = P.sbuf("mb_dAg", [128, 2 * GWS], F32)
        dbug = P.sbuf("mb_dbug", [128, NB * GWS], BF16)
        cyg = P.sbuf("mb_cyg", [128, NGS * TH], BF16)
        hbg = P.sbuf("mb_hbg", [128, GWS], BF16)
        yg = P.sbuf("mb_yg", [128, NGRP * 2 * TH], BF16)
        ysum = P.sbuf("mb_ysum", [128, TH], BF16)
        yout = P.sbuf("mb_yout", [128, TH], BF16)
        yrev = P.sbuf("mb_yrev", [128, TH], BF16) if dirn == 1 else None
        ps_d2 = P.psum("mb_psd2", [128, TC], F32)
        acc2 = P.sbuf("mb_acc2", [128, 2 * TH], F32)
        accS = acc2[:, 0:TH]
        sp2 = acc2[:, TH:2 * TH]
        i = P.V.memset(dAg[:, 0:2 * GWS:SEGW], 0.0)
        P.tick("v", i)
        for nb in range(NB):
            i = P.G.memset(dbug[:, nb * GWS:(nb + 1) * GWS:SEGW], 0.0)
        P.tick("g", i)

        gctr = 0
        snap_scan = None
        snap_scan_p = {}
        state_v = None
        for half in range(2):
            t0 = half * TH
            P.wait("y", "g")
            P.wait("y", "v")
            for n in range(DST):
                P.sdma("y", "bcbS", bcb[:, n * TH:(n + 1) * TH],
                       P.dram["scr_b"][n:n + 1, t0:t0 + TH].broadcast_to([128, TH]))
                P.sdma("y", "bcbS", bcb[:, (DST + n) * TH:(DST + n + 1) * TH],
                       P.dram["scr_c"][n:n + 1, t0:t0 + TH].broadcast_to([128, TH]))
            bc_dy = P.cnt["bcbS"]
            load_xs(0)
            usv_h = [0]

            def chain(kt):
                P.wait("v", "s")
                P.wait("v", "g")
                conv_tile(kt, accS, t0, TH)
                vconv[kt % 2] = P.cnt["v"]
                if kt + 1 < KI:
                    load_xs(kt + 1)
                P.wait("s", "v")
                i = P.S.activation(xcs2[:, (kt % 2) * TH:(kt % 2 + 1) * TH], accS, AF.Silu,
                                   bias=pcb[:, kt:kt + 1])
                P.tick("s", i)
                for c in range(2):
                    P.wait("t", "v")
                    P.wait("t", "s")
                    i = P.PE.matmul(ps_d2[:, :], dtw[:, kt * 128:(kt + 1) * 128],
                                    dtpre[:, t0 + c * TC:t0 + (c + 1) * TC], start=True, stop=True)
                    P.tick("t", i)
                    P.wait("v", "t")
                    i = P.V.tensor_scalar(accS[:, c * TC:(c + 1) * TC], ps_d2[:, :],
                                          pdtb[:, kt:kt + 1], 0.0, ALU.add, ALU.max)
                    i = P.V.tensor_scalar(sp2[:, c * TC:(c + 1) * TC], ps_d2[:, :],
                                          pdtb[:, kt:kt + 1], None, ALU.add)
                    P.tick("v", i)
                P.wait("s", "v")
                i = P.S.activation(sp2, sp2, AF.Abs)
                P.S.drain()
                i = P.S.activation(sp2, sp2, AF.Exp, scale=-1.0)
                P.S.drain()
                i = P.S.activation(sp2, sp2, AF.Ln, bias=1.0)
                P.tick("s", i)
                P.wait("v", "s")
                i = P.V.tensor_add(dts[:, :], accS, sp2)
                i = P.V.tensor_tensor(us[:, :], dts[:, :],
                                      xcs2[:, (kt % 2) * TH:(kt % 2 + 1) * TH], ALU.mult)
                P.tick("v", i)
                usv_h[0] = P.cnt["v"]

            chain(0)
            for kt in range(KI):
                us_v = usv_h[0]
                xc = xcs2[:, (kt % 2) * TH:(kt % 2 + 1) * TH]
                for g in range(NGRP):
                    gp = gctr % NB
                    ap = gctr % 2
                    gctr += 1
                    n0 = g * NGS
                    s0 = kt * DST + n0
                    P.waitv("s", "v", max(us_v, snap_scan_p.get(("a", ap)) or 0))
                    for j in range(NGS):
                        i = P.S.activation(dAg[:, ap * GWS + j * SEGW + 1:ap * GWS + (j + 1) * SEGW],
                                           dts[:, :], AF.Exp, scale=pA[:, s0 + j:s0 + j + 1])
                    P.tick("s", i)
                    sexp_g = P.cnt["s"]
                    P.waitv("g", "v", max(us_v, snap_scan_p.get(("b", gp)) or 0, state_v or 0))
                    P.waitv("g", "bcbS", bc_dy)
                    for j in range(NGS):
                        i = P.G.tensor_tensor(dbug[:, gp * GWS + j * SEGW + 1:gp * GWS + (j + 1) * SEGW],
                                              us[:, :], bcb[:, (n0 + j) * TH:(n0 + j + 1) * TH],
                                              ALU.mult)
                    if half == 1:
                        i = P.G.tensor_copy(dbug[:, gp * GWS:gp * GWS + GWS:SEGW],
                                            state[:, s0:s0 + NGS])
                    P.tick("g", i)
                    gfill_g = P.cnt["g"]
                    P.waitv("v", "s", sexp_g)
                    P.waitv("v", "g", gfill_g)
                    i = P.V.tensor_tensor_scan(hbg[:, :], dAg[:, ap * GWS:(ap + 1) * GWS],
                                               dbug[:, gp * GWS:(gp + 1) * GWS],
                                               0.0, ALU.mult, ALU.add)
                    P.tick("v", i)
                    snap_scan = P.cnt["v"]
                    snap_scan_p[("a", ap)] = snap_scan
                    snap_scan_p[("b", gp)] = snap_scan
                    if half == 0:
                        i = P.V.tensor_copy(state[:, s0:s0 + NGS], hbg[:, TH:GWS:SEGW])
                    i = P.V.tensor_tensor(
                        cyg[:, :].rearrange("p (n t) -> p n t", n=NGS),
                        hbg[:, :].rearrange("p (n t) -> p n t", n=NGS)[:, :, 1:SEGW],
                        bcb[:, DST * TH + n0 * TH:DST * TH + (n0 + NGS) * TH]
                           .rearrange("p (n t) -> p n t", n=NGS),
                        ALU.mult)
                    i = P.V.tensor_add(yg[:, g * 2 * TH:(g + 1) * 2 * TH],
                                       cyg[:, 0:2 * TH], cyg[:, 2 * TH:4 * TH])
                    P.tick("v", i)
                if kt + 1 < KI:
                    chain(kt + 1)
                if "yspS" in P.cnt:
                    P.waitv("v", "yspS", P.cnt["yspS"])
                i = P.V.tensor_add(cyg[:, 0:2 * TH], yg[:, 0:2 * TH], yg[:, 2 * TH:4 * TH])
                i = P.V.tensor_add(cyg[:, 2 * TH:4 * TH], yg[:, 4 * TH:6 * TH], yg[:, 6 * TH:8 * TH])
                i = P.V.tensor_add(yg[:, 0:2 * TH], cyg[:, 0:2 * TH], cyg[:, 2 * TH:4 * TH])
                i = P.V.tensor_add(ysum[:, :], yg[:, 0:TH], yg[:, TH:2 * TH])
                i = P.V.scalar_tensor_tensor(yout[:, :], xc, pD[:, kt:kt + 1], ysum[:, :],
                                             ALU.mult, ALU.add)
                P.tick("v", i)
                if dirn == 0:
                    P.wait("y", "v")
                    P.sdma("y", "yspS", ydram[kt * 128:(kt + 1) * 128, t0:t0 + TH], yout[:, :])
                else:
                    i = P.V.tensor_copy(yrev[:, :], yout[:, TH - 1::-1])
                    P.tick("v", i)
                    P.wait("y", "v")
                    P.sdma("y", "yspS", ydram[kt * 128:(kt + 1) * 128, T - t0 - TH:T - t0], yrev[:, :])
            if half == 0:
                state_v = P.cnt["v"]
        P.free(dtpre, ps_dt, state, bcb, dts, xcs2, us, dAg, dbug, cyg, acc2,
               hbg, yg, ysum, yout, ps_d2, xprojT, dtwT, xst)
        if yrev is not None:
            P.free(yrev)
        P.barrier()
        if P.stop("mamba_dir%d" % dirn):
            return

    ych = P.sbuf("mg_ych", [128, 32 * TC], BF16)
    mw = P.sbuf("mg_mw", [128, 2 * 32 * 128], BF16)
    zs = P.sbuf("mg_zs", [128, 2 * TC], BF16)
    ymb = P.sbuf("mg_ym", [128, 2 * TC], BF16)
    ps_g = P.psum("mg_psg", [128, TC], F32)
    ps_g2 = P.psum("mg_psg2", [128, TC], F32)
    mwd = P.dram["L0_attn_merge_w"]
    snap = {}
    for c in range(NCH):
        P.wait("y", "t")
        P.sdma("y", "ychS", ych[:, 0:16 * TC].rearrange("p (kt w) -> p kt w", kt=KI),
               P.dram["scr_yf"][:, c * TC:(c + 1) * TC].rearrange("(kt p) w -> p kt w", p=128))
        P.sdma("y", "ychS", ych[:, 16 * TC:32 * TC].rearrange("p (kt w) -> p kt w", kt=KI),
               P.dram["scr_yb"][:, c * TC:(c + 1) * TC].rearrange("(kt p) w -> p kt w", p=128))
        ych_dy = P.cnt["ychS"]
        for m in range(KI):
            par = m % 2
            base = par * 32 * 128
            P.waitv("g", "t", snap.get(("t", par)))
            mn = "mwp%d" % par
            P.sdma("g", mn, mw[:, base:base + 32 * 128].rearrange("p (kt mm) -> p kt mm", kt=32),
                   mwd[:, m * 128:(m + 1) * 128].rearrange("(kt p) mm -> p kt mm", p=128))
            zn2 = "mzsp%d" % par
            P.waitv("y", "v", snap.get(("v", par)))
            P.sdma("y", zn2, zs[:, par * TC:(par + 1) * TC],
                   P.dram["scr_z"][m * 128:(m + 1) * 128, c * TC:(c + 1) * TC])
            pss = ps_g if par == 0 else ps_g2
            P.waitv("t", mn, P.cnt[mn])
            P.waitv("t", "ychS", ych_dy)
            P.waitv("t", "v", snap.get(("v", par)))
            for kt in range(32):
                i = P.PE.matmul(pss[:, :], mw[:, base + kt * 128:base + (kt + 1) * 128],
                                ych[:, kt * TC:(kt + 1) * TC], start=(kt == 0), stop=(kt == 31))
            P.tick("t", i)
            P.wait("v", "t")
            P.waitv("v", zn2, P.cnt[zn2])
            ymn = "ymbp%d" % par
            if ymn in P.cnt:
                P.waitv("v", ymn, P.cnt[ymn])
            i = P.V.tensor_tensor(ymb[:, par * TC:(par + 1) * TC], pss[:, :],
                                  zs[:, par * TC:(par + 1) * TC], ALU.mult)
            P.tick("v", i)
            snap[("v", par)] = P.cnt["v"]
            snap[("t", par)] = P.cnt["t"]
            P.wait("y", "v")
            P.sdma("y", ymn, P.dram["scr_ym"][m * 128:(m + 1) * 128, c * TC:(c + 1) * TC],
                   ymb[:, par * TC:(par + 1) * TC])
    P.free(ych, mw, zs, ymb, ps_g, ps_g2)
    P.barrier()
    if P.stop("mamba_merge"):
        return


def emit_outw_and_residual(P, x, ab2):
    ymch = P.sbuf("ow_ymch", [128, KI * TC], BF16)
    ow = P.sbuf("ow_w", [128, 2 * KI * 128], BF16)
    ps_o = P.psum("ow_ps", [128, TC], F32)
    ps_o2 = P.psum("ow_ps2", [128, TC], F32)
    tmp = P.sbuf("ow_tmp", [128, TC], F32)
    owd = P.dram["L0_attn_out_w"]
    a_b = ab2[:, 0:T]
    b_b = ab2[:, T:2 * T]
    snap = {}
    for c in range(NCH):
        P.wait("y", "t")
        P.sdma("y", "ymchS", ymch[:, :].rearrange("p (kt w) -> p kt w", kt=KI),
               P.dram["scr_ym"][:, c * TC:(c + 1) * TC].rearrange("(kt p) w -> p kt w", p=128))
        ych_dy = P.cnt["ymchS"]
        for mo in range(KD):
            par = mo % 2
            base = par * KI * 128
            P.waitv("g", "t", snap.get(("t", par)))
            on = "owp%d" % par
            P.sdma("g", on, ow[:, base:base + KI * 128].rearrange("p (kt mm) -> p kt mm", kt=KI),
                   owd[:, mo * 128:(mo + 1) * 128].rearrange("(kt p) mm -> p kt mm", p=128))
            pss = ps_o if par == 0 else ps_o2
            P.waitv("t", on, P.cnt[on])
            P.waitv("t", "ymchS", ych_dy)
            P.waitv("t", "v", snap.get(("v", par)))
            for kt in range(KI):
                i = P.PE.matmul(pss[:, :], ow[:, base + kt * 128:base + (kt + 1) * 128],
                                ymch[:, kt * TC:(kt + 1) * TC], start=(kt == 0), stop=(kt == KI - 1))
            P.tick("t", i)
            xsl = x[:, mo * T + c * TC:mo * T + (c + 1) * TC]
            P.wait("v", "t")
            i = P.V.tensor_tensor(tmp[:, :], xsl, a_b[:, c * TC:(c + 1) * TC], ALU.mult)
            i = P.V.tensor_sub(tmp[:, :], tmp[:, :], b_b[:, c * TC:(c + 1) * TC])
            i = P.V.tensor_add(tmp[:, :], tmp[:, :], pss[:, :])
            i = P.V.tensor_add(xsl, xsl, tmp[:, :])
            P.tick("v", i)
            snap[("v", par)] = P.cnt["v"]
            snap[("t", par)] = P.cnt["t"]
    P.free(ymch, ow, ps_o, ps_o2, tmp)


def self_val(P, e):
    return P.cnt[e]



def build_program(want_dumps=(), stop_after=None):
    nc = bass.Bass("TRN2", target_bir_lowering=False)
    P = Prog(nc, want_dumps)
    P.stop_after = stop_after

    P.dram["xt"] = nc.dram_tensor("xt", [D, T], F32, kind="ExternalInput").ap()
    P.dram["xo"] = nc.dram_tensor("xo", [D, T], F32, kind="ExternalOutput").ap()
    for nm, (shape, dt) in {**CONST_SPECS, **ROT_SPECS}.items():
        P.dram[nm] = nc.dram_tensor(nm, shape, dt, kind="ExternalInput").ap()
    for nm, shape in WEIGHT_SPECS.items():
        P.dram[nm] = nc.dram_tensor(nm, shape, F32, kind="ExternalInput").ap()
    for nm, shape, dt in [
        ("scr_x", [D, T], F32), ("scr_z", [DI, T], BF16),
        ("scr_yf", [DI, T], BF16), ("scr_yb", [DI, T], BF16),
        ("scr_ym", [DI, T], BF16), ("scr_b", [DST, T], BF16), ("scr_c", [DST, T], BF16),
        ("scr_ab", [128, 2 * T], F32), ("scr_rows", [2, T], F32),
        ("scr_xp", [D, T], F32), ("scr_xs", [DI, XSW], BF16),
    ]:
        P.dram[nm] = nc.dram_tensor(nm, shape, dt, kind="Internal").ap()
    if want_dumps:
        P.dbg = nc.dram_tensor("dbg", [128, 400000], F32, kind="ExternalOutput").ap()

    for nm, (shape, dt) in CONST_SPECS.items():
        P.consts[nm] = P.sbuf(nm, shape, dt)
        P.dma("y", P.consts[nm][:, :], P.dram[nm])
    nc.const_aps.register(F32, EPS, P.consts["cMisc"][:, 0:1])

    def par_alloc(nm, shape, dtype=F32):
        P.par[nm] = P.sbuf("par_" + nm, shape, dtype)
        return P.par[nm]

    qsd, ksd = P.dram["L0_ffn_q_scale"], P.dram["L0_ffn_k_scale"]
    par_alloc("qs", [128, 2]); par_alloc("ks", [128, 2])
    for l in range(2):
        for half in range(2):
            P.dma("y", P.par["qs"][half * 64:(half + 1) * 64, l:l + 1],
                  qsd[l, :].rearrange("(d one) -> d one", one=1))
            P.dma("y", P.par["ks"][half * 64:(half + 1) * 64, l:l + 1],
                  ksd[l, :].rearrange("(d one) -> d one", one=1))
    par_alloc("lng", [128, 2 * KD]); par_alloc("lnb", [128, 2 * KD])
    par_alloc("b2", [128, 2 * KD])
    for l in range(2):
        P.dma("y", P.par["lng"][:, l * KD:(l + 1) * KD],
              P.dram["L0_ffn_ln_g"][l].rearrange("(kt p) -> p kt", p=128), slow=True)
        P.dma("y", P.par["lnb"][:, l * KD:(l + 1) * KD],
              P.dram["L0_ffn_ln_b"][l].rearrange("(kt p) -> p kt", p=128), slow=True)
        P.dma("y", P.par["b2"][:, l * KD:(l + 1) * KD],
              P.dram["L0_ffn_b2"][l].rearrange("(kt p) -> p kt", p=128), slow=True)
    par_alloc("b1a", [128, 2 * 22]); par_alloc("b1g", [128, 2 * 22])
    for l in range(2):
        for which, off0 in (("b1a", 0), ("b1g", HID)):
            P.dma("y", P.par[which][:, l * 22:l * 22 + 21],
                  P.dram["L0_ffn_b1"][l, off0:off0 + 2688].rearrange("(j p) -> p j", p=128), slow=True)
            P.dma("y", P.par[which][0:42, l * 22 + 21:l * 22 + 22],
                  P.dram["L0_ffn_b1"][l, off0 + 2688:off0 + 2730].rearrange("(d one) -> d one", one=1))
    par_alloc("gamma", [128, KD]); par_alloc("beta", [128, KD])
    P.dma("y", P.par["gamma"][:, :], P.dram["L0_attn_gamma"].rearrange("(kt p) -> p kt", p=128), slow=True)
    P.dma("y", P.par["beta"][:, :], P.dram["L0_attn_beta"].rearrange("(kt p) -> p kt", p=128), slow=True)
    for sfx, wn in (("", ""), ("_b", "_b")):
        par_alloc("A" + sfx, [128, KI * DST])
        P.dma("y", P.par["A" + sfx][:, :].rearrange("p (kt n) -> p kt n", kt=KI),
              P.dram["L0_attn_A_log" + wn].rearrange("(kt p) n -> p kt n", p=128))
        par_alloc("cw" + sfx, [128, KI * 4])
        P.dma("y", P.par["cw" + sfx][:, :].rearrange("p (kt k) -> p kt k", kt=KI),
              P.dram["L0_attn_conv_w" + wn].rearrange("(kt p) k -> p kt k", p=128))
        for pn, dn in (("cb", "conv_b"), ("dtb", "dt_b"), ("Dsk", "D_skip")):
            par_alloc(pn + sfx, [128, KI])
            P.dma("y", P.par[pn + sfx][:, :],
                  P.dram["L0_attn_" + dn + wn].rearrange("(kt p) -> p kt", p=128), slow=True)
    P.barrier()
    for sfx in ("", "_b"):
        i = P.S.activation(P.par["A" + sfx][:, :], P.par["A" + sfx][:, :], AF.Exp)
        P.tick("s", i)
        P.wait("v", "s")
        i = P.V.tensor_scalar_mul(P.par["A" + sfx][:, :], P.par["A" + sfx][:, :], -1.0)
        P.tick("v", i)
    P.barrier()

    x = P.sbuf("x_arena", [128, KD * T], F32, hi=True)
    P.dma("y", x[:, :].rearrange("p (kt w) -> p kt w", kt=KD),
          P.dram["xt"].rearrange("(kt p) w -> p kt w", p=128))
    P.barrier()
    xtiles = lambda xx: [xx[:, kt * T:(kt + 1) * T] for kt in range(KD)]

    def run_ffn_macaron(tagp):
        xp = P.sbuf("xp_arena", [128, KD * T], F32, hi=True)
        emit_ln(P, xtiles(x), None, direct_dst=lambda kt: xp[:, kt * T:(kt + 1) * T])
        P.dump(tagp + "ln", xp[:, :], "v")
        P.wait("y", "v")
        P.dma("y", P.dram["scr_x"].rearrange("(kt p) w -> p kt w", p=128),
              x[:, :].rearrange("p (kt w) -> p kt w", kt=KD))
        P.barrier()
        P.free(x)
        xp = emit_ffn_call(P, xp, dump_pref=tagp)
        if P.stop_flag:
            return xp, None
        x2 = P.sbuf("x_arena", [128, KD * T], F32, hi=True)
        P.dma("y", x2[:, :].rearrange("p (kt w) -> p kt w", kt=KD),
              P.dram["scr_x"].rearrange("(kt p) w -> p kt w", p=128))
        P.barrier()
        for kt in range(KD):
            i = P.V.scalar_tensor_tensor(x2[:, kt * T:(kt + 1) * T], xp[:, kt * T:(kt + 1) * T],
                                         0.5, x2[:, kt * T:(kt + 1) * T], ALU.mult, ALU.add)
            P.tick("v", i)
        P.free(xp)
        P.dump(tagp + "x_out", x2[:, :], "v")
        return None, x2

    if stop_after == "ln1_only":
        P.dump("x0", x[:, :], "v")
        xp = P.sbuf("xp_arena", [128, KD * T], F32, hi=True)
        emit_ln(P, xtiles(x), None, direct_dst=lambda kt: xp[:, kt * T:(kt + 1) * T])
        P.dump("ln1", xp[:, :], "v")
        P.barrier()
        P.dump("ln1b", xp[:, :], "v")
        P.stop_flag = True
        finish(P)
        return nc, P
    xp, x = run_ffn_macaron("f1_")
    if P.stop_flag:
        finish(P)
        return nc, P

    lnm = P.sbuf("mb_lnm", [128, KD * T], BF16, hi=True)

    def wrm(kt, tmp, g, b):
        P.wait("s", "v")
        i = P.S.activation(lnm[:, kt * T:(kt + 1) * T], tmp[:, :], AF.Identity, bias=b, scale=g)
        P.tick("s", i)

    ab, ab2 = emit_ln(P, xtiles(x), wrm, gamma_ap=P.par["gamma"], beta_ap=P.par["beta"],
                      fuse_double=True, keep_ab=True)
    P.dump("mb_lnm", lnm[:, :], "s")
    P.wait("y", "v")
    P.dma("y", P.dram["scr_x"].rearrange("(kt p) w -> p kt w", p=128),
          x[:, :].rearrange("p (kt w) -> p kt w", kt=KD))
    P.dma("y", P.dram["scr_ab"], ab2[:, :])
    P.barrier()
    P.free(x, ab, ab2)
    if P.stop("ln2"):
        finish(P)
        return nc, P
    emit_mamba(P, lnm, dump_pref="mb_")
    if P.stop_flag:
        finish(P)
        return nc, P
    x = P.sbuf("x_arena", [128, KD * T], F32, hi=True)
    P.dma("y", x[:, :].rearrange("p (kt w) -> p kt w", kt=KD),
          P.dram["scr_x"].rearrange("(kt p) w -> p kt w", p=128))
    ab2 = P.sbuf("ow_ab2", [128, 2 * T], F32)
    P.dma("y", ab2[:, :], P.dram["scr_ab"])
    P.barrier()
    emit_outw_and_residual(P, x, ab2)
    P.free(ab2)
    P.dump("mb_x_out", x[:, :], "v")
    P.barrier()
    if P.stop("mamba"):
        finish(P)
        return nc, P

    xp, x = run_ffn_macaron("f2_")
    if P.stop_flag:
        finish(P)
        return nc, P

    P.wait("y", "v")
    P.dma("y", P.dram["xo"].rearrange("(kt p) w -> p kt w", p=128),
          x[:, :].rearrange("p (kt w) -> p kt w", kt=KD))
    finish(P)
    return nc, P


def finish(P):
    P.barrier()
    for q in ["dy", "dg", "ds"]:
        P.SY.wait_ge(P.sem[q], P.cnt[q])



_BUILD_CACHE = {}


def kernel(**inputs):
    key = ("plain",)
    if key not in _BUILD_CACHE:
        _BUILD_CACHE[key] = build_program()
    nc, P = _BUILD_CACHE[key]
    consts = host_consts()
    x = np.asarray(inputs["L0_attn_x"], np.float32)
    base = {}
    for nm in WEIGHT_SPECS:
        base[nm] = _f(inputs[nm])
    base.update(consts)
    in_maps = []
    for b in range(8):
        m = dict(base)
        m["xt"] = np.ascontiguousarray(x[b].T)
        in_maps.append(m)
    res = run_bass_kernel_spmd(nc, in_maps, core_ids=list(range(8)))
    out = np.stack([np.asarray(r["xo"]).T for r in res.results])
    return out.astype(np.float32)
